# version 09
# speedup vs baseline: 1.3424x; 1.0621x over previous
import numpy as np
import ml_dtypes
from contextlib import ExitStack
import concourse.bass as bass
import concourse.mybir as mybir
from concourse.bass_utils import run_bass_kernel_spmd

F32 = mybir.dt.float32
BF16 = mybir.dt.bfloat16
AF = mybir.ActivationFunctionType
ALU = mybir.AluOpType
AX = mybir.AxisListType

NCORES = 8
SEG = 2048
B = 2
NTOK = B * SEG
HS = SEG + 3
D = 1024


class Prog:
    COMPUTE = ("tensor", "vector", "scalar", "gpsimd", "sync")

    def __init__(self, name="prog"):
        self.nc = bass.Bass("TRN2", target_bir_lowering=False, debug=False)
        self.ctx = ExitStack()
        self.name = name
        self.streams = {}
        for e in self.COMPUTE:
            self.streams[e] = dict(
                sem=self.ctx.enter_context(self.nc.semaphore(f"sc_{e}")),
                count=0, inc=1)
            self.streams["dma_" + e] = dict(
                sem=self.ctx.enter_context(self.nc.semaphore(f"sd_{e}")),
                count=0, inc=16)
        self.q = {e: [] for e in self.COMPUTE}
        self.acc = {}
        self.sbuf_bytes = 0
        self._consts = {}
        self._const_tile = None

    def constcol(self, value):
        value = float(value)
        if value in self._consts:
            return self._consts[value]
        if self._const_tile is None or self._const_next >= 16:
            self._const_tile = self.sb(f"cconst{len(self._consts)}", [128, 16], F32)
            self._const_next = 0
        tile, j = self._const_tile, self._const_next
        self._const_next += 1
        self.op("vector", lambda E: E.memset(tile[:, j:j + 1], value), writes=[tile])
        ap = tile[:, j:j + 1]
        self._consts[value] = ap
        return ap

    pfx = ""

    def sb(self, name, shape, dt=F32):
        elems = 1
        for s in shape[1:]:
            elems *= s
        self.sbuf_bytes += elems * mybir.dt.size(dt)
        ctx = getattr(self, "_sect_ctx", None) or self.ctx
        return ctx.enter_context(self.nc.sbuf_tensor(self.pfx + name, shape, dt))

    def ps(self, name, shape, dt=F32):
        ctx = getattr(self, "_sect_ctx", None) or self.ctx
        return ctx.enter_context(self.nc.psum_tensor(self.pfx + name, shape, dt))

    def section_open(self):
        assert getattr(self, "_sect_ctx", None) is None
        self._sect_ctx = ExitStack()

    def section_close(self):
        self._sect_ctx.close()
        self._sect_ctx = None
        self.barrier_all()

    def barrier_all(self):
        finals = [(k, st["count"]) for k, st in self.streams.items()
                  if st["count"] > 0]
        for e in self.COMPUTE:
            self.q[e].append((None, list(finals), None, None))

    def din(self, name, shape, dt=F32):
        return self.nc.dram_tensor(name, shape, dt, kind="ExternalInput").ap()

    def dout(self, name, shape, dt=F32):
        return self.nc.dram_tensor(name, shape, dt, kind="ExternalOutput").ap()

    def dtmp(self, name, shape, dt=F32):
        return self.nc.dram_tensor(name, shape, dt, kind="Internal").ap()

    @staticmethod
    def _name(x):
        if isinstance(x, tuple):
            return x[1]
        if isinstance(x, bass.AP) or hasattr(x, "tensor"):
            return x.tensor.name
        return x.name

    def _deps(self, reads, writes, stream):
        waits = {}
        def add(st, val):
            if st == stream == "tensor":
                return
            if st.startswith("dma_"):
                val = self.streams[st]["count"]
            if waits.get(st, -1) < val:
                waits[st] = val
        for x in reads:
            rec = self.acc.get(self._name(x))
            if rec and rec["w"]:
                add(*rec["w"])
        for x in writes:
            rec = self.acc.get(self._name(x))
            if rec:
                if rec["w"]:
                    add(*rec["w"])
                for st, val in rec["r"].items():
                    if st != stream:
                        add(st, val)
        return waits

    def _record(self, reads, writes, stream, val):
        for x in reads:
            rec = self.acc.setdefault(self._name(x), {"w": None, "r": {}})
            if rec["r"].get(stream, -1) < val:
                rec["r"][stream] = val
        for x in writes:
            rec = self.acc.setdefault(self._name(x), {"w": None, "r": {}})
            rec["w"] = (stream, val)
            rec["r"] = {}

    def op(self, eng, fn, reads=(), writes=(), dma=False):
        stream = ("dma_" + eng) if dma else eng
        st = self.streams[stream]
        waits = self._deps(reads, writes, stream)
        val = st["count"] + st["inc"]
        st["count"] = val
        self.q[eng].append((waits, fn, st["sem"], st["inc"]))
        self._record(reads, writes, stream, val)

    def dma(self, eng, out, in_, **kw):
        self.op(eng, lambda E: E.dma_start(out=out, in_=in_, **kw),
                reads=[in_], writes=[out], dma=True)

    def barrier_end(self):
        finals = [(k, s["count"]) for k, s in self.streams.items()
                  if s["count"] > 0 and k != "sync"]
        self.q["sync"].append((None, finals, None, None))

    def realize(self):
        block = self.ctx.enter_context(self.nc.Block())
        waited = {e: {} for e in self.COMPUTE}

        def make_body(eng_name):
            def body(E):
                for waits, fn, sem, inc in self.q[eng_name]:
                    if sem is None:
                        w = waited[eng_name]
                        for stname, v2 in fn:
                            if stname == eng_name:
                                continue
                            if w.get(stname, -1) < v2:
                                E.wait_ge(self.streams[stname]["sem"], v2)
                                w[stname] = v2
                        continue
                    if waits:
                        for stname, v in waits.items():
                            w = waited[eng_name]
                            if w.get(stname, -1) < v:
                                E.wait_ge(self.streams[stname]["sem"], v)
                                w[stname] = v
                    inst = fn(E)
                    inst.then_inc(sem, inc)
            return body

        for e in self.COMPUTE:
            if self.q[e] or e == "sync":
                getattr(block, "sync" if e == "sync" else e)(make_body(e))
        self.ctx.close()
        return self.nc


def make_identity(p, name="ident"):
    io = p.sb(name + "_io", [128, 128], F32)
    idf = p.sb(name + "_f", [128, 128], F32)
    idb = p.sb(name, [128, 128], BF16)
    p.op("gpsimd", lambda E: E.iota(io[:], pattern=[[1, 128]], channel_multiplier=-1,
                                    allow_small_or_imprecise_dtypes=True),
         writes=[io])
    p.op("vector", lambda E: E.tensor_scalar(idf[:], io[:], 0.0, None, ALU.is_equal),
         reads=[io], writes=[idf])
    p.op("vector", lambda E: E.tensor_copy(idb[:], idf[:]), reads=[idf], writes=[idb])
    return idb


def load_weight_bf16(p, name, w_dram, K, N, eng="gpsimd"):
    kc = K // 128
    sbw = p.sb(name, [128, kc * N], BF16)
    src = w_dram.rearrange("(kc p) n -> p kc n", p=128)
    dst = sbw[:].rearrange("p (kc n) -> p kc n", kc=kc)
    for k in range(kc):
        p.op(eng, lambda E, k=k: E.dma_start(out=dst[:, k, :], in_=src[:, k, :]),
             reads=[w_dram], writes=[sbw], dma=True)
    return sbw


def chunk(sbw, k, N, cols=None):
    if cols is None:
        return sbw[:, k * N:(k + 1) * N]
    c0, c1 = cols
    return sbw[:, k * N + c0:k * N + c1]


def emit_ln(p, y, x, scr, eps, dcols=D):
    s_sum = scr[:, 0:1]
    s_rstd = scr[:, 1:2]
    sq = scr[:, 2:2 + dcols]
    p.op("vector", lambda E: E.reduce_sum(s_sum, x, axis=AX.X), reads=[x], writes=[scr])
    p.op("vector", lambda E: E.tensor_scalar(s_sum, s_sum, 1.0 / dcols, None, ALU.mult),
         reads=[scr], writes=[scr])
    p.op("vector", lambda E: E.tensor_scalar(y, x, s_sum, None, ALU.subtract),
         reads=[x, scr], writes=[y])
    p.op("scalar", lambda E: E.activation(sq, y, AF.Square), reads=[y], writes=[scr])
    p.op("vector", lambda E: E.reduce_sum(s_rstd, sq, axis=AX.X), reads=[scr], writes=[scr])
    p.op("vector", lambda E: E.tensor_scalar(s_rstd, s_rstd, 1.0 / dcols, float(eps),
                                             ALU.mult, ALU.add), reads=[scr], writes=[scr])
    p.op("scalar", lambda E: E.activation(s_rstd, s_rstd, AF.Sqrt), reads=[scr], writes=[scr])
    p.op("vector", lambda E: E.reciprocal(s_rstd, s_rstd), reads=[scr], writes=[scr])
    p.op("vector", lambda E: E.tensor_scalar(y, y, s_rstd, None, ALU.mult),
         reads=[y, scr], writes=[y])


def emit_cast_transpose(p, dstT, src_f32, cast_buf, ident, psT, ncol=D):
    p.op("vector", lambda E: E.tensor_copy(cast_buf[:, 0:ncol], src_f32),
         reads=[src_f32], writes=[cast_buf])
    nch = ncol // 128
    for k in range(nch):
        p.op("tensor", lambda E, k=k: E.transpose(psT[:], cast_buf[:, k * 128:(k + 1) * 128],
                                                  ident[:]),
             reads=[cast_buf, ident], writes=[psT])
        p.op("scalar", lambda E, k=k: E.copy(dstT[:, k * 128:(k + 1) * 128], psT[:]),
             reads=[psT], writes=[dstT])


def emit_gemm_tokmajor(p, ps_bank, xT, wsb, K, N, cols, extra_reads=()):
    kc = K // 128
    c0, c1 = cols
    for k in range(kc):
        p.op("tensor",
             lambda E, k=k: E.matmul(ps_bank[:, 0:c1 - c0],
                                     xT[:, k * 128:(k + 1) * 128],
                                     wsb[:, k * N + c0:k * N + c1],
                                     start=(k == 0), stop=(k == kc - 1)),
             reads=[xT, wsb] + list(extra_reads), writes=[ps_bank])


def build_L1():
    p = Prog("L1")
    x0 = p.din("x0", [NTOK, D])
    wq_d = p.din("Wq", [D, 512]); wk_d = p.din("Wk", [D, 512])
    wv_d = p.din("Wv", [D, D]); wif_d = p.din("Wif", [D, 16])
    wog_d = p.din("Wog", [D, D])
    o_q = p.dout("q", [NTOK, 512]); o_k = p.dout("k", [NTOK, 512])
    o_v = p.dout("v", [NTOK, D]); o_if = p.dout("ifr", [NTOK, 16])
    o_o = p.dout("og", [NTOK, D])

    ident = make_identity(p)
    c_inv15 = p.constcol(1.0 / 15.0)
    c_15 = p.constcol(15.0)
    wq = load_weight_bf16(p, "wq", wq_d, D, 512)
    wk = load_weight_bf16(p, "wk", wk_d, D, 512)
    wv = load_weight_bf16(p, "wv", wv_d, D, D)
    wif = load_weight_bf16(p, "wif", wif_d, D, 16)
    wog = load_weight_bf16(p, "wog", wog_d, D, D)

    NB = 2
    xs = [p.sb(f"xt{i}", [128, D]) for i in range(NB)]
    ys = [p.sb(f"yy{i}", [128, D]) for i in range(NB)]
    cast = [p.sb(f"cast{i}", [128, D], BF16) for i in range(NB)]
    yT = [p.sb(f"yT{i}", [128, D], BF16) for i in range(NB)]
    scr = [p.sb(f"scr{i}", [128, D + 2]) for i in range(NB)]
    st_q = [p.sb(f"st_q{i}", [128, 512]) for i in range(NB)]
    st_k = [p.sb(f"st_k{i}", [128, 512]) for i in range(NB)]
    st_v = [p.sb(f"st_v{i}", [128, D]) for i in range(NB)]
    st_o = [p.sb(f"st_o{i}", [128, D]) for i in range(NB)]
    st_if = [p.sb(f"st_if{i}", [128, 16]) for i in range(NB)]
    tmp_t = [p.sb(f"tmp_t{i}", [128, D]) for i in range(NB)]

    psT = p.ps("psT", [128, 128], BF16)
    ps_a = p.ps("ps_a", [128, 512]); ps_b = p.ps("ps_b", [128, 512])
    ps_c = p.ps("ps_c", [128, 512]); ps_d = p.ps("ps_d", [128, 512])

    ntiles = NTOK // 128
    for t in range(ntiles):
        i = t % NB
        r0 = t * 128
        p.dma("sync", xs[i][:], x0[r0:r0 + 128, :])
        emit_ln(p, ys[i][:], xs[i][:], scr[i], 1e-5)
        emit_cast_transpose(p, yT[i], ys[i][:], cast[i], ident, psT)
        emit_gemm_tokmajor(p, ps_a, yT[i], wq, D, 512, (0, 512))
        p.op("vector", lambda E, i=i: E.tensor_copy(st_q[i][:], ps_a[:]),
             reads=[ps_a], writes=[st_q[i]])
        emit_gemm_tokmajor(p, ps_b, yT[i], wk, D, 512, (0, 512))
        p.op("vector", lambda E, i=i: E.tensor_copy(st_k[i][:], ps_b[:]),
             reads=[ps_b], writes=[st_k[i]])
        for h in range(2):
            bank = ps_c if h == 0 else ps_d
            emit_gemm_tokmajor(p, bank, yT[i], wv, D, D, (h * 512, (h + 1) * 512))
            p.op("vector", lambda E, i=i, h=h, bank=bank:
                 E.tensor_copy(st_v[i][:, h * 512:(h + 1) * 512], bank[:]),
                 reads=[bank], writes=[st_v[i]])
        emit_gemm_tokmajor(p, ps_a, yT[i], wif, D, 16, (0, 16))
        p.op("vector", lambda E, i=i: E.tensor_copy(st_if[i][:], ps_a[:, 0:16]),
             reads=[ps_a], writes=[st_if[i]])
        for h in range(2):
            bank = ps_b if h == 0 else ps_c
            emit_gemm_tokmajor(p, bank, yT[i], wog, D, D, (h * 512, (h + 1) * 512))
            p.op("scalar", lambda E, i=i, h=h, bank=bank:
                 E.activation(tmp_t[i][:, h * 512:(h + 1) * 512], bank[:], AF.Tanh,
                              scale=c_inv15),
                 reads=[bank, c_inv15], writes=[tmp_t[i]])
        p.op("scalar", lambda E, i=i: E.activation(st_o[i][:], tmp_t[i][:], AF.Sigmoid,
                                                   scale=c_15),
             reads=[tmp_t[i], c_15], writes=[st_o[i]])
        p.dma("gpsimd", o_q[r0:r0 + 128, :], st_q[i][:])
        p.dma("gpsimd", o_k[r0:r0 + 128, :], st_k[i][:])
        p.dma("gpsimd", o_v[r0:r0 + 128, :], st_v[i][:])
        p.dma("gpsimd", o_if[r0:r0 + 128, :], st_if[i][:])
        p.dma("gpsimd", o_o[r0:r0 + 128, :], st_o[i][:])
    p.barrier_end()
    return p.realize()


def run_L1(inputs):
    d = inputs
    x0 = d["L0_attn_x"]
    wif = np.ascontiguousarray(np.concatenate([d["L0_attn_Wi"], d["L0_attn_Wf"]], 1))
    nc = build_L1()
    in_maps = []
    for c in range(NCORES):
        xs = np.ascontiguousarray(
            x0[:, c * SEG:(c + 1) * SEG, :].reshape(NTOK, D))
        in_maps.append({
            "x0": xs, "Wq": d["L0_attn_Wq"], "Wk": d["L0_attn_Wk"],
            "Wv": d["L0_attn_Wv"], "Wif": wif, "Wog": d["L0_attn_Wog"],
        })
    res = run_bass_kernel_spmd(nc, in_maps, core_ids=list(range(NCORES)))
    outs = {}
    for key, w in [("q", 512), ("k", 512), ("v", D), ("ifr", 16), ("og", D)]:
        full = np.stack([res.results[c][key].reshape(B, SEG, w)
                         for c in range(NCORES)], axis=1)
        outs[key] = full.reshape(B, NCORES * SEG, w)
    return outs


def _kernel_impl(inputs):
    l1 = run_L1(inputs)
    l2 = run_L2(inputs, l1)
    l3 = run_L3(inputs, l1, l2)
    l4 = run_L4(inputs, l3)
    l4c = run_L4c(inputs, l4)
    l5 = run_L5(inputs, l3, l4c)
    return np.ascontiguousarray(l5["x4"].astype(np.float32))


SL = 16384
NSEG = 8
C = 128


def emit_mlstm_bh(p, t, bh, qT_d, kT_d, v_d, i_d, f_d, bicol, bfcol, lnw_rep, lnb_rep,
                  hn_out):
    ALU_ = ALU
    c_m1 = p.constcol(-1.0)
    c_1 = p.constcol(1.0)
    p.op("vector", lambda E: E.memset(t.c_f32[:], 0.0), writes=[t.c_f32])
    p.op("vector", lambda E: E.memset(t.mstate[:], 0.0), writes=[t.mstate])
    for seg in range(NSEG):
        s0 = seg * SEG
        p.dma("sync", t.qT[:], qT_d[bh, :, s0:s0 + SEG])
        p.dma("sync", t.kT[:], kT_d[bh, :, s0:s0 + SEG])
        p.dma("sync", t.v_sb[:].rearrange("p (cc d) -> p cc d", d=128),
              v_d[bh, s0:s0 + SEG, :].rearrange("(cc p) d -> p cc d", p=128))
        p.dma("sync", t.i_r[0:1, :], i_d[bh:bh + 1, s0:s0 + SEG])
        p.dma("sync", t.f_r[0:1, :], f_d[bh:bh + 1, s0:s0 + SEG])
        p.op("vector", lambda E: E.tensor_scalar(t.qTb[:], t.qT[:], 0.125, None, ALU_.mult),
             reads=[t.qT], writes=[t.qTb])
        p.op("vector", lambda E: E.tensor_copy(t.kTb[:], t.kT[:]), reads=[t.kT], writes=[t.kTb])
        p.op("vector", lambda E: E.tensor_copy(t.vb[:], t.v_sb[:]), reads=[t.v_sb], writes=[t.vb])
        R = lambda x: x[0:1, :]
        p.op("vector", lambda E: E.tensor_scalar(R(t.fb), R(t.f_r), bfcol, None, ALU_.add),
             reads=[t.f_r, bfcol], writes=[t.fb])
        p.op("vector", lambda E: E.tensor_scalar(R(t.g1), R(t.fb), -1.0, None, ALU_.mult),
             reads=[t.fb], writes=[t.g1])
        p.op("vector", lambda E: E.tensor_tensor(R(t.g2), R(t.fb), R(t.g1), ALU_.max),
             reads=[t.fb, t.g1], writes=[t.g2])
        p.op("scalar", lambda E: E.activation(R(t.g1), R(t.g2), AF.Exp, scale=c_m1[0:1, 0:1]),
             reads=[t.g2, c_m1], writes=[t.g1])
        p.op("scalar", lambda E: E.activation(R(t.g2), R(t.g1), AF.Ln, bias=c_1[0:1, 0:1]),
             reads=[t.g1, c_1], writes=[t.g2])
        p.op("vector", lambda E: E.tensor_scalar(R(t.g1), R(t.fb), 0.0, None, ALU_.min),
             reads=[t.fb], writes=[t.g1])
        p.op("vector", lambda E: E.tensor_tensor(R(t.flog), R(t.g1), R(t.g2), ALU_.subtract),
             reads=[t.g1, t.g2], writes=[t.flog])
        p.op("vector", lambda E: E.tensor_tensor_scan(R(t.Fc), R(t.ones), R(t.flog),
                                                      0.0, ALU_.mult, ALU_.add),
             reads=[t.ones, t.flog], writes=[t.Fc])
        p.op("vector", lambda E: E.tensor_scalar(R(t.g1), R(t.i_r), bicol, None, ALU_.add),
             reads=[t.i_r, bicol], writes=[t.g1])
        p.op("vector", lambda E: E.tensor_tensor(R(t.bg), R(t.g1), R(t.Fc), ALU_.subtract),
             reads=[t.g1, t.Fc], writes=[t.bg])
        p.op("vector", lambda E: E.tensor_tensor_scan(R(t.M), R(t.bg), R(t.bg),
                                                      t.mstate[0:1, 0:1], ALU_.max, ALU_.max),
             reads=[t.bg, t.mstate], writes=[t.M])
        for cc in range(SEG // C):
            c0 = cc * C
            if cc == 0:
                src = t.mstate[0:1, 0:1]
                rd = t.mstate
            else:
                src = t.M[0:1, c0 - 1:c0]
                rd = t.M
            p.op("vector", lambda E, c0=c0, src=src, rd=rd:
                 E.tensor_scalar(t.mp[0:1, c0:c0 + C], t.ones[0:1, c0:c0 + C], src,
                                 None, ALU_.mult),
                 reads=[t.ones, rd], writes=[t.mp])
            p.op("vector", lambda E, c0=c0:
                 E.tensor_scalar(t.nm[0:1, c0:c0 + C], t.ones[0:1, c0:c0 + C],
                                 t.M[0:1, c0 + C - 1:c0 + C], -1.0, ALU_.mult, ALU_.mult),
                 reads=[t.ones, t.M], writes=[t.nm])
        p.op("vector", lambda E: E.tensor_tensor(t.mstate[0:1, 0:1], t.Fc[0:1, SEG - 1:SEG],
                                                 t.M[0:1, SEG - 1:SEG], ALU_.add),
             reads=[t.Fc, t.M], writes=[t.mstate])
        for cc in range(SEG // C):
            c0 = cc * C
            sl = slice(c0, c0 + C)
            p.op("tensor", lambda E, sl=sl: E.transpose(t.psG[:, 0:1], t.bg[0:1, sl], t.identf[0:1, 0:1]),
                 reads=[t.bg, t.identf], writes=[t.psG])
            p.op("tensor", lambda E, sl=sl: E.transpose(t.psG[:, 1:2], t.mp[0:1, sl], t.identf[0:1, 0:1]),
                 reads=[t.mp, t.identf], writes=[t.psG])
            p.op("tensor", lambda E, sl=sl: E.transpose(t.psG[:, 2:3], t.nm[0:1, sl], t.identf[0:1, 0:1]),
                 reads=[t.nm, t.identf], writes=[t.psG])
            p.op("scalar", lambda E: E.copy(t.gcols[:, 0:3], t.psG[:, 0:3]),
                 reads=[t.psG], writes=[t.gcols])
            p.op("tensor", lambda E, sl=sl: E.matmul(t.psMrep[:], t.sel0[:], t.M[:, sl],
                                                     start=True, stop=True),
                 reads=[t.sel0, t.M], writes=[t.psMrep])
            p.op("scalar", lambda E: E.activation(t.W_sb[:], t.psMrep[:], AF.Exp,
                                                  scale=c_m1, bias=t.gcols[:, 0:1]),
                 reads=[t.psMrep, t.gcols, c_m1], writes=[t.W_sb])
            p.op("vector", lambda E: E.tensor_tensor(t.Wm[:], t.W_sb[:], t.trimask[:], ALU_.mult),
                 reads=[t.W_sb, t.trimask], writes=[t.Wm])
            p.op("tensor", lambda E, sl=sl: E.matmul(t.psSb[:], t.kTb[:, sl], t.qTb[:, sl],
                                                     start=True, stop=True),
                 reads=[t.kTb, t.qTb], writes=[t.psSb])
            p.op("vector", lambda E: E.tensor_tensor(t.DSb[:], t.Wm[:], t.psSb[:], ALU_.mult),
                 reads=[t.Wm, t.psSb], writes=[t.DSb])
            p.op("scalar", lambda E: E.activation(t.e_rep[:], t.psMrep[0:64, :], AF.Exp,
                                                  scale=c_m1[0:64, 0:1], bias=t.gcols[0:64, 1:2]),
                 reads=[t.psMrep, t.gcols, c_m1], writes=[t.e_rep])
            p.op("vector", lambda E, sl=sl: E.tensor_tensor(t.qs[:], t.qTb[0:64, sl],
                                                            t.e_rep[:], ALU_.mult),
                 reads=[t.qTb, t.e_rep], writes=[t.qs])
            p.op("vector", lambda E: E.tensor_copy(t.c_bf[:], t.c_f32[:]),
                 reads=[t.c_f32], writes=[t.c_bf])
            p.op("tensor", lambda E, cc=cc: E.matmul(t.psH[:], t.DSb[:],
                                                     t.vb[:, cc * 128:(cc + 1) * 128],
                                                     start=True, stop=False),
                 reads=[t.DSb, t.vb], writes=[t.psH])
            p.op("tensor", lambda E: E.matmul(t.psH[:], t.qs[:], t.c_bf[:],
                                              start=False, stop=True),
                 reads=[t.qs, t.c_bf], writes=[t.psH])
            p.op("scalar", lambda E: E.activation(t.wcol[:], t.gcols[:, 0:1], AF.Exp,
                                                  bias=t.gcols[:, 2:3]),
                 reads=[t.gcols], writes=[t.wcol])
            p.op("scalar", lambda E: E.activation(t.dec[:], t.gcols[0:64, 1:2], AF.Exp,
                                                  bias=t.gcols[0:64, 2:3]),
                 reads=[t.gcols], writes=[t.dec])
            p.op("tensor", lambda E, sl=sl: E.transpose(t.psKt[:], t.kT[:, sl], t.identf[0:64, 0:64]),
                 reads=[t.kT, t.identf], writes=[t.psKt])
            p.op("vector", lambda E: E.tensor_scalar(t.kw[:], t.psKt[:], t.wcol[:, 0:1],
                                                     None, ALU_.mult),
                 reads=[t.psKt, t.wcol], writes=[t.kw])
            p.op("tensor", lambda E, cc=cc: E.matmul(t.psC[:], t.kw[:],
                                                     t.vb[:, cc * 128:(cc + 1) * 128],
                                                     start=True, stop=True),
                 reads=[t.kw, t.vb], writes=[t.psC])
            p.op("vector", lambda E: E.tensor_scalar(t.c_f32[:], t.c_f32[:], t.dec[:, 0:1],
                                                     None, ALU_.mult),
                 reads=[t.c_f32, t.dec], writes=[t.c_f32])
            p.op("vector", lambda E: E.tensor_tensor(t.c_f32[:], t.c_f32[:], t.psC[:],
                                                     ALU_.add),
                 reads=[t.c_f32, t.psC], writes=[t.c_f32])
            scr = t.scr
            p.op("vector", lambda E: E.reduce_sum(scr[:, 0:1], t.psH[:], axis=AX.X),
                 reads=[t.psH], writes=[scr])
            p.op("vector", lambda E: E.tensor_scalar(scr[:, 0:1], scr[:, 0:1], 1.0 / 128,
                                                     None, ALU_.mult),
                 reads=[scr], writes=[scr])
            p.op("vector", lambda E: E.tensor_scalar(t.hbuf[:], t.psH[:], scr[:, 0:1],
                                                     None, ALU_.subtract),
                 reads=[t.psH, scr], writes=[t.hbuf])
            p.op("scalar", lambda E: E.activation(scr[:, 2:130], t.hbuf[:], AF.Square),
                 reads=[t.hbuf], writes=[scr])
            p.op("vector", lambda E: E.reduce_sum(scr[:, 1:2], scr[:, 2:130], axis=AX.X),
                 reads=[scr], writes=[scr])
            p.op("vector", lambda E: E.tensor_scalar(scr[:, 1:2], scr[:, 1:2], 1.0 / 128,
                                                     1e-6, ALU_.mult, ALU_.add),
                 reads=[scr], writes=[scr])
            p.op("scalar", lambda E: E.activation(scr[:, 1:2], scr[:, 1:2], AF.Sqrt),
                 reads=[scr], writes=[scr])
            p.op("vector", lambda E: E.reciprocal(scr[:, 1:2], scr[:, 1:2]),
                 reads=[scr], writes=[scr])
            p.op("vector", lambda E: E.scalar_tensor_tensor(t.hbuf[:], t.hbuf[:],
                                                            scr[:, 1:2], lnw_rep,
                                                            ALU_.mult, ALU_.mult),
                 reads=[t.hbuf, scr, lnw_rep], writes=[t.hbuf])
            p.op("vector", lambda E, c0=c0: E.tensor_tensor(
                t.hn_st[:, c0:c0 + C], t.hbuf[:], lnb_rep, ALU_.add),
                 reads=[t.hbuf, lnb_rep], writes=[t.hn_st])
        p.dma("gpsimd", hn_out[bh, s0:s0 + SEG, :].rearrange("(cc p) d -> p cc d", p=128),
              t.hn_st[:].rearrange("p (cc d) -> p cc d", d=128))


class _T:
    pass


def build_L2():
    p = Prog("L2")
    qT_d = p.din("qT", [2, 64, SL])
    kT_d = p.din("kT", [2, 64, SL])
    v_d = p.din("v2", [2, SL, 128])
    i_d = p.din("i2", [2, SL])
    f_d = p.din("f2", [2, SL])
    bi_d = p.din("bi2", [2, 1])
    bf_d = p.din("bf2", [2, 1])
    lnw_d = p.din("lnw2", [2, 128, 128])
    lnb_d = p.din("lnb2", [2, 128, 128])
    hn_out = p.dout("hn", [2, SL, 128])

    t = _T()
    identb = make_identity(p)
    t.identf = p.sb("identfp", [128, 128], F32)
    p.op("vector", lambda E: E.tensor_copy(t.identf[:], identb[:]),
         reads=[identb], writes=[t.identf])
    iop = p.sb("iop", [128, 128], F32)
    p.op("gpsimd", lambda E: E.iota(iop[:], pattern=[[0, 128]], channel_multiplier=1,
                                    allow_small_or_imprecise_dtypes=True), writes=[iop])
    t.sel0 = p.sb("sel0", [128, 128], F32)
    p.op("vector", lambda E: E.tensor_scalar(t.sel0[:], iop[:], 0.0, None, ALU.is_equal),
         reads=[iop], writes=[t.sel0])
    iop2 = p.sb("iop2", [128, 128], F32)
    p.op("gpsimd", lambda E: E.iota(iop2[:], pattern=[[1, 128]], channel_multiplier=-1,
                                    allow_small_or_imprecise_dtypes=True), writes=[iop2])
    t.trimask = p.sb("trimask", [128, 128], F32)
    p.op("vector", lambda E: E.tensor_scalar(t.trimask[:], iop2[:], 0.0, None, ALU.is_ge),
         reads=[iop2], writes=[t.trimask])

    t.qT = p.sb("qT_s", [64, SEG]); t.kT = p.sb("kT_s", [64, SEG])
    t.qTb = p.sb("qTb", [64, SEG], BF16); t.kTb = p.sb("kTb", [64, SEG], BF16)
    t.v_sb = p.sb("v_sb", [128, SEG]); t.vb = p.sb("vb", [128, SEG], BF16)
    t.i_r = p.sb("i_r", [128, SEG]); t.f_r = p.sb("f_r", [128, SEG])
    t.fb = p.sb("fb", [128, SEG]); t.g1 = p.sb("g1", [128, SEG])
    t.g2 = p.sb("g2", [128, SEG]); t.flog = p.sb("flog", [128, SEG])
    t.Fc = p.sb("Fcg", [128, SEG]); t.bg = p.sb("bgg", [128, SEG])
    t.M = p.sb("Mg", [128, SEG]); t.mp = p.sb("mpg", [128, SEG])
    t.nm = p.sb("nmg", [128, SEG]); t.ones = p.sb("ones1", [128, SEG])
    t.mstate = p.sb("mstate", [128, 1])
    t.gcols = p.sb("gcols", [128, 4])
    t.W_sb = p.sb("W_sb", [128, 128]); t.Wm = p.sb("Wm", [128, 128])
    t.DSb = p.sb("DSbb", [128, 128], BF16)
    t.e_rep = p.sb("e_rep", [64, 128], BF16)
    t.qs = p.sb("qs", [64, 128], BF16)
    t.c_f32 = p.sb("c_f32", [64, 128]); t.c_bf = p.sb("c_bfb", [64, 128], BF16)
    t.wcol = p.sb("wcol", [128, 1]); t.dec = p.sb("dec", [64, 1])
    t.kw = p.sb("kwb", [128, 64], BF16)
    t.scr = p.sb("scrL2", [128, 130])
    t.hbuf = p.sb("hbuf", [128, 128])
    t.hn_st = p.sb("hn_st", [128, SEG])
    bicol = p.sb("bicol", [1, 1]); bfcol = p.sb("bfcol", [1, 1])
    lnw_rep = p.sb("lnw_rep", [128, 128]); lnb_rep = p.sb("lnb_rep", [128, 128])

    t.psG = p.ps("psG", [128, 4], F32)
    t.psMrep = p.ps("psMrep", [128, 128], F32)
    t.psSb = p.ps("psSb", [128, 128], F32)
    t.psH = p.ps("psH", [128, 128], F32)
    t.psKt = p.ps("psKt", [128, 64], F32)
    t.psC = p.ps("psC", [64, 128], F32)

    p.op("vector", lambda E: E.memset(t.M[:], 0.0), writes=[t.M])
    p.op("vector", lambda E: E.memset(t.ones[:], 1.0), writes=[t.ones])

    for bh in range(2):
        p.dma("sync", bicol[0:1, :], bi_d[bh:bh + 1, :])
        p.dma("sync", bfcol[0:1, :], bf_d[bh:bh + 1, :])
        p.dma("sync", lnw_rep[:], lnw_d[bh, :, :])
        p.dma("sync", lnb_rep[:], lnb_d[bh, :, :])
        emit_mlstm_bh(p, t, bh, qT_d, kT_d, v_d, i_d, f_d,
                      bicol[0:1, 0:1], bfcol[0:1, 0:1], lnw_rep[:], lnb_rep[:], hn_out)
    p.barrier_end()
    return p.realize()


def run_L2(d, l1):
    q = l1["q"].reshape(B, SL, 8, 64)
    k = l1["k"].reshape(B, SL, 8, 64)
    v = l1["v"].reshape(B, SL, 8, 128)
    ifr = l1["ifr"]
    lnw, lnb = d["L0_attn_ln_w"], d["L0_attn_ln_b"]
    bi, bf = d["L0_attn_bi"], d["L0_attn_bf"]
    nc = build_L2()
    in_maps = []
    for c in range(NCORES):
        pairs = [(2 * c) // 8, (2 * c) % 8], [(2 * c + 1) // 8, (2 * c + 1) % 8]
        qT = np.stack([np.ascontiguousarray(q[b_, :, h_, :].T) for b_, h_ in pairs])
        kT = np.stack([np.ascontiguousarray(k[b_, :, h_, :].T) for b_, h_ in pairs])
        v2 = np.stack([np.ascontiguousarray(v[b_, :, h_, :]) for b_, h_ in pairs])
        i2 = np.stack([np.ascontiguousarray(ifr[b_, :, h_]) for b_, h_ in pairs])
        f2 = np.stack([np.ascontiguousarray(ifr[b_, :, 8 + h_]) for b_, h_ in pairs])
        bi2 = np.array([[bi[h_]] for b_, h_ in pairs], np.float32)
        bf2 = np.array([[bf[h_]] for b_, h_ in pairs], np.float32)
        lnw2 = np.stack([np.broadcast_to(lnw[h_][None, :], (128, 128)).copy()
                         for b_, h_ in pairs])
        lnb2 = np.stack([np.broadcast_to(lnb[h_][None, :], (128, 128)).copy()
                         for b_, h_ in pairs])
        in_maps.append({"qT": qT, "kT": kT, "v2": v2, "i2": i2, "f2": f2,
                        "bi2": bi2, "bf2": bf2, "lnw2": lnw2, "lnb2": lnb2})
    res = run_bass_kernel_spmd(nc, in_maps, core_ids=list(range(NCORES)))
    hn = np.zeros((B, SL, 8, 128), np.float32)
    for c in range(NCORES):
        for j in range(2):
            bh = 2 * c + j
            hn[bh // 8, :, bh % 8, :] = res.results[c]["hn"][j]
    return {"hn": hn.reshape(B, SL, 1024)}


GELU_C = 0.7978845608028654


def emit_gelu_tanh(p, out_bf, ps_in, b1col, t, c_gelu):
    p.op("vector", lambda E: E.tensor_scalar(t.zs[:], ps_in, b1col, None, ALU.add),
         reads=[ps_in, b1col], writes=[t.zs])
    p.op("scalar", lambda E: E.activation(t.z2[:], t.zs[:], AF.Square),
         reads=[t.zs], writes=[t.z2])
    p.op("vector", lambda E: E.tensor_tensor(t.z3[:], t.z2[:], t.zs[:], ALU.mult),
         reads=[t.z2, t.zs], writes=[t.z3])
    p.op("vector", lambda E: E.scalar_tensor_tensor(t.u[:], t.z3[:], 0.044715, t.zs[:],
                                                    ALU.mult, ALU.add),
         reads=[t.z3, t.zs], writes=[t.u])
    p.op("scalar", lambda E: E.activation(t.u[:], t.u[:], AF.Tanh, scale=c_gelu),
         reads=[t.u, c_gelu], writes=[t.u])
    p.op("vector", lambda E: E.tensor_tensor(t.z3[:], t.zs[:], t.u[:], ALU.mult),
         reads=[t.zs, t.u], writes=[t.z3])
    p.op("vector", lambda E: E.tensor_tensor(t.z2[:], t.z3[:], t.zs[:], ALU.add),
         reads=[t.z3, t.zs], writes=[t.z2])
    p.op("vector", lambda E: E.tensor_scalar(out_bf, t.z2[:], 0.5, None, ALU.mult),
         reads=[t.z2], writes=[out_bf])


def emit_L3(p, io):
    x0 = io["x0"]
    hn_d = io["hn"]
    og_d = io["og"]
    wout_d = io["Wout"]
    u1_d = io["U1"]; v1_d = io["V1"]
    u2_d = io["U2"]; v2_d = io["V2"]
    win_d = io["W_in"]
    b1s_d = io["b1s"]
    b2r_d = io["b2r"]
    lnwr_d = io["lnwr"]
    lnbr_d = io["lnbr"]
    o_x2 = io["x2"]
    o_y2 = io["y2"]
    o_xin = io["xinT"]
    o_zg = io["zgT"]

    ident = make_identity(p)
    c_gelu = p.constcol(GELU_C)
    wout = load_weight_bf16(p, "wout", wout_d, D, D)
    u1 = load_weight_bf16(p, "u1", u1_d, D, 256)
    v1 = load_weight_bf16(p, "v1", v1_d, 256, 4096)
    u2 = load_weight_bf16(p, "u2", u2_d, 4096, 256)
    v2 = load_weight_bf16(p, "v2", v2_d, 256, D)
    win = load_weight_bf16(p, "win", win_d, D, 4096)
    b1s = p.sb("b1s_s", [128, 32])
    b2r = p.sb("b2r_s", [128, D])
    lnwr = p.sb("lnwr_s", [128, D]); lnbr = p.sb("lnbr_s", [128, D])
    p.dma("sync", b1s[:], b1s_d)
    p.dma("sync", b2r[:], b2r_d)
    p.dma("sync", lnwr[:], lnwr_d)
    p.dma("sync", lnbr[:], lnbr_d)

    t = _T()
    NB = 1
    bufs = {}
    for nm, shape, dt in [("xb", [128, D + 2], F32), ("hnb", [128, D], F32),
                          ("ogb", [128, D], F32),
                          ("x1b", [128, D], F32), ("y1b", [128, D], F32),
                          ("castb", [128, D], BF16), ("xTb", [128, D], BF16),
                          ("x2b", [128, D], F32), ("y2b", [128, D], F32),
                          ("pbb", [128, 256], BF16), ("ptb", [128, 256], BF16),
                          ("htb", [128, 32 * 128], BF16),
                          ("qbb", [128, 256], BF16), ("qtb2", [128, 256], BF16),
                          ("xzst", [128, 16 * 128], F32), ("zgst", [128, 16 * 128], F32)]:
        bufs[nm] = [p.sb(f"{nm}{i}", shape, dt) for i in range(NB)]
    for nm, shape in [("zs", [128, 128]), ("z2", [128, 128]), ("z3", [128, 128]),
                      ("u", [128, 128])]:
        setattr(t, nm, p.sb(nm + "g", shape))
    psT = p.ps("psT", [128, 128], BF16)
    ps_a = p.ps("ps_a", [128, 512]); ps_b = p.ps("ps_b", [128, 512])
    psF = p.ps("psF", [128, 128])
    psQ = p.ps("psQ", [128, 256])
    psE = p.ps("psE", [128, 128])

    ntiles = NTOK // 128
    for ti in range(ntiles):
        i = ti % NB
        r0 = ti * 128
        xb = bufs["xb"][i]; hnb = bufs["hnb"][i]; ogb = bufs["ogb"][i]
        gb = hnb; x1b = bufs["x1b"][i]; y1b = bufs["y1b"][i]
        cast = bufs["castb"][i]; xT = bufs["xTb"][i]
        x2b = bufs["x2b"][i]; y2b = bufs["y2b"][i]; y2nb = y1b
        xnb = x1b; scr = xb
        pb = bufs["pbb"][i]; pt = bufs["ptb"][i]; ht = bufs["htb"][i]
        qb = bufs["qbb"][i]; qt = bufs["qtb2"][i]; wbuf = ogb
        xzst = bufs["xzst"][i]; zgst = bufs["zgst"][i]

        p.dma("sync", xb[:, 0:D], x0[r0:r0 + 128, :])
        p.dma("sync", hnb[:], hn_d[r0:r0 + 128, :])
        p.dma("sync", ogb[:], og_d[r0:r0 + 128, :])
        p.op("vector", lambda E, hnb=hnb, ogb=ogb, gb=gb:
             E.tensor_tensor(gb[:], hnb[:], ogb[:], ALU.mult),
             reads=[hnb, ogb], writes=[gb])
        emit_cast_transpose(p, xT, gb[:], cast, ident, psT)
        for h in range(2):
            bank = ps_a if h == 0 else ps_b
            emit_gemm_tokmajor(p, bank, xT, wout, D, D, (h * 512, (h + 1) * 512))
            p.op("vector", lambda E, bank=bank, h=h, x1b=x1b, xb=xb:
                 E.scalar_tensor_tensor(x1b[:, h * 512:(h + 1) * 512], bank[:], 1.0,
                                        xb[:, h * 512:(h + 1) * 512], ALU.mult, ALU.add),
                 reads=[bank, xb], writes=[x1b])
        emit_ln(p, y1b[:], x1b[:], scr, 1e-5)
        emit_cast_transpose(p, xT, y1b[:], cast, ident, psT)
        emit_gemm_tokmajor(p, psQ, xT, u1, D, 256, (0, 256))
        p.op("vector", lambda E, pb=pb: E.tensor_copy(pb[:], psQ[:]),
             reads=[psQ], writes=[pb])
        for kk in range(2):
            p.op("tensor", lambda E, kk=kk, pb=pb: E.transpose(
                psT[:], pb[:, kk * 128:(kk + 1) * 128], ident[:]),
                 reads=[pb, ident], writes=[psT])
            p.op("scalar", lambda E, kk=kk, pt=pt: E.copy(pt[:, kk * 128:(kk + 1) * 128], psT[:]),
                 reads=[psT], writes=[pt])
        for fch in range(32):
            for kk in range(2):
                p.op("tensor", lambda E, fch=fch, kk=kk, pt=pt: E.matmul(
                    psF[:], v1[:, kk * 4096 + fch * 128:kk * 4096 + (fch + 1) * 128],
                    pt[:, kk * 128:(kk + 1) * 128],
                    start=(kk == 0), stop=(kk == 1)),
                     reads=[v1, pt], writes=[psF])
            emit_gelu_tanh(p, ht[:, fch * 128:(fch + 1) * 128], psF[:],
                           b1s[:, fch:fch + 1], t, c_gelu)
        for kk in range(32):
            p.op("tensor", lambda E, kk=kk, ht=ht: E.matmul(
                psQ[:], ht[:, kk * 128:(kk + 1) * 128], u2[:, kk * 256:(kk + 1) * 256],
                start=(kk == 0), stop=(kk == 31)),
                 reads=[ht, u2], writes=[psQ])
        p.op("vector", lambda E, qb=qb: E.tensor_copy(qb[:], psQ[:]),
             reads=[psQ], writes=[qb])
        for kk in range(2):
            p.op("tensor", lambda E, kk=kk, qb=qb: E.transpose(
                psT[:], qb[:, kk * 128:(kk + 1) * 128], ident[:]),
                 reads=[qb, ident], writes=[psT])
            p.op("scalar", lambda E, kk=kk, qt=qt: E.copy(qt[:, kk * 128:(kk + 1) * 128], psT[:]),
                 reads=[psT], writes=[qt])
        p.op("vector", lambda E, wbuf=wbuf, x1b=x1b: E.tensor_tensor(
            wbuf[:], x1b[:], b2r[:], ALU.add),
             reads=[x1b, b2r], writes=[wbuf])
        for h in range(2):
            bank = ps_a if h == 0 else ps_b
            for kk in range(2):
                p.op("tensor", lambda E, kk=kk, h=h, bank=bank, qt=qt: E.matmul(
                    bank[:], qt[:, kk * 128:(kk + 1) * 128],
                    v2[:, kk * D + h * 512:kk * D + (h + 1) * 512],
                    start=(kk == 0), stop=(kk == 1)),
                     reads=[qt, v2], writes=[bank])
            p.op("vector", lambda E, bank=bank, h=h, x2b=x2b, wbuf=wbuf:
                 E.scalar_tensor_tensor(x2b[:, h * 512:(h + 1) * 512], bank[:], 1.0,
                                        wbuf[:, h * 512:(h + 1) * 512], ALU.mult, ALU.add),
                 reads=[bank, wbuf], writes=[x2b])
        emit_ln(p, y2b[:], x2b[:], scr, 1e-5)
        emit_ln(p, y2nb[:], y2b[:], scr, 1e-5)
        p.op("vector", lambda E, y2nb=y2nb, xnb=xnb: E.tensor_tensor(
            xnb[:], y2nb[:], lnwr[:], ALU.mult),
             reads=[y2nb, lnwr], writes=[xnb])
        p.op("vector", lambda E, xnb=xnb: E.tensor_tensor(xnb[:], xnb[:], lnbr[:], ALU.add),
             reads=[xnb, lnbr], writes=[xnb])
        emit_cast_transpose(p, xT, xnb[:], cast, ident, psT)
        for half, stage in ((0, xzst), (1, zgst)):
            for ech in range(16):
                e0 = half * 2048 + ech * 128
                for kk in range(8):
                    p.op("tensor", lambda E, e0=e0, kk=kk, xT=xT: E.matmul(
                        psE[:], win[:, kk * 4096 + e0:kk * 4096 + e0 + 128],
                        xT[:, kk * 128:(kk + 1) * 128],
                        start=(kk == 0), stop=(kk == 7)),
                         reads=[win, xT], writes=[psE])
                p.op("scalar", lambda E, ech=ech, stage=stage: E.copy(
                    stage[:, ech * 128:(ech + 1) * 128], psE[:]),
                     reads=[psE], writes=[stage])
        p.dma("gpsimd", o_x2[r0:r0 + 128, :], x2b[:])
        p.dma("gpsimd", o_y2[r0:r0 + 128, :], y2b[:])
        p.dma("gpsimd", o_xin[:, r0:r0 + 128].rearrange("(ech q) t -> q ech t", q=128),
              xzst[:].rearrange("q (ech t) -> q ech t", t=128))
        p.dma("gpsimd", o_zg[:, r0:r0 + 128].rearrange("(ech q) t -> q ech t", q=128),
              zgst[:].rearrange("q (ech t) -> q ech t", t=128))


def build_L3():
    p = Prog("build_L3")
    io = {
        "x0": p.din("x0", [NTOK, D]),
        "hn": p.din("hn", [NTOK, D]),
        "og": p.din("og", [NTOK, D]),
        "Wout": p.din("Wout", [D, D]),
        "U1": p.din("U1", [D, 256]),
        "V1": p.din("V1", [256, 4096]),
        "U2": p.din("U2", [4096, 256]),
        "V2": p.din("V2", [256, D]),
        "W_in": p.din("W_in", [D, 4096]),
        "b1s": p.din("b1s", [128, 32]),
        "b2r": p.din("b2r", [128, D]),
        "lnwr": p.din("lnwr", [128, D]),
        "lnbr": p.din("lnbr", [128, D]),
        "x2": p.dout("x2", [NTOK, D]),
        "y2": p.dout("y2", [NTOK, D]),
        "xinT": p.dout("xinT", [2048, NTOK]),
        "zgT": p.dout("zgT", [2048, NTOK]),
    }
    emit_L3(p, io)
    p.barrier_end()
    return p.realize()


def run_L3(d, l1, l2):
    x0 = d["L0_attn_x"]
    nc = build_L3()
    b1s = np.ascontiguousarray(d["L0_ffn_b1"].reshape(32, 128).T)
    b2r = np.ascontiguousarray(np.broadcast_to(d["L0_ffn_b2"][None, :], (128, D)))
    lnwr = np.ascontiguousarray(np.broadcast_to(d["L1_attn_ln_w"][None, :], (128, D)))
    lnbr = np.ascontiguousarray(np.broadcast_to(d["L1_attn_ln_b"][None, :], (128, D)))
    in_maps = []
    for c in range(NCORES):
        sl = slice(c * SEG, (c + 1) * SEG)
        in_maps.append({
            "x0": np.ascontiguousarray(x0[:, sl, :].reshape(NTOK, D)),
            "hn": np.ascontiguousarray(l2["hn"][:, sl, :].reshape(NTOK, D)),
            "og": np.ascontiguousarray(l1["og"][:, sl, :].reshape(NTOK, D)),
            "Wout": d["L0_attn_Wout"], "U1": d["L0_ffn_U1"], "V1": d["L0_ffn_V1"],
            "U2": d["L0_ffn_U2"], "V2": d["L0_ffn_V2"], "W_in": d["L1_attn_W_in"],
            "b1s": b1s, "b2r": b2r, "lnwr": lnwr, "lnbr": lnbr,
        })
    res = run_bass_kernel_spmd(nc, in_maps, core_ids=list(range(NCORES)))
    outs = {}
    for key, w in [("x2", D), ("y2", D)]:
        full = np.stack([res.results[c][key].reshape(B, SEG, w) for c in range(NCORES)],
                        axis=1)
        outs[key] = full.reshape(B, NCORES * SEG, w)
    for key in ("xinT", "zgT"):
        full = np.stack([res.results[c][key].reshape(2048, B, SEG)
                         for c in range(NCORES)], axis=2)
        outs[key] = full.reshape(2048, B, NCORES * SEG)
    return outs


def emit_softplus(p, out_ap, z_ap, t, c_m1, c_1, reads_extra=()):
    p.op("vector", lambda E: E.tensor_scalar(t.sp1[:], z_ap, -1.0, None, ALU.mult),
         reads=[z_ap] + list(reads_extra), writes=[t.sp1])
    p.op("vector", lambda E: E.tensor_tensor(t.sp2[:], z_ap, t.sp1[:], ALU.max),
         reads=[z_ap, t.sp1], writes=[t.sp2])
    p.op("scalar", lambda E: E.activation(t.sp1[:], t.sp2[:], AF.Exp, scale=c_m1),
         reads=[t.sp2, c_m1], writes=[t.sp1])
    p.op("scalar", lambda E: E.activation(t.sp2[:], t.sp1[:], AF.Ln, bias=c_1),
         reads=[t.sp1, c_1], writes=[t.sp2])
    p.op("vector", lambda E: E.tensor_scalar(t.sp1[:], z_ap, 0.0, None, ALU.max),
         reads=[z_ap], writes=[t.sp1])
    p.op("vector", lambda E: E.tensor_tensor(out_ap, t.sp1[:], t.sp2[:], ALU.add),
         reads=[t.sp1, t.sp2], writes=[out_ap])


def emit_L4(p, io):
    xin_d = io["xinh"]
    cw_d = io["cw4"]
    cb_d = io["cb4"]
    wxp_d = io["Wxp"]
    wdt_d = io["Wdt"]
    bdt_d = io["bdt"]
    o_dt = io["dtT"]
    o_B = io["Bf"]
    o_C = io["Cf"]
    o_xc = io["xcT"]

    t = _T()
    c_m1 = p.constcol(-1.0); c_1 = p.constcol(1.0)
    wxp = load_weight_bf16(p, "wxp", wxp_d, 2048, 96)
    wdt = p.sb("wdt", [64, 2048], BF16)
    p.op("gpsimd", lambda E: E.dma_start(out=wdt[:], in_=wdt_d), reads=[wdt_d],
         writes=[wdt], dma=True)
    cw = p.sb("cw_s", [128, 16 * 4]); cb = p.sb("cb_s", [128, 16])
    bdt = p.sb("bdt_s", [128, 16])
    p.dma("sync", cw[:].rearrange("p (e k) -> p e k", k=4), cw_d)
    p.dma("sync", cb[:], cb_d)
    p.dma("sync", bdt[:], bdt_d)

    xinb = p.sb("xinb", [128, HS])
    acc = p.sb("accc", [128, SEG])
    xcf = p.sb("xcf", [128, SEG])
    xcb = p.sb("xcbb", [128, 16 * SEG], BF16)
    dtrb = p.sb("dtrb", [64, SEG], BF16)
    bst = p.sb("bst", [16, SEG]); cst = p.sb("cst", [16, SEG])
    t.sp1 = p.sb("sp1", [128, 512]); t.sp2 = p.sb("sp2", [128, 512])
    dtt = p.sb("dtt", [128, 512])
    psDtr = p.ps("psDtr", [64, 512])
    psB = p.ps("psB", [16, 512]); psC2 = p.ps("psC2", [16, 512])
    psDt = p.ps("psDt", [128, 512])

    halo_sb = io.get("halo_sb")
    for b_ in range(B):
        tok0 = b_ * SEG
        for ech in range(16):
            if halo_sb is None:
                p.dma("sync", xinb[:], xin_d[ech * 128:(ech + 1) * 128, b_, :])
            else:
                p.dma("sync", xinb[:, 3:3 + SEG],
                      xin_d[ech * 128:(ech + 1) * 128, tok0:tok0 + SEG])
                p.op("vector", lambda E, ech=ech, b_=b_: E.tensor_copy(
                    xinb[:, 0:3], halo_sb[:, ech * 6 + b_ * 3:ech * 6 + b_ * 3 + 3]),
                     reads=[halo_sb], writes=[xinb])
            for k in range(4):
                wcol = lambda k=k, ech=ech: cw[:, ech * 4 + k:ech * 4 + k + 1]
                if k == 0:
                    p.op("vector", lambda E, k=k, wcol=wcol: E.tensor_scalar(
                        acc[:], xinb[:, k:k + SEG], wcol(), None, ALU.mult),
                         reads=[xinb, cw], writes=[acc])
                else:
                    p.op("vector", lambda E, k=k, wcol=wcol: E.scalar_tensor_tensor(
                        acc[:], xinb[:, k:k + SEG], wcol(), acc[:],
                        ALU.mult, ALU.add),
                         reads=[xinb, cw, acc], writes=[acc])
            p.op("scalar", lambda E, ech=ech: E.activation(xcf[:], acc[:], AF.Silu,
                                                           bias=cb[:, ech:ech + 1]),
                 reads=[acc, cb], writes=[xcf])
            p.dma("scalar", o_xc[ech * 128:(ech + 1) * 128, tok0:tok0 + SEG], xcf[:])
            p.op("vector", lambda E, ech=ech: E.tensor_copy(
                xcb[:, ech * SEG:(ech + 1) * SEG], xcf[:]),
                 reads=[xcf], writes=[xcb])
        for tt in range(SEG // 512):
            ts_ = slice(tt * 512, (tt + 1) * 512)
            for ech in range(16):
                st_ = (ech == 0); sp_ = (ech == 15)
                p.op("tensor", lambda E, ech=ech, ts_=ts_, st_=st_, sp_=sp_: E.matmul(
                    psDtr[:], wxp[:, ech * 96:ech * 96 + 64],
                    xcb[:, ech * SEG:(ech + 1) * SEG][:, ts_], start=st_, stop=sp_),
                     reads=[wxp, xcb], writes=[psDtr])
            for ech in range(16):
                st_ = (ech == 0); sp_ = (ech == 15)
                p.op("tensor", lambda E, ech=ech, ts_=ts_, st_=st_, sp_=sp_: E.matmul(
                    psB[:], wxp[:, ech * 96 + 64:ech * 96 + 80],
                    xcb[:, ech * SEG:(ech + 1) * SEG][:, ts_], start=st_, stop=sp_),
                     reads=[wxp, xcb], writes=[psB])
            for ech in range(16):
                st_ = (ech == 0); sp_ = (ech == 15)
                p.op("tensor", lambda E, ech=ech, ts_=ts_, st_=st_, sp_=sp_: E.matmul(
                    psC2[:], wxp[:, ech * 96 + 80:ech * 96 + 96],
                    xcb[:, ech * SEG:(ech + 1) * SEG][:, ts_], start=st_, stop=sp_),
                     reads=[wxp, xcb], writes=[psC2])
            p.op("vector", lambda E, ts_=ts_: E.tensor_copy(dtrb[:, ts_], psDtr[:]),
                 reads=[psDtr], writes=[dtrb])
            p.op("vector", lambda E, ts_=ts_: E.tensor_copy(bst[:, ts_], psB[:]),
                 reads=[psB], writes=[bst])
            p.op("vector", lambda E, ts_=ts_: E.tensor_copy(cst[:, ts_], psC2[:]),
                 reads=[psC2], writes=[cst])
        p.dma("scalar", o_B[:, tok0:tok0 + SEG], bst[:])
        p.dma("scalar", o_C[:, tok0:tok0 + SEG], cst[:])
        for m in range(16):
            for tt in range(SEG // 512):
                ts_ = slice(tt * 512, (tt + 1) * 512)
                p.op("tensor", lambda E, m=m, ts_=ts_: E.matmul(
                    psDt[:], wdt[:, m * 128:(m + 1) * 128], dtrb[:, ts_],
                    start=True, stop=True),
                     reads=[wdt, dtrb], writes=[psDt])
                p.op("vector", lambda E, m=m: E.tensor_scalar(
                    dtt[:], psDt[:], bdt[:, m:m + 1], None, ALU.add),
                     reads=[psDt, bdt], writes=[dtt])
                emit_softplus(p, dtt[:], dtt[:], t, c_m1, c_1)
                p.dma("scalar", o_dt[m * 128:(m + 1) * 128, tok0 + tt * 512:
                                     tok0 + (tt + 1) * 512], dtt[:])


def build_L4():
    p = Prog("build_L4")
    io = {
        "xinh": p.din("xinh", [2048, B, HS]),
        "cw4": p.din("cw4", [128, 16, 4]),
        "cb4": p.din("cb4", [128, 16]),
        "Wxp": p.din("Wxp", [2048, 96]),
        "Wdt": p.din("Wdt", [64, 2048]),
        "bdt": p.din("bdt", [128, 16]),
        "dtT": p.dout("dtT", [2048, NTOK]),
        "Bf": p.dout("Bf", [16, NTOK]),
        "Cf": p.dout("Cf", [16, NTOK]),
        "xcT": p.dout("xcT", [2048, NTOK]),
    }
    emit_L4(p, io)
    p.barrier_end()
    return p.realize()


def run_L4(d, l3):
    xinT = l3["xinT"]
    nc = build_L4()
    cw4 = np.ascontiguousarray(
        np.transpose(d["L1_attn_conv_w"].reshape(16, 128, 1, 4)[:, :, 0, :], (1, 0, 2)))
    cb4 = np.ascontiguousarray(d["L1_attn_conv_b"].reshape(16, 128).T)
    bdt = np.ascontiguousarray(d["L1_attn_b_dt"].reshape(16, 128).T)
    in_maps = []
    for c in range(NCORES):
        s0 = c * SEG
        halo = np.zeros((2048, B, 3), np.float32) if c == 0 else xinT[:, :, s0 - 3:s0]
        xinh = np.ascontiguousarray(np.concatenate([halo, xinT[:, :, s0:s0 + SEG]], 2))
        in_maps.append({"xinh": xinh, "cw4": cw4, "cb4": cb4,
                        "Wxp": d["L1_attn_W_xproj"], "Wdt": d["L1_attn_W_dt"],
                        "bdt": bdt})
    res = run_bass_kernel_spmd(nc, in_maps, core_ids=list(range(NCORES)))
    outs = {}
    for key, rows in [("dtT", 2048), ("Bf", 16), ("Cf", 16), ("xcT", 2048)]:
        full = np.stack([res.results[c][key].reshape(rows, B, SEG)
                         for c in range(NCORES)], axis=2)
        outs[key] = full.reshape(rows, B, NCORES * SEG)
    return outs


def emit_L4c(p, io):
    TC = 512
    dt_d = io["dts"]
    xc_d = io["xcs"]
    B_d = io["Bs"]
    C_d = io["Cs"]
    alog_d = io["Alog"]
    dpar_d = io["Dpar"]
    o_y = io["yssm"]

    t = _T()
    iop = p.sb("iop", [128, 128], F32)
    p.op("gpsimd", lambda E: E.iota(iop[:], pattern=[[0, 128]], channel_multiplier=1,
                                    allow_small_or_imprecise_dtypes=True), writes=[iop])
    sel = p.sb("seln", [128, 16 * 128], F32)
    for n in range(16):
        p.op("vector", lambda E, n=n: E.tensor_scalar(
            sel[:, n * 128:(n + 1) * 128], iop[:], float(n), None, ALU.is_equal),
             reads=[iop], writes=[sel])
    alog = p.sb("alog_s", [128, 2 * 16])
    p.dma("sync", alog[:].rearrange("p (d n) -> p d n", n=16),
          alog_d.rearrange("(d p) n -> p d n", p=128))
    A_sb = p.sb("A_sb", [128, 2 * 16])
    p.op("scalar", lambda E: E.activation(A_sb[:], alog[:], AF.Exp),
         reads=[alog], writes=[A_sb])
    p.op("vector", lambda E: E.tensor_scalar(A_sb[:], A_sb[:], -1.0, None, ALU.mult),
         reads=[A_sb], writes=[A_sb])
    dpar = p.sb("dpar_s", [128, 2])
    p.op("sync", lambda E: E.dma_start(out=dpar[:],
                                       in_=dpar_d.rearrange("(d p) o -> p (d o)", p=128),
                                       allow_slow_non_contiguous=True),
         reads=[dpar_d], writes=[dpar], dma=True)

    dtb = p.sb("dtb", [128, TC]); xcb2 = p.sb("xcb2", [128, TC])
    udt = p.sb("udt", [128, TC])
    dAb = p.sb("dAb", [128, TC]); bmb = p.sb("bmb", [128, TC])
    hb = p.sb("hb", [128, TC]); ytmp = p.sb("ytmp", [128, TC])
    yacc = p.sb("yacc", [128, TC])
    Bsb = p.sb("Bsb", [128, TC]); Csb = p.sb("Csb", [128, TC])
    hstate = p.sb("hstate", [128, 32])
    psBr = p.ps("psBr", [128, TC]); psCr = p.ps("psCr", [128, TC])
    p.op("vector", lambda E: E.memset(Bsb[:], 0.0), writes=[Bsb])
    p.op("vector", lambda E: E.memset(Csb[:], 0.0), writes=[Csb])

    for b_ in range(B):
        p.op("vector", lambda E: E.memset(hstate[:], 0.0), writes=[hstate])
        for tc in range(SL // TC):
            t0 = tc * TC
            p.dma("sync", Bsb[0:16, :], B_d[:, b_, t0:t0 + TC])
            p.dma("sync", Csb[0:16, :], C_d[:, b_, t0:t0 + TC])
            for dch in range(2):
                p.dma("sync", dtb[:], dt_d[dch * 128:(dch + 1) * 128, b_, t0:t0 + TC])
                p.dma("sync", xcb2[:], xc_d[dch * 128:(dch + 1) * 128, b_, t0:t0 + TC])
                p.op("vector", lambda E: E.tensor_tensor(udt[:], dtb[:], xcb2[:], ALU.mult),
                     reads=[dtb, xcb2], writes=[udt])
                for n in range(16):
                    scol = dch * 16 + n
                    p.op("scalar", lambda E, dch=dch, n=n: E.activation(
                        dAb[:], dtb[:], AF.Exp, scale=A_sb[:, dch * 16 + n:dch * 16 + n + 1]),
                         reads=[dtb, A_sb], writes=[dAb])
                    p.op("tensor", lambda E, n=n: E.matmul(
                        psBr[:], sel[:, n * 128:(n + 1) * 128], Bsb[:],
                        start=True, stop=True),
                         reads=[sel, Bsb], writes=[psBr])
                    p.op("vector", lambda E: E.tensor_tensor(bmb[:], udt[:], psBr[:],
                                                             ALU.mult),
                         reads=[udt, psBr], writes=[bmb])
                    p.op("vector", lambda E, scol=scol: E.tensor_tensor_scan(
                        hb[:], dAb[:], bmb[:], hstate[:, scol:scol + 1],
                        ALU.mult, ALU.add),
                         reads=[dAb, bmb, hstate], writes=[hb])
                    p.op("vector", lambda E, scol=scol: E.tensor_copy(
                        hstate[:, scol:scol + 1], hb[:, TC - 1:TC]),
                         reads=[hb], writes=[hstate])
                    p.op("tensor", lambda E, n=n: E.matmul(
                        psCr[:], sel[:, n * 128:(n + 1) * 128], Csb[:],
                        start=True, stop=True),
                         reads=[sel, Csb], writes=[psCr])
                    if n == 0:
                        p.op("vector", lambda E: E.tensor_tensor(yacc[:], hb[:], psCr[:],
                                                                 ALU.mult),
                             reads=[hb, psCr], writes=[yacc])
                    else:
                        p.op("vector", lambda E: E.tensor_tensor(ytmp[:], hb[:], psCr[:],
                                                                 ALU.mult),
                             reads=[hb, psCr], writes=[ytmp])
                        p.op("vector", lambda E: E.tensor_tensor(yacc[:], yacc[:], ytmp[:],
                                                                 ALU.add),
                             reads=[yacc, ytmp], writes=[yacc])
                p.op("vector", lambda E, dch=dch: E.scalar_tensor_tensor(
                    yacc[:], xcb2[:], dpar[:, dch:dch + 1], yacc[:], ALU.mult, ALU.add),
                     reads=[xcb2, dpar, yacc], writes=[yacc])
                p.dma("gpsimd", o_y[dch * 128:(dch + 1) * 128, b_, t0:t0 + TC], yacc[:])


def build_L4c():
    p = Prog("build_L4c")
    io = {
        "dts": p.din("dts", [256, B, SL]),
        "xcs": p.din("xcs", [256, B, SL]),
        "Bs": p.din("Bs", [16, B, SL]),
        "Cs": p.din("Cs", [16, B, SL]),
        "Alog": p.din("Alog", [256, 16]),
        "Dpar": p.din("Dpar", [256, 1]),
        "yssm": p.dout("yssm", [256, B, SL]),
    }
    emit_L4c(p, io)
    p.barrier_end()
    return p.realize()


def run_L4c(d, l4):
    nc = build_L4c()
    in_maps = []
    for c in range(NCORES):
        ch = slice(c * 256, (c + 1) * 256)
        in_maps.append({
            "dts": np.ascontiguousarray(l4["dtT"][ch]),
            "xcs": np.ascontiguousarray(l4["xcT"][ch]),
            "Bs": l4["Bf"], "Cs": l4["Cf"],
            "Alog": np.ascontiguousarray(d["L1_attn_A_log"][ch]),
            "Dpar": np.ascontiguousarray(d["L1_attn_D_param"][ch][:, None]),
        })
    res = run_bass_kernel_spmd(nc, in_maps, core_ids=list(range(NCORES)))
    full = np.concatenate([res.results[c]["yssm"] for c in range(NCORES)], axis=0)
    return {"yssm": full}


def emit_L5(p, io):
    x2_d = io["x2"]
    y2_d = io["y2"]
    ysm_d = io["ysm"]
    zg_d = io["zgT"]
    wo1_d = io["Wo1"]
    wr_d = io["Wr"]
    brr_d = io["brr"]
    w1_d = io["W1m"]
    b1s_d = io["b1s3"]
    w2_d = io["W2m"]
    b2r_d = io["b2r3"]
    lnw_d = io["lnw3r"]
    lnb_d = io["lnb3r"]
    o_x4 = io["x4"]

    w1sc = io.get("w1sc")
    if w1sc is None:
        w1sc = p.dtmp("w1sc", [8, 8, 128, 4096], BF16)
        w2sc = p.dtmp("w2sc", [8, 128, 32 * 1024], BF16)
        prestaged = False
    else:
        w2sc = io["w2sc"]
        prestaged = True
    x3sc = p.dtmp("x3sc", [NTOK, D])

    ident = make_identity(p)
    t = _T()
    arena = p.sb("arena", [128, 32768], BF16)
    wo1 = arena[:, 0:16 * D]
    w2s = arena[:, 0:32 * D]
    b1s = p.sb("b1s_s", [128, 8 * 32]); brr = p.sb("brr_s", [128, 8])
    lnwr = p.sb("lnwr_s", [128, D]); lnbr = p.sb("lnbr_s", [128, D])
    p.dma("sync", b1s[:], b1s_d)
    p.dma("sync", brr[:], brr_d)
    p.dma("sync", lnwr[:], lnw_d)
    p.dma("sync", lnbr[:], lnb_d)

    psT = p.ps("psT", [128, 128], BF16)
    ps1 = p.ps("ps1", [128, 512]); ps2 = p.ps("ps2", [128, 512])
    ps3 = p.ps("ps3", [128, 512]); ps4 = p.ps("ps4", [128, 512])
    psR = p.ps("psR", [128, 8])

    wrb = p.sb("wrb", [8, D], BF16)
    p.op("gpsimd", lambda E: E.dma_start(out=wrb[:], in_=wr_d), reads=[wr_d],
         writes=[wrb], dma=True)
    wrT = p.sb("wrT", [128, 8 * 8], BF16)
    for kk in range(8):
        p.op("tensor", lambda E, kk=kk: E.transpose(psT[:, 0:8],
                                                    wrb[:, kk * 128:(kk + 1) * 128],
                                                    ident[0:8, 0:8]),
             reads=[wrb, ident], writes=[psT])
        p.op("scalar", lambda E, kk=kk: E.copy(wrT[:, kk * 8:(kk + 1) * 8], psT[:, 0:8]),
             reads=[psT], writes=[wrT])

    hT = p.sb("hT5", [128, 32 * 512], BF16)
    if not prestaged:
        for e in range(8):
            p.op("gpsimd", lambda E, e=e: E.dma_start(
                out=arena[:].rearrange("q (g m) -> q g m", m=4096),
                in_=w1_d[e].rearrange("g q m -> q g m")),
                 reads=[w1_d], writes=[arena], dma=True)
            p.op("sync", lambda E, e=e: E.dma_start(
                out=w1sc[e].rearrange("g q m -> q g m"),
                in_=arena[:].rearrange("q (g m) -> q g m", m=4096)),
                 reads=[arena], writes=[w1sc], dma=True)
            for hh in range(4):
                p.op("gpsimd", lambda E, e=e, hh=hh: E.dma_start(
                    out=hT[:, 0:8192], in_=w2_d[e][:, hh * 8192:(hh + 1) * 8192]),
                     reads=[w2_d], writes=[hT], dma=True)
                p.op("sync", lambda E, e=e, hh=hh: E.dma_start(
                    out=w2sc[e][:, hh * 8192:(hh + 1) * 8192], in_=hT[:, 0:8192]),
                     reads=[hT], writes=[w2sc], dma=True)

    TB = 512
    x2t = p.sb("x2t", [128, D]); y2t = p.sb("y2t", [128, D])
    ygf = p.sb("ygf", [128, 16 * 128]); zgf = p.sb("zgf", [128, 16 * 128])
    ygb = p.sb("ygb", [128, 16 * 128], BF16)
    x3t = [p.sb(f"x3t{i}", [128, D]) for i in range(2)]
    acct = [p.sb(f"acct{i}", [128, D]) for i in range(4)]
    cwt = [p.sb(f"cwt{i}", [128, 8]) for i in range(4)]
    y3b = zgf[:, 1024:2048]
    y3c = ygb[:, 0:1024]
    y3T = p.sb("y3T", [128, 8 * TB], BF16)
    scr = ygf[:, 1022:2048]
    lgt = p.sb("lgt", [128, 8]); rsc = p.sb("rsc", [128, 8])
    w1big = [p.sb(f"w1b{i}", [128, 4096], BF16) for i in range(3)]
    b2s = [zgf[:, 0:1024]] * 2
    zb = y2t; znb = x2t

    ntb = NTOK // TB
    wo1_src = wo1_d.rearrange("(kc q) n -> q kc n", q=128)
    for bb in range(ntb):
        p.op("gpsimd", lambda E: E.dma_start(
            out=arena[:, 0:16 * D].rearrange("q (kc n) -> q kc n", kc=16),
            in_=wo1_src),
             reads=[wo1_d], writes=[arena], dma=True)
        for ti in range(4):
            r0 = bb * TB + ti * 128
            p.dma("sync", x2t[:], x2_d[r0:r0 + 128, :])
            p.dma("sync", y2t[:], y2_d[r0:r0 + 128, :])
            p.dma("sync", ygf[:].rearrange("q (e f) -> q e f", f=128),
                  ysm_d[:, r0:r0 + 128].rearrange("(e q) f -> q e f", q=128))
            p.dma("sync", zgf[:].rearrange("q (e f) -> q e f", f=128),
                  zg_d[:, r0:r0 + 128].rearrange("(e q) f -> q e f", q=128))
            p.op("scalar", lambda E: E.activation(zgf[:], zgf[:], AF.Silu),
                 reads=[zgf], writes=[zgf])
            p.op("vector", lambda E: E.tensor_tensor(ygf[:], ygf[:], zgf[:], ALU.mult),
                 reads=[ygf, zgf], writes=[ygf])
            p.op("vector", lambda E: E.tensor_copy(ygb[:], ygf[:]), reads=[ygf], writes=[ygb])
            p.op("vector", lambda E: E.tensor_tensor(x2t[:], x2t[:], y2t[:], ALU.add),
                 reads=[x2t, y2t], writes=[x2t])
            for h in range(2):
                bank = ps1 if h == 0 else ps2
                for kk in range(16):
                    p.op("tensor", lambda E, kk=kk, h=h, bank=bank: E.matmul(
                        bank[:], ygb[:, kk * 128:(kk + 1) * 128],
                        wo1[:, kk * D + h * 512:kk * D + (h + 1) * 512],
                        start=(kk == 0), stop=(kk == 15)),
                         reads=[ygb, arena], writes=[bank])
                p.op("vector", lambda E, bank=bank, h=h, ti=ti: E.scalar_tensor_tensor(
                    x3t[ti % 2][:, h * 512:(h + 1) * 512], bank[:], 1.0,
                    x2t[:, h * 512:(h + 1) * 512], ALU.mult, ALU.add),
                     reads=[bank, x2t], writes=[x3t[ti % 2]])
            emit_ln(p, y3b, x3t[ti % 2][:], scr, 1e-5)
            p.dma("gpsimd", x3sc[r0:r0 + 128, :], x3t[ti % 2][:])
            p.op("vector", lambda E: E.tensor_copy(y3c, y3b), reads=[y3b], writes=[y3c])
            for kk in range(8):
                p.op("tensor", lambda E, kk=kk: E.transpose(
                    psT[:], y3c[:, kk * 128:(kk + 1) * 128], ident[:]),
                     reads=[y3c, ident], writes=[psT])
                p.op("scalar", lambda E, kk=kk, ti=ti: E.copy(
                    y3T[:, kk * TB + ti * 128:kk * TB + (ti + 1) * 128], psT[:]),
                     reads=[psT], writes=[y3T])
            for kk in range(8):
                p.op("tensor", lambda E, kk=kk, ti=ti: E.matmul(
                    psR[:], y3T[:, kk * TB + ti * 128:kk * TB + (ti + 1) * 128],
                    wrT[:, kk * 8:(kk + 1) * 8], start=(kk == 0), stop=(kk == 7)),
                     reads=[y3T, wrT], writes=[psR])
            p.op("vector", lambda E: E.tensor_tensor(lgt[:], psR[:], brr[:], ALU.add),
                 reads=[psR, brr], writes=[lgt])
            p.op("vector", lambda E: E.reduce_max(scr[:, 0:1], lgt[:], axis=AX.X),
                 reads=[lgt], writes=[scr])
            p.op("vector", lambda E: E.tensor_scalar(rsc[:], lgt[:], scr[:, 0:1], None,
                                                     ALU.is_equal),
                 reads=[lgt, scr], writes=[rsc])
            p.op("vector", lambda E: E.scalar_tensor_tensor(lgt[:], rsc[:], -1e30, lgt[:],
                                                            ALU.mult, ALU.add),
                 reads=[rsc, lgt], writes=[lgt])
            p.op("vector", lambda E: E.reduce_max(scr[:, 1:2], lgt[:], axis=AX.X),
                 reads=[lgt], writes=[scr])
            p.op("vector", lambda E: E.tensor_tensor(scr[:, 2:3], scr[:, 1:2], scr[:, 0:1],
                                                     ALU.subtract),
                 reads=[scr], writes=[scr])
            p.op("scalar", lambda E: E.activation(scr[:, 2:3], scr[:, 2:3], AF.Exp),
                 reads=[scr], writes=[scr])
            p.op("vector", lambda E: E.tensor_scalar(scr[:, 3:4], scr[:, 2:3], 1.0, None,
                                                     ALU.add),
                 reads=[scr], writes=[scr])
            p.op("vector", lambda E: E.reciprocal(scr[:, 3:4], scr[:, 3:4]),
                 reads=[scr], writes=[scr])
            p.op("vector", lambda E: E.tensor_tensor(scr[:, 4:5], scr[:, 2:3], scr[:, 3:4],
                                                     ALU.mult),
                 reads=[scr], writes=[scr])
            p.op("vector", lambda E: E.tensor_scalar(lgt[:], lgt[:], scr[:, 1:2], None,
                                                     ALU.is_equal),
                 reads=[lgt, scr], writes=[lgt])
            p.op("vector", lambda E, ti=ti: E.tensor_scalar(cwt[ti][:], rsc[:],
                                                            scr[:, 3:4], None, ALU.mult),
                 reads=[rsc, scr], writes=[cwt[ti]])
            p.op("vector", lambda E, ti=ti: E.scalar_tensor_tensor(
                cwt[ti][:], lgt[:], scr[:, 4:5], cwt[ti][:], ALU.mult, ALU.add),
                 reads=[lgt, scr, cwt[ti]], writes=[cwt[ti]])
            p.op("vector", lambda E, ti=ti: E.memset(acct[ti][:], 0.0), writes=[acct[ti]])
        for e in range(8):
            p.dma("gpsimd", b2s[e % 2][:], b2r_d[e, :, :])
            for fch in range(32):
                g, j = fch // 4, fch % 4
                wbuf = w1big[g % 3]
                if j == 0:
                    p.op("sync", lambda E, e=e, g=g, wbuf=wbuf: E.dma_start(
                        out=wbuf[:], in_=w1sc[e, g]),
                         reads=[w1sc], writes=[wbuf], dma=True)
                bank = ps1 if fch % 2 == 0 else ps2
                for kk in range(8):
                    p.op("tensor", lambda E, kk=kk, j=j, wbuf=wbuf, bank=bank: E.matmul(
                        bank[:], wbuf[:, j * 1024 + kk * 128:j * 1024 + (kk + 1) * 128],
                        y3T[:, kk * TB:(kk + 1) * TB],
                        start=(kk == 0), stop=(kk == 7)),
                         reads=[wbuf, y3T], writes=[bank])
                p.op("scalar", lambda E, fch=fch, e=e, bank=bank: E.activation(
                    hT[:, fch * TB:(fch + 1) * TB], bank[:], AF.Gelu,
                    bias=b1s[:, e * 32 + fch:e * 32 + fch + 1]),
                     reads=[bank, b1s], writes=[hT])
            for hh in range(4):
                p.op("gpsimd", lambda E, e=e, hh=hh: E.dma_start(
                    out=w2s[:, hh * 8192:hh * 8192 + 8192],
                    in_=w2sc[e][:, hh * 8192:(hh + 1) * 8192]),
                     reads=[w2sc], writes=[arena], dma=True)
            for ti in range(4):
                for h in range(2):
                    bank = ps3 if h == 0 else ps4
                    for fch in range(32):
                        p.op("tensor", lambda E, fch=fch, ti=ti, h=h, bank=bank: E.matmul(
                            bank[:], hT[:, fch * TB + ti * 128:fch * TB + (ti + 1) * 128],
                            w2s[:, fch * D + h * 512:fch * D + (h + 1) * 512],
                            start=(fch == 0), stop=(fch == 31)),
                             reads=[hT, arena], writes=[bank])
                    p.op("vector", lambda E, bank=bank, ti=ti, h=h, e=e: E.scalar_tensor_tensor(
                        acct[ti][:, h * 512:(h + 1) * 512], bank[:],
                        cwt[ti][:, e:e + 1], acct[ti][:, h * 512:(h + 1) * 512],
                        ALU.mult, ALU.add),
                         reads=[bank, cwt[ti], acct[ti]], writes=[acct[ti]])
                p.op("vector", lambda E, ti=ti, e=e: E.scalar_tensor_tensor(
                    acct[ti][:], b2s[e % 2][:], cwt[ti][:, e:e + 1], acct[ti][:],
                    ALU.mult, ALU.add),
                     reads=[b2s[e % 2], cwt[ti], acct[ti]], writes=[acct[ti]])
        for ti in range(4):
            r0 = bb * TB + ti * 128
            p.dma("sync", x3t[ti % 2][:], x3sc[r0:r0 + 128, :])
            emit_ln(p, y3b, x3t[ti % 2][:], scr, 1e-5)
            p.op("vector", lambda E, ti=ti: E.tensor_tensor(zb[:], acct[ti][:], y3b,
                                                            ALU.add),
                 reads=[acct[ti], y3b], writes=[zb])
            emit_ln(p, znb[:], zb[:], scr, 1e-12)
            p.op("vector", lambda E: E.tensor_tensor(znb[:], znb[:], lnwr[:], ALU.mult),
                 reads=[znb, lnwr], writes=[znb])
            p.op("vector", lambda E: E.tensor_tensor(znb[:], znb[:], lnbr[:], ALU.add),
                 reads=[znb, lnbr], writes=[znb])
            p.op("vector", lambda E, ti=ti: E.tensor_tensor(znb[:], znb[:], x3t[ti % 2][:],
                                                            ALU.add),
                 reads=[znb, x3t[ti % 2]], writes=[znb])
            p.dma("gpsimd", o_x4[r0:r0 + 128, :], znb[:])


def emit_moe_stage(p, io, stg):
    w1_d = io["W1m"]; w2_d = io["W2m"]
    w1sc = io["w1sc"]; w2sc = io["w2sc"]
    for e in range(8):
        for hh in range(2):
            p.op("gpsimd", lambda E, e=e, hh=hh: E.dma_start(
                out=stg.rearrange("q (g m) -> q g m", m=4096),
                in_=w1_d[e][hh * 4:(hh + 1) * 4].rearrange("g q m -> q g m")),
                 reads=[w1_d], writes=[stg], dma=True)
            p.op("gpsimd", lambda E, e=e, hh=hh: E.dma_start(
                out=w1sc[e][hh * 4:(hh + 1) * 4].rearrange("g q m -> q g m"),
                in_=stg.rearrange("q (g m) -> q g m", m=4096)),
                 reads=[stg], writes=[w1sc], dma=True)
        for hh in range(4):
            p.op("gpsimd", lambda E, e=e, hh=hh: E.dma_start(
                out=stg[:, 0:8192], in_=w2_d[e][:, hh * 8192:(hh + 1) * 8192]),
                 reads=[w2_d], writes=[stg], dma=True)
            p.op("gpsimd", lambda E, e=e, hh=hh: E.dma_start(
                out=w2sc[e][:, hh * 8192:(hh + 1) * 8192], in_=stg[:, 0:8192]),
                 reads=[stg], writes=[w2sc], dma=True)


def build_L5():
    p = Prog("build_L5")
    io = {
        "x2": p.din("x2", [NTOK, D]),
        "y2": p.din("y2", [NTOK, D]),
        "ysm": p.din("ysm", [2048, NTOK]),
        "zgT": p.din("zgT", [2048, NTOK]),
        "Wo1": p.din("Wo1", [2048, D]),
        "Wr": p.din("Wr", [8, D]),
        "brr": p.din("brr", [128, 8]),
        "W1m": p.din("W1m", [8, 8, 128, 4096]),
        "b1s3": p.din("b1s3", [128, 8 * 32]),
        "W2m": p.din("W2m", [8, 128, 32 * 1024]),
        "b2r3": p.din("b2r3", [8, 128, D]),
        "lnw3r": p.din("lnw3r", [128, D]),
        "lnb3r": p.din("lnb3r", [128, D]),
        "x4": p.dout("x4", [NTOK, D]),
    }
    emit_L5(p, io)
    p.barrier_end()
    return p.realize()


def moe_w_host(d):
    w1h = np.ascontiguousarray(
        d["L1_ffn_W1"].reshape(8, 8, 4, 128, 8, 128).transpose(0, 1, 5, 2, 4, 3)
        .reshape(8, 8, 128, 4096))
    w2h = np.ascontiguousarray(
        d["L1_ffn_W2"].reshape(8, 1024, 32, 128).transpose(0, 3, 2, 1)
        .reshape(8, 128, 32 * 1024))
    return w1h, w2h


def run_L5(d, l3, l4c):
    nc = build_L5()
    w1h, w2h = moe_w_host(d)
    b1s3 = np.ascontiguousarray(
        np.transpose(d["L1_ffn_b1"].reshape(8, 32, 128), (2, 0, 1)).reshape(128, 256))
    brr = np.ascontiguousarray(np.broadcast_to(d["L1_ffn_br"][None, :], (128, 8)))
    b2r3 = np.ascontiguousarray(
        np.broadcast_to(d["L1_ffn_b2"][:, None, :], (8, 128, D)))
    lnw3r = np.ascontiguousarray(np.broadcast_to(d["L1_ffn_ln_w"][None, :], (128, D)))
    lnb3r = np.ascontiguousarray(np.broadcast_to(d["L1_ffn_ln_b"][None, :], (128, D)))
    ysm = l4c["yssm"]
    zgT = l3["zgT"]
    in_maps = []
    for c in range(NCORES):
        sl = slice(c * SEG, (c + 1) * SEG)
        in_maps.append({
            "x2": np.ascontiguousarray(l3["x2"][:, sl, :].reshape(NTOK, D)),
            "y2": np.ascontiguousarray(l3["y2"][:, sl, :].reshape(NTOK, D)),
            "ysm": np.ascontiguousarray(ysm[:, :, sl].reshape(2048, NTOK)),
            "zgT": np.ascontiguousarray(zgT[:, :, sl].reshape(2048, NTOK)),
            "Wo1": d["L1_attn_W_out"], "Wr": d["L1_ffn_Wr"], "brr": brr,
            "W1m": w1h, "b1s3": b1s3, "W2m": w2h,
            "b2r3": b2r3, "lnw3r": lnw3r, "lnb3r": lnb3r,
        })
    res = run_bass_kernel_spmd(nc, in_maps, core_ids=list(range(NCORES)))
    full = np.stack([res.results[c]["x4"].reshape(B, SEG, D) for c in range(NCORES)],
                    axis=1)
    return {"x4": full.reshape(B, NCORES * SEG, D)}


def kernel(**inputs):
    return run_fused(inputs)


LN8 = -2.0794415416798357

def emit_l1_fused(p, io):
    x0 = io["x0"]
    ident = make_identity(p)
    c_inv15 = p.constcol(1.0 / 15.0)
    c_15 = p.constcol(15.0)
    wq = load_weight_bf16(p, "wq", io["Wq"], D, 512)
    wk = load_weight_bf16(p, "wk", io["Wk"], D, 512)
    wv = load_weight_bf16(p, "wv", io["Wv"], D, D)
    wif = load_weight_bf16(p, "wif", io["Wif"], D, 16)
    wog = load_weight_bf16(p, "wog", io["Wog"], D, D)
    xs = p.sb("xs", [128, D]); ys = p.sb("ys", [128, D])
    cast = p.sb("cast", [128, D], BF16)
    scr = p.sb("scr", [128, D + 2])
    y0T = p.sb("y0T", [128, 8 * 512], BF16)
    stq = p.sb("stq", [128, 512], BF16); stk = p.sb("stk", [128, 512], BF16)
    stv = p.sb("stv", [128, D], BF16)
    sto = p.sb("sto", [128, D]); tmp = p.sb("tmpg", [128, D])
    stif = p.sb("stif", [16, 512])
    psT = p.ps("psT", [128, 128], BF16)
    ps_a = p.ps("ps_a", [128, 512]); ps_b = p.ps("ps_b", [128, 512])
    psI = p.ps("psI", [16, 512])

    for bb in range(NTOK // 512):
        t0 = bb * 512
        for ti in range(4):
            r0 = t0 + ti * 128
            p.dma("sync", xs[:], x0[r0:r0 + 128, :])
            emit_ln(p, ys[:], xs[:], scr, 1e-5)
            p.op("vector", lambda E: E.tensor_copy(cast[:], ys[:]), reads=[ys], writes=[cast])
            for kk in range(8):
                p.op("tensor", lambda E, kk=kk: E.transpose(psT[:], cast[:, kk * 128:(kk + 1) * 128], ident[:]),
                     reads=[cast, ident], writes=[psT])
                p.op("scalar", lambda E, kk=kk, ti=ti: E.copy(
                    y0T[:, kk * 512 + ti * 128:kk * 512 + (ti + 1) * 128], psT[:]),
                     reads=[psT], writes=[y0T])
        for pr in range(4):
            for which, wsb, scrout in (("q", wq, io["qT_scr"]), ("k", wk, io["kT_scr"])):
                st = stq if which == "q" else stk
                for kk in range(8):
                    p.op("tensor", lambda E, kk=kk, pr=pr, wsb=wsb: E.matmul(
                        ps_a[:], wsb[:, kk * 512 + pr * 128:kk * 512 + (pr + 1) * 128],
                        y0T[:, kk * 512:(kk + 1) * 512],
                        start=(kk == 0), stop=(kk == 7)),
                         reads=[wsb, y0T], writes=[ps_a])
                p.op("vector", lambda E, st=st: E.tensor_copy(st[:], ps_a[:]),
                     reads=[ps_a], writes=[st])
                p.dma("gpsimd", scrout[pr, :, t0:t0 + 512], st[:])
        for kk in range(8):
            p.op("tensor", lambda E, kk=kk: E.matmul(
                psI[:], wif[:, kk * 16:(kk + 1) * 16], y0T[:, kk * 512:(kk + 1) * 512],
                start=(kk == 0), stop=(kk == 7)),
                 reads=[wif, y0T], writes=[psI])
        p.op("vector", lambda E: E.tensor_copy(stif[:], psI[:]), reads=[psI], writes=[stif])
        p.dma("gpsimd", io["if_scr"][:, t0:t0 + 512], stif[:])
        for ti in range(4):
            r0 = t0 + ti * 128
            for h in range(2):
                bank = ps_a if h == 0 else ps_b
                for kk in range(8):
                    p.op("tensor", lambda E, kk=kk, ti=ti, h=h, bank=bank: E.matmul(
                        bank[:], y0T[:, kk * 512 + ti * 128:kk * 512 + (ti + 1) * 128],
                        wv[:, kk * D + h * 512:kk * D + (h + 1) * 512],
                        start=(kk == 0), stop=(kk == 7)),
                         reads=[y0T, wv], writes=[bank])
                p.op("vector", lambda E, bank=bank, h=h: E.tensor_copy(
                    stv[:, h * 512:(h + 1) * 512], bank[:]), reads=[bank], writes=[stv])
            p.dma("gpsimd", io["v_scr"][r0:r0 + 128, :], stv[:])
            for h in range(2):
                bank = ps_a if h == 0 else ps_b
                for kk in range(8):
                    p.op("tensor", lambda E, kk=kk, ti=ti, h=h, bank=bank: E.matmul(
                        bank[:], y0T[:, kk * 512 + ti * 128:kk * 512 + (ti + 1) * 128],
                        wog[:, kk * D + h * 512:kk * D + (h + 1) * 512],
                        start=(kk == 0), stop=(kk == 7)),
                         reads=[y0T, wog], writes=[bank])
                p.op("scalar", lambda E, bank=bank, h=h: E.activation(
                    tmp[:, h * 512:(h + 1) * 512], bank[:], AF.Tanh, scale=c_inv15),
                     reads=[bank, c_inv15], writes=[tmp])
            p.op("scalar", lambda E: E.activation(sto[:], tmp[:], AF.Sigmoid, scale=c_15),
                 reads=[tmp, c_15], writes=[sto])
            p.dma("gpsimd", io["og_scr"][r0:r0 + 128, :], sto[:])


def emit_mlstm_fused(p, io, phase, g):
    c_m1 = p.constcol(-1.0)
    c_1 = p.constcol(1.0)
    t = _T()
    identb = make_identity(p)
    t.identf = p.sb("identfp", [128, 128], F32)
    p.op("vector", lambda E: E.tensor_copy(t.identf[:], identb[:]),
         reads=[identb], writes=[t.identf])
    iop = p.sb("iop", [128, 128], F32)
    p.op("gpsimd", lambda E: E.iota(iop[:], pattern=[[0, 128]], channel_multiplier=1,
                                    allow_small_or_imprecise_dtypes=True), writes=[iop])
    t.sel0 = p.sb("sel0", [128, 128], F32)
    p.op("vector", lambda E: E.tensor_scalar(t.sel0[:], iop[:], 0.0, None, ALU.is_equal),
         reads=[iop], writes=[t.sel0])
    iop2 = p.sb("iop2", [128, 128], F32)
    p.op("gpsimd", lambda E: E.iota(iop2[:], pattern=[[1, 128]], channel_multiplier=-1,
                                    allow_small_or_imprecise_dtypes=True), writes=[iop2])
    t.trimask = p.sb("trimask", [128, 128], F32)
    p.op("vector", lambda E: E.tensor_scalar(t.trimask[:], iop2[:], 0.0, 0.125,
                                             ALU.is_ge, ALU.mult),
         reads=[iop2], writes=[t.trimask])

    t.qTb = p.sb("qTb", [64, SEG], BF16); t.kTb = p.sb("kTb", [64, SEG], BF16)
    t.vb = p.sb("vb", [128, SEG], BF16)
    t.i_r = p.sb("i_r", [1, SEG]); t.f_r = p.sb("f_r", [1, SEG])
    t.fb = p.sb("fb", [1, SEG]); t.g1 = p.sb("g1", [1, SEG])
    t.g2 = p.sb("g2", [1, SEG]); t.flog = p.sb("flog", [1, SEG])
    t.Fc = p.sb("Fcg", [1, SEG]); t.bg = p.sb("bgg", [1, SEG])
    t.M = p.sb("Mg", [128, SEG]); t.mp = p.sb("mpg", [1, SEG])
    t.nm = p.sb("nmg", [1, SEG])
    t.mstate = p.sb("mstate", [1, 1])
    t.gcols = p.sb("gcols", [128, 6])
    t.c_f32 = p.sb("c_f32", [64, 128]); t.c_bf = p.sb("c_bfb", [64, 128], BF16)
    t.wcol = p.sb("wcol", [128, 1]); t.dec = p.sb("dec", [64, 1])
    t.kw = p.sb("kwb", [128, 64], BF16)
    t.kf = p.sb("kf", [128, 64])
    t.psG = p.ps("psG", [128, 8], F32)
    t.psKt = p.ps("psKt", [128, 64], BF16)
    t.psC = p.ps("psC", [64, 128], F32)
    if phase == 2:
        t.W_sb = p.sb("W_sb", [128, 128]); t.Wm = p.sb("Wm", [128, 128])
        t.DSb = p.sb("DSbb", [128, 128], BF16)
        t.e_rep = p.sb("e_rep", [64, 128], BF16)
        t.qs = p.sb("qs", [64, 128], BF16)
        t.scr = p.sb("scrM", [128, 130])
        t.hbuf = p.sb("hbuf", [128, 128])
        t.hn_st = p.sb("hn_st", [128, SEG])
        t.lnw_rep = p.sb("lnw_rep", [128, 128]); t.lnb_rep = p.sb("lnb_rep", [128, 128])
        t.psMrep = p.ps("psMrep", [128, 128], F32)
        t.psSb = p.ps("psSb", [128, 128], F32)
        t.psH = p.ps("psH", [128, 128], F32)
    t.sc2 = p.sb("sc2", [1, 2])
    t.zro = p.sb("zro", [1, SEG])
    p.op("vector", lambda E: E.memset(t.M[:], 0.0), writes=[t.M])
    p.op("vector", lambda E: E.memset(t.zro[:], 0.0), writes=[t.zro])

    zrow = t.zro[0:1, :]
    for bh in range(16):
        b_, h_ = bh // 8, bh % 8
        pr, half = h_ // 2, h_ % 2
        sl_t = slice(b_ * SEG, (b_ + 1) * SEG)
        p.dma("sync", t.qTb[:], io["qT_scr"][pr, half * 64:half * 64 + 64, sl_t])
        p.dma("sync", t.kTb[:], io["kT_scr"][pr, half * 64:half * 64 + 64, sl_t])
        p.dma("sync", t.vb[:].rearrange("p (cc d) -> p cc d", d=128),
              io["v_scr"][sl_t, h_ * 128:(h_ + 1) * 128].rearrange(
                  "(cc p) d -> p cc d", p=128))
        p.dma("sync", t.i_r[0:1, :], io["if_scr"][h_:h_ + 1, sl_t])
        p.dma("sync", t.f_r[0:1, :], io["if_scr"][8 + h_:9 + h_, sl_t])
        if phase == 2:
            p.dma("sync", t.lnw_rep[:], io["lnw8"][h_, :, :])
            p.dma("sync", t.lnb_rep[:], io["lnb8"][h_, :, :])
        if phase == 1:
            p.op("vector", lambda E: E.memset(t.c_f32[:], 0.0), writes=[t.c_f32])
            p.op("vector", lambda E: E.memset(t.mstate[:], -1e30), writes=[t.mstate])
        else:
            p.op("vector", lambda E, bh=bh: E.tensor_copy(
                t.c_f32[:], g.c_in_sb[:, bh * 128:(bh + 1) * 128]),
                 reads=[g.c_in_sb], writes=[t.c_f32])
            p.op("vector", lambda E, bh=bh: E.tensor_copy(
                t.mstate[0:1, :], g.m_in_sb[0:1, bh:bh + 1]),
                 reads=[g.m_in_sb], writes=[t.mstate])
        R = lambda x: x[0:1, :]
        bicol = io["bif_sb"][h_:h_ + 1, 0:1] if False else None
        p.op("vector", lambda E, h_=h_: E.tensor_scalar(
            R(t.fb), R(t.f_r), g.bif[0:1, 8 + h_:9 + h_], None, ALU.add),
             reads=[t.f_r, g.bif], writes=[t.fb])
        p.op("vector", lambda E: E.tensor_scalar(R(t.g1), R(t.fb), -1.0, None, ALU.mult),
             reads=[t.fb], writes=[t.g1])
        p.op("vector", lambda E: E.tensor_tensor(R(t.g2), R(t.fb), R(t.g1), ALU.max),
             reads=[t.fb, t.g1], writes=[t.g2])
        p.op("scalar", lambda E: E.activation(R(t.g1), R(t.g2), AF.Exp, scale=c_m1[0:1, 0:1]),
             reads=[t.g2, c_m1], writes=[t.g1])
        p.op("scalar", lambda E: E.activation(R(t.g2), R(t.g1), AF.Ln, bias=c_1[0:1, 0:1]),
             reads=[t.g1, c_1], writes=[t.g2])
        p.op("vector", lambda E: E.tensor_scalar(R(t.g1), R(t.fb), 0.0, None, ALU.min),
             reads=[t.fb], writes=[t.g1])
        p.op("vector", lambda E: E.tensor_tensor(R(t.flog), R(t.g1), R(t.g2), ALU.subtract),
             reads=[t.g1, t.g2], writes=[t.flog])
        p.op("vector", lambda E: E.tensor_tensor_scan(R(t.Fc), R(t.flog), zrow,
                                                      0.0, ALU.add, ALU.add),
             reads=[t.flog, t.zro], writes=[t.Fc])
        p.op("vector", lambda E, h_=h_: E.tensor_scalar(
            R(t.g1), R(t.i_r), g.bif[0:1, h_:h_ + 1], None, ALU.add),
             reads=[t.i_r, g.bif], writes=[t.g1])
        p.op("vector", lambda E: E.tensor_tensor(R(t.bg), R(t.g1), R(t.Fc), ALU.subtract),
             reads=[t.g1, t.Fc], writes=[t.bg])
        p.op("vector", lambda E: E.tensor_tensor_scan(t.M[0:1, :], R(t.bg), R(t.bg),
                                                      t.mstate[0:1, 0:1], ALU.max, ALU.max),
             reads=[t.bg, t.mstate], writes=[t.M])
        for cc in range(SEG // C):
            c0 = cc * C
            if cc == 0:
                src_ap, src_rd = t.mstate[0:1, 0:1], t.mstate
            else:
                src_ap, src_rd = t.M[0:1, c0 - 1:c0], t.M
            p.op("vector", lambda E, c0=c0, src_ap=src_ap, src_rd=src_rd:
                 E.tensor_scalar(t.mp[0:1, c0:c0 + C], zrow[:, c0:c0 + C], src_ap,
                                 None, ALU.add),
                 reads=[t.zro, src_rd], writes=[t.mp])
            p.op("vector", lambda E, c0=c0: E.tensor_scalar(
                t.nm[0:1, c0:c0 + C], zrow[:, c0:c0 + C],
                t.M[0:1, c0 + C - 1:c0 + C], -1.0, ALU.add, ALU.mult),
                 reads=[t.zro, t.M], writes=[t.nm])
        p.op("vector", lambda E: E.tensor_tensor(t.sc2[0:1, 0:1], t.Fc[0:1, SEG - 1:SEG],
                                                 t.M[0:1, SEG - 1:SEG], ALU.add),
             reads=[t.Fc, t.M], writes=[t.sc2])
        for cc in range(SEG // C):
            c0 = cc * C
            sl = slice(c0, c0 + C)
            p.op("tensor", lambda E, sl=sl: E.transpose(t.psG[:, 0:1], t.bg[0:1, sl],
                                                        t.identf[0:1, 0:1]),
                 reads=[t.bg, t.identf], writes=[t.psG])
            p.op("tensor", lambda E, sl=sl: E.transpose(t.psG[:, 1:2], t.mp[0:1, sl],
                                                        t.identf[0:1, 0:1]),
                 reads=[t.mp, t.identf], writes=[t.psG])
            p.op("tensor", lambda E, sl=sl: E.transpose(t.psG[:, 2:3], t.nm[0:1, sl],
                                                        t.identf[0:1, 0:1]),
                 reads=[t.nm, t.identf], writes=[t.psG])
            p.op("scalar", lambda E: E.copy(t.gcols[:, 0:3], t.psG[:, 0:3]),
                 reads=[t.psG], writes=[t.gcols])
            if phase == 2:
                p.op("vector", lambda E: E.tensor_scalar(t.gcols[:, 3:4], t.gcols[:, 1:2],
                                                         LN8, None, ALU.add),
                     reads=[t.gcols], writes=[t.gcols])
            p.op("scalar", lambda E: E.activation(t.wcol[:], t.gcols[:, 0:1], AF.Exp,
                                                  bias=t.gcols[:, 2:3]),
                 reads=[t.gcols], writes=[t.wcol])
            p.op("scalar", lambda E: E.activation(t.dec[:], t.gcols[0:64, 1:2], AF.Exp,
                                                  bias=t.gcols[0:64, 2:3]),
                 reads=[t.gcols], writes=[t.dec])
            p.op("tensor", lambda E, sl=sl: E.transpose(t.psKt[:], t.kTb[:, sl],
                                                        identb[0:64, 0:64]),
                 reads=[t.kTb, identb], writes=[t.psKt])
            p.op("vector", lambda E: E.tensor_copy(t.kf[:], t.psKt[:]),
                 reads=[t.psKt], writes=[t.kf])
            p.op("vector", lambda E: E.tensor_scalar(t.kw[:], t.kf[:], t.wcol[:, 0:1],
                                                     None, ALU.mult),
                 reads=[t.kf, t.wcol], writes=[t.kw])
            if phase == 2:
                p.op("tensor", lambda E, sl=sl: E.matmul(t.psMrep[:], t.sel0[:], t.M[:, sl],
                                                         start=True, stop=True),
                     reads=[t.sel0, t.M], writes=[t.psMrep])
                p.op("scalar", lambda E: E.activation(t.W_sb[:], t.psMrep[:], AF.Exp,
                                                      scale=c_m1, bias=t.gcols[:, 0:1]),
                     reads=[t.psMrep, t.gcols, c_m1], writes=[t.W_sb])
                p.op("vector", lambda E: E.tensor_tensor(t.Wm[:], t.W_sb[:], t.trimask[:],
                                                         ALU.mult),
                     reads=[t.W_sb, t.trimask], writes=[t.Wm])
                p.op("tensor", lambda E, sl=sl: E.matmul(t.psSb[:], t.kTb[:, sl],
                                                         t.qTb[:, sl], start=True, stop=True),
                     reads=[t.kTb, t.qTb], writes=[t.psSb])
                p.op("vector", lambda E: E.tensor_tensor(t.DSb[:], t.Wm[:], t.psSb[:],
                                                         ALU.mult),
                     reads=[t.Wm, t.psSb], writes=[t.DSb])
                p.op("scalar", lambda E: E.activation(t.e_rep[:], t.psMrep[0:64, :], AF.Exp,
                                                      scale=c_m1[0:64, 0:1],
                                                      bias=t.gcols[0:64, 3:4]),
                     reads=[t.psMrep, t.gcols, c_m1], writes=[t.e_rep])
                p.op("vector", lambda E, sl=sl: E.tensor_tensor(t.qs[:], t.qTb[0:64, sl],
                                                                t.e_rep[:], ALU.mult),
                     reads=[t.qTb, t.e_rep], writes=[t.qs])
                p.op("vector", lambda E: E.tensor_copy(t.c_bf[:], t.c_f32[:]),
                     reads=[t.c_f32], writes=[t.c_bf])
                p.op("tensor", lambda E, cc=cc: E.matmul(t.psH[:], t.DSb[:],
                                                         t.vb[:, cc * 128:(cc + 1) * 128],
                                                         start=True, stop=False),
                     reads=[t.DSb, t.vb], writes=[t.psH])
                p.op("tensor", lambda E: E.matmul(t.psH[:], t.qs[:], t.c_bf[:],
                                                  start=False, stop=True),
                     reads=[t.qs, t.c_bf], writes=[t.psH])
            p.op("tensor", lambda E, cc=cc: E.matmul(t.psC[:], t.kw[:],
                                                     t.vb[:, cc * 128:(cc + 1) * 128],
                                                     start=True, stop=True),
                 reads=[t.kw, t.vb], writes=[t.psC])
            p.op("vector", lambda E: E.tensor_scalar(t.c_f32[:], t.c_f32[:], t.dec[:, 0:1],
                                                     None, ALU.mult),
                 reads=[t.c_f32, t.dec], writes=[t.c_f32])
            p.op("vector", lambda E: E.tensor_tensor(t.c_f32[:], t.c_f32[:], t.psC[:],
                                                     ALU.add),
                 reads=[t.c_f32, t.psC], writes=[t.c_f32])
            if phase == 2:
                scrm = t.scr
                p.op("vector", lambda E: E.reduce_sum(scrm[:, 0:1], t.psH[:], axis=AX.X),
                     reads=[t.psH], writes=[scrm])
                p.op("vector", lambda E: E.tensor_scalar(scrm[:, 0:1], scrm[:, 0:1],
                                                         1.0 / 128, None, ALU.mult),
                     reads=[scrm], writes=[scrm])
                p.op("vector", lambda E: E.tensor_scalar(t.hbuf[:], t.psH[:], scrm[:, 0:1],
                                                         None, ALU.subtract),
                     reads=[t.psH, scrm], writes=[t.hbuf])
                p.op("scalar", lambda E: E.activation(scrm[:, 2:130], t.hbuf[:], AF.Square),
                     reads=[t.hbuf], writes=[scrm])
                p.op("vector", lambda E: E.reduce_sum(scrm[:, 1:2], scrm[:, 2:130], axis=AX.X),
                     reads=[scrm], writes=[scrm])
                p.op("vector", lambda E: E.tensor_scalar(scrm[:, 1:2], scrm[:, 1:2],
                                                         1.0 / 128, 1e-6, ALU.mult, ALU.add),
                     reads=[scrm], writes=[scrm])
                p.op("scalar", lambda E: E.activation(scrm[:, 1:2], scrm[:, 1:2], AF.Sqrt),
                     reads=[scrm], writes=[scrm])
                p.op("vector", lambda E: E.reciprocal(scrm[:, 1:2], scrm[:, 1:2]),
                     reads=[scrm], writes=[scrm])
                p.op("vector", lambda E: E.scalar_tensor_tensor(t.hbuf[:], t.hbuf[:],
                                                                scrm[:, 1:2], t.lnw_rep[:],
                                                                ALU.mult, ALU.mult),
                     reads=[t.hbuf, scrm, t.lnw_rep], writes=[t.hbuf])
                p.op("vector", lambda E, c0=c0: E.tensor_tensor(
                    t.hn_st[:, c0:c0 + C], t.hbuf[:], t.lnb_rep[:], ALU.add),
                     reads=[t.hbuf, t.lnb_rep], writes=[t.hn_st])
        if phase == 1:
            p.dma("gpsimd", io["ccm_in"][bh:bh + 1, 0:8192].rearrange(
                "o (q f) -> (o q) f", q=64), t.c_f32[:])
            p.op("vector", lambda E: E.tensor_copy(t.sc2[0:1, 1:2], t.sc2[0:1, 0:1]),
                 reads=[t.sc2], writes=[t.sc2])
            p.op("vector", lambda E: E.tensor_copy(t.sc2[0:1, 0:1], t.Fc[0:1, SEG - 1:SEG]),
                 reads=[t.Fc, t.sc2], writes=[t.sc2])
            p.dma("gpsimd", io["ccm_in"][bh:bh + 1, 8192:8194], t.sc2[0:1, 0:2])
        else:
            p.dma("gpsimd",
                  io["hn_scr"][sl_t, h_ * 128:(h_ + 1) * 128].rearrange(
                      "(cc p) d -> p cc d", p=128),
                  t.hn_st[:].rearrange("p (cc d) -> p cc d", d=128))


def emit_combine_mlstm(p, io, g):
    scalF = p.sb("scalF", [64, 128]); scalM = p.sb("scalM", [64, 128])
    mzro = p.sb("mzro", [64, 8]); mneg = p.sb("mneg", [64, 8])
    c_j = p.sb("c_j", [64, 16 * 128])
    t1 = p.sb("t1c", [64, 16]); t2 = p.sb("t2c", [64, 16])
    w1 = p.sb("w1c", [64, 16]); w2 = p.sb("w2c", [64, 16])
    p.dma("sync", mzro[:], io["mzro64"])
    p.dma("sync", mneg[:], io["mneg64"])
    identb2 = make_identity(p, "identc")
    identf2 = p.sb("identf2", [128, 128], F32)
    p.op("vector", lambda E: E.tensor_copy(identf2[:], identb2[:]),
         reads=[identb2], writes=[identf2])
    ones64 = p.sb("ones64", [1, 64], F32)
    p.op("vector", lambda E: E.memset(ones64[:], 1.0), writes=[ones64])
    sc_raw = p.sb("sc_raw", [128, 2])
    sc_row = p.sb("sc_row", [1, 128])
    psScT = p.ps("psScT", [128, 128], F32)
    p.dma("sync", sc_raw[:], io["ccm_out"][:, 8192:8194])
    for col, dst in ((0, scalF), (1, scalM)):
        p.op("tensor", lambda E, col=col: E.transpose(psScT[0:1, 0:128],
                                                      sc_raw[:, col:col + 1], identf2[:]),
             reads=[sc_raw, identf2], writes=[psScT])
        p.op("vector", lambda E: E.tensor_copy(sc_row[:], psScT[0:1, 0:128]),
             reads=[psScT], writes=[sc_row])
        p.op("tensor", lambda E: E.matmul(psScT[0:64, 0:128], ones64[:], sc_row[:],
                                          start=True, stop=True),
             reads=[ones64, sc_row], writes=[psScT])
        p.op("vector", lambda E, dst=dst: E.tensor_copy(dst[:], psScT[0:64, 0:128]),
             reads=[psScT], writes=[dst])
    p.op("vector", lambda E: E.memset(g.m_in_sb[:], 0.0), writes=[g.m_in_sb])
    p.op("vector", lambda E: E.memset(g.c_in_sb[:], 0.0), writes=[g.c_in_sb])
    for j in range(8):
        p.dma("sync", c_j[:].rearrange("p (bh f) -> p bh f", f=128),
              io["ccm_out"][j * 16:(j + 1) * 16, 0:8192].rearrange(
                  "bh (p f) -> p bh f", p=64))
        p.op("vector", lambda E, j=j: E.tensor_scalar(t1[:], scalF[:, j * 16:(j + 1) * 16],
                                                      mzro[:, j:j + 1], None, ALU.mult),
             reads=[scalF, mzro], writes=[t1])
        p.op("vector", lambda E, j=j: E.tensor_scalar(t2[:], scalM[:, j * 16:(j + 1) * 16],
                                                      mneg[:, j:j + 1], None, ALU.add),
             reads=[scalM, mneg], writes=[t2])
        p.op("vector", lambda E: E.tensor_tensor(t1[:], g.m_in_sb[0:64, :], t1[:], ALU.add),
             reads=[g.m_in_sb, t1], writes=[t1])
        p.op("vector", lambda E: E.tensor_tensor(w1[:], t1[:], t2[:], ALU.max),
             reads=[t1, t2], writes=[w1])
        p.op("vector", lambda E: E.tensor_tensor(t1[:], t1[:], w1[:], ALU.subtract),
             reads=[t1, w1], writes=[t1])
        p.op("vector", lambda E: E.tensor_tensor(t2[:], t2[:], w1[:], ALU.subtract),
             reads=[t2, w1], writes=[t2])
        p.op("vector", lambda E: E.tensor_copy(g.m_in_sb[0:64, :], w1[:]),
             reads=[w1], writes=[g.m_in_sb])
        p.op("scalar", lambda E: E.activation(w1[:], t1[:], AF.Exp),
             reads=[t1], writes=[w1])
        p.op("scalar", lambda E: E.activation(w2[:], t2[:], AF.Exp),
             reads=[t2], writes=[w2])
        for bh in range(16):
            p.op("vector", lambda E, bh=bh: E.tensor_scalar(
                g.c_in_sb[:, bh * 128:(bh + 1) * 128],
                g.c_in_sb[:, bh * 128:(bh + 1) * 128], w1[:, bh:bh + 1], None, ALU.mult),
                 reads=[g.c_in_sb, w1], writes=[g.c_in_sb])
            p.op("vector", lambda E, bh=bh: E.scalar_tensor_tensor(
                g.c_in_sb[:, bh * 128:(bh + 1) * 128],
                c_j[:, bh * 128:(bh + 1) * 128], w2[:, bh:bh + 1],
                g.c_in_sb[:, bh * 128:(bh + 1) * 128], ALU.mult, ALU.add),
                 reads=[c_j, w2, g.c_in_sb], writes=[g.c_in_sb])


def emit_halo(p, io, g):
    mh = p.sb("mh", [128, 8])
    hall = p.sb("hall", [128, 8 * 16 * 6])
    p.dma("sync", mh[:], io["mhalo"])
    p.dma("sync", hall[:].rearrange("q (j e c) -> q j e c", e=16, c=6),
          io["cch_out"].rearrange("(j e q) c -> q j e c", q=128, e=16))
    for j in range(8):
        if j == 0:
            p.op("vector", lambda E, j=j: E.tensor_scalar(
                g.halo_sb[:], hall[:, j * 96:(j + 1) * 96], mh[:, j:j + 1], None, ALU.mult),
                 reads=[hall, mh], writes=[g.halo_sb])
        else:
            p.op("vector", lambda E, j=j: E.scalar_tensor_tensor(
                g.halo_sb[:], hall[:, j * 96:(j + 1) * 96], mh[:, j:j + 1],
                g.halo_sb[:], ALU.mult, ALU.add),
                 reads=[hall, mh, g.halo_sb], writes=[g.halo_sb])


def emit_mamba_p1(p, io, g):
    TC = 512
    iop = p.sb("iop", [128, 128], F32)
    p.op("gpsimd", lambda E: E.iota(iop[:], pattern=[[0, 128]], channel_multiplier=1,
                                    allow_small_or_imprecise_dtypes=True), writes=[iop])
    sel = p.sb("seln", [128, 16 * 128], F32)
    for n in range(16):
        p.op("vector", lambda E, n=n: E.tensor_scalar(
            sel[:, n * 128:(n + 1) * 128], iop[:], float(n), None, ALU.is_equal),
             reads=[iop], writes=[sel])
    dpar = p.sb("dpar_s", [128, 16])
    p.dma("sync", dpar[:], io["dpar16"])
    dtf = p.sb("dtf", [128, SEG]); xcf = p.sb("xcf", [128, SEG])
    Sf = p.sb("Sf", [128, SEG]); udt = p.sb("udt", [128, SEG])
    zer = p.sb("zer", [128, SEG])
    dAb = p.sb("dAb", [128, TC]); bmb = p.sb("bmb", [128, TC])
    hb = p.sb("hb", [128, TC]); ytmp = p.sb("ytmp", [128, TC])
    yacc = p.sb("yacc", [128, TC])
    Bsb = p.sb("Bsb", [128, SEG]); Csb = p.sb("Csb", [128, SEG])
    hstate = p.sb("hstate", [128, 256])
    sdtc = p.sb("sdtc", [128, 16])
    psBr = p.ps("psBr", [128, TC]); psCr = p.ps("psCr", [128, TC])
    p.op("vector", lambda E: E.memset(zer[:], 0.0), writes=[zer])
    p.op("vector", lambda E: E.memset(Bsb[:], 0.0), writes=[Bsb])
    p.op("vector", lambda E: E.memset(Csb[:], 0.0), writes=[Csb])
    for b_ in range(B):
        tok0 = b_ * SEG
        p.op("vector", lambda E: E.memset(hstate[:], 0.0), writes=[hstate])
        p.dma("sync", Bsb[0:16, :], io["Bf_scr"][:, tok0:tok0 + SEG])
        p.dma("sync", Csb[0:16, :], io["Cf_scr"][:, tok0:tok0 + SEG])
        for dch in range(16):
            p.dma("sync", dtf[:], io["dt_scr"][dch * 128:(dch + 1) * 128, tok0:tok0 + SEG])
            p.dma("sync", xcf[:], io["xc_scr"][dch * 128:(dch + 1) * 128, tok0:tok0 + SEG])
            p.op("vector", lambda E: E.tensor_tensor_scan(Sf[:], dtf[:], zer[:],
                                                          0.0, ALU.add, ALU.add),
                 reads=[dtf, zer], writes=[Sf])
            p.dma("gpsimd", io["S_scr"][dch * 128:(dch + 1) * 128, tok0:tok0 + SEG], Sf[:])
            p.op("vector", lambda E, dch=dch: E.tensor_copy(sdtc[:, dch:dch + 1],
                                                            Sf[:, SEG - 1:SEG]),
                 reads=[Sf], writes=[sdtc])
            p.op("vector", lambda E: E.tensor_tensor(udt[:], dtf[:], xcf[:], ALU.mult),
                 reads=[dtf, xcf], writes=[udt])
            for tc in range(SEG // TC):
                t0 = tc * TC
                for n in range(16):
                    scol = dch * 16 + n
                    p.op("scalar", lambda E, dch=dch, n=n, t0=t0: E.activation(
                        dAb[:], dtf[:, t0:t0 + TC], AF.Exp,
                        scale=g.A_sb[:, dch * 16 + n:dch * 16 + n + 1]),
                         reads=[dtf, g.A_sb], writes=[dAb])
                    p.op("tensor", lambda E, n=n, t0=t0: E.matmul(
                        psBr[:], sel[:, n * 128:(n + 1) * 128], Bsb[:, t0:t0 + TC],
                        start=True, stop=True),
                         reads=[sel, Bsb], writes=[psBr])
                    p.op("vector", lambda E, t0=t0: E.tensor_tensor(
                        bmb[:], udt[:, t0:t0 + TC], psBr[:], ALU.mult),
                         reads=[udt, psBr], writes=[bmb])
                    p.op("vector", lambda E, scol=scol: E.tensor_tensor_scan(
                        hb[:], dAb[:], bmb[:], hstate[:, scol:scol + 1],
                        ALU.mult, ALU.add),
                         reads=[dAb, bmb, hstate], writes=[hb])
                    p.op("vector", lambda E, scol=scol: E.tensor_copy(
                        hstate[:, scol:scol + 1], hb[:, TC - 1:TC]),
                         reads=[hb], writes=[hstate])
                    p.op("tensor", lambda E, n=n, t0=t0: E.matmul(
                        psCr[:], sel[:, n * 128:(n + 1) * 128], Csb[:, t0:t0 + TC],
                        start=True, stop=True),
                         reads=[sel, Csb], writes=[psCr])
                    if n == 0:
                        p.op("vector", lambda E: E.tensor_tensor(yacc[:], hb[:], psCr[:],
                                                                 ALU.mult),
                             reads=[hb, psCr], writes=[yacc])
                    else:
                        p.op("vector", lambda E: E.tensor_tensor(ytmp[:], hb[:], psCr[:],
                                                                 ALU.mult),
                             reads=[hb, psCr], writes=[ytmp])
                        p.op("vector", lambda E: E.tensor_tensor(yacc[:], yacc[:], ytmp[:],
                                                                 ALU.add),
                             reads=[yacc, ytmp], writes=[yacc])
                p.op("vector", lambda E, dch=dch, t0=t0: E.scalar_tensor_tensor(
                    yacc[:], xcf[:, t0:t0 + TC], dpar[:, dch:dch + 1], yacc[:],
                    ALU.mult, ALU.add),
                     reads=[xcf, dpar, yacc], writes=[yacc])
                p.dma("gpsimd", io["yssm_scr"][dch * 128:(dch + 1) * 128,
                                               tok0 + t0:tok0 + t0 + TC], yacc[:])
        ccs4 = io["ccs_in"].rearrange("(bb dch q) c -> q bb dch c", bb=2, q=128)
        comb = p.sb(f"comb{b_}", [128, 16 * 17])
        cv = comb[:].rearrange("q (dch c) -> q dch c", c=17)
        p.op("vector", lambda E, cv=cv: E.tensor_copy(
            cv[:, :, 0:16], hstate[:].rearrange("q (dch n) -> q dch n", n=16)),
             reads=[hstate], writes=[comb])
        p.op("vector", lambda E, cv=cv: E.tensor_copy(
            cv[:, :, 16:17], sdtc[:].rearrange("q (dch o) -> q dch o", o=1)),
             reads=[sdtc], writes=[comb])
        p.dma("gpsimd", ccs4[:, b_, :, 0:17], cv)


def emit_combine_mamba(p, io, g):
    m01 = p.sb("m01", [128, 8])
    hs_j = p.sb("hs_j", [128, 16 * 17])
    Ej = p.sb("Ej", [128, 256]); Dj = p.sb("Dj", [128, 256])
    p.dma("sync", m01[:], io["m01"])
    for b_ in range(B):
        hin = g.h_in_sb[:, b_ * 256:(b_ + 1) * 256]
        p.op("vector", lambda E, hin=hin: E.memset(hin, 0.0), writes=[g.h_in_sb])
        for j in range(8):
            rows = slice(j * 4096 + b_ * 2048, j * 4096 + (b_ + 1) * 2048)
            p.dma("sync", hs_j[:].rearrange("q (dch c) -> q dch c", c=17),
                  io["ccs_out"][rows, :].rearrange("(dch q) c -> q dch c", q=128))
            hsv = hs_j[:].rearrange("q (dch c) -> q dch c", c=17)
            sdt_j = hsv[:, :, 16]
            h_j = hsv[:, :, 0:16]
            p.op("vector", lambda E, j=j, sdt_j=sdt_j: E.tensor_tensor(
                Ej[:].rearrange("q (dch n) -> q dch n", n=16),
                g.A_sb[:].rearrange("q (dch n) -> q dch n", n=16),
                sdt_j.rearrange("q dch -> q dch ()").broadcast_to((128, 16, 16)),
                ALU.mult),
                 reads=[g.A_sb, hs_j], writes=[Ej])
            p.op("vector", lambda E, j=j: E.tensor_scalar(Ej[:], Ej[:], m01[:, j:j + 1],
                                                          None, ALU.mult),
                 reads=[Ej, m01], writes=[Ej])
            p.op("scalar", lambda E: E.activation(Dj[:], Ej[:], AF.Exp),
                 reads=[Ej], writes=[Dj])
            p.op("vector", lambda E, j=j, h_j=h_j: E.tensor_scalar(
                h_j, h_j, m01[:, j:j + 1], None, ALU.mult),
                 reads=[hs_j, m01], writes=[hs_j])
            p.op("vector", lambda E, hin=hin: E.tensor_tensor(hin, hin, Dj[:], ALU.mult),
                 reads=[g.h_in_sb, Dj], writes=[g.h_in_sb])
            p.op("vector", lambda E, hin=hin, h_j=h_j: E.tensor_tensor(
                hin.rearrange("q (dch n) -> q dch n", n=16), 
                hin.rearrange("q (dch n) -> q dch n", n=16), h_j, ALU.add),
                 reads=[g.h_in_sb, hs_j], writes=[g.h_in_sb])


def emit_mamba_p2(p, io, g):
    TC = 512
    iop = p.sb("iop", [128, 128], F32)
    p.op("gpsimd", lambda E: E.iota(iop[:], pattern=[[0, 128]], channel_multiplier=1,
                                    allow_small_or_imprecise_dtypes=True), writes=[iop])
    sel = p.sb("seln", [128, 16 * 128], F32)
    for n in range(16):
        p.op("vector", lambda E, n=n: E.tensor_scalar(
            sel[:, n * 128:(n + 1) * 128], iop[:], float(n), None, ALU.is_equal),
             reads=[iop], writes=[sel])
    Sf = p.sb("Sf", [128, SEG])
    Csb = p.sb("Csb", [128, SEG])
    P = p.sb("Pp", [128, TC]); hp = p.sb("hp", [128, TC])
    ycn = p.sb("ycn", [128, TC]); yacc = p.sb("yacc", [128, TC])
    ylo = p.sb("ylo", [128, TC])
    psCr = p.ps("psCr", [128, TC])
    p.op("vector", lambda E: E.memset(Csb[:], 0.0), writes=[Csb])
    for b_ in range(B):
        tok0 = b_ * SEG
        p.dma("sync", Csb[0:16, :], io["Cf_scr"][:, tok0:tok0 + SEG])
        for dch in range(16):
            p.dma("sync", Sf[:], io["S_scr"][dch * 128:(dch + 1) * 128, tok0:tok0 + SEG])
            for tc in range(SEG // TC):
                t0 = tc * TC
                for n in range(16):
                    hcol = b_ * 256 + dch * 16 + n
                    p.op("scalar", lambda E, dch=dch, n=n, t0=t0: E.activation(
                        P[:], Sf[:, t0:t0 + TC], AF.Exp,
                        scale=g.A_sb[:, dch * 16 + n:dch * 16 + n + 1]),
                         reads=[Sf, g.A_sb], writes=[P])
                    p.op("vector", lambda E, hcol=hcol: E.tensor_scalar(
                        hp[:], P[:], g.h_in_sb[:, hcol:hcol + 1], None, ALU.mult),
                         reads=[P, g.h_in_sb], writes=[hp])
                    p.op("tensor", lambda E, n=n, t0=t0: E.matmul(
                        psCr[:], sel[:, n * 128:(n + 1) * 128], Csb[:, t0:t0 + TC],
                        start=True, stop=True),
                         reads=[sel, Csb], writes=[psCr])
                    if n == 0:
                        p.op("vector", lambda E: E.tensor_tensor(yacc[:], hp[:], psCr[:],
                                                                 ALU.mult),
                             reads=[hp, psCr], writes=[yacc])
                    else:
                        p.op("vector", lambda E: E.tensor_tensor(ycn[:], hp[:], psCr[:],
                                                                 ALU.mult),
                             reads=[hp, psCr], writes=[ycn])
                        p.op("vector", lambda E: E.tensor_tensor(yacc[:], yacc[:], ycn[:],
                                                                 ALU.add),
                             reads=[yacc, ycn], writes=[yacc])
                p.dma("sync", ylo[:], io["yssm_scr"][dch * 128:(dch + 1) * 128,
                                                     tok0 + t0:tok0 + t0 + TC])
                p.op("vector", lambda E: E.tensor_tensor(ylo[:], ylo[:], yacc[:], ALU.add),
                     reads=[ylo, yacc], writes=[ylo])
                p.dma("gpsimd", io["yssm_scr"][dch * 128:(dch + 1) * 128,
                                               tok0 + t0:tok0 + t0 + TC], ylo[:])


def build_fused():
    p = Prog("fused")
    io = {}
    E_ = lambda n, shape, dt=F32: io.setdefault(n, p.din(n, shape, dt))
    S_ = lambda n, shape, dt=F32: io.setdefault(n, p.dtmp(n, shape, dt))
    E_("x0", [NTOK, D])
    for n, shape in [("Wq", [D, 512]), ("Wk", [D, 512]), ("Wv", [D, D]),
                     ("Wif", [D, 16]), ("Wog", [D, D]), ("Wout", [D, D]),
                     ("U1", [D, 256]), ("V1", [256, 4096]), ("U2", [4096, 256]),
                     ("V2", [256, D]), ("W_in", [D, 4096]),
                     ("b1s", [128, 32]), ("b2r", [128, D]),
                     ("lnwr", [128, D]), ("lnbr", [128, D]),
                     ("lnw8", [8, 128, 128]), ("lnb8", [8, 128, 128]),
                     ("bif_in", [1, 16]),
                     ("cw4", [128, 16, 4]), ("cb4", [128, 16]),
                     ("Wxp", [2048, 96]), ("Wdt", [64, 2048]), ("bdt", [128, 16]),
                     ("dpar16", [128, 16]), ("alog16", [128, 256]),
                     ("Wo1", [2048, D]), ("Wr", [8, D]), ("brr", [128, 8]),
                     ("W1m", [8, 8, 128, 4096]), ("b1s3", [128, 8 * 32]),
                     ("W2m", [8, 128, 32 * 1024]), ("b2r3", [8, 128, D]),
                     ("lnw3r", [128, D]), ("lnb3r", [128, D]),
                     ("mzro64", [64, 8]), ("mneg64", [64, 8]),
                     ("mhalo", [128, 8]), ("m01", [128, 8])]:
        E_(n, shape)
    io["x4"] = p.dout("x4", [NTOK, D])
    S_("qT_scr", [4, 128, NTOK], BF16); S_("kT_scr", [4, 128, NTOK], BF16)
    S_("v_scr", [NTOK, D], BF16)
    S_("if_scr", [16, NTOK]); S_("og_scr", [NTOK, D]); S_("hn_scr", [NTOK, D])
    S_("x2", [NTOK, D]); S_("y2", [NTOK, D])
    S_("xinT", [2048, NTOK]); S_("zgT", [2048, NTOK])
    S_("dtT", [2048, NTOK]); S_("xcT", [2048, NTOK])
    S_("Bf", [16, NTOK]); S_("Cf", [16, NTOK])
    S_("S_scr", [2048, NTOK]); S_("yssm_scr", [2048, NTOK])
    S_("w1sc", [8, 8, 128, 4096], BF16); S_("w2sc", [8, 128, 32 * 1024], BF16)
    S_("ccm_in", [16, 8194]); S_("ccm_out", [128, 8194])
    S_("cch_in", [2048, 6]); S_("cch_out", [16384, 6])
    S_("ccs_in", [4096, 17]); S_("ccs_out", [32768, 17])
    io["hn"] = io["hn_scr"]; io["og"] = io["og_scr"]
    io["xinh"] = io["xinT"]
    io["dt_scr"] = io["dtT"]; io["xc_scr"] = io["xcT"]
    io["Bf_scr"] = io["Bf"]; io["Cf_scr"] = io["Cf"]
    io["ysm"] = io["yssm_scr"]

    class G:
        pass
    g = G()
    g.m_in_sb = p.sb("g_m_in", [64, 16])
    g.c_in_sb = p.sb("g_c_in", [64, 16 * 128])
    g.h_in_sb = p.sb("g_h_in", [128, 2 * 256])
    g.A_sb = p.sb("g_A", [128, 256])
    g.halo_sb = p.sb("g_halo", [128, 16 * 6])
    g.bif = p.sb("g_bif", [1, 16])
    for v in (-1.0, 1.0, 1.0 / 15.0, 15.0, GELU_C):
        p.constcol(v)
    p.dma("sync", g.bif[0:1, :], io["bif_in"])
    p.dma("sync", g.A_sb[:], io["alog16"])
    p.op("scalar", lambda E: E.activation(g.A_sb[:], g.A_sb[:], AF.Exp),
         reads=[g.A_sb], writes=[g.A_sb])
    p.op("vector", lambda E: E.tensor_scalar(g.A_sb[:], g.A_sb[:], -1.0, None, ALU.mult),
         reads=[g.A_sb], writes=[g.A_sb])

    def sect(pfx, fn, *a):
        p.pfx = pfx
        p.section_open()
        fn(*a)
        p.section_close()
        p.pfx = ""

    def collective(cin, cout):
        p.op("gpsimd", lambda E: E.collective_compute(
            "AllGather", ALU.bypass, replica_groups=[list(range(NCORES))],
            ins=[cin.opt()], outs=[cout.opt()]),
             reads=[cin], writes=[cout])
        p.barrier_all()

    sect("a_", emit_l1_fused, p, io)
    sect("b_", emit_mlstm_fused, p, io, 1, g)
    collective(io["ccm_in"], io["ccm_out"])
    sect("c_", emit_combine_mlstm, p, io, g)
    sect("d_", emit_mlstm_fused, p, io, 2, g)
    sect("e_", emit_L3, p, io)
    for b_ in range(B):
        p.dma("sync", io["cch_in"][:, b_ * 3:(b_ + 1) * 3],
              io["xinT"][:, b_ * SEG + SEG - 3:b_ * SEG + SEG])
    collective(io["cch_in"], io["cch_out"])
    sect("f_", emit_halo, p, io, g)
    io["halo_sb"] = g.halo_sb
    def g_with_stage():
        stg = p.sb("moestg", [128, 16384], BF16)
        emit_L4(p, io)
        emit_moe_stage(p, io, stg[:])
    sect("g_", g_with_stage)
    sect("h_", emit_mamba_p1, p, io, g)
    collective(io["ccs_in"], io["ccs_out"])
    sect("i_", emit_combine_mamba, p, io, g)
    sect("j_", emit_mamba_p2, p, io, g)
    sect("k_", emit_L5, p, io)
    p.barrier_end()
    return p.realize()


def run_fused(d):
    x0 = d["L0_attn_x"]
    wif = np.ascontiguousarray(np.concatenate([d["L0_attn_Wi"], d["L0_attn_Wf"]], 1))
    lnw8 = np.ascontiguousarray(np.broadcast_to(
        d["L0_attn_ln_w"][:, None, :], (8, 128, 128)))
    lnb8 = np.ascontiguousarray(np.broadcast_to(
        d["L0_attn_ln_b"][:, None, :], (8, 128, 128)))
    bif = np.ascontiguousarray(np.concatenate(
        [d["L0_attn_bi"], d["L0_attn_bf"]])[None, :])
    b1s = np.ascontiguousarray(d["L0_ffn_b1"].reshape(32, 128).T)
    b2r = np.ascontiguousarray(np.broadcast_to(d["L0_ffn_b2"][None, :], (128, D)))
    lnwr = np.ascontiguousarray(np.broadcast_to(d["L1_attn_ln_w"][None, :], (128, D)))
    lnbr = np.ascontiguousarray(np.broadcast_to(d["L1_attn_ln_b"][None, :], (128, D)))
    cw4 = np.ascontiguousarray(
        np.transpose(d["L1_attn_conv_w"].reshape(16, 128, 1, 4)[:, :, 0, :], (1, 0, 2)))
    cb4 = np.ascontiguousarray(d["L1_attn_conv_b"].reshape(16, 128).T)
    bdt = np.ascontiguousarray(d["L1_attn_b_dt"].reshape(16, 128).T)
    dpar16 = np.ascontiguousarray(d["L1_attn_D_param"].reshape(16, 128).T)
    alog16 = np.ascontiguousarray(
        np.transpose(d["L1_attn_A_log"].reshape(16, 128, 16), (1, 0, 2)).reshape(128, 256))
    b1s3 = np.ascontiguousarray(
        np.transpose(d["L1_ffn_b1"].reshape(8, 32, 128), (2, 0, 1)).reshape(128, 256))
    brr = np.ascontiguousarray(np.broadcast_to(d["L1_ffn_br"][None, :], (128, 8)))
    b2r3 = np.ascontiguousarray(np.broadcast_to(d["L1_ffn_b2"][:, None, :], (8, 128, D)))
    lnw3r = np.ascontiguousarray(np.broadcast_to(d["L1_ffn_ln_w"][None, :], (128, D)))
    lnb3r = np.ascontiguousarray(np.broadcast_to(d["L1_ffn_ln_b"][None, :], (128, D)))
    w1h, w2h = moe_w_host(d)
    nc = build_fused()
    in_maps = []
    for c in range(NCORES):
        sl = slice(c * SEG, (c + 1) * SEG)
        mzro = np.zeros((64, 8), np.float32); mzro[:, :c] = 1.0
        mneg = np.full((64, 8), -1e30, np.float32); mneg[:, :c] = 0.0
        mhalo = np.zeros((128, 8), np.float32)
        if c > 0:
            mhalo[:, c - 1] = 1.0
        m01 = np.zeros((128, 8), np.float32); m01[:, :c] = 1.0
        in_maps.append({
            "x0": np.ascontiguousarray(x0[:, sl, :].reshape(NTOK, D)),
            "Wq": d["L0_attn_Wq"], "Wk": d["L0_attn_Wk"], "Wv": d["L0_attn_Wv"],
            "Wif": wif, "Wog": d["L0_attn_Wog"], "Wout": d["L0_attn_Wout"],
            "U1": d["L0_ffn_U1"], "V1": d["L0_ffn_V1"], "U2": d["L0_ffn_U2"],
            "V2": d["L0_ffn_V2"], "W_in": d["L1_attn_W_in"],
            "b1s": b1s, "b2r": b2r, "lnwr": lnwr, "lnbr": lnbr,
            "lnw8": lnw8, "lnb8": lnb8, "bif_in": bif,
            "cw4": cw4, "cb4": cb4, "Wxp": d["L1_attn_W_xproj"],
            "Wdt": d["L1_attn_W_dt"], "bdt": bdt,
            "dpar16": dpar16, "alog16": alog16,
            "Wo1": d["L1_attn_W_out"], "Wr": d["L1_ffn_Wr"], "brr": brr,
            "W1m": w1h, "b1s3": b1s3, "W2m": w2h,
            "b2r3": b2r3, "lnw3r": lnw3r, "lnb3r": lnb3r,
            "mzro64": mzro, "mneg64": mneg, "mhalo": mhalo, "m01": m01,
        })
    res = run_bass_kernel_spmd(nc, in_maps, core_ids=list(range(NCORES)))
    full = np.stack([res.results[c]["x4"].reshape(B, SEG, D) for c in range(NCORES)],
                    axis=1)
    return np.ascontiguousarray(full.reshape(B, NCORES * SEG, D).astype(np.float32))


# version 11
# speedup vs baseline: 1.3875x; 1.0336x over previous
import numpy as np
import ml_dtypes
from contextlib import ExitStack
import concourse.bass as bass
import concourse.mybir as mybir
from concourse.bass_utils import run_bass_kernel_spmd

F32 = mybir.dt.float32
BF16 = mybir.dt.bfloat16
AF = mybir.ActivationFunctionType
ALU = mybir.AluOpType
AX = mybir.AxisListType

NCORES = 8
SEG = 2048
B = 2
NTOK = B * SEG
HS = SEG + 3
D = 1024


class Prog:
    COMPUTE = ("tensor", "vector", "scalar", "gpsimd", "sync")

    def __init__(self, name="prog"):
        self.nc = bass.Bass("TRN2", target_bir_lowering=False, debug=False)
        self.ctx = ExitStack()
        self.name = name
        self.streams = {}
        for e in self.COMPUTE:
            self.streams[e] = dict(
                sem=self.ctx.enter_context(self.nc.semaphore(f"sc_{e}")),
                count=0, inc=1)
            self.streams["dma_" + e] = dict(
                sem=self.ctx.enter_context(self.nc.semaphore(f"sd_{e}")),
                count=0, inc=16)
        self.q = {e: [] for e in self.COMPUTE}
        self.acc = {}
        self.sbuf_bytes = 0
        self._consts = {}
        self._const_tile = None

    def constcol(self, value):
        value = float(value)
        if value in self._consts:
            return self._consts[value]
        if self._const_tile is None or self._const_next >= 16:
            self._const_tile = self.sb(f"cconst{len(self._consts)}", [128, 16], F32)
            self._const_next = 0
        tile, j = self._const_tile, self._const_next
        self._const_next += 1
        self.op("vector", lambda E: E.memset(tile[:, j:j + 1], value), writes=[tile])
        ap = tile[:, j:j + 1]
        self._consts[value] = ap
        return ap

    pfx = ""

    def sb(self, name, shape, dt=F32):
        elems = 1
        for s in shape[1:]:
            elems *= s
        self.sbuf_bytes += elems * mybir.dt.size(dt)
        ctx = getattr(self, "_sect_ctx", None) or self.ctx
        return ctx.enter_context(self.nc.sbuf_tensor(self.pfx + name, shape, dt))

    def ps(self, name, shape, dt=F32):
        ctx = getattr(self, "_sect_ctx", None) or self.ctx
        return ctx.enter_context(self.nc.psum_tensor(self.pfx + name, shape, dt))

    def section_open(self):
        assert getattr(self, "_sect_ctx", None) is None
        self._sect_ctx = ExitStack()

    def section_close(self):
        self._sect_ctx.close()
        self._sect_ctx = None
        self.barrier_all()

    def barrier_all(self):
        finals = [(k, st["count"]) for k, st in self.streams.items()
                  if st["count"] > 0]
        for e in self.COMPUTE:
            self.q[e].append((None, list(finals), None, None))

    def din(self, name, shape, dt=F32):
        return self.nc.dram_tensor(name, shape, dt, kind="ExternalInput").ap()

    def dout(self, name, shape, dt=F32):
        return self.nc.dram_tensor(name, shape, dt, kind="ExternalOutput").ap()

    def dtmp(self, name, shape, dt=F32):
        return self.nc.dram_tensor(name, shape, dt, kind="Internal").ap()

    @staticmethod
    def _name(x):
        if isinstance(x, tuple):
            return x[1]
        if isinstance(x, bass.AP) or hasattr(x, "tensor"):
            return x.tensor.name
        return x.name

    def _deps(self, reads, writes, stream):
        waits = {}
        def add(st, val):
            if st == stream == "tensor":
                return
            if st.startswith("dma_"):
                val = self.streams[st]["count"]
            if waits.get(st, -1) < val:
                waits[st] = val
        for x in reads:
            rec = self.acc.get(self._name(x))
            if rec and rec["w"]:
                add(*rec["w"])
        for x in writes:
            rec = self.acc.get(self._name(x))
            if rec:
                if rec["w"]:
                    add(*rec["w"])
                for st, val in rec["r"].items():
                    if st != stream:
                        add(st, val)
        return waits

    def _record(self, reads, writes, stream, val):
        for x in reads:
            rec = self.acc.setdefault(self._name(x), {"w": None, "r": {}})
            if rec["r"].get(stream, -1) < val:
                rec["r"][stream] = val
        for x in writes:
            rec = self.acc.setdefault(self._name(x), {"w": None, "r": {}})
            rec["w"] = (stream, val)
            rec["r"] = {}

    def op(self, eng, fn, reads=(), writes=(), dma=False):
        stream = ("dma_" + eng) if dma else eng
        st = self.streams[stream]
        waits = self._deps(reads, writes, stream)
        val = st["count"] + st["inc"]
        st["count"] = val
        self.q[eng].append((waits, fn, st["sem"], st["inc"]))
        self._record(reads, writes, stream, val)

    def dma(self, eng, out, in_, **kw):
        self.op(eng, lambda E: E.dma_start(out=out, in_=in_, **kw),
                reads=[in_], writes=[out], dma=True)

    def barrier_end(self):
        finals = [(k, s["count"]) for k, s in self.streams.items()
                  if s["count"] > 0 and k != "sync"]
        self.q["sync"].append((None, finals, None, None))

    def realize(self):
        block = self.ctx.enter_context(self.nc.Block())
        waited = {e: {} for e in self.COMPUTE}

        def make_body(eng_name):
            def body(E):
                for waits, fn, sem, inc in self.q[eng_name]:
                    if sem is None:
                        w = waited[eng_name]
                        for stname, v2 in fn:
                            if stname == eng_name:
                                continue
                            if w.get(stname, -1) < v2:
                                E.wait_ge(self.streams[stname]["sem"], v2)
                                w[stname] = v2
                        continue
                    if waits:
                        for stname, v in waits.items():
                            w = waited[eng_name]
                            if w.get(stname, -1) < v:
                                E.wait_ge(self.streams[stname]["sem"], v)
                                w[stname] = v
                    inst = fn(E)
                    inst.then_inc(sem, inc)
            return body

        for e in self.COMPUTE:
            if self.q[e] or e == "sync":
                getattr(block, "sync" if e == "sync" else e)(make_body(e))
        self.ctx.close()
        return self.nc


def make_identity(p, name="ident"):
    io = p.sb(name + "_io", [128, 128], F32)
    idf = p.sb(name + "_f", [128, 128], F32)
    idb = p.sb(name, [128, 128], BF16)
    p.op("gpsimd", lambda E: E.iota(io[:], pattern=[[1, 128]], channel_multiplier=-1,
                                    allow_small_or_imprecise_dtypes=True),
         writes=[io])
    p.op("vector", lambda E: E.tensor_scalar(idf[:], io[:], 0.0, None, ALU.is_equal),
         reads=[io], writes=[idf])
    p.op("vector", lambda E: E.tensor_copy(idb[:], idf[:]), reads=[idf], writes=[idb])
    return idb


def load_weight_bf16(p, name, w_dram, K, N, eng="gpsimd"):
    kc = K // 128
    sbw = p.sb(name, [128, kc * N], BF16)
    src = w_dram.rearrange("(kc p) n -> p kc n", p=128)
    dst = sbw[:].rearrange("p (kc n) -> p kc n", kc=kc)
    for k in range(kc):
        p.op(eng, lambda E, k=k: E.dma_start(out=dst[:, k, :], in_=src[:, k, :]),
             reads=[w_dram], writes=[sbw], dma=True)
    return sbw


def chunk(sbw, k, N, cols=None):
    if cols is None:
        return sbw[:, k * N:(k + 1) * N]
    c0, c1 = cols
    return sbw[:, k * N + c0:k * N + c1]


def emit_ln(p, y, x, scr, eps, dcols=D):
    s_sum = scr[:, 0:1]
    s_rstd = scr[:, 1:2]
    sq = scr[:, 2:2 + dcols]
    p.op("vector", lambda E: E.reduce_sum(s_sum, x, axis=AX.X), reads=[x], writes=[scr])
    p.op("vector", lambda E: E.tensor_scalar(s_sum, s_sum, 1.0 / dcols, None, ALU.mult),
         reads=[scr], writes=[scr])
    p.op("vector", lambda E: E.tensor_scalar(y, x, s_sum, None, ALU.subtract),
         reads=[x, scr], writes=[y])
    p.op("scalar", lambda E: E.activation(sq, y, AF.Square), reads=[y], writes=[scr])
    p.op("vector", lambda E: E.reduce_sum(s_rstd, sq, axis=AX.X), reads=[scr], writes=[scr])
    p.op("vector", lambda E: E.tensor_scalar(s_rstd, s_rstd, 1.0 / dcols, float(eps),
                                             ALU.mult, ALU.add), reads=[scr], writes=[scr])
    p.op("scalar", lambda E: E.activation(s_rstd, s_rstd, AF.Sqrt), reads=[scr], writes=[scr])
    p.op("vector", lambda E: E.reciprocal(s_rstd, s_rstd), reads=[scr], writes=[scr])
    p.op("vector", lambda E: E.tensor_scalar(y, y, s_rstd, None, ALU.mult),
         reads=[y, scr], writes=[y])


def emit_cast_transpose(p, dstT, src_f32, cast_buf, ident, psT, ncol=D):
    p.op("vector", lambda E: E.tensor_copy(cast_buf[:, 0:ncol], src_f32),
         reads=[src_f32], writes=[cast_buf])
    nch = ncol // 128
    for k in range(nch):
        p.op("tensor", lambda E, k=k: E.transpose(psT[:], cast_buf[:, k * 128:(k + 1) * 128],
                                                  ident[:]),
             reads=[cast_buf, ident], writes=[psT])
        p.op("scalar", lambda E, k=k: E.copy(dstT[:, k * 128:(k + 1) * 128], psT[:]),
             reads=[psT], writes=[dstT])


def emit_gemm_tokmajor(p, ps_bank, xT, wsb, K, N, cols, extra_reads=()):
    kc = K // 128
    c0, c1 = cols
    for k in range(kc):
        p.op("tensor",
             lambda E, k=k: E.matmul(ps_bank[:, 0:c1 - c0],
                                     xT[:, k * 128:(k + 1) * 128],
                                     wsb[:, k * N + c0:k * N + c1],
                                     start=(k == 0), stop=(k == kc - 1)),
             reads=[xT, wsb] + list(extra_reads), writes=[ps_bank])


def build_L1():
    p = Prog("L1")
    x0 = p.din("x0", [NTOK, D])
    wq_d = p.din("Wq", [D, 512]); wk_d = p.din("Wk", [D, 512])
    wv_d = p.din("Wv", [D, D]); wif_d = p.din("Wif", [D, 16])
    wog_d = p.din("Wog", [D, D])
    o_q = p.dout("q", [NTOK, 512]); o_k = p.dout("k", [NTOK, 512])
    o_v = p.dout("v", [NTOK, D]); o_if = p.dout("ifr", [NTOK, 16])
    o_o = p.dout("og", [NTOK, D])

    ident = make_identity(p)
    c_inv15 = p.constcol(1.0 / 15.0)
    c_15 = p.constcol(15.0)
    wq = load_weight_bf16(p, "wq", wq_d, D, 512)
    wk = load_weight_bf16(p, "wk", wk_d, D, 512)
    wv = load_weight_bf16(p, "wv", wv_d, D, D)
    wif = load_weight_bf16(p, "wif", wif_d, D, 16)
    wog = load_weight_bf16(p, "wog", wog_d, D, D)

    NB = 2
    xs = [p.sb(f"xt{i}", [128, D]) for i in range(NB)]
    ys = [p.sb(f"yy{i}", [128, D]) for i in range(NB)]
    cast = [p.sb(f"cast{i}", [128, D], BF16) for i in range(NB)]
    yT = [p.sb(f"yT{i}", [128, D], BF16) for i in range(NB)]
    scr = [p.sb(f"scr{i}", [128, D + 2]) for i in range(NB)]
    st_q = [p.sb(f"st_q{i}", [128, 512]) for i in range(NB)]
    st_k = [p.sb(f"st_k{i}", [128, 512]) for i in range(NB)]
    st_v = [p.sb(f"st_v{i}", [128, D]) for i in range(NB)]
    st_o = [p.sb(f"st_o{i}", [128, D]) for i in range(NB)]
    st_if = [p.sb(f"st_if{i}", [128, 16]) for i in range(NB)]
    tmp_t = [p.sb(f"tmp_t{i}", [128, D]) for i in range(NB)]

    psT = p.ps("psT", [128, 128], BF16)
    ps_a = p.ps("ps_a", [128, 512]); ps_b = p.ps("ps_b", [128, 512])
    ps_c = p.ps("ps_c", [128, 512]); ps_d = p.ps("ps_d", [128, 512])

    ntiles = NTOK // 128
    for t in range(ntiles):
        i = t % NB
        r0 = t * 128
        p.dma("sync", xs[i][:], x0[r0:r0 + 128, :])
        emit_ln(p, ys[i][:], xs[i][:], scr[i], 1e-5)
        emit_cast_transpose(p, yT[i], ys[i][:], cast[i], ident, psT)
        emit_gemm_tokmajor(p, ps_a, yT[i], wq, D, 512, (0, 512))
        p.op("vector", lambda E, i=i: E.tensor_copy(st_q[i][:], ps_a[:]),
             reads=[ps_a], writes=[st_q[i]])
        emit_gemm_tokmajor(p, ps_b, yT[i], wk, D, 512, (0, 512))
        p.op("vector", lambda E, i=i: E.tensor_copy(st_k[i][:], ps_b[:]),
             reads=[ps_b], writes=[st_k[i]])
        for h in range(2):
            bank = ps_c if h == 0 else ps_d
            emit_gemm_tokmajor(p, bank, yT[i], wv, D, D, (h * 512, (h + 1) * 512))
            p.op("vector", lambda E, i=i, h=h, bank=bank:
                 E.tensor_copy(st_v[i][:, h * 512:(h + 1) * 512], bank[:]),
                 reads=[bank], writes=[st_v[i]])
        emit_gemm_tokmajor(p, ps_a, yT[i], wif, D, 16, (0, 16))
        p.op("vector", lambda E, i=i: E.tensor_copy(st_if[i][:], ps_a[:, 0:16]),
             reads=[ps_a], writes=[st_if[i]])
        for h in range(2):
            bank = ps_b if h == 0 else ps_c
            emit_gemm_tokmajor(p, bank, yT[i], wog, D, D, (h * 512, (h + 1) * 512))
            p.op("scalar", lambda E, i=i, h=h, bank=bank:
                 E.activation(tmp_t[i][:, h * 512:(h + 1) * 512], bank[:], AF.Tanh,
                              scale=c_inv15),
                 reads=[bank, c_inv15], writes=[tmp_t[i]])
        p.op("scalar", lambda E, i=i: E.activation(st_o[i][:], tmp_t[i][:], AF.Sigmoid,
                                                   scale=c_15),
             reads=[tmp_t[i], c_15], writes=[st_o[i]])
        p.dma("gpsimd", o_q[r0:r0 + 128, :], st_q[i][:])
        p.dma("gpsimd", o_k[r0:r0 + 128, :], st_k[i][:])
        p.dma("gpsimd", o_v[r0:r0 + 128, :], st_v[i][:])
        p.dma("gpsimd", o_if[r0:r0 + 128, :], st_if[i][:])
        p.dma("gpsimd", o_o[r0:r0 + 128, :], st_o[i][:])
    p.barrier_end()
    return p.realize()


def run_L1(inputs):
    d = inputs
    x0 = d["L0_attn_x"]
    wif = np.ascontiguousarray(np.concatenate([d["L0_attn_Wi"], d["L0_attn_Wf"]], 1))
    nc = build_L1()
    in_maps = []
    for c in range(NCORES):
        xs = np.ascontiguousarray(
            x0[:, c * SEG:(c + 1) * SEG, :].reshape(NTOK, D))
        in_maps.append({
            "x0": xs, "Wq": d["L0_attn_Wq"], "Wk": d["L0_attn_Wk"],
            "Wv": d["L0_attn_Wv"], "Wif": wif, "Wog": d["L0_attn_Wog"],
        })
    res = run_bass_kernel_spmd(nc, in_maps, core_ids=list(range(NCORES)))
    outs = {}
    for key, w in [("q", 512), ("k", 512), ("v", D), ("ifr", 16), ("og", D)]:
        full = np.stack([res.results[c][key].reshape(B, SEG, w)
                         for c in range(NCORES)], axis=1)
        outs[key] = full.reshape(B, NCORES * SEG, w)
    return outs


def _kernel_impl(inputs):
    l1 = run_L1(inputs)
    l2 = run_L2(inputs, l1)
    l3 = run_L3(inputs, l1, l2)
    l4 = run_L4(inputs, l3)
    l4c = run_L4c(inputs, l4)
    l5 = run_L5(inputs, l3, l4c)
    return np.ascontiguousarray(l5["x4"].astype(np.float32))


SL = 16384
NSEG = 8
C = 128


def emit_mlstm_bh(p, t, bh, qT_d, kT_d, v_d, i_d, f_d, bicol, bfcol, lnw_rep, lnb_rep,
                  hn_out):
    ALU_ = ALU
    c_m1 = p.constcol(-1.0)
    c_1 = p.constcol(1.0)
    p.op("vector", lambda E: E.memset(t.c_f32[:], 0.0), writes=[t.c_f32])
    p.op("vector", lambda E: E.memset(t.mstate[:], 0.0), writes=[t.mstate])
    for seg in range(NSEG):
        s0 = seg * SEG
        p.dma("sync", t.qT[:], qT_d[bh, :, s0:s0 + SEG])
        p.dma("sync", t.kT[:], kT_d[bh, :, s0:s0 + SEG])
        p.dma("sync", t.v_sb[:].rearrange("p (cc d) -> p cc d", d=128),
              v_d[bh, s0:s0 + SEG, :].rearrange("(cc p) d -> p cc d", p=128))
        p.dma("sync", t.i_r[0:1, :], i_d[bh:bh + 1, s0:s0 + SEG])
        p.dma("sync", t.f_r[0:1, :], f_d[bh:bh + 1, s0:s0 + SEG])
        p.op("vector", lambda E: E.tensor_scalar(t.qTb[:], t.qT[:], 0.125, None, ALU_.mult),
             reads=[t.qT], writes=[t.qTb])
        p.op("vector", lambda E: E.tensor_copy(t.kTb[:], t.kT[:]), reads=[t.kT], writes=[t.kTb])
        p.op("vector", lambda E: E.tensor_copy(t.vb[:], t.v_sb[:]), reads=[t.v_sb], writes=[t.vb])
        R = lambda x: x[0:1, :]
        p.op("vector", lambda E: E.tensor_scalar(R(t.fb), R(t.f_r), bfcol, None, ALU_.add),
             reads=[t.f_r, bfcol], writes=[t.fb])
        p.op("vector", lambda E: E.tensor_scalar(R(t.g1), R(t.fb), -1.0, None, ALU_.mult),
             reads=[t.fb], writes=[t.g1])
        p.op("vector", lambda E: E.tensor_tensor(R(t.g2), R(t.fb), R(t.g1), ALU_.max),
             reads=[t.fb, t.g1], writes=[t.g2])
        p.op("scalar", lambda E: E.activation(R(t.g1), R(t.g2), AF.Exp, scale=c_m1[0:1, 0:1]),
             reads=[t.g2, c_m1], writes=[t.g1])
        p.op("scalar", lambda E: E.activation(R(t.g2), R(t.g1), AF.Ln, bias=c_1[0:1, 0:1]),
             reads=[t.g1, c_1], writes=[t.g2])
        p.op("vector", lambda E: E.tensor_scalar(R(t.g1), R(t.fb), 0.0, None, ALU_.min),
             reads=[t.fb], writes=[t.g1])
        p.op("vector", lambda E: E.tensor_tensor(R(t.flog), R(t.g1), R(t.g2), ALU_.subtract),
             reads=[t.g1, t.g2], writes=[t.flog])
        p.op("vector", lambda E: E.tensor_tensor_scan(R(t.Fc), R(t.ones), R(t.flog),
                                                      0.0, ALU_.mult, ALU_.add),
             reads=[t.ones, t.flog], writes=[t.Fc])
        p.op("vector", lambda E: E.tensor_scalar(R(t.g1), R(t.i_r), bicol, None, ALU_.add),
             reads=[t.i_r, bicol], writes=[t.g1])
        p.op("vector", lambda E: E.tensor_tensor(R(t.bg), R(t.g1), R(t.Fc), ALU_.subtract),
             reads=[t.g1, t.Fc], writes=[t.bg])
        p.op("vector", lambda E: E.tensor_tensor_scan(R(t.M), R(t.bg), R(t.bg),
                                                      t.mstate[0:1, 0:1], ALU_.max, ALU_.max),
             reads=[t.bg, t.mstate], writes=[t.M])
        for cc in range(SEG // C):
            c0 = cc * C
            if cc == 0:
                src = t.mstate[0:1, 0:1]
                rd = t.mstate
            else:
                src = t.M[0:1, c0 - 1:c0]
                rd = t.M
            p.op("vector", lambda E, c0=c0, src=src, rd=rd:
                 E.tensor_scalar(t.mp[0:1, c0:c0 + C], t.ones[0:1, c0:c0 + C], src,
                                 None, ALU_.mult),
                 reads=[t.ones, rd], writes=[t.mp])
            p.op("vector", lambda E, c0=c0:
                 E.tensor_scalar(t.nm[0:1, c0:c0 + C], t.ones[0:1, c0:c0 + C],
                                 t.M[0:1, c0 + C - 1:c0 + C], -1.0, ALU_.mult, ALU_.mult),
                 reads=[t.ones, t.M], writes=[t.nm])
        p.op("vector", lambda E: E.tensor_tensor(t.mstate[0:1, 0:1], t.Fc[0:1, SEG - 1:SEG],
                                                 t.M[0:1, SEG - 1:SEG], ALU_.add),
             reads=[t.Fc, t.M], writes=[t.mstate])
        for cc in range(SEG // C):
            c0 = cc * C
            sl = slice(c0, c0 + C)
            p.op("tensor", lambda E, sl=sl: E.transpose(t.psG[:, 0:1], t.bg[0:1, sl], t.identf[0:1, 0:1]),
                 reads=[t.bg, t.identf], writes=[t.psG])
            p.op("tensor", lambda E, sl=sl: E.transpose(t.psG[:, 1:2], t.mp[0:1, sl], t.identf[0:1, 0:1]),
                 reads=[t.mp, t.identf], writes=[t.psG])
            p.op("tensor", lambda E, sl=sl: E.transpose(t.psG[:, 2:3], t.nm[0:1, sl], t.identf[0:1, 0:1]),
                 reads=[t.nm, t.identf], writes=[t.psG])
            p.op("scalar", lambda E: E.copy(t.gcols[:, 0:3], t.psG[:, 0:3]),
                 reads=[t.psG], writes=[t.gcols])
            p.op("tensor", lambda E, sl=sl: E.matmul(t.psMrep[:], t.sel0[:], t.M[:, sl],
                                                     start=True, stop=True),
                 reads=[t.sel0, t.M], writes=[t.psMrep])
            p.op("scalar", lambda E: E.activation(t.W_sb[:], t.psMrep[:], AF.Exp,
                                                  scale=c_m1, bias=t.gcols[:, 0:1]),
                 reads=[t.psMrep, t.gcols, c_m1], writes=[t.W_sb])
            p.op("vector", lambda E: E.tensor_tensor(t.Wm[:], t.W_sb[:], t.trimask[:], ALU_.mult),
                 reads=[t.W_sb, t.trimask], writes=[t.Wm])
            p.op("tensor", lambda E, sl=sl: E.matmul(t.psSb[:], t.kTb[:, sl], t.qTb[:, sl],
                                                     start=True, stop=True),
                 reads=[t.kTb, t.qTb], writes=[t.psSb])
            p.op("vector", lambda E: E.tensor_tensor(t.DSb[:], t.Wm[:], t.psSb[:], ALU_.mult),
                 reads=[t.Wm, t.psSb], writes=[t.DSb])
            p.op("scalar", lambda E: E.activation(t.e_rep[:], t.psMrep[0:64, :], AF.Exp,
                                                  scale=c_m1[0:64, 0:1], bias=t.gcols[0:64, 1:2]),
                 reads=[t.psMrep, t.gcols, c_m1], writes=[t.e_rep])
            p.op("vector", lambda E, sl=sl: E.tensor_tensor(t.qs[:], t.qTb[0:64, sl],
                                                            t.e_rep[:], ALU_.mult),
                 reads=[t.qTb, t.e_rep], writes=[t.qs])
            p.op("vector", lambda E: E.tensor_copy(t.c_bf[:], t.c_f32[:]),
                 reads=[t.c_f32], writes=[t.c_bf])
            p.op("tensor", lambda E, cc=cc: E.matmul(t.psH[:], t.DSb[:],
                                                     t.vb[:, cc * 128:(cc + 1) * 128],
                                                     start=True, stop=False),
                 reads=[t.DSb, t.vb], writes=[t.psH])
            p.op("tensor", lambda E: E.matmul(t.psH[:], t.qs[:], t.c_bf[:],
                                              start=False, stop=True),
                 reads=[t.qs, t.c_bf], writes=[t.psH])
            p.op("scalar", lambda E: E.activation(t.wcol[:], t.gcols[:, 0:1], AF.Exp,
                                                  bias=t.gcols[:, 2:3]),
                 reads=[t.gcols], writes=[t.wcol])
            p.op("scalar", lambda E: E.activation(t.dec[:], t.gcols[0:64, 1:2], AF.Exp,
                                                  bias=t.gcols[0:64, 2:3]),
                 reads=[t.gcols], writes=[t.dec])
            p.op("tensor", lambda E, sl=sl: E.transpose(t.psKt[:], t.kT[:, sl], t.identf[0:64, 0:64]),
                 reads=[t.kT, t.identf], writes=[t.psKt])
            p.op("vector", lambda E: E.tensor_scalar(t.kw[:], t.psKt[:], t.wcol[:, 0:1],
                                                     None, ALU_.mult),
                 reads=[t.psKt, t.wcol], writes=[t.kw])
            p.op("tensor", lambda E, cc=cc: E.matmul(t.psC[:], t.kw[:],
                                                     t.vb[:, cc * 128:(cc + 1) * 128],
                                                     start=True, stop=True),
                 reads=[t.kw, t.vb], writes=[t.psC])
            p.op("vector", lambda E: E.tensor_scalar(t.c_f32[:], t.c_f32[:], t.dec[:, 0:1],
                                                     None, ALU_.mult),
                 reads=[t.c_f32, t.dec], writes=[t.c_f32])
            p.op("vector", lambda E: E.tensor_tensor(t.c_f32[:], t.c_f32[:], t.psC[:],
                                                     ALU_.add),
                 reads=[t.c_f32, t.psC], writes=[t.c_f32])
            scr = t.scr
            p.op("vector", lambda E: E.reduce_sum(scr[:, 0:1], t.psH[:], axis=AX.X),
                 reads=[t.psH], writes=[scr])
            p.op("vector", lambda E: E.tensor_scalar(scr[:, 0:1], scr[:, 0:1], 1.0 / 128,
                                                     None, ALU_.mult),
                 reads=[scr], writes=[scr])
            p.op("vector", lambda E: E.tensor_scalar(t.hbuf[:], t.psH[:], scr[:, 0:1],
                                                     None, ALU_.subtract),
                 reads=[t.psH, scr], writes=[t.hbuf])
            p.op("scalar", lambda E: E.activation(scr[:, 2:130], t.hbuf[:], AF.Square),
                 reads=[t.hbuf], writes=[scr])
            p.op("vector", lambda E: E.reduce_sum(scr[:, 1:2], scr[:, 2:130], axis=AX.X),
                 reads=[scr], writes=[scr])
            p.op("vector", lambda E: E.tensor_scalar(scr[:, 1:2], scr[:, 1:2], 1.0 / 128,
                                                     1e-6, ALU_.mult, ALU_.add),
                 reads=[scr], writes=[scr])
            p.op("scalar", lambda E: E.activation(scr[:, 1:2], scr[:, 1:2], AF.Sqrt),
                 reads=[scr], writes=[scr])
            p.op("vector", lambda E: E.reciprocal(scr[:, 1:2], scr[:, 1:2]),
                 reads=[scr], writes=[scr])
            p.op("vector", lambda E: E.scalar_tensor_tensor(t.hbuf[:], t.hbuf[:],
                                                            scr[:, 1:2], lnw_rep,
                                                            ALU_.mult, ALU_.mult),
                 reads=[t.hbuf, scr, lnw_rep], writes=[t.hbuf])
            p.op("vector", lambda E, c0=c0: E.tensor_tensor(
                t.hn_st[:, c0:c0 + C], t.hbuf[:], lnb_rep, ALU_.add),
                 reads=[t.hbuf, lnb_rep], writes=[t.hn_st])
        p.dma("gpsimd", hn_out[bh, s0:s0 + SEG, :].rearrange("(cc p) d -> p cc d", p=128),
              t.hn_st[:].rearrange("p (cc d) -> p cc d", d=128))


class _T:
    pass


def build_L2():
    p = Prog("L2")
    qT_d = p.din("qT", [2, 64, SL])
    kT_d = p.din("kT", [2, 64, SL])
    v_d = p.din("v2", [2, SL, 128])
    i_d = p.din("i2", [2, SL])
    f_d = p.din("f2", [2, SL])
    bi_d = p.din("bi2", [2, 1])
    bf_d = p.din("bf2", [2, 1])
    lnw_d = p.din("lnw2", [2, 128, 128])
    lnb_d = p.din("lnb2", [2, 128, 128])
    hn_out = p.dout("hn", [2, SL, 128])

    t = _T()
    identb = make_identity(p)
    t.identf = p.sb("identfp", [128, 128], F32)
    p.op("vector", lambda E: E.tensor_copy(t.identf[:], identb[:]),
         reads=[identb], writes=[t.identf])
    iop = p.sb("iop", [128, 128], F32)
    p.op("gpsimd", lambda E: E.iota(iop[:], pattern=[[0, 128]], channel_multiplier=1,
                                    allow_small_or_imprecise_dtypes=True), writes=[iop])
    t.sel0 = p.sb("sel0", [128, 128], F32)
    p.op("vector", lambda E: E.tensor_scalar(t.sel0[:], iop[:], 0.0, None, ALU.is_equal),
         reads=[iop], writes=[t.sel0])
    iop2 = p.sb("iop2", [128, 128], F32)
    p.op("gpsimd", lambda E: E.iota(iop2[:], pattern=[[1, 128]], channel_multiplier=-1,
                                    allow_small_or_imprecise_dtypes=True), writes=[iop2])
    t.trimask = p.sb("trimask", [128, 128], F32)
    p.op("vector", lambda E: E.tensor_scalar(t.trimask[:], iop2[:], 0.0, None, ALU.is_ge),
         reads=[iop2], writes=[t.trimask])

    t.qT = p.sb("qT_s", [64, SEG]); t.kT = p.sb("kT_s", [64, SEG])
    t.qTb = p.sb("qTb", [64, SEG], BF16); t.kTb = p.sb("kTb", [64, SEG], BF16)
    t.v_sb = p.sb("v_sb", [128, SEG]); t.vb = p.sb("vb", [128, SEG], BF16)
    t.i_r = p.sb("i_r", [128, SEG]); t.f_r = p.sb("f_r", [128, SEG])
    t.fb = p.sb("fb", [128, SEG]); t.g1 = p.sb("g1", [128, SEG])
    t.g2 = p.sb("g2", [128, SEG]); t.flog = p.sb("flog", [128, SEG])
    t.Fc = p.sb("Fcg", [128, SEG]); t.bg = p.sb("bgg", [128, SEG])
    t.M = p.sb("Mg", [128, SEG]); t.mp = p.sb("mpg", [128, SEG])
    t.nm = p.sb("nmg", [128, SEG]); t.ones = p.sb("ones1", [128, SEG])
    t.mstate = p.sb("mstate", [128, 1])
    t.gcols = p.sb("gcols", [128, 4])
    t.W_sb = p.sb("W_sb", [128, 128]); t.Wm = p.sb("Wm", [128, 128])
    t.DSb = p.sb("DSbb", [128, 128], BF16)
    t.e_rep = p.sb("e_rep", [64, 128], BF16)
    t.qs = p.sb("qs", [64, 128], BF16)
    t.c_f32 = p.sb("c_f32", [64, 128]); t.c_bf = p.sb("c_bfb", [64, 128], BF16)
    t.wcol = p.sb("wcol", [128, 1]); t.dec = p.sb("dec", [64, 1])
    t.kw = p.sb("kwb", [128, 64], BF16)
    t.scr = p.sb("scrL2", [128, 130])
    t.hbuf = p.sb("hbuf", [128, 128])
    t.hn_st = p.sb("hn_st", [128, SEG])
    bicol = p.sb("bicol", [1, 1]); bfcol = p.sb("bfcol", [1, 1])
    lnw_rep = p.sb("lnw_rep", [128, 128]); lnb_rep = p.sb("lnb_rep", [128, 128])

    t.psG = p.ps("psG", [128, 4], F32)
    t.psMrep = p.ps("psMrep", [128, 128], F32)
    t.psSb = p.ps("psSb", [128, 128], F32)
    t.psH = p.ps("psH", [128, 128], F32)
    t.psKt = p.ps("psKt", [128, 64], F32)
    t.psC = p.ps("psC", [64, 128], F32)

    p.op("vector", lambda E: E.memset(t.M[:], 0.0), writes=[t.M])
    p.op("vector", lambda E: E.memset(t.ones[:], 1.0), writes=[t.ones])

    for bh in range(2):
        p.dma("sync", bicol[0:1, :], bi_d[bh:bh + 1, :])
        p.dma("sync", bfcol[0:1, :], bf_d[bh:bh + 1, :])
        p.dma("sync", lnw_rep[:], lnw_d[bh, :, :])
        p.dma("sync", lnb_rep[:], lnb_d[bh, :, :])
        emit_mlstm_bh(p, t, bh, qT_d, kT_d, v_d, i_d, f_d,
                      bicol[0:1, 0:1], bfcol[0:1, 0:1], lnw_rep[:], lnb_rep[:], hn_out)
    p.barrier_end()
    return p.realize()


def run_L2(d, l1):
    q = l1["q"].reshape(B, SL, 8, 64)
    k = l1["k"].reshape(B, SL, 8, 64)
    v = l1["v"].reshape(B, SL, 8, 128)
    ifr = l1["ifr"]
    lnw, lnb = d["L0_attn_ln_w"], d["L0_attn_ln_b"]
    bi, bf = d["L0_attn_bi"], d["L0_attn_bf"]
    nc = build_L2()
    in_maps = []
    for c in range(NCORES):
        pairs = [(2 * c) // 8, (2 * c) % 8], [(2 * c + 1) // 8, (2 * c + 1) % 8]
        qT = np.stack([np.ascontiguousarray(q[b_, :, h_, :].T) for b_, h_ in pairs])
        kT = np.stack([np.ascontiguousarray(k[b_, :, h_, :].T) for b_, h_ in pairs])
        v2 = np.stack([np.ascontiguousarray(v[b_, :, h_, :]) for b_, h_ in pairs])
        i2 = np.stack([np.ascontiguousarray(ifr[b_, :, h_]) for b_, h_ in pairs])
        f2 = np.stack([np.ascontiguousarray(ifr[b_, :, 8 + h_]) for b_, h_ in pairs])
        bi2 = np.array([[bi[h_]] for b_, h_ in pairs], np.float32)
        bf2 = np.array([[bf[h_]] for b_, h_ in pairs], np.float32)
        lnw2 = np.stack([np.broadcast_to(lnw[h_][None, :], (128, 128)).copy()
                         for b_, h_ in pairs])
        lnb2 = np.stack([np.broadcast_to(lnb[h_][None, :], (128, 128)).copy()
                         for b_, h_ in pairs])
        in_maps.append({"qT": qT, "kT": kT, "v2": v2, "i2": i2, "f2": f2,
                        "bi2": bi2, "bf2": bf2, "lnw2": lnw2, "lnb2": lnb2})
    res = run_bass_kernel_spmd(nc, in_maps, core_ids=list(range(NCORES)))
    hn = np.zeros((B, SL, 8, 128), np.float32)
    for c in range(NCORES):
        for j in range(2):
            bh = 2 * c + j
            hn[bh // 8, :, bh % 8, :] = res.results[c]["hn"][j]
    return {"hn": hn.reshape(B, SL, 1024)}


GELU_C = 0.7978845608028654


def emit_gelu_tanh(p, out_bf, ps_in, b1col, t, c_gelu):
    p.op("vector", lambda E: E.tensor_scalar(t.zs[:], ps_in, b1col, None, ALU.add),
         reads=[ps_in, b1col], writes=[t.zs])
    p.op("scalar", lambda E: E.activation(t.z2[:], t.zs[:], AF.Square),
         reads=[t.zs], writes=[t.z2])
    p.op("vector", lambda E: E.tensor_tensor(t.z3[:], t.z2[:], t.zs[:], ALU.mult),
         reads=[t.z2, t.zs], writes=[t.z3])
    p.op("vector", lambda E: E.scalar_tensor_tensor(t.u[:], t.z3[:], 0.044715, t.zs[:],
                                                    ALU.mult, ALU.add),
         reads=[t.z3, t.zs], writes=[t.u])
    p.op("scalar", lambda E: E.activation(t.u[:], t.u[:], AF.Tanh, scale=c_gelu),
         reads=[t.u, c_gelu], writes=[t.u])
    p.op("vector", lambda E: E.tensor_tensor(t.z3[:], t.zs[:], t.u[:], ALU.mult),
         reads=[t.zs, t.u], writes=[t.z3])
    p.op("vector", lambda E: E.tensor_tensor(t.z2[:], t.z3[:], t.zs[:], ALU.add),
         reads=[t.z3, t.zs], writes=[t.z2])
    p.op("vector", lambda E: E.tensor_scalar(out_bf, t.z2[:], 0.5, None, ALU.mult),
         reads=[t.z2], writes=[out_bf])


def emit_L3(p, io):
    x0 = io["x0"]
    hn_d = io["hn"]
    og_d = io["og"]
    wout_d = io["Wout"]
    u1_d = io["U1"]; v1_d = io["V1"]
    u2_d = io["U2"]; v2_d = io["V2"]
    win_d = io["W_in"]
    b1s_d = io["b1s"]
    b2r_d = io["b2r"]
    lnwr_d = io["lnwr"]
    lnbr_d = io["lnbr"]
    o_x2 = io["x2"]
    o_y2 = io["y2"]
    o_xin = io["xinT"]
    o_zg = io["zgT"]

    ident = make_identity(p)
    c_gelu = p.constcol(GELU_C)
    wout = load_weight_bf16(p, "wout", wout_d, D, D)
    u1 = load_weight_bf16(p, "u1", u1_d, D, 256)
    v1 = load_weight_bf16(p, "v1", v1_d, 256, 4096)
    u2 = load_weight_bf16(p, "u2", u2_d, 4096, 256)
    v2 = load_weight_bf16(p, "v2", v2_d, 256, D)
    win = load_weight_bf16(p, "win", win_d, D, 4096)
    b1s = p.sb("b1s_s", [128, 32])
    b2r = p.sb("b2r_s", [128, D])
    lnwr = p.sb("lnwr_s", [128, D]); lnbr = p.sb("lnbr_s", [128, D])
    p.dma("sync", b1s[:], b1s_d)
    p.dma("sync", b2r[:], b2r_d)
    p.dma("sync", lnwr[:], lnwr_d)
    p.dma("sync", lnbr[:], lnbr_d)

    t = _T()
    NB = 1
    bufs = {}
    for nm, shape, dt in [("xb", [128, D + 2], F32), ("hnb", [128, D], F32),
                          ("ogb", [128, D], F32),
                          ("x1b", [128, D], F32), ("y1b", [128, D], F32),
                          ("castb", [128, D], BF16), ("xTb", [128, D], BF16),
                          ("x2b", [128, D], F32), ("y2b", [128, D], F32),
                          ("pbb", [128, 256], BF16), ("ptb", [128, 256], BF16),
                          ("htb", [128, 32 * 128], BF16),
                          ("qbb", [128, 256], BF16), ("qtb2", [128, 256], BF16),
                          ("xzst", [128, 16 * 128], F32), ("zgst", [128, 16 * 128], F32)]:
        bufs[nm] = [p.sb(f"{nm}{i}", shape, dt) for i in range(NB)]
    t.zs = bufs["ogb"][0][:, 0:512]
    t.z2 = bufs["ogb"][0][:, 512:1024]
    t.z3 = bufs["hnb"][0][:, 0:512]
    t.u = bufs["hnb"][0][:, 512:1024]
    psT = p.ps("psT", [128, 128], BF16)
    ps_a = p.ps("ps_a", [128, 512]); ps_b = p.ps("ps_b", [128, 512])
    psF = p.ps("psF", [128, 512])
    psQ = p.ps("psQ", [128, 256])
    psE = p.ps("psE", [128, 128])

    ntiles = NTOK // 128
    for ti in range(ntiles):
        i = ti % NB
        r0 = ti * 128
        xb = bufs["xb"][i]; hnb = bufs["hnb"][i]; ogb = bufs["ogb"][i]
        gb = hnb; x1b = bufs["x1b"][i]; y1b = bufs["y1b"][i]
        cast = bufs["castb"][i]; xT = bufs["xTb"][i]
        x2b = bufs["x2b"][i]; y2b = bufs["y2b"][i]; y2nb = y1b
        xnb = x1b; scr = xb
        pb = bufs["pbb"][i]; pt = bufs["ptb"][i]; ht = bufs["htb"][i]
        qb = bufs["qbb"][i]; qt = bufs["qtb2"][i]; wbuf = ogb
        xzst = bufs["xzst"][i]; zgst = bufs["zgst"][i]

        p.dma("sync", xb[:, 0:D], x0[r0:r0 + 128, :])
        p.dma("sync", hnb[:], hn_d[r0:r0 + 128, :])
        p.dma("sync", ogb[:], og_d[r0:r0 + 128, :])
        p.op("vector", lambda E, hnb=hnb, ogb=ogb, gb=gb:
             E.tensor_tensor(gb[:], hnb[:], ogb[:], ALU.mult),
             reads=[hnb, ogb], writes=[gb])
        emit_cast_transpose(p, xT, gb[:], cast, ident, psT)
        for h in range(2):
            bank = ps_a if h == 0 else ps_b
            emit_gemm_tokmajor(p, bank, xT, wout, D, D, (h * 512, (h + 1) * 512))
            p.op("vector", lambda E, bank=bank, h=h, x1b=x1b, xb=xb:
                 E.scalar_tensor_tensor(x1b[:, h * 512:(h + 1) * 512], bank[:], 1.0,
                                        xb[:, h * 512:(h + 1) * 512], ALU.mult, ALU.add),
                 reads=[bank, xb], writes=[x1b])
        emit_ln(p, y1b[:], x1b[:], scr, 1e-5)
        emit_cast_transpose(p, xT, y1b[:], cast, ident, psT)
        emit_gemm_tokmajor(p, psQ, xT, u1, D, 256, (0, 256))
        p.op("vector", lambda E, pb=pb: E.tensor_copy(pb[:], psQ[:]),
             reads=[psQ], writes=[pb])
        for kk in range(2):
            p.op("tensor", lambda E, kk=kk, pb=pb: E.transpose(
                psT[:], pb[:, kk * 128:(kk + 1) * 128], ident[:]),
                 reads=[pb, ident], writes=[psT])
            p.op("scalar", lambda E, kk=kk, pt=pt: E.copy(pt[:, kk * 128:(kk + 1) * 128], psT[:]),
                 reads=[psT], writes=[pt])
        for f4 in range(8):
            for j in range(4):
                fch = f4 * 4 + j
                for kk in range(2):
                    p.op("tensor", lambda E, fch=fch, kk=kk, j=j, pt=pt: E.matmul(
                        psF[:, j * 128:(j + 1) * 128],
                        v1[:, kk * 4096 + fch * 128:kk * 4096 + (fch + 1) * 128],
                        pt[:, kk * 128:(kk + 1) * 128],
                        start=(kk == 0), stop=(kk == 1)),
                         reads=[v1, pt], writes=[psF])
            for j in range(4):
                fch = f4 * 4 + j
                p.op("vector", lambda E, fch=fch, j=j: E.tensor_scalar(
                    t.zs[:, j * 128:(j + 1) * 128], psF[:, j * 128:(j + 1) * 128],
                    b1s[:, fch:fch + 1], None, ALU.add),
                     reads=[psF, b1s], writes=[t.zs])
            out_bf = ht[:, f4 * 512:(f4 + 1) * 512]
            p.op("scalar", lambda E: E.activation(t.z2[:], t.zs[:], AF.Square),
                 reads=[t.zs], writes=[t.z2])
            p.op("vector", lambda E: E.tensor_tensor(t.z3[:], t.z2[:], t.zs[:], ALU.mult),
                 reads=[t.z2, t.zs], writes=[t.z3])
            p.op("vector", lambda E: E.scalar_tensor_tensor(
                t.u[:], t.z3[:], 0.044715, t.zs[:], ALU.mult, ALU.add),
                 reads=[t.z3, t.zs], writes=[t.u])
            p.op("scalar", lambda E: E.activation(t.u[:], t.u[:], AF.Tanh, scale=c_gelu),
                 reads=[t.u, c_gelu], writes=[t.u])
            p.op("vector", lambda E: E.tensor_tensor(t.z3[:], t.zs[:], t.u[:], ALU.mult),
                 reads=[t.zs, t.u], writes=[t.z3])
            p.op("vector", lambda E: E.tensor_tensor(t.z2[:], t.z3[:], t.zs[:], ALU.add),
                 reads=[t.z3, t.zs], writes=[t.z2])
            p.op("vector", lambda E, out_bf=out_bf: E.tensor_scalar(
                out_bf, t.z2[:], 0.5, None, ALU.mult),
                 reads=[t.z2], writes=[ht])
        for kk in range(32):
            p.op("tensor", lambda E, kk=kk, ht=ht: E.matmul(
                psQ[:], ht[:, kk * 128:(kk + 1) * 128], u2[:, kk * 256:(kk + 1) * 256],
                start=(kk == 0), stop=(kk == 31)),
                 reads=[ht, u2], writes=[psQ])
        p.op("vector", lambda E, qb=qb: E.tensor_copy(qb[:], psQ[:]),
             reads=[psQ], writes=[qb])
        for kk in range(2):
            p.op("tensor", lambda E, kk=kk, qb=qb: E.transpose(
                psT[:], qb[:, kk * 128:(kk + 1) * 128], ident[:]),
                 reads=[qb, ident], writes=[psT])
            p.op("scalar", lambda E, kk=kk, qt=qt: E.copy(qt[:, kk * 128:(kk + 1) * 128], psT[:]),
                 reads=[psT], writes=[qt])
        p.op("vector", lambda E, wbuf=wbuf, x1b=x1b: E.tensor_tensor(
            wbuf[:], x1b[:], b2r[:], ALU.add),
             reads=[x1b, b2r], writes=[wbuf])
        for h in range(2):
            bank = ps_a if h == 0 else ps_b
            for kk in range(2):
                p.op("tensor", lambda E, kk=kk, h=h, bank=bank, qt=qt: E.matmul(
                    bank[:], qt[:, kk * 128:(kk + 1) * 128],
                    v2[:, kk * D + h * 512:kk * D + (h + 1) * 512],
                    start=(kk == 0), stop=(kk == 1)),
                     reads=[qt, v2], writes=[bank])
            p.op("vector", lambda E, bank=bank, h=h, x2b=x2b, wbuf=wbuf:
                 E.scalar_tensor_tensor(x2b[:, h * 512:(h + 1) * 512], bank[:], 1.0,
                                        wbuf[:, h * 512:(h + 1) * 512], ALU.mult, ALU.add),
                 reads=[bank, wbuf], writes=[x2b])
        emit_ln(p, y2b[:], x2b[:], scr, 1e-5)
        emit_ln(p, y2nb[:], y2b[:], scr, 1e-5)
        p.op("vector", lambda E, y2nb=y2nb, xnb=xnb: E.tensor_tensor(
            xnb[:], y2nb[:], lnwr[:], ALU.mult),
             reads=[y2nb, lnwr], writes=[xnb])
        p.op("vector", lambda E, xnb=xnb: E.tensor_tensor(xnb[:], xnb[:], lnbr[:], ALU.add),
             reads=[xnb, lnbr], writes=[xnb])
        emit_cast_transpose(p, xT, xnb[:], cast, ident, psT)
        for half, stage in ((0, xzst), (1, zgst)):
            for ech in range(16):
                e0 = half * 2048 + ech * 128
                for kk in range(8):
                    p.op("tensor", lambda E, e0=e0, kk=kk, xT=xT: E.matmul(
                        psE[:], win[:, kk * 4096 + e0:kk * 4096 + e0 + 128],
                        xT[:, kk * 128:(kk + 1) * 128],
                        start=(kk == 0), stop=(kk == 7)),
                         reads=[win, xT], writes=[psE])
                p.op("scalar", lambda E, ech=ech, stage=stage: E.copy(
                    stage[:, ech * 128:(ech + 1) * 128], psE[:]),
                     reads=[psE], writes=[stage])
        p.dma("gpsimd", o_x2[r0:r0 + 128, :], x2b[:])
        p.dma("gpsimd", o_y2[r0:r0 + 128, :], y2b[:])
        p.dma("gpsimd", o_xin[:, r0:r0 + 128].rearrange("(ech q) t -> q ech t", q=128),
              xzst[:].rearrange("q (ech t) -> q ech t", t=128))
        p.dma("gpsimd", o_zg[:, r0:r0 + 128].rearrange("(ech q) t -> q ech t", q=128),
              zgst[:].rearrange("q (ech t) -> q ech t", t=128))


def build_L3():
    p = Prog("build_L3")
    io = {
        "x0": p.din("x0", [NTOK, D]),
        "hn": p.din("hn", [NTOK, D]),
        "og": p.din("og", [NTOK, D]),
        "Wout": p.din("Wout", [D, D]),
        "U1": p.din("U1", [D, 256]),
        "V1": p.din("V1", [256, 4096]),
        "U2": p.din("U2", [4096, 256]),
        "V2": p.din("V2", [256, D]),
        "W_in": p.din("W_in", [D, 4096]),
        "b1s": p.din("b1s", [128, 32]),
        "b2r": p.din("b2r", [128, D]),
        "lnwr": p.din("lnwr", [128, D]),
        "lnbr": p.din("lnbr", [128, D]),
        "x2": p.dout("x2", [NTOK, D]),
        "y2": p.dout("y2", [NTOK, D]),
        "xinT": p.dout("xinT", [2048, NTOK]),
        "zgT": p.dout("zgT", [2048, NTOK]),
    }
    emit_L3(p, io)
    p.barrier_end()
    return p.realize()


def run_L3(d, l1, l2):
    x0 = d["L0_attn_x"]
    nc = build_L3()
    b1s = np.ascontiguousarray(d["L0_ffn_b1"].reshape(32, 128).T)
    b2r = np.ascontiguousarray(np.broadcast_to(d["L0_ffn_b2"][None, :], (128, D)))
    lnwr = np.ascontiguousarray(np.broadcast_to(d["L1_attn_ln_w"][None, :], (128, D)))
    lnbr = np.ascontiguousarray(np.broadcast_to(d["L1_attn_ln_b"][None, :], (128, D)))
    in_maps = []
    for c in range(NCORES):
        sl = slice(c * SEG, (c + 1) * SEG)
        in_maps.append({
            "x0": np.ascontiguousarray(x0[:, sl, :].reshape(NTOK, D)),
            "hn": np.ascontiguousarray(l2["hn"][:, sl, :].reshape(NTOK, D)),
            "og": np.ascontiguousarray(l1["og"][:, sl, :].reshape(NTOK, D)),
            "Wout": d["L0_attn_Wout"], "U1": d["L0_ffn_U1"], "V1": d["L0_ffn_V1"],
            "U2": d["L0_ffn_U2"], "V2": d["L0_ffn_V2"], "W_in": d["L1_attn_W_in"],
            "b1s": b1s, "b2r": b2r, "lnwr": lnwr, "lnbr": lnbr,
        })
    res = run_bass_kernel_spmd(nc, in_maps, core_ids=list(range(NCORES)))
    outs = {}
    for key, w in [("x2", D), ("y2", D)]:
        full = np.stack([res.results[c][key].reshape(B, SEG, w) for c in range(NCORES)],
                        axis=1)
        outs[key] = full.reshape(B, NCORES * SEG, w)
    for key in ("xinT", "zgT"):
        full = np.stack([res.results[c][key].reshape(2048, B, SEG)
                         for c in range(NCORES)], axis=2)
        outs[key] = full.reshape(2048, B, NCORES * SEG)
    return outs


def emit_softplus(p, out_ap, z_ap, t, c_m1, c_1, reads_extra=()):
    p.op("vector", lambda E: E.tensor_scalar(t.sp1[:], z_ap, -1.0, None, ALU.mult),
         reads=[z_ap] + list(reads_extra), writes=[t.sp1])
    p.op("vector", lambda E: E.tensor_tensor(t.sp2[:], z_ap, t.sp1[:], ALU.max),
         reads=[z_ap, t.sp1], writes=[t.sp2])
    p.op("scalar", lambda E: E.activation(t.sp1[:], t.sp2[:], AF.Exp, scale=c_m1),
         reads=[t.sp2, c_m1], writes=[t.sp1])
    p.op("scalar", lambda E: E.activation(t.sp2[:], t.sp1[:], AF.Ln, bias=c_1),
         reads=[t.sp1, c_1], writes=[t.sp2])
    p.op("vector", lambda E: E.tensor_scalar(t.sp1[:], z_ap, 0.0, None, ALU.max),
         reads=[z_ap], writes=[t.sp1])
    p.op("vector", lambda E: E.tensor_tensor(out_ap, t.sp1[:], t.sp2[:], ALU.add),
         reads=[t.sp1, t.sp2], writes=[out_ap])


def emit_L4(p, io):
    xin_d = io["xinh"]
    cw_d = io["cw4"]
    cb_d = io["cb4"]
    wxp_d = io["Wxp"]
    wdt_d = io["Wdt"]
    bdt_d = io["bdt"]
    o_dt = io["dtT"]
    o_B = io["Bf"]
    o_C = io["Cf"]
    o_xc = io["xcT"]

    t = _T()
    c_m1 = p.constcol(-1.0); c_1 = p.constcol(1.0)
    wxp = load_weight_bf16(p, "wxp", wxp_d, 2048, 96)
    wdt = p.sb("wdt", [64, 2048], BF16)
    p.op("gpsimd", lambda E: E.dma_start(out=wdt[:], in_=wdt_d), reads=[wdt_d],
         writes=[wdt], dma=True)
    cw = p.sb("cw_s", [128, 16 * 4]); cb = p.sb("cb_s", [128, 16])
    bdt = p.sb("bdt_s", [128, 16])
    p.dma("sync", cw[:].rearrange("p (e k) -> p e k", k=4), cw_d)
    p.dma("sync", cb[:], cb_d)
    p.dma("sync", bdt[:], bdt_d)

    xinb = p.sb("xinb", [128, HS])
    acc = p.sb("accc", [128, SEG])
    xcf = p.sb("xcf", [128, SEG])
    xcb = p.sb("xcbb", [128, 16 * SEG], BF16)
    dtrb = p.sb("dtrb", [64, SEG], BF16)
    bst = p.sb("bst", [16, SEG]); cst = p.sb("cst", [16, SEG])
    t.sp1 = p.sb("sp1", [128, 512]); t.sp2 = p.sb("sp2", [128, 512])
    dtt = p.sb("dtt", [128, 512])
    psDtr = p.ps("psDtr", [64, 512])
    psB = p.ps("psB", [16, 512]); psC2 = p.ps("psC2", [16, 512])
    psDt = p.ps("psDt", [128, 512])

    halo_sb = io.get("halo_sb")
    for b_ in range(B):
        tok0 = b_ * SEG
        for ech in range(16):
            if halo_sb is None:
                p.dma("sync", xinb[:], xin_d[ech * 128:(ech + 1) * 128, b_, :])
            else:
                p.dma("sync", xinb[:, 3:3 + SEG],
                      xin_d[ech * 128:(ech + 1) * 128, tok0:tok0 + SEG])
                p.op("vector", lambda E, ech=ech, b_=b_: E.tensor_copy(
                    xinb[:, 0:3], halo_sb[:, ech * 6 + b_ * 3:ech * 6 + b_ * 3 + 3]),
                     reads=[halo_sb], writes=[xinb])
            for k in range(4):
                wcol = lambda k=k, ech=ech: cw[:, ech * 4 + k:ech * 4 + k + 1]
                if k == 0:
                    p.op("vector", lambda E, k=k, wcol=wcol: E.tensor_scalar(
                        acc[:], xinb[:, k:k + SEG], wcol(), None, ALU.mult),
                         reads=[xinb, cw], writes=[acc])
                else:
                    p.op("vector", lambda E, k=k, wcol=wcol: E.scalar_tensor_tensor(
                        acc[:], xinb[:, k:k + SEG], wcol(), acc[:],
                        ALU.mult, ALU.add),
                         reads=[xinb, cw, acc], writes=[acc])
            p.op("scalar", lambda E, ech=ech: E.activation(xcf[:], acc[:], AF.Silu,
                                                           bias=cb[:, ech:ech + 1]),
                 reads=[acc, cb], writes=[xcf])
            p.dma("scalar", o_xc[ech * 128:(ech + 1) * 128, tok0:tok0 + SEG], xcf[:])
            p.op("vector", lambda E, ech=ech: E.tensor_copy(
                xcb[:, ech * SEG:(ech + 1) * SEG], xcf[:]),
                 reads=[xcf], writes=[xcb])
        for tt in range(SEG // 512):
            ts_ = slice(tt * 512, (tt + 1) * 512)
            for ech in range(16):
                st_ = (ech == 0); sp_ = (ech == 15)
                p.op("tensor", lambda E, ech=ech, ts_=ts_, st_=st_, sp_=sp_: E.matmul(
                    psDtr[:], wxp[:, ech * 96:ech * 96 + 64],
                    xcb[:, ech * SEG:(ech + 1) * SEG][:, ts_], start=st_, stop=sp_),
                     reads=[wxp, xcb], writes=[psDtr])
            for ech in range(16):
                st_ = (ech == 0); sp_ = (ech == 15)
                p.op("tensor", lambda E, ech=ech, ts_=ts_, st_=st_, sp_=sp_: E.matmul(
                    psB[:], wxp[:, ech * 96 + 64:ech * 96 + 80],
                    xcb[:, ech * SEG:(ech + 1) * SEG][:, ts_], start=st_, stop=sp_),
                     reads=[wxp, xcb], writes=[psB])
            for ech in range(16):
                st_ = (ech == 0); sp_ = (ech == 15)
                p.op("tensor", lambda E, ech=ech, ts_=ts_, st_=st_, sp_=sp_: E.matmul(
                    psC2[:], wxp[:, ech * 96 + 80:ech * 96 + 96],
                    xcb[:, ech * SEG:(ech + 1) * SEG][:, ts_], start=st_, stop=sp_),
                     reads=[wxp, xcb], writes=[psC2])
            p.op("vector", lambda E, ts_=ts_: E.tensor_copy(dtrb[:, ts_], psDtr[:]),
                 reads=[psDtr], writes=[dtrb])
            p.op("vector", lambda E, ts_=ts_: E.tensor_copy(bst[:, ts_], psB[:]),
                 reads=[psB], writes=[bst])
            p.op("vector", lambda E, ts_=ts_: E.tensor_copy(cst[:, ts_], psC2[:]),
                 reads=[psC2], writes=[cst])
        p.dma("scalar", o_B[:, tok0:tok0 + SEG], bst[:])
        p.dma("scalar", o_C[:, tok0:tok0 + SEG], cst[:])
        for m in range(16):
            for tt in range(SEG // 512):
                ts_ = slice(tt * 512, (tt + 1) * 512)
                p.op("tensor", lambda E, m=m, ts_=ts_: E.matmul(
                    psDt[:], wdt[:, m * 128:(m + 1) * 128], dtrb[:, ts_],
                    start=True, stop=True),
                     reads=[wdt, dtrb], writes=[psDt])
                p.op("vector", lambda E, m=m: E.tensor_scalar(
                    dtt[:], psDt[:], bdt[:, m:m + 1], None, ALU.add),
                     reads=[psDt, bdt], writes=[dtt])
                emit_softplus(p, dtt[:], dtt[:], t, c_m1, c_1)
                p.dma("scalar", o_dt[m * 128:(m + 1) * 128, tok0 + tt * 512:
                                     tok0 + (tt + 1) * 512], dtt[:])


def build_L4():
    p = Prog("build_L4")
    io = {
        "xinh": p.din("xinh", [2048, B, HS]),
        "cw4": p.din("cw4", [128, 16, 4]),
        "cb4": p.din("cb4", [128, 16]),
        "Wxp": p.din("Wxp", [2048, 96]),
        "Wdt": p.din("Wdt", [64, 2048]),
        "bdt": p.din("bdt", [128, 16]),
        "dtT": p.dout("dtT", [2048, NTOK]),
        "Bf": p.dout("Bf", [16, NTOK]),
        "Cf": p.dout("Cf", [16, NTOK]),
        "xcT": p.dout("xcT", [2048, NTOK]),
    }
    emit_L4(p, io)
    p.barrier_end()
    return p.realize()


def run_L4(d, l3):
    xinT = l3["xinT"]
    nc = build_L4()
    cw4 = np.ascontiguousarray(
        np.transpose(d["L1_attn_conv_w"].reshape(16, 128, 1, 4)[:, :, 0, :], (1, 0, 2)))
    cb4 = np.ascontiguousarray(d["L1_attn_conv_b"].reshape(16, 128).T)
    bdt = np.ascontiguousarray(d["L1_attn_b_dt"].reshape(16, 128).T)
    in_maps = []
    for c in range(NCORES):
        s0 = c * SEG
        halo = np.zeros((2048, B, 3), np.float32) if c == 0 else xinT[:, :, s0 - 3:s0]
        xinh = np.ascontiguousarray(np.concatenate([halo, xinT[:, :, s0:s0 + SEG]], 2))
        in_maps.append({"xinh": xinh, "cw4": cw4, "cb4": cb4,
                        "Wxp": d["L1_attn_W_xproj"], "Wdt": d["L1_attn_W_dt"],
                        "bdt": bdt})
    res = run_bass_kernel_spmd(nc, in_maps, core_ids=list(range(NCORES)))
    outs = {}
    for key, rows in [("dtT", 2048), ("Bf", 16), ("Cf", 16), ("xcT", 2048)]:
        full = np.stack([res.results[c][key].reshape(rows, B, SEG)
                         for c in range(NCORES)], axis=2)
        outs[key] = full.reshape(rows, B, NCORES * SEG)
    return outs


def emit_L4c(p, io):
    TC = 512
    dt_d = io["dts"]
    xc_d = io["xcs"]
    B_d = io["Bs"]
    C_d = io["Cs"]
    alog_d = io["Alog"]
    dpar_d = io["Dpar"]
    o_y = io["yssm"]

    t = _T()
    iop = p.sb("iop", [128, 128], F32)
    p.op("gpsimd", lambda E: E.iota(iop[:], pattern=[[0, 128]], channel_multiplier=1,
                                    allow_small_or_imprecise_dtypes=True), writes=[iop])
    sel = p.sb("seln", [128, 16 * 128], F32)
    for n in range(16):
        p.op("vector", lambda E, n=n: E.tensor_scalar(
            sel[:, n * 128:(n + 1) * 128], iop[:], float(n), None, ALU.is_equal),
             reads=[iop], writes=[sel])
    alog = p.sb("alog_s", [128, 2 * 16])
    p.dma("sync", alog[:].rearrange("p (d n) -> p d n", n=16),
          alog_d.rearrange("(d p) n -> p d n", p=128))
    A_sb = p.sb("A_sb", [128, 2 * 16])
    p.op("scalar", lambda E: E.activation(A_sb[:], alog[:], AF.Exp),
         reads=[alog], writes=[A_sb])
    p.op("vector", lambda E: E.tensor_scalar(A_sb[:], A_sb[:], -1.0, None, ALU.mult),
         reads=[A_sb], writes=[A_sb])
    dpar = p.sb("dpar_s", [128, 2])
    p.op("sync", lambda E: E.dma_start(out=dpar[:],
                                       in_=dpar_d.rearrange("(d p) o -> p (d o)", p=128),
                                       allow_slow_non_contiguous=True),
         reads=[dpar_d], writes=[dpar], dma=True)

    dtb = p.sb("dtb", [128, TC]); xcb2 = p.sb("xcb2", [128, TC])
    udt = p.sb("udt", [128, TC])
    dAb = p.sb("dAb", [128, TC]); bmb = p.sb("bmb", [128, TC])
    hb = p.sb("hb", [128, TC]); ytmp = p.sb("ytmp", [128, TC])
    yacc = p.sb("yacc", [128, TC])
    Bsb = p.sb("Bsb", [128, TC]); Csb = p.sb("Csb", [128, TC])
    hstate = p.sb("hstate", [128, 32])
    psBr = p.ps("psBr", [128, TC]); psCr = p.ps("psCr", [128, TC])
    p.op("vector", lambda E: E.memset(Bsb[:], 0.0), writes=[Bsb])
    p.op("vector", lambda E: E.memset(Csb[:], 0.0), writes=[Csb])

    for b_ in range(B):
        p.op("vector", lambda E: E.memset(hstate[:], 0.0), writes=[hstate])
        for tc in range(SL // TC):
            t0 = tc * TC
            p.dma("sync", Bsb[0:16, :], B_d[:, b_, t0:t0 + TC])
            p.dma("sync", Csb[0:16, :], C_d[:, b_, t0:t0 + TC])
            for dch in range(2):
                p.dma("sync", dtb[:], dt_d[dch * 128:(dch + 1) * 128, b_, t0:t0 + TC])
                p.dma("sync", xcb2[:], xc_d[dch * 128:(dch + 1) * 128, b_, t0:t0 + TC])
                p.op("vector", lambda E: E.tensor_tensor(udt[:], dtb[:], xcb2[:], ALU.mult),
                     reads=[dtb, xcb2], writes=[udt])
                for n in range(16):
                    scol = dch * 16 + n
                    p.op("scalar", lambda E, dch=dch, n=n: E.activation(
                        dAb[:], dtb[:], AF.Exp, scale=A_sb[:, dch * 16 + n:dch * 16 + n + 1]),
                         reads=[dtb, A_sb], writes=[dAb])
                    p.op("tensor", lambda E, n=n: E.matmul(
                        psBr[:], sel[:, n * 128:(n + 1) * 128], Bsb[:],
                        start=True, stop=True),
                         reads=[sel, Bsb], writes=[psBr])
                    p.op("vector", lambda E: E.tensor_tensor(bmb[:], udt[:], psBr[:],
                                                             ALU.mult),
                         reads=[udt, psBr], writes=[bmb])
                    p.op("vector", lambda E, scol=scol: E.tensor_tensor_scan(
                        hb[:], dAb[:], bmb[:], hstate[:, scol:scol + 1],
                        ALU.mult, ALU.add),
                         reads=[dAb, bmb, hstate], writes=[hb])
                    p.op("vector", lambda E, scol=scol: E.tensor_copy(
                        hstate[:, scol:scol + 1], hb[:, TC - 1:TC]),
                         reads=[hb], writes=[hstate])
                    p.op("tensor", lambda E, n=n: E.matmul(
                        psCr[:], sel[:, n * 128:(n + 1) * 128], Csb[:],
                        start=True, stop=True),
                         reads=[sel, Csb], writes=[psCr])
                    if n == 0:
                        p.op("vector", lambda E: E.tensor_tensor(yacc[:], hb[:], psCr[:],
                                                                 ALU.mult),
                             reads=[hb, psCr], writes=[yacc])
                    else:
                        p.op("vector", lambda E: E.tensor_tensor(ytmp[:], hb[:], psCr[:],
                                                                 ALU.mult),
                             reads=[hb, psCr], writes=[ytmp])
                        p.op("vector", lambda E: E.tensor_tensor(yacc[:], yacc[:], ytmp[:],
                                                                 ALU.add),
                             reads=[yacc, ytmp], writes=[yacc])
                p.op("vector", lambda E, dch=dch: E.scalar_tensor_tensor(
                    yacc[:], xcb2[:], dpar[:, dch:dch + 1], yacc[:], ALU.mult, ALU.add),
                     reads=[xcb2, dpar, yacc], writes=[yacc])
                p.dma("gpsimd", o_y[dch * 128:(dch + 1) * 128, b_, t0:t0 + TC], yacc[:])


def build_L4c():
    p = Prog("build_L4c")
    io = {
        "dts": p.din("dts", [256, B, SL]),
        "xcs": p.din("xcs", [256, B, SL]),
        "Bs": p.din("Bs", [16, B, SL]),
        "Cs": p.din("Cs", [16, B, SL]),
        "Alog": p.din("Alog", [256, 16]),
        "Dpar": p.din("Dpar", [256, 1]),
        "yssm": p.dout("yssm", [256, B, SL]),
    }
    emit_L4c(p, io)
    p.barrier_end()
    return p.realize()


def run_L4c(d, l4):
    nc = build_L4c()
    in_maps = []
    for c in range(NCORES):
        ch = slice(c * 256, (c + 1) * 256)
        in_maps.append({
            "dts": np.ascontiguousarray(l4["dtT"][ch]),
            "xcs": np.ascontiguousarray(l4["xcT"][ch]),
            "Bs": l4["Bf"], "Cs": l4["Cf"],
            "Alog": np.ascontiguousarray(d["L1_attn_A_log"][ch]),
            "Dpar": np.ascontiguousarray(d["L1_attn_D_param"][ch][:, None]),
        })
    res = run_bass_kernel_spmd(nc, in_maps, core_ids=list(range(NCORES)))
    full = np.concatenate([res.results[c]["yssm"] for c in range(NCORES)], axis=0)
    return {"yssm": full}


def emit_L5(p, io):
    x2_d = io["x2"]
    y2_d = io["y2"]
    ysm_d = io["ysm"]
    zg_d = io["zgT"]
    wo1_d = io["Wo1"]
    wr_d = io["Wr"]
    brr_d = io["brr"]
    w1_d = io["W1m"]
    b1s_d = io["b1s3"]
    w2_d = io["W2m"]
    b2r_d = io["b2r3"]
    lnw_d = io["lnw3r"]
    lnb_d = io["lnb3r"]
    o_x4 = io["x4"]

    w1sc = io.get("w1sc")
    if w1sc is None:
        w1sc = p.dtmp("w1sc", [8, 8, 128, 4096], BF16)
        w2sc = p.dtmp("w2sc", [8, 128, 32 * 1024], BF16)
        prestaged = False
    else:
        w2sc = io["w2sc"]
        prestaged = True
    x3sc = p.dtmp("x3sc", [NTOK, D])

    ident = make_identity(p)
    t = _T()
    arena = p.sb("arena", [128, 32768], BF16)
    wo1 = arena[:, 0:16 * D]
    w2s = arena[:, 0:32 * D]
    b1s = p.sb("b1s_s", [128, 8 * 32]); brr = p.sb("brr_s", [128, 8])
    lnwr = p.sb("lnwr_s", [128, D]); lnbr = p.sb("lnbr_s", [128, D])
    p.dma("sync", b1s[:], b1s_d)
    p.dma("sync", brr[:], brr_d)
    p.dma("sync", lnwr[:], lnw_d)
    p.dma("sync", lnbr[:], lnb_d)

    psT = p.ps("psT", [128, 128], BF16)
    ps1 = p.ps("ps1", [128, 512]); ps2 = p.ps("ps2", [128, 512])
    ps3 = p.ps("ps3", [128, 512]); ps4 = p.ps("ps4", [128, 512])
    psR = p.ps("psR", [128, 8])

    wrb = p.sb("wrb", [8, D], BF16)
    p.op("gpsimd", lambda E: E.dma_start(out=wrb[:], in_=wr_d), reads=[wr_d],
         writes=[wrb], dma=True)
    wrT = p.sb("wrT", [128, 8 * 8], BF16)
    for kk in range(8):
        p.op("tensor", lambda E, kk=kk: E.transpose(psT[:, 0:8],
                                                    wrb[:, kk * 128:(kk + 1) * 128],
                                                    ident[0:8, 0:8]),
             reads=[wrb, ident], writes=[psT])
        p.op("scalar", lambda E, kk=kk: E.copy(wrT[:, kk * 8:(kk + 1) * 8], psT[:, 0:8]),
             reads=[psT], writes=[wrT])

    hT = p.sb("hT5", [128, 32 * 512], BF16)
    if not prestaged:
        for e in range(8):
            p.op("gpsimd", lambda E, e=e: E.dma_start(
                out=arena[:].rearrange("q (g m) -> q g m", m=4096),
                in_=w1_d[e].rearrange("g q m -> q g m")),
                 reads=[w1_d], writes=[arena], dma=True)
            p.op("sync", lambda E, e=e: E.dma_start(
                out=w1sc[e].rearrange("g q m -> q g m"),
                in_=arena[:].rearrange("q (g m) -> q g m", m=4096)),
                 reads=[arena], writes=[w1sc], dma=True)
            for hh in range(4):
                p.op("gpsimd", lambda E, e=e, hh=hh: E.dma_start(
                    out=hT[:, 0:8192], in_=w2_d[e][:, hh * 8192:(hh + 1) * 8192]),
                     reads=[w2_d], writes=[hT], dma=True)
                p.op("sync", lambda E, e=e, hh=hh: E.dma_start(
                    out=w2sc[e][:, hh * 8192:(hh + 1) * 8192], in_=hT[:, 0:8192]),
                     reads=[hT], writes=[w2sc], dma=True)

    TB = 512
    x2t = p.sb("x2t", [128, D]); y2t = p.sb("y2t", [128, D])
    ygf = p.sb("ygf", [128, 16 * 128]); zgf = p.sb("zgf", [128, 16 * 128])
    ygb = p.sb("ygb", [128, 16 * 128], BF16)
    x3t = [p.sb(f"x3t{i}", [128, D]) for i in range(2)]
    acct = [p.sb(f"acct{i}", [128, D]) for i in range(4)]
    cwt = [p.sb(f"cwt{i}", [128, 8]) for i in range(4)]
    y3b = zgf[:, 1024:2048]
    y3c = ygb[:, 0:1024]
    y3T = p.sb("y3T", [128, 8 * TB], BF16)
    scr = ygf[:, 1022:2048]
    lgt = p.sb("lgt", [128, 8]); rsc = p.sb("rsc", [128, 8])
    w1big = [p.sb(f"w1b{i}", [128, 4096], BF16) for i in range(3)]
    b2s = [zgf[:, 0:1024]] * 2
    zb = y2t; znb = x2t

    ntb = NTOK // TB
    wo1_src = wo1_d.rearrange("(kc q) n -> q kc n", q=128)
    for bb in range(ntb):
        p.op("gpsimd", lambda E: E.dma_start(
            out=arena[:, 0:16 * D].rearrange("q (kc n) -> q kc n", kc=16),
            in_=wo1_src),
             reads=[wo1_d], writes=[arena], dma=True)
        for ti in range(4):
            r0 = bb * TB + ti * 128
            p.dma("sync", x2t[:], x2_d[r0:r0 + 128, :])
            p.dma("sync", y2t[:], y2_d[r0:r0 + 128, :])
            p.dma("sync", ygf[:].rearrange("q (e f) -> q e f", f=128),
                  ysm_d[:, r0:r0 + 128].rearrange("(e q) f -> q e f", q=128))
            p.dma("sync", zgf[:].rearrange("q (e f) -> q e f", f=128),
                  zg_d[:, r0:r0 + 128].rearrange("(e q) f -> q e f", q=128))
            p.op("scalar", lambda E: E.activation(zgf[:], zgf[:], AF.Silu),
                 reads=[zgf], writes=[zgf])
            p.op("vector", lambda E: E.tensor_tensor(ygf[:], ygf[:], zgf[:], ALU.mult),
                 reads=[ygf, zgf], writes=[ygf])
            p.op("vector", lambda E: E.tensor_copy(ygb[:], ygf[:]), reads=[ygf], writes=[ygb])
            p.op("vector", lambda E: E.tensor_tensor(x2t[:], x2t[:], y2t[:], ALU.add),
                 reads=[x2t, y2t], writes=[x2t])
            for h in range(2):
                bank = ps1 if h == 0 else ps2
                for kk in range(16):
                    p.op("tensor", lambda E, kk=kk, h=h, bank=bank: E.matmul(
                        bank[:], ygb[:, kk * 128:(kk + 1) * 128],
                        wo1[:, kk * D + h * 512:kk * D + (h + 1) * 512],
                        start=(kk == 0), stop=(kk == 15)),
                         reads=[ygb, arena], writes=[bank])
                p.op("vector", lambda E, bank=bank, h=h, ti=ti: E.scalar_tensor_tensor(
                    x3t[ti % 2][:, h * 512:(h + 1) * 512], bank[:], 1.0,
                    x2t[:, h * 512:(h + 1) * 512], ALU.mult, ALU.add),
                     reads=[bank, x2t], writes=[x3t[ti % 2]])
            emit_ln(p, y3b, x3t[ti % 2][:], scr, 1e-5)
            p.dma("gpsimd", x3sc[r0:r0 + 128, :], x3t[ti % 2][:])
            p.op("vector", lambda E: E.tensor_copy(y3c, y3b), reads=[y3b], writes=[y3c])
            for kk in range(8):
                p.op("tensor", lambda E, kk=kk: E.transpose(
                    psT[:], y3c[:, kk * 128:(kk + 1) * 128], ident[:]),
                     reads=[y3c, ident], writes=[psT])
                p.op("scalar", lambda E, kk=kk, ti=ti: E.copy(
                    y3T[:, kk * TB + ti * 128:kk * TB + (ti + 1) * 128], psT[:]),
                     reads=[psT], writes=[y3T])
            for kk in range(8):
                p.op("tensor", lambda E, kk=kk, ti=ti: E.matmul(
                    psR[:], y3T[:, kk * TB + ti * 128:kk * TB + (ti + 1) * 128],
                    wrT[:, kk * 8:(kk + 1) * 8], start=(kk == 0), stop=(kk == 7)),
                     reads=[y3T, wrT], writes=[psR])
            p.op("vector", lambda E: E.tensor_tensor(lgt[:], psR[:], brr[:], ALU.add),
                 reads=[psR, brr], writes=[lgt])
            p.op("vector", lambda E: E.reduce_max(scr[:, 0:1], lgt[:], axis=AX.X),
                 reads=[lgt], writes=[scr])
            p.op("vector", lambda E: E.tensor_scalar(rsc[:], lgt[:], scr[:, 0:1], None,
                                                     ALU.is_equal),
                 reads=[lgt, scr], writes=[rsc])
            p.op("vector", lambda E: E.scalar_tensor_tensor(lgt[:], rsc[:], -1e30, lgt[:],
                                                            ALU.mult, ALU.add),
                 reads=[rsc, lgt], writes=[lgt])
            p.op("vector", lambda E: E.reduce_max(scr[:, 1:2], lgt[:], axis=AX.X),
                 reads=[lgt], writes=[scr])
            p.op("vector", lambda E: E.tensor_tensor(scr[:, 2:3], scr[:, 1:2], scr[:, 0:1],
                                                     ALU.subtract),
                 reads=[scr], writes=[scr])
            p.op("scalar", lambda E: E.activation(scr[:, 2:3], scr[:, 2:3], AF.Exp),
                 reads=[scr], writes=[scr])
            p.op("vector", lambda E: E.tensor_scalar(scr[:, 3:4], scr[:, 2:3], 1.0, None,
                                                     ALU.add),
                 reads=[scr], writes=[scr])
            p.op("vector", lambda E: E.reciprocal(scr[:, 3:4], scr[:, 3:4]),
                 reads=[scr], writes=[scr])
            p.op("vector", lambda E: E.tensor_tensor(scr[:, 4:5], scr[:, 2:3], scr[:, 3:4],
                                                     ALU.mult),
                 reads=[scr], writes=[scr])
            p.op("vector", lambda E: E.tensor_scalar(lgt[:], lgt[:], scr[:, 1:2], None,
                                                     ALU.is_equal),
                 reads=[lgt, scr], writes=[lgt])
            p.op("vector", lambda E, ti=ti: E.tensor_scalar(cwt[ti][:], rsc[:],
                                                            scr[:, 3:4], None, ALU.mult),
                 reads=[rsc, scr], writes=[cwt[ti]])
            p.op("vector", lambda E, ti=ti: E.scalar_tensor_tensor(
                cwt[ti][:], lgt[:], scr[:, 4:5], cwt[ti][:], ALU.mult, ALU.add),
                 reads=[lgt, scr, cwt[ti]], writes=[cwt[ti]])
            p.op("vector", lambda E, ti=ti: E.memset(acct[ti][:], 0.0), writes=[acct[ti]])
        for e in range(8):
            p.dma("gpsimd", b2s[e % 2][:], b2r_d[e, :, :])
            for fch in range(32):
                g, j = fch // 4, fch % 4
                wbuf = w1big[g % 3]
                if j == 0:
                    p.op("sync", lambda E, e=e, g=g, wbuf=wbuf: E.dma_start(
                        out=wbuf[:], in_=w1sc[e, g]),
                         reads=[w1sc], writes=[wbuf], dma=True)
                bank = ps1 if fch % 2 == 0 else ps2
                for kk in range(8):
                    p.op("tensor", lambda E, kk=kk, j=j, wbuf=wbuf, bank=bank: E.matmul(
                        bank[:], wbuf[:, j * 1024 + kk * 128:j * 1024 + (kk + 1) * 128],
                        y3T[:, kk * TB:(kk + 1) * TB],
                        start=(kk == 0), stop=(kk == 7)),
                         reads=[wbuf, y3T], writes=[bank])
                p.op("scalar", lambda E, fch=fch, e=e, bank=bank: E.activation(
                    hT[:, fch * TB:(fch + 1) * TB], bank[:], AF.Gelu,
                    bias=b1s[:, e * 32 + fch:e * 32 + fch + 1]),
                     reads=[bank, b1s], writes=[hT])
            for hh in range(4):
                p.op("gpsimd", lambda E, e=e, hh=hh: E.dma_start(
                    out=w2s[:, hh * 8192:hh * 8192 + 8192],
                    in_=w2sc[e][:, hh * 8192:(hh + 1) * 8192]),
                     reads=[w2sc], writes=[arena], dma=True)
            for ti in range(4):
                for h in range(2):
                    bank = ps3 if h == 0 else ps4
                    for fch in range(32):
                        p.op("tensor", lambda E, fch=fch, ti=ti, h=h, bank=bank: E.matmul(
                            bank[:], hT[:, fch * TB + ti * 128:fch * TB + (ti + 1) * 128],
                            w2s[:, fch * D + h * 512:fch * D + (h + 1) * 512],
                            start=(fch == 0), stop=(fch == 31)),
                             reads=[hT, arena], writes=[bank])
                    p.op("vector", lambda E, bank=bank, ti=ti, h=h, e=e: E.scalar_tensor_tensor(
                        acct[ti][:, h * 512:(h + 1) * 512], bank[:],
                        cwt[ti][:, e:e + 1], acct[ti][:, h * 512:(h + 1) * 512],
                        ALU.mult, ALU.add),
                         reads=[bank, cwt[ti], acct[ti]], writes=[acct[ti]])
                p.op("vector", lambda E, ti=ti, e=e: E.scalar_tensor_tensor(
                    acct[ti][:], b2s[e % 2][:], cwt[ti][:, e:e + 1], acct[ti][:],
                    ALU.mult, ALU.add),
                     reads=[b2s[e % 2], cwt[ti], acct[ti]], writes=[acct[ti]])
        for ti in range(4):
            r0 = bb * TB + ti * 128
            p.dma("sync", x3t[ti % 2][:], x3sc[r0:r0 + 128, :])
            emit_ln(p, y3b, x3t[ti % 2][:], scr, 1e-5)
            p.op("vector", lambda E, ti=ti: E.tensor_tensor(zb[:], acct[ti][:], y3b,
                                                            ALU.add),
                 reads=[acct[ti], y3b], writes=[zb])
            emit_ln(p, znb[:], zb[:], scr, 1e-12)
            p.op("vector", lambda E: E.tensor_tensor(znb[:], znb[:], lnwr[:], ALU.mult),
                 reads=[znb, lnwr], writes=[znb])
            p.op("vector", lambda E: E.tensor_tensor(znb[:], znb[:], lnbr[:], ALU.add),
                 reads=[znb, lnbr], writes=[znb])
            p.op("vector", lambda E, ti=ti: E.tensor_tensor(znb[:], znb[:], x3t[ti % 2][:],
                                                            ALU.add),
                 reads=[znb, x3t[ti % 2]], writes=[znb])
            p.dma("gpsimd", o_x4[r0:r0 + 128, :], znb[:])


def emit_moe_stage(p, io, stg):
    w1_d = io["W1m"]; w2_d = io["W2m"]
    w1sc = io["w1sc"]; w2sc = io["w2sc"]
    for e in range(8):
        for hh in range(2):
            p.op("gpsimd", lambda E, e=e, hh=hh: E.dma_start(
                out=stg.rearrange("q (g m) -> q g m", m=4096),
                in_=w1_d[e][hh * 4:(hh + 1) * 4].rearrange("g q m -> q g m")),
                 reads=[w1_d], writes=[stg], dma=True)
            p.op("gpsimd", lambda E, e=e, hh=hh: E.dma_start(
                out=w1sc[e][hh * 4:(hh + 1) * 4].rearrange("g q m -> q g m"),
                in_=stg.rearrange("q (g m) -> q g m", m=4096)),
                 reads=[stg], writes=[w1sc], dma=True)
        for hh in range(4):
            p.op("gpsimd", lambda E, e=e, hh=hh: E.dma_start(
                out=stg[:, 0:8192], in_=w2_d[e][:, hh * 8192:(hh + 1) * 8192]),
                 reads=[w2_d], writes=[stg], dma=True)
            p.op("gpsimd", lambda E, e=e, hh=hh: E.dma_start(
                out=w2sc[e][:, hh * 8192:(hh + 1) * 8192], in_=stg[:, 0:8192]),
                 reads=[stg], writes=[w2sc], dma=True)


def build_L5():
    p = Prog("build_L5")
    io = {
        "x2": p.din("x2", [NTOK, D]),
        "y2": p.din("y2", [NTOK, D]),
        "ysm": p.din("ysm", [2048, NTOK]),
        "zgT": p.din("zgT", [2048, NTOK]),
        "Wo1": p.din("Wo1", [2048, D]),
        "Wr": p.din("Wr", [8, D]),
        "brr": p.din("brr", [128, 8]),
        "W1m": p.din("W1m", [8, 8, 128, 4096]),
        "b1s3": p.din("b1s3", [128, 8 * 32]),
        "W2m": p.din("W2m", [8, 128, 32 * 1024]),
        "b2r3": p.din("b2r3", [8, 128, D]),
        "lnw3r": p.din("lnw3r", [128, D]),
        "lnb3r": p.din("lnb3r", [128, D]),
        "x4": p.dout("x4", [NTOK, D]),
    }
    emit_L5(p, io)
    p.barrier_end()
    return p.realize()


def moe_w_host(d):
    w1h = np.ascontiguousarray(
        d["L1_ffn_W1"].reshape(8, 8, 4, 128, 8, 128).transpose(0, 1, 5, 2, 4, 3)
        .reshape(8, 8, 128, 4096))
    w2h = np.ascontiguousarray(
        d["L1_ffn_W2"].reshape(8, 1024, 32, 128).transpose(0, 3, 2, 1)
        .reshape(8, 128, 32 * 1024))
    return w1h, w2h


def run_L5(d, l3, l4c):
    nc = build_L5()
    w1h, w2h = moe_w_host(d)
    b1s3 = np.ascontiguousarray(
        np.transpose(d["L1_ffn_b1"].reshape(8, 32, 128), (2, 0, 1)).reshape(128, 256))
    brr = np.ascontiguousarray(np.broadcast_to(d["L1_ffn_br"][None, :], (128, 8)))
    b2r3 = np.ascontiguousarray(
        np.broadcast_to(d["L1_ffn_b2"][:, None, :], (8, 128, D)))
    lnw3r = np.ascontiguousarray(np.broadcast_to(d["L1_ffn_ln_w"][None, :], (128, D)))
    lnb3r = np.ascontiguousarray(np.broadcast_to(d["L1_ffn_ln_b"][None, :], (128, D)))
    ysm = l4c["yssm"]
    zgT = l3["zgT"]
    in_maps = []
    for c in range(NCORES):
        sl = slice(c * SEG, (c + 1) * SEG)
        in_maps.append({
            "x2": np.ascontiguousarray(l3["x2"][:, sl, :].reshape(NTOK, D)),
            "y2": np.ascontiguousarray(l3["y2"][:, sl, :].reshape(NTOK, D)),
            "ysm": np.ascontiguousarray(ysm[:, :, sl].reshape(2048, NTOK)),
            "zgT": np.ascontiguousarray(zgT[:, :, sl].reshape(2048, NTOK)),
            "Wo1": d["L1_attn_W_out"], "Wr": d["L1_ffn_Wr"], "brr": brr,
            "W1m": w1h, "b1s3": b1s3, "W2m": w2h,
            "b2r3": b2r3, "lnw3r": lnw3r, "lnb3r": lnb3r,
        })
    res = run_bass_kernel_spmd(nc, in_maps, core_ids=list(range(NCORES)))
    full = np.stack([res.results[c]["x4"].reshape(B, SEG, D) for c in range(NCORES)],
                    axis=1)
    return {"x4": full.reshape(B, NCORES * SEG, D)}


def kernel(**inputs):
    return run_fused(inputs)


LN8 = -2.0794415416798357

def emit_l1_fused(p, io):
    x0 = io["x0"]
    ident = make_identity(p)
    c_inv15 = p.constcol(1.0 / 15.0)
    c_15 = p.constcol(15.0)
    wq = load_weight_bf16(p, "wq", io["Wq"], D, 512)
    wk = load_weight_bf16(p, "wk", io["Wk"], D, 512)
    wv = load_weight_bf16(p, "wv", io["Wv"], D, D)
    wif = load_weight_bf16(p, "wif", io["Wif"], D, 16)
    wog = load_weight_bf16(p, "wog", io["Wog"], D, D)
    xs = p.sb("xs", [128, D]); ys = p.sb("ys", [128, D])
    cast = p.sb("cast", [128, D], BF16)
    scr = p.sb("scr", [128, D + 2])
    y0T = p.sb("y0T", [128, 8 * 512], BF16)
    stq = p.sb("stq", [128, 512], BF16); stk = p.sb("stk", [128, 512], BF16)
    stv = p.sb("stv", [128, D], BF16)
    sto = p.sb("sto", [128, D]); tmp = p.sb("tmpg", [128, D])
    stif = p.sb("stif", [16, 512])
    psT = p.ps("psT", [128, 128], BF16)
    ps_a = p.ps("ps_a", [128, 512]); ps_b = p.ps("ps_b", [128, 512])
    psI = p.ps("psI", [16, 512])

    for bb in range(NTOK // 512):
        t0 = bb * 512
        for ti in range(4):
            r0 = t0 + ti * 128
            p.dma("sync", xs[:], x0[r0:r0 + 128, :])
            emit_ln(p, ys[:], xs[:], scr, 1e-5)
            p.op("vector", lambda E: E.tensor_copy(cast[:], ys[:]), reads=[ys], writes=[cast])
            for kk in range(8):
                p.op("tensor", lambda E, kk=kk: E.transpose(psT[:], cast[:, kk * 128:(kk + 1) * 128], ident[:]),
                     reads=[cast, ident], writes=[psT])
                p.op("scalar", lambda E, kk=kk, ti=ti: E.copy(
                    y0T[:, kk * 512 + ti * 128:kk * 512 + (ti + 1) * 128], psT[:]),
                     reads=[psT], writes=[y0T])
        for pr in range(4):
            for which, wsb, scrout in (("q", wq, io["qT_scr"]), ("k", wk, io["kT_scr"])):
                st = stq if which == "q" else stk
                for kk in range(8):
                    p.op("tensor", lambda E, kk=kk, pr=pr, wsb=wsb: E.matmul(
                        ps_a[:], wsb[:, kk * 512 + pr * 128:kk * 512 + (pr + 1) * 128],
                        y0T[:, kk * 512:(kk + 1) * 512],
                        start=(kk == 0), stop=(kk == 7)),
                         reads=[wsb, y0T], writes=[ps_a])
                p.op("vector", lambda E, st=st: E.tensor_copy(st[:], ps_a[:]),
                     reads=[ps_a], writes=[st])
                p.dma("gpsimd", scrout[pr, :, t0:t0 + 512], st[:])
        for kk in range(8):
            p.op("tensor", lambda E, kk=kk: E.matmul(
                psI[:], wif[:, kk * 16:(kk + 1) * 16], y0T[:, kk * 512:(kk + 1) * 512],
                start=(kk == 0), stop=(kk == 7)),
                 reads=[wif, y0T], writes=[psI])
        p.op("vector", lambda E: E.tensor_copy(stif[:], psI[:]), reads=[psI], writes=[stif])
        p.dma("gpsimd", io["if_scr"][:, t0:t0 + 512], stif[:])
        for ti in range(4):
            r0 = t0 + ti * 128
            for h in range(2):
                bank = ps_a if h == 0 else ps_b
                for kk in range(8):
                    p.op("tensor", lambda E, kk=kk, ti=ti, h=h, bank=bank: E.matmul(
                        bank[:], y0T[:, kk * 512 + ti * 128:kk * 512 + (ti + 1) * 128],
                        wv[:, kk * D + h * 512:kk * D + (h + 1) * 512],
                        start=(kk == 0), stop=(kk == 7)),
                         reads=[y0T, wv], writes=[bank])
                p.op("vector", lambda E, bank=bank, h=h: E.tensor_copy(
                    stv[:, h * 512:(h + 1) * 512], bank[:]), reads=[bank], writes=[stv])
            p.dma("gpsimd", io["v_scr"][r0:r0 + 128, :], stv[:])
            for h in range(2):
                bank = ps_a if h == 0 else ps_b
                for kk in range(8):
                    p.op("tensor", lambda E, kk=kk, ti=ti, h=h, bank=bank: E.matmul(
                        bank[:], y0T[:, kk * 512 + ti * 128:kk * 512 + (ti + 1) * 128],
                        wog[:, kk * D + h * 512:kk * D + (h + 1) * 512],
                        start=(kk == 0), stop=(kk == 7)),
                         reads=[y0T, wog], writes=[bank])
                p.op("scalar", lambda E, bank=bank, h=h: E.activation(
                    tmp[:, h * 512:(h + 1) * 512], bank[:], AF.Tanh, scale=c_inv15),
                     reads=[bank, c_inv15], writes=[tmp])
            p.op("scalar", lambda E: E.activation(sto[:], tmp[:], AF.Sigmoid, scale=c_15),
                 reads=[tmp, c_15], writes=[sto])
            p.dma("gpsimd", io["og_scr"][r0:r0 + 128, :], sto[:])


def emit_mlstm_fused(p, io, phase, g):
    c_m1 = p.constcol(-1.0)
    c_1 = p.constcol(1.0)
    t = _T()
    identb = make_identity(p)
    t.identf = p.sb("identfp", [128, 128], F32)
    p.op("vector", lambda E: E.tensor_copy(t.identf[:], identb[:]),
         reads=[identb], writes=[t.identf])
    iop = p.sb("iop", [128, 128], F32)
    p.op("gpsimd", lambda E: E.iota(iop[:], pattern=[[0, 128]], channel_multiplier=1,
                                    allow_small_or_imprecise_dtypes=True), writes=[iop])
    t.sel0 = p.sb("sel0", [128, 128], F32)
    p.op("vector", lambda E: E.tensor_scalar(t.sel0[:], iop[:], 0.0, None, ALU.is_equal),
         reads=[iop], writes=[t.sel0])
    iop2 = p.sb("iop2", [128, 128], F32)
    p.op("gpsimd", lambda E: E.iota(iop2[:], pattern=[[1, 128]], channel_multiplier=-1,
                                    allow_small_or_imprecise_dtypes=True), writes=[iop2])
    t.trimask = p.sb("trimask", [128, 128], F32)
    p.op("vector", lambda E: E.tensor_scalar(t.trimask[:], iop2[:], 0.0, 0.125,
                                             ALU.is_ge, ALU.mult),
         reads=[iop2], writes=[t.trimask])

    t.qTb = p.sb("qTb", [64, SEG], BF16); t.kTb = p.sb("kTb", [64, SEG], BF16)
    t.vb = p.sb("vb", [128, SEG], BF16)
    t.i_r = p.sb("i_r", [1, SEG]); t.f_r = p.sb("f_r", [1, SEG])
    t.fb = p.sb("fb", [1, SEG]); t.g1 = p.sb("g1", [1, SEG])
    t.g2 = p.sb("g2", [1, SEG]); t.flog = p.sb("flog", [1, SEG])
    t.Fc = p.sb("Fcg", [1, SEG]); t.bg = p.sb("bgg", [1, SEG])
    t.M = p.sb("Mg", [128, SEG]); t.mp = p.sb("mpg", [1, SEG])
    t.nm = p.sb("nmg", [1, SEG])
    t.mstate = p.sb("mstate", [1, 1])
    t.gcols = p.sb("gcols", [128, 6])
    t.c_f32 = p.sb("c_f32", [64, 128]); t.c_bf = p.sb("c_bfb", [64, 128], BF16)
    t.wcol = p.sb("wcol", [128, 1]); t.dec = p.sb("dec", [64, 1])
    t.kw = p.sb("kwb", [128, 64], BF16)
    t.kf = p.sb("kf", [128, 64])
    t.psG = p.ps("psG", [128, 8], F32)
    t.psKt = p.ps("psKt", [128, 64], BF16)
    t.psC = p.ps("psC", [64, 128], F32)
    if phase == 2:
        t.W_sb = p.sb("W_sb", [128, 128]); t.Wm = p.sb("Wm", [128, 128])
        t.DSb = p.sb("DSbb", [128, 128], BF16)
        t.e_rep = p.sb("e_rep", [64, 128], BF16)
        t.qs = p.sb("qs", [64, 128], BF16)
        t.scr = p.sb("scrM", [128, 130])
        t.hbuf = p.sb("hbuf", [128, 128])
        t.hn_st = p.sb("hn_st", [128, SEG])
        t.lnw_rep = p.sb("lnw_rep", [128, 128]); t.lnb_rep = p.sb("lnb_rep", [128, 128])
        t.psMrep = p.ps("psMrep", [128, 128], F32)
        t.psSb = p.ps("psSb", [128, 128], F32)
        t.psH = p.ps("psH", [128, 128], F32)
    t.sc2 = p.sb("sc2", [1, 2])
    t.zro = p.sb("zro", [1, SEG])
    p.op("vector", lambda E: E.memset(t.M[:], 0.0), writes=[t.M])
    p.op("vector", lambda E: E.memset(t.zro[:], 0.0), writes=[t.zro])

    zrow = t.zro[0:1, :]
    for bh in range(16):
        b_, h_ = bh // 8, bh % 8
        pr, half = h_ // 2, h_ % 2
        sl_t = slice(b_ * SEG, (b_ + 1) * SEG)
        p.dma("sync", t.qTb[:], io["qT_scr"][pr, half * 64:half * 64 + 64, sl_t])
        p.dma("sync", t.kTb[:], io["kT_scr"][pr, half * 64:half * 64 + 64, sl_t])
        p.dma("sync", t.vb[:].rearrange("p (cc d) -> p cc d", d=128),
              io["v_scr"][sl_t, h_ * 128:(h_ + 1) * 128].rearrange(
                  "(cc p) d -> p cc d", p=128))
        p.dma("sync", t.i_r[0:1, :], io["if_scr"][h_:h_ + 1, sl_t])
        p.dma("sync", t.f_r[0:1, :], io["if_scr"][8 + h_:9 + h_, sl_t])
        if phase == 2:
            p.dma("sync", t.lnw_rep[:], io["lnw8"][h_, :, :])
            p.dma("sync", t.lnb_rep[:], io["lnb8"][h_, :, :])
        if phase == 1:
            p.op("vector", lambda E: E.memset(t.c_f32[:], 0.0), writes=[t.c_f32])
            p.op("vector", lambda E: E.memset(t.mstate[:], -1e30), writes=[t.mstate])
        else:
            p.op("vector", lambda E, bh=bh: E.tensor_copy(
                t.c_f32[:], g.c_in_sb[:, bh * 128:(bh + 1) * 128]),
                 reads=[g.c_in_sb], writes=[t.c_f32])
            p.op("vector", lambda E, bh=bh: E.tensor_copy(
                t.mstate[0:1, :], g.m_in_sb[0:1, bh:bh + 1]),
                 reads=[g.m_in_sb], writes=[t.mstate])
        R = lambda x: x[0:1, :]
        bicol = io["bif_sb"][h_:h_ + 1, 0:1] if False else None
        p.op("vector", lambda E, h_=h_: E.tensor_scalar(
            R(t.fb), R(t.f_r), g.bif[0:1, 8 + h_:9 + h_], None, ALU.add),
             reads=[t.f_r, g.bif], writes=[t.fb])
        p.op("vector", lambda E: E.tensor_scalar(R(t.g1), R(t.fb), -1.0, None, ALU.mult),
             reads=[t.fb], writes=[t.g1])
        p.op("vector", lambda E: E.tensor_tensor(R(t.g2), R(t.fb), R(t.g1), ALU.max),
             reads=[t.fb, t.g1], writes=[t.g2])
        p.op("scalar", lambda E: E.activation(R(t.g1), R(t.g2), AF.Exp, scale=c_m1[0:1, 0:1]),
             reads=[t.g2, c_m1], writes=[t.g1])
        p.op("scalar", lambda E: E.activation(R(t.g2), R(t.g1), AF.Ln, bias=c_1[0:1, 0:1]),
             reads=[t.g1, c_1], writes=[t.g2])
        p.op("vector", lambda E: E.tensor_scalar(R(t.g1), R(t.fb), 0.0, None, ALU.min),
             reads=[t.fb], writes=[t.g1])
        p.op("vector", lambda E: E.tensor_tensor(R(t.flog), R(t.g1), R(t.g2), ALU.subtract),
             reads=[t.g1, t.g2], writes=[t.flog])
        p.op("vector", lambda E: E.tensor_tensor_scan(R(t.Fc), R(t.flog), zrow,
                                                      0.0, ALU.add, ALU.add),
             reads=[t.flog, t.zro], writes=[t.Fc])
        p.op("vector", lambda E, h_=h_: E.tensor_scalar(
            R(t.g1), R(t.i_r), g.bif[0:1, h_:h_ + 1], None, ALU.add),
             reads=[t.i_r, g.bif], writes=[t.g1])
        p.op("vector", lambda E: E.tensor_tensor(R(t.bg), R(t.g1), R(t.Fc), ALU.subtract),
             reads=[t.g1, t.Fc], writes=[t.bg])
        p.op("vector", lambda E: E.tensor_tensor_scan(t.M[0:1, :], R(t.bg), R(t.bg),
                                                      t.mstate[0:1, 0:1], ALU.max, ALU.max),
             reads=[t.bg, t.mstate], writes=[t.M])
        for cc in range(SEG // C):
            c0 = cc * C
            if cc == 0:
                src_ap, src_rd = t.mstate[0:1, 0:1], t.mstate
            else:
                src_ap, src_rd = t.M[0:1, c0 - 1:c0], t.M
            p.op("vector", lambda E, c0=c0, src_ap=src_ap, src_rd=src_rd:
                 E.tensor_scalar(t.mp[0:1, c0:c0 + C], zrow[:, c0:c0 + C], src_ap,
                                 None, ALU.add),
                 reads=[t.zro, src_rd], writes=[t.mp])
            p.op("vector", lambda E, c0=c0: E.tensor_scalar(
                t.nm[0:1, c0:c0 + C], zrow[:, c0:c0 + C],
                t.M[0:1, c0 + C - 1:c0 + C], -1.0, ALU.add, ALU.mult),
                 reads=[t.zro, t.M], writes=[t.nm])
        p.op("vector", lambda E: E.tensor_tensor(t.sc2[0:1, 0:1], t.Fc[0:1, SEG - 1:SEG],
                                                 t.M[0:1, SEG - 1:SEG], ALU.add),
             reads=[t.Fc, t.M], writes=[t.sc2])
        for cc in range(SEG // C):
            c0 = cc * C
            sl = slice(c0, c0 + C)
            p.op("tensor", lambda E, sl=sl: E.transpose(t.psG[:, 0:1], t.bg[0:1, sl],
                                                        t.identf[0:1, 0:1]),
                 reads=[t.bg, t.identf], writes=[t.psG])
            p.op("tensor", lambda E, sl=sl: E.transpose(t.psG[:, 1:2], t.mp[0:1, sl],
                                                        t.identf[0:1, 0:1]),
                 reads=[t.mp, t.identf], writes=[t.psG])
            p.op("tensor", lambda E, sl=sl: E.transpose(t.psG[:, 2:3], t.nm[0:1, sl],
                                                        t.identf[0:1, 0:1]),
                 reads=[t.nm, t.identf], writes=[t.psG])
            p.op("scalar", lambda E: E.copy(t.gcols[:, 0:3], t.psG[:, 0:3]),
                 reads=[t.psG], writes=[t.gcols])
            if phase == 2:
                p.op("vector", lambda E: E.tensor_scalar(t.gcols[:, 3:4], t.gcols[:, 1:2],
                                                         LN8, None, ALU.add),
                     reads=[t.gcols], writes=[t.gcols])
            p.op("scalar", lambda E: E.activation(t.wcol[:], t.gcols[:, 0:1], AF.Exp,
                                                  bias=t.gcols[:, 2:3]),
                 reads=[t.gcols], writes=[t.wcol])
            p.op("scalar", lambda E: E.activation(t.dec[:], t.gcols[0:64, 1:2], AF.Exp,
                                                  bias=t.gcols[0:64, 2:3]),
                 reads=[t.gcols], writes=[t.dec])
            p.op("tensor", lambda E, sl=sl: E.transpose(t.psKt[:], t.kTb[:, sl],
                                                        identb[0:64, 0:64]),
                 reads=[t.kTb, identb], writes=[t.psKt])
            p.op("vector", lambda E: E.tensor_copy(t.kf[:], t.psKt[:]),
                 reads=[t.psKt], writes=[t.kf])
            p.op("vector", lambda E: E.tensor_scalar(t.kw[:], t.kf[:], t.wcol[:, 0:1],
                                                     None, ALU.mult),
                 reads=[t.kf, t.wcol], writes=[t.kw])
            if phase == 2:
                p.op("tensor", lambda E, sl=sl: E.matmul(t.psMrep[:], t.sel0[:], t.M[:, sl],
                                                         start=True, stop=True),
                     reads=[t.sel0, t.M], writes=[t.psMrep])
                p.op("scalar", lambda E: E.activation(t.W_sb[:], t.psMrep[:], AF.Exp,
                                                      scale=c_m1, bias=t.gcols[:, 0:1]),
                     reads=[t.psMrep, t.gcols, c_m1], writes=[t.W_sb])
                p.op("vector", lambda E: E.tensor_tensor(t.Wm[:], t.W_sb[:], t.trimask[:],
                                                         ALU.mult),
                     reads=[t.W_sb, t.trimask], writes=[t.Wm])
                p.op("tensor", lambda E, sl=sl: E.matmul(t.psSb[:], t.kTb[:, sl],
                                                         t.qTb[:, sl], start=True, stop=True),
                     reads=[t.kTb, t.qTb], writes=[t.psSb])
                p.op("vector", lambda E: E.tensor_tensor(t.DSb[:], t.Wm[:], t.psSb[:],
                                                         ALU.mult),
                     reads=[t.Wm, t.psSb], writes=[t.DSb])
                p.op("scalar", lambda E: E.activation(t.e_rep[:], t.psMrep[0:64, :], AF.Exp,
                                                      scale=c_m1[0:64, 0:1],
                                                      bias=t.gcols[0:64, 3:4]),
                     reads=[t.psMrep, t.gcols, c_m1], writes=[t.e_rep])
                p.op("vector", lambda E, sl=sl: E.tensor_tensor(t.qs[:], t.qTb[0:64, sl],
                                                                t.e_rep[:], ALU.mult),
                     reads=[t.qTb, t.e_rep], writes=[t.qs])
                p.op("vector", lambda E: E.tensor_copy(t.c_bf[:], t.c_f32[:]),
                     reads=[t.c_f32], writes=[t.c_bf])
                p.op("tensor", lambda E, cc=cc: E.matmul(t.psH[:], t.DSb[:],
                                                         t.vb[:, cc * 128:(cc + 1) * 128],
                                                         start=True, stop=False),
                     reads=[t.DSb, t.vb], writes=[t.psH])
                p.op("tensor", lambda E: E.matmul(t.psH[:], t.qs[:], t.c_bf[:],
                                                  start=False, stop=True),
                     reads=[t.qs, t.c_bf], writes=[t.psH])
            p.op("tensor", lambda E, cc=cc: E.matmul(t.psC[:], t.kw[:],
                                                     t.vb[:, cc * 128:(cc + 1) * 128],
                                                     start=True, stop=True),
                 reads=[t.kw, t.vb], writes=[t.psC])
            p.op("vector", lambda E: E.tensor_scalar(t.c_f32[:], t.c_f32[:], t.dec[:, 0:1],
                                                     None, ALU.mult),
                 reads=[t.c_f32, t.dec], writes=[t.c_f32])
            p.op("vector", lambda E: E.tensor_tensor(t.c_f32[:], t.c_f32[:], t.psC[:],
                                                     ALU.add),
                 reads=[t.c_f32, t.psC], writes=[t.c_f32])
            if phase == 2:
                scrm = t.scr
                p.op("vector", lambda E: E.reduce_sum(scrm[:, 0:1], t.psH[:], axis=AX.X),
                     reads=[t.psH], writes=[scrm])
                p.op("vector", lambda E: E.tensor_scalar(scrm[:, 0:1], scrm[:, 0:1],
                                                         1.0 / 128, None, ALU.mult),
                     reads=[scrm], writes=[scrm])
                p.op("vector", lambda E: E.tensor_scalar(t.hbuf[:], t.psH[:], scrm[:, 0:1],
                                                         None, ALU.subtract),
                     reads=[t.psH, scrm], writes=[t.hbuf])
                p.op("scalar", lambda E: E.activation(scrm[:, 2:130], t.hbuf[:], AF.Square),
                     reads=[t.hbuf], writes=[scrm])
                p.op("vector", lambda E: E.reduce_sum(scrm[:, 1:2], scrm[:, 2:130], axis=AX.X),
                     reads=[scrm], writes=[scrm])
                p.op("vector", lambda E: E.tensor_scalar(scrm[:, 1:2], scrm[:, 1:2],
                                                         1.0 / 128, 1e-6, ALU.mult, ALU.add),
                     reads=[scrm], writes=[scrm])
                p.op("scalar", lambda E: E.activation(scrm[:, 1:2], scrm[:, 1:2], AF.Sqrt),
                     reads=[scrm], writes=[scrm])
                p.op("vector", lambda E: E.reciprocal(scrm[:, 1:2], scrm[:, 1:2]),
                     reads=[scrm], writes=[scrm])
                p.op("vector", lambda E: E.scalar_tensor_tensor(t.hbuf[:], t.hbuf[:],
                                                                scrm[:, 1:2], t.lnw_rep[:],
                                                                ALU.mult, ALU.mult),
                     reads=[t.hbuf, scrm, t.lnw_rep], writes=[t.hbuf])
                p.op("vector", lambda E, c0=c0: E.tensor_tensor(
                    t.hn_st[:, c0:c0 + C], t.hbuf[:], t.lnb_rep[:], ALU.add),
                     reads=[t.hbuf, t.lnb_rep], writes=[t.hn_st])
        if phase == 1:
            p.dma("gpsimd", io["ccm_in"][bh:bh + 1, 0:8192].rearrange(
                "o (q f) -> (o q) f", q=64), t.c_f32[:])
            p.op("vector", lambda E: E.tensor_copy(t.sc2[0:1, 1:2], t.sc2[0:1, 0:1]),
                 reads=[t.sc2], writes=[t.sc2])
            p.op("vector", lambda E: E.tensor_copy(t.sc2[0:1, 0:1], t.Fc[0:1, SEG - 1:SEG]),
                 reads=[t.Fc, t.sc2], writes=[t.sc2])
            p.dma("gpsimd", io["ccm_in"][bh:bh + 1, 8192:8194], t.sc2[0:1, 0:2])
        else:
            p.dma("gpsimd",
                  io["hn_scr"][sl_t, h_ * 128:(h_ + 1) * 128].rearrange(
                      "(cc p) d -> p cc d", p=128),
                  t.hn_st[:].rearrange("p (cc d) -> p cc d", d=128))


def emit_combine_mlstm(p, io, g):
    scalF = p.sb("scalF", [64, 128]); scalM = p.sb("scalM", [64, 128])
    mzro = p.sb("mzro", [64, 8]); mneg = p.sb("mneg", [64, 8])
    c_j = p.sb("c_j", [64, 16 * 128])
    t1 = p.sb("t1c", [64, 16]); t2 = p.sb("t2c", [64, 16])
    w1 = p.sb("w1c", [64, 16]); w2 = p.sb("w2c", [64, 16])
    p.dma("sync", mzro[:], io["mzro64"])
    p.dma("sync", mneg[:], io["mneg64"])
    identb2 = make_identity(p, "identc")
    identf2 = p.sb("identf2", [128, 128], F32)
    p.op("vector", lambda E: E.tensor_copy(identf2[:], identb2[:]),
         reads=[identb2], writes=[identf2])
    ones64 = p.sb("ones64", [1, 64], F32)
    p.op("vector", lambda E: E.memset(ones64[:], 1.0), writes=[ones64])
    sc_raw = p.sb("sc_raw", [128, 2])
    sc_row = p.sb("sc_row", [1, 128])
    psScT = p.ps("psScT", [128, 128], F32)
    p.dma("sync", sc_raw[:], io["ccm_out"][:, 8192:8194])
    for col, dst in ((0, scalF), (1, scalM)):
        p.op("tensor", lambda E, col=col: E.transpose(psScT[0:1, 0:128],
                                                      sc_raw[:, col:col + 1], identf2[:]),
             reads=[sc_raw, identf2], writes=[psScT])
        p.op("vector", lambda E: E.tensor_copy(sc_row[:], psScT[0:1, 0:128]),
             reads=[psScT], writes=[sc_row])
        p.op("tensor", lambda E: E.matmul(psScT[0:64, 0:128], ones64[:], sc_row[:],
                                          start=True, stop=True),
             reads=[ones64, sc_row], writes=[psScT])
        p.op("vector", lambda E, dst=dst: E.tensor_copy(dst[:], psScT[0:64, 0:128]),
             reads=[psScT], writes=[dst])
    p.op("vector", lambda E: E.memset(g.m_in_sb[:], 0.0), writes=[g.m_in_sb])
    p.op("vector", lambda E: E.memset(g.c_in_sb[:], 0.0), writes=[g.c_in_sb])
    for j in range(8):
        p.dma("sync", c_j[:].rearrange("p (bh f) -> p bh f", f=128),
              io["ccm_out"][j * 16:(j + 1) * 16, 0:8192].rearrange(
                  "bh (p f) -> p bh f", p=64))
        p.op("vector", lambda E, j=j: E.tensor_scalar(t1[:], scalF[:, j * 16:(j + 1) * 16],
                                                      mzro[:, j:j + 1], None, ALU.mult),
             reads=[scalF, mzro], writes=[t1])
        p.op("vector", lambda E, j=j: E.tensor_scalar(t2[:], scalM[:, j * 16:(j + 1) * 16],
                                                      mneg[:, j:j + 1], None, ALU.add),
             reads=[scalM, mneg], writes=[t2])
        p.op("vector", lambda E: E.tensor_tensor(t1[:], g.m_in_sb[0:64, :], t1[:], ALU.add),
             reads=[g.m_in_sb, t1], writes=[t1])
        p.op("vector", lambda E: E.tensor_tensor(w1[:], t1[:], t2[:], ALU.max),
             reads=[t1, t2], writes=[w1])
        p.op("vector", lambda E: E.tensor_tensor(t1[:], t1[:], w1[:], ALU.subtract),
             reads=[t1, w1], writes=[t1])
        p.op("vector", lambda E: E.tensor_tensor(t2[:], t2[:], w1[:], ALU.subtract),
             reads=[t2, w1], writes=[t2])
        p.op("vector", lambda E: E.tensor_copy(g.m_in_sb[0:64, :], w1[:]),
             reads=[w1], writes=[g.m_in_sb])
        p.op("scalar", lambda E: E.activation(w1[:], t1[:], AF.Exp),
             reads=[t1], writes=[w1])
        p.op("scalar", lambda E: E.activation(w2[:], t2[:], AF.Exp),
             reads=[t2], writes=[w2])
        for bh in range(16):
            p.op("vector", lambda E, bh=bh: E.tensor_scalar(
                g.c_in_sb[:, bh * 128:(bh + 1) * 128],
                g.c_in_sb[:, bh * 128:(bh + 1) * 128], w1[:, bh:bh + 1], None, ALU.mult),
                 reads=[g.c_in_sb, w1], writes=[g.c_in_sb])
            p.op("vector", lambda E, bh=bh: E.scalar_tensor_tensor(
                g.c_in_sb[:, bh * 128:(bh + 1) * 128],
                c_j[:, bh * 128:(bh + 1) * 128], w2[:, bh:bh + 1],
                g.c_in_sb[:, bh * 128:(bh + 1) * 128], ALU.mult, ALU.add),
                 reads=[c_j, w2, g.c_in_sb], writes=[g.c_in_sb])


def emit_halo(p, io, g):
    mh = p.sb("mh", [128, 8])
    hall = p.sb("hall", [128, 8 * 16 * 6])
    p.dma("sync", mh[:], io["mhalo"])
    p.dma("sync", hall[:].rearrange("q (j e c) -> q j e c", e=16, c=6),
          io["cch_out"].rearrange("(j e q) c -> q j e c", q=128, e=16))
    for j in range(8):
        if j == 0:
            p.op("vector", lambda E, j=j: E.tensor_scalar(
                g.halo_sb[:], hall[:, j * 96:(j + 1) * 96], mh[:, j:j + 1], None, ALU.mult),
                 reads=[hall, mh], writes=[g.halo_sb])
        else:
            p.op("vector", lambda E, j=j: E.scalar_tensor_tensor(
                g.halo_sb[:], hall[:, j * 96:(j + 1) * 96], mh[:, j:j + 1],
                g.halo_sb[:], ALU.mult, ALU.add),
                 reads=[hall, mh, g.halo_sb], writes=[g.halo_sb])


def emit_mamba_p1(p, io, g):
    TC = 512
    iop = p.sb("iop", [128, 128], F32)
    p.op("gpsimd", lambda E: E.iota(iop[:], pattern=[[0, 128]], channel_multiplier=1,
                                    allow_small_or_imprecise_dtypes=True), writes=[iop])
    sel = p.sb("seln", [128, 16 * 128], F32)
    for n in range(16):
        p.op("vector", lambda E, n=n: E.tensor_scalar(
            sel[:, n * 128:(n + 1) * 128], iop[:], float(n), None, ALU.is_equal),
             reads=[iop], writes=[sel])
    dpar = p.sb("dpar_s", [128, 16])
    p.dma("sync", dpar[:], io["dpar16"])
    dtf = p.sb("dtf", [128, SEG]); xcf = p.sb("xcf", [128, SEG])
    Sf = p.sb("Sf", [128, SEG]); udt = p.sb("udt", [128, SEG])
    zer = p.sb("zer", [128, SEG])
    dAb = p.sb("dAb", [128, TC]); bmb = p.sb("bmb", [128, TC])
    hb = p.sb("hb", [128, TC]); ytmp = p.sb("ytmp", [128, TC])
    yacc = p.sb("yacc", [128, TC])
    Bsb = p.sb("Bsb", [128, SEG]); Csb = p.sb("Csb", [128, SEG])
    hstate = p.sb("hstate", [128, 256])
    sdtc = p.sb("sdtc", [128, 16])
    psBr = p.ps("psBr", [128, TC]); psCr = p.ps("psCr", [128, TC])
    p.op("vector", lambda E: E.memset(zer[:], 0.0), writes=[zer])
    p.op("vector", lambda E: E.memset(Bsb[:], 0.0), writes=[Bsb])
    p.op("vector", lambda E: E.memset(Csb[:], 0.0), writes=[Csb])
    for b_ in range(B):
        tok0 = b_ * SEG
        p.op("vector", lambda E: E.memset(hstate[:], 0.0), writes=[hstate])
        p.dma("sync", Bsb[0:16, :], io["Bf_scr"][:, tok0:tok0 + SEG])
        p.dma("sync", Csb[0:16, :], io["Cf_scr"][:, tok0:tok0 + SEG])
        for dch in range(16):
            p.dma("sync", dtf[:], io["dt_scr"][dch * 128:(dch + 1) * 128, tok0:tok0 + SEG])
            p.dma("sync", xcf[:], io["xc_scr"][dch * 128:(dch + 1) * 128, tok0:tok0 + SEG])
            p.op("vector", lambda E: E.tensor_tensor_scan(Sf[:], dtf[:], zer[:],
                                                          0.0, ALU.add, ALU.add),
                 reads=[dtf, zer], writes=[Sf])
            p.dma("gpsimd", io["S_scr"][dch * 128:(dch + 1) * 128, tok0:tok0 + SEG], Sf[:])
            p.op("vector", lambda E, dch=dch: E.tensor_copy(sdtc[:, dch:dch + 1],
                                                            Sf[:, SEG - 1:SEG]),
                 reads=[Sf], writes=[sdtc])
            p.op("vector", lambda E: E.tensor_tensor(udt[:], dtf[:], xcf[:], ALU.mult),
                 reads=[dtf, xcf], writes=[udt])
            for tc in range(SEG // TC):
                t0 = tc * TC
                for n in range(16):
                    scol = dch * 16 + n
                    p.op("scalar", lambda E, dch=dch, n=n, t0=t0: E.activation(
                        dAb[:], dtf[:, t0:t0 + TC], AF.Exp,
                        scale=g.A_sb[:, dch * 16 + n:dch * 16 + n + 1]),
                         reads=[dtf, g.A_sb], writes=[dAb])
                    p.op("tensor", lambda E, n=n, t0=t0: E.matmul(
                        psBr[:], sel[:, n * 128:(n + 1) * 128], Bsb[:, t0:t0 + TC],
                        start=True, stop=True),
                         reads=[sel, Bsb], writes=[psBr])
                    p.op("vector", lambda E, t0=t0: E.tensor_tensor(
                        bmb[:], udt[:, t0:t0 + TC], psBr[:], ALU.mult),
                         reads=[udt, psBr], writes=[bmb])
                    p.op("vector", lambda E, scol=scol: E.tensor_tensor_scan(
                        hb[:], dAb[:], bmb[:], hstate[:, scol:scol + 1],
                        ALU.mult, ALU.add),
                         reads=[dAb, bmb, hstate], writes=[hb])
                    p.op("vector", lambda E, scol=scol: E.tensor_copy(
                        hstate[:, scol:scol + 1], hb[:, TC - 1:TC]),
                         reads=[hb], writes=[hstate])
                    p.op("tensor", lambda E, n=n, t0=t0: E.matmul(
                        psCr[:], sel[:, n * 128:(n + 1) * 128], Csb[:, t0:t0 + TC],
                        start=True, stop=True),
                         reads=[sel, Csb], writes=[psCr])
                    if n == 0:
                        p.op("vector", lambda E: E.tensor_tensor(yacc[:], hb[:], psCr[:],
                                                                 ALU.mult),
                             reads=[hb, psCr], writes=[yacc])
                    else:
                        p.op("vector", lambda E: E.tensor_tensor(ytmp[:], hb[:], psCr[:],
                                                                 ALU.mult),
                             reads=[hb, psCr], writes=[ytmp])
                        p.op("vector", lambda E: E.tensor_tensor(yacc[:], yacc[:], ytmp[:],
                                                                 ALU.add),
                             reads=[yacc, ytmp], writes=[yacc])
                p.op("vector", lambda E, dch=dch, t0=t0: E.scalar_tensor_tensor(
                    yacc[:], xcf[:, t0:t0 + TC], dpar[:, dch:dch + 1], yacc[:],
                    ALU.mult, ALU.add),
                     reads=[xcf, dpar, yacc], writes=[yacc])
                p.dma("gpsimd", io["yssm_scr"][dch * 128:(dch + 1) * 128,
                                               tok0 + t0:tok0 + t0 + TC], yacc[:])
        ccs4 = io["ccs_in"].rearrange("(bb dch q) c -> q bb dch c", bb=2, q=128)
        comb = p.sb(f"comb{b_}", [128, 16 * 17])
        cv = comb[:].rearrange("q (dch c) -> q dch c", c=17)
        p.op("vector", lambda E, cv=cv: E.tensor_copy(
            cv[:, :, 0:16], hstate[:].rearrange("q (dch n) -> q dch n", n=16)),
             reads=[hstate], writes=[comb])
        p.op("vector", lambda E, cv=cv: E.tensor_copy(
            cv[:, :, 16:17], sdtc[:].rearrange("q (dch o) -> q dch o", o=1)),
             reads=[sdtc], writes=[comb])
        p.dma("gpsimd", ccs4[:, b_, :, 0:17], cv)


def emit_combine_mamba(p, io, g):
    m01 = p.sb("m01", [128, 8])
    hs_j = p.sb("hs_j", [128, 16 * 17])
    Ej = p.sb("Ej", [128, 256]); Dj = p.sb("Dj", [128, 256])
    p.dma("sync", m01[:], io["m01"])
    for b_ in range(B):
        hin = g.h_in_sb[:, b_ * 256:(b_ + 1) * 256]
        p.op("vector", lambda E, hin=hin: E.memset(hin, 0.0), writes=[g.h_in_sb])
        for j in range(8):
            rows = slice(j * 4096 + b_ * 2048, j * 4096 + (b_ + 1) * 2048)
            p.dma("sync", hs_j[:].rearrange("q (dch c) -> q dch c", c=17),
                  io["ccs_out"][rows, :].rearrange("(dch q) c -> q dch c", q=128))
            hsv = hs_j[:].rearrange("q (dch c) -> q dch c", c=17)
            sdt_j = hsv[:, :, 16]
            h_j = hsv[:, :, 0:16]
            p.op("vector", lambda E, j=j, sdt_j=sdt_j: E.tensor_tensor(
                Ej[:].rearrange("q (dch n) -> q dch n", n=16),
                g.A_sb[:].rearrange("q (dch n) -> q dch n", n=16),
                sdt_j.rearrange("q dch -> q dch ()").broadcast_to((128, 16, 16)),
                ALU.mult),
                 reads=[g.A_sb, hs_j], writes=[Ej])
            p.op("vector", lambda E, j=j: E.tensor_scalar(Ej[:], Ej[:], m01[:, j:j + 1],
                                                          None, ALU.mult),
                 reads=[Ej, m01], writes=[Ej])
            p.op("scalar", lambda E: E.activation(Dj[:], Ej[:], AF.Exp),
                 reads=[Ej], writes=[Dj])
            p.op("vector", lambda E, j=j, h_j=h_j: E.tensor_scalar(
                h_j, h_j, m01[:, j:j + 1], None, ALU.mult),
                 reads=[hs_j, m01], writes=[hs_j])
            p.op("vector", lambda E, hin=hin: E.tensor_tensor(hin, hin, Dj[:], ALU.mult),
                 reads=[g.h_in_sb, Dj], writes=[g.h_in_sb])
            p.op("vector", lambda E, hin=hin, h_j=h_j: E.tensor_tensor(
                hin.rearrange("q (dch n) -> q dch n", n=16), 
                hin.rearrange("q (dch n) -> q dch n", n=16), h_j, ALU.add),
                 reads=[g.h_in_sb, hs_j], writes=[g.h_in_sb])


def emit_mamba_p2(p, io, g):
    TC = 512
    iop = p.sb("iop", [128, 128], F32)
    p.op("gpsimd", lambda E: E.iota(iop[:], pattern=[[0, 128]], channel_multiplier=1,
                                    allow_small_or_imprecise_dtypes=True), writes=[iop])
    sel = p.sb("seln", [128, 16 * 128], F32)
    for n in range(16):
        p.op("vector", lambda E, n=n: E.tensor_scalar(
            sel[:, n * 128:(n + 1) * 128], iop[:], float(n), None, ALU.is_equal),
             reads=[iop], writes=[sel])
    Sf = p.sb("Sf", [128, SEG])
    Csb = p.sb("Csb", [128, SEG])
    P = p.sb("Pp", [128, TC]); hp = p.sb("hp", [128, TC])
    ycn = p.sb("ycn", [128, TC]); yacc = p.sb("yacc", [128, TC])
    ylo = p.sb("ylo", [128, TC])
    psCr = p.ps("psCr", [128, TC])
    p.op("vector", lambda E: E.memset(Csb[:], 0.0), writes=[Csb])
    for b_ in range(B):
        tok0 = b_ * SEG
        p.dma("sync", Csb[0:16, :], io["Cf_scr"][:, tok0:tok0 + SEG])
        for dch in range(16):
            p.dma("sync", Sf[:], io["S_scr"][dch * 128:(dch + 1) * 128, tok0:tok0 + SEG])
            for tc in range(SEG // TC):
                t0 = tc * TC
                for n in range(16):
                    hcol = b_ * 256 + dch * 16 + n
                    p.op("scalar", lambda E, dch=dch, n=n, t0=t0: E.activation(
                        P[:], Sf[:, t0:t0 + TC], AF.Exp,
                        scale=g.A_sb[:, dch * 16 + n:dch * 16 + n + 1]),
                         reads=[Sf, g.A_sb], writes=[P])
                    p.op("vector", lambda E, hcol=hcol: E.tensor_scalar(
                        hp[:], P[:], g.h_in_sb[:, hcol:hcol + 1], None, ALU.mult),
                         reads=[P, g.h_in_sb], writes=[hp])
                    p.op("tensor", lambda E, n=n, t0=t0: E.matmul(
                        psCr[:], sel[:, n * 128:(n + 1) * 128], Csb[:, t0:t0 + TC],
                        start=True, stop=True),
                         reads=[sel, Csb], writes=[psCr])
                    if n == 0:
                        p.op("vector", lambda E: E.tensor_tensor(yacc[:], hp[:], psCr[:],
                                                                 ALU.mult),
                             reads=[hp, psCr], writes=[yacc])
                    else:
                        p.op("vector", lambda E: E.tensor_tensor(ycn[:], hp[:], psCr[:],
                                                                 ALU.mult),
                             reads=[hp, psCr], writes=[ycn])
                        p.op("vector", lambda E: E.tensor_tensor(yacc[:], yacc[:], ycn[:],
                                                                 ALU.add),
                             reads=[yacc, ycn], writes=[yacc])
                p.dma("sync", ylo[:], io["yssm_scr"][dch * 128:(dch + 1) * 128,
                                                     tok0 + t0:tok0 + t0 + TC])
                p.op("vector", lambda E: E.tensor_tensor(ylo[:], ylo[:], yacc[:], ALU.add),
                     reads=[ylo, yacc], writes=[ylo])
                p.dma("gpsimd", io["yssm_scr"][dch * 128:(dch + 1) * 128,
                                               tok0 + t0:tok0 + t0 + TC], ylo[:])


def build_fused():
    p = Prog("fused")
    io = {}
    E_ = lambda n, shape, dt=F32: io.setdefault(n, p.din(n, shape, dt))
    S_ = lambda n, shape, dt=F32: io.setdefault(n, p.dtmp(n, shape, dt))
    E_("x0", [NTOK, D])
    for n, shape in [("Wq", [D, 512]), ("Wk", [D, 512]), ("Wv", [D, D]),
                     ("Wif", [D, 16]), ("Wog", [D, D]), ("Wout", [D, D]),
                     ("U1", [D, 256]), ("V1", [256, 4096]), ("U2", [4096, 256]),
                     ("V2", [256, D]), ("W_in", [D, 4096]),
                     ("b1s", [128, 32]), ("b2r", [128, D]),
                     ("lnwr", [128, D]), ("lnbr", [128, D]),
                     ("lnw8", [8, 128, 128]), ("lnb8", [8, 128, 128]),
                     ("bif_in", [1, 16]),
                     ("cw4", [128, 16, 4]), ("cb4", [128, 16]),
                     ("Wxp", [2048, 96]), ("Wdt", [64, 2048]), ("bdt", [128, 16]),
                     ("dpar16", [128, 16]), ("alog16", [128, 256]),
                     ("Wo1", [2048, D]), ("Wr", [8, D]), ("brr", [128, 8]),
                     ("W1m", [8, 8, 128, 4096]), ("b1s3", [128, 8 * 32]),
                     ("W2m", [8, 128, 32 * 1024]), ("b2r3", [8, 128, D]),
                     ("lnw3r", [128, D]), ("lnb3r", [128, D]),
                     ("mzro64", [64, 8]), ("mneg64", [64, 8]),
                     ("mhalo", [128, 8]), ("m01", [128, 8])]:
        E_(n, shape)
    io["x4"] = p.dout("x4", [NTOK, D])
    S_("qT_scr", [4, 128, NTOK], BF16); S_("kT_scr", [4, 128, NTOK], BF16)
    S_("v_scr", [NTOK, D], BF16)
    S_("if_scr", [16, NTOK]); S_("og_scr", [NTOK, D]); S_("hn_scr", [NTOK, D])
    S_("x2", [NTOK, D]); S_("y2", [NTOK, D])
    S_("xinT", [2048, NTOK]); S_("zgT", [2048, NTOK])
    S_("dtT", [2048, NTOK]); S_("xcT", [2048, NTOK])
    S_("Bf", [16, NTOK]); S_("Cf", [16, NTOK])
    S_("S_scr", [2048, NTOK]); S_("yssm_scr", [2048, NTOK])
    S_("w1sc", [8, 8, 128, 4096], BF16); S_("w2sc", [8, 128, 32 * 1024], BF16)
    S_("ccm_in", [16, 8194]); S_("ccm_out", [128, 8194])
    S_("cch_in", [2048, 6]); S_("cch_out", [16384, 6])
    S_("ccs_in", [4096, 17]); S_("ccs_out", [32768, 17])
    io["hn"] = io["hn_scr"]; io["og"] = io["og_scr"]
    io["xinh"] = io["xinT"]
    io["dt_scr"] = io["dtT"]; io["xc_scr"] = io["xcT"]
    io["Bf_scr"] = io["Bf"]; io["Cf_scr"] = io["Cf"]
    io["ysm"] = io["yssm_scr"]

    class G:
        pass
    g = G()
    g.m_in_sb = p.sb("g_m_in", [64, 16])
    g.c_in_sb = p.sb("g_c_in", [64, 16 * 128])
    g.h_in_sb = p.sb("g_h_in", [128, 2 * 256])
    g.A_sb = p.sb("g_A", [128, 256])
    g.halo_sb = p.sb("g_halo", [128, 16 * 6])
    g.bif = p.sb("g_bif", [1, 16])
    for v in (-1.0, 1.0, 1.0 / 15.0, 15.0, GELU_C):
        p.constcol(v)
    p.dma("sync", g.bif[0:1, :], io["bif_in"])
    p.dma("sync", g.A_sb[:], io["alog16"])
    p.op("scalar", lambda E: E.activation(g.A_sb[:], g.A_sb[:], AF.Exp),
         reads=[g.A_sb], writes=[g.A_sb])
    p.op("vector", lambda E: E.tensor_scalar(g.A_sb[:], g.A_sb[:], -1.0, None, ALU.mult),
         reads=[g.A_sb], writes=[g.A_sb])

    def sect(pfx, fn, *a):
        p.pfx = pfx
        p.section_open()
        fn(*a)
        p.section_close()
        p.pfx = ""

    def collective(cin, cout):
        p.op("gpsimd", lambda E: E.collective_compute(
            "AllGather", ALU.bypass, replica_groups=[list(range(NCORES))],
            ins=[cin.opt()], outs=[cout.opt()]),
             reads=[cin], writes=[cout])
        p.barrier_all()

    sect("a_", emit_l1_fused, p, io)
    sect("b_", emit_mlstm_fused, p, io, 1, g)
    collective(io["ccm_in"], io["ccm_out"])
    sect("c_", emit_combine_mlstm, p, io, g)
    sect("d_", emit_mlstm_fused, p, io, 2, g)
    sect("e_", emit_L3, p, io)
    for b_ in range(B):
        p.dma("sync", io["cch_in"][:, b_ * 3:(b_ + 1) * 3],
              io["xinT"][:, b_ * SEG + SEG - 3:b_ * SEG + SEG])
    collective(io["cch_in"], io["cch_out"])
    sect("f_", emit_halo, p, io, g)
    io["halo_sb"] = g.halo_sb
    def g_with_stage():
        stg = p.sb("moestg", [128, 16384], BF16)
        emit_L4(p, io)
        emit_moe_stage(p, io, stg[:])
    sect("g_", g_with_stage)
    sect("h_", emit_mamba_p1, p, io, g)
    collective(io["ccs_in"], io["ccs_out"])
    sect("i_", emit_combine_mamba, p, io, g)
    sect("j_", emit_mamba_p2, p, io, g)
    sect("k_", emit_L5, p, io)
    p.barrier_end()
    return p.realize()


def run_fused(d):
    x0 = d["L0_attn_x"]
    wif = np.ascontiguousarray(np.concatenate([d["L0_attn_Wi"], d["L0_attn_Wf"]], 1))
    lnw8 = np.ascontiguousarray(np.broadcast_to(
        d["L0_attn_ln_w"][:, None, :], (8, 128, 128)))
    lnb8 = np.ascontiguousarray(np.broadcast_to(
        d["L0_attn_ln_b"][:, None, :], (8, 128, 128)))
    bif = np.ascontiguousarray(np.concatenate(
        [d["L0_attn_bi"], d["L0_attn_bf"]])[None, :])
    b1s = np.ascontiguousarray(d["L0_ffn_b1"].reshape(32, 128).T)
    b2r = np.ascontiguousarray(np.broadcast_to(d["L0_ffn_b2"][None, :], (128, D)))
    lnwr = np.ascontiguousarray(np.broadcast_to(d["L1_attn_ln_w"][None, :], (128, D)))
    lnbr = np.ascontiguousarray(np.broadcast_to(d["L1_attn_ln_b"][None, :], (128, D)))
    cw4 = np.ascontiguousarray(
        np.transpose(d["L1_attn_conv_w"].reshape(16, 128, 1, 4)[:, :, 0, :], (1, 0, 2)))
    cb4 = np.ascontiguousarray(d["L1_attn_conv_b"].reshape(16, 128).T)
    bdt = np.ascontiguousarray(d["L1_attn_b_dt"].reshape(16, 128).T)
    dpar16 = np.ascontiguousarray(d["L1_attn_D_param"].reshape(16, 128).T)
    alog16 = np.ascontiguousarray(
        np.transpose(d["L1_attn_A_log"].reshape(16, 128, 16), (1, 0, 2)).reshape(128, 256))
    b1s3 = np.ascontiguousarray(
        np.transpose(d["L1_ffn_b1"].reshape(8, 32, 128), (2, 0, 1)).reshape(128, 256))
    brr = np.ascontiguousarray(np.broadcast_to(d["L1_ffn_br"][None, :], (128, 8)))
    b2r3 = np.ascontiguousarray(np.broadcast_to(d["L1_ffn_b2"][:, None, :], (8, 128, D)))
    lnw3r = np.ascontiguousarray(np.broadcast_to(d["L1_ffn_ln_w"][None, :], (128, D)))
    lnb3r = np.ascontiguousarray(np.broadcast_to(d["L1_ffn_ln_b"][None, :], (128, D)))
    w1h, w2h = moe_w_host(d)
    nc = build_fused()
    in_maps = []
    for c in range(NCORES):
        sl = slice(c * SEG, (c + 1) * SEG)
        mzro = np.zeros((64, 8), np.float32); mzro[:, :c] = 1.0
        mneg = np.full((64, 8), -1e30, np.float32); mneg[:, :c] = 0.0
        mhalo = np.zeros((128, 8), np.float32)
        if c > 0:
            mhalo[:, c - 1] = 1.0
        m01 = np.zeros((128, 8), np.float32); m01[:, :c] = 1.0
        in_maps.append({
            "x0": np.ascontiguousarray(x0[:, sl, :].reshape(NTOK, D)),
            "Wq": d["L0_attn_Wq"], "Wk": d["L0_attn_Wk"], "Wv": d["L0_attn_Wv"],
            "Wif": wif, "Wog": d["L0_attn_Wog"], "Wout": d["L0_attn_Wout"],
            "U1": d["L0_ffn_U1"], "V1": d["L0_ffn_V1"], "U2": d["L0_ffn_U2"],
            "V2": d["L0_ffn_V2"], "W_in": d["L1_attn_W_in"],
            "b1s": b1s, "b2r": b2r, "lnwr": lnwr, "lnbr": lnbr,
            "lnw8": lnw8, "lnb8": lnb8, "bif_in": bif,
            "cw4": cw4, "cb4": cb4, "Wxp": d["L1_attn_W_xproj"],
            "Wdt": d["L1_attn_W_dt"], "bdt": bdt,
            "dpar16": dpar16, "alog16": alog16,
            "Wo1": d["L1_attn_W_out"], "Wr": d["L1_ffn_Wr"], "brr": brr,
            "W1m": w1h, "b1s3": b1s3, "W2m": w2h,
            "b2r3": b2r3, "lnw3r": lnw3r, "lnb3r": lnb3r,
            "mzro64": mzro, "mneg64": mneg, "mhalo": mhalo, "m01": m01,
        })
    res = run_bass_kernel_spmd(nc, in_maps, core_ids=list(range(NCORES)))
    full = np.stack([res.results[c]["x4"].reshape(B, SEG, D) for c in range(NCORES)],
                    axis=1)
    return np.ascontiguousarray(full.reshape(B, NCORES * SEG, D).astype(np.float32))


# version 12 (final)
# speedup vs baseline: 1.4170x; 1.0212x over previous
import numpy as np
import ml_dtypes
from contextlib import ExitStack
import concourse.bass as bass
import concourse.mybir as mybir
from concourse.bass_utils import run_bass_kernel_spmd

F32 = mybir.dt.float32
BF16 = mybir.dt.bfloat16
AF = mybir.ActivationFunctionType
ALU = mybir.AluOpType
AX = mybir.AxisListType

NCORES = 8
SEG = 2048
B = 2
NTOK = B * SEG
HS = SEG + 3
D = 1024


class Prog:
    COMPUTE = ("tensor", "vector", "scalar", "gpsimd", "sync")

    def __init__(self, name="prog"):
        self.nc = bass.Bass("TRN2", target_bir_lowering=False, debug=False)
        self.ctx = ExitStack()
        self.name = name
        self.streams = {}
        for e in self.COMPUTE:
            self.streams[e] = dict(
                sem=self.ctx.enter_context(self.nc.semaphore(f"sc_{e}")),
                count=0, inc=1)
            self.streams["dma_" + e] = dict(
                sem=self.ctx.enter_context(self.nc.semaphore(f"sd_{e}")),
                count=0, inc=16)
        self.q = {e: [] for e in self.COMPUTE}
        self.acc = {}
        self.sbuf_bytes = 0
        self._consts = {}
        self._const_tile = None

    def constcol(self, value):
        value = float(value)
        if value in self._consts:
            return self._consts[value]
        if self._const_tile is None or self._const_next >= 16:
            self._const_tile = self.sb(f"cconst{len(self._consts)}", [128, 16], F32)
            self._const_next = 0
        tile, j = self._const_tile, self._const_next
        self._const_next += 1
        self.op("vector", lambda E: E.memset(tile[:, j:j + 1], value), writes=[tile])
        ap = tile[:, j:j + 1]
        self._consts[value] = ap
        return ap

    pfx = ""

    def sb(self, name, shape, dt=F32):
        elems = 1
        for s in shape[1:]:
            elems *= s
        self.sbuf_bytes += elems * mybir.dt.size(dt)
        ctx = getattr(self, "_sect_ctx", None) or self.ctx
        return ctx.enter_context(self.nc.sbuf_tensor(self.pfx + name, shape, dt))

    def ps(self, name, shape, dt=F32):
        ctx = getattr(self, "_sect_ctx", None) or self.ctx
        return ctx.enter_context(self.nc.psum_tensor(self.pfx + name, shape, dt))

    def section_open(self):
        assert getattr(self, "_sect_ctx", None) is None
        self._sect_ctx = ExitStack()

    def section_close(self):
        self._sect_ctx.close()
        self._sect_ctx = None
        self.barrier_all()

    def barrier_all(self):
        finals = [(k, st["count"]) for k, st in self.streams.items()
                  if st["count"] > 0]
        for e in self.COMPUTE:
            self.q[e].append((None, list(finals), None, None))

    def din(self, name, shape, dt=F32):
        return self.nc.dram_tensor(name, shape, dt, kind="ExternalInput").ap()

    def dout(self, name, shape, dt=F32):
        return self.nc.dram_tensor(name, shape, dt, kind="ExternalOutput").ap()

    def dtmp(self, name, shape, dt=F32):
        return self.nc.dram_tensor(name, shape, dt, kind="Internal").ap()

    @staticmethod
    def _name(x):
        if isinstance(x, tuple):
            return x[1]
        if isinstance(x, bass.AP) or hasattr(x, "tensor"):
            return x.tensor.name
        return x.name

    def _deps(self, reads, writes, stream):
        waits = {}
        def add(st, val):
            if st == stream == "tensor":
                return
            if st.startswith("dma_"):
                val = self.streams[st]["count"]
            if waits.get(st, -1) < val:
                waits[st] = val
        for x in reads:
            rec = self.acc.get(self._name(x))
            if rec and rec["w"]:
                add(*rec["w"])
        for x in writes:
            rec = self.acc.get(self._name(x))
            if rec:
                if rec["w"]:
                    add(*rec["w"])
                for st, val in rec["r"].items():
                    if st != stream:
                        add(st, val)
        return waits

    def _record(self, reads, writes, stream, val):
        for x in reads:
            rec = self.acc.setdefault(self._name(x), {"w": None, "r": {}})
            if rec["r"].get(stream, -1) < val:
                rec["r"][stream] = val
        for x in writes:
            rec = self.acc.setdefault(self._name(x), {"w": None, "r": {}})
            rec["w"] = (stream, val)
            rec["r"] = {}

    def op(self, eng, fn, reads=(), writes=(), dma=False):
        stream = ("dma_" + eng) if dma else eng
        st = self.streams[stream]
        waits = self._deps(reads, writes, stream)
        val = st["count"] + st["inc"]
        st["count"] = val
        self.q[eng].append((waits, fn, st["sem"], st["inc"]))
        self._record(reads, writes, stream, val)

    def dma(self, eng, out, in_, **kw):
        self.op(eng, lambda E: E.dma_start(out=out, in_=in_, **kw),
                reads=[in_], writes=[out], dma=True)

    def barrier_end(self):
        finals = [(k, s["count"]) for k, s in self.streams.items()
                  if s["count"] > 0 and k != "sync"]
        self.q["sync"].append((None, finals, None, None))

    def realize(self):
        block = self.ctx.enter_context(self.nc.Block())
        waited = {e: {} for e in self.COMPUTE}

        def make_body(eng_name):
            def body(E):
                for waits, fn, sem, inc in self.q[eng_name]:
                    if sem is None:
                        w = waited[eng_name]
                        for stname, v2 in fn:
                            if stname == eng_name:
                                continue
                            if w.get(stname, -1) < v2:
                                E.wait_ge(self.streams[stname]["sem"], v2)
                                w[stname] = v2
                        continue
                    if waits:
                        for stname, v in waits.items():
                            w = waited[eng_name]
                            if w.get(stname, -1) < v:
                                E.wait_ge(self.streams[stname]["sem"], v)
                                w[stname] = v
                    inst = fn(E)
                    inst.then_inc(sem, inc)
            return body

        for e in self.COMPUTE:
            if self.q[e] or e == "sync":
                getattr(block, "sync" if e == "sync" else e)(make_body(e))
        self.ctx.close()
        return self.nc


def make_identity(p, name="ident"):
    io = p.sb(name + "_io", [128, 128], F32)
    idf = p.sb(name + "_f", [128, 128], F32)
    idb = p.sb(name, [128, 128], BF16)
    p.op("gpsimd", lambda E: E.iota(io[:], pattern=[[1, 128]], channel_multiplier=-1,
                                    allow_small_or_imprecise_dtypes=True),
         writes=[io])
    p.op("vector", lambda E: E.tensor_scalar(idf[:], io[:], 0.0, None, ALU.is_equal),
         reads=[io], writes=[idf])
    p.op("vector", lambda E: E.tensor_copy(idb[:], idf[:]), reads=[idf], writes=[idb])
    return idb


def load_weight_bf16(p, name, w_dram, K, N, eng="gpsimd"):
    kc = K // 128
    sbw = p.sb(name, [128, kc * N], BF16)
    src = w_dram.rearrange("(kc p) n -> p kc n", p=128)
    dst = sbw[:].rearrange("p (kc n) -> p kc n", kc=kc)
    for k in range(kc):
        p.op(eng, lambda E, k=k: E.dma_start(out=dst[:, k, :], in_=src[:, k, :]),
             reads=[w_dram], writes=[sbw], dma=True)
    return sbw


def chunk(sbw, k, N, cols=None):
    if cols is None:
        return sbw[:, k * N:(k + 1) * N]
    c0, c1 = cols
    return sbw[:, k * N + c0:k * N + c1]


def emit_ln(p, y, x, scr, eps, dcols=D):
    s_sum = scr[:, 0:1]
    s_rstd = scr[:, 1:2]
    sq = scr[:, 2:2 + dcols]
    p.op("vector", lambda E: E.reduce_sum(s_sum, x, axis=AX.X), reads=[x], writes=[scr])
    p.op("vector", lambda E: E.tensor_scalar(s_sum, s_sum, 1.0 / dcols, None, ALU.mult),
         reads=[scr], writes=[scr])
    p.op("vector", lambda E: E.tensor_scalar(y, x, s_sum, None, ALU.subtract),
         reads=[x, scr], writes=[y])
    p.op("scalar", lambda E: E.activation(sq, y, AF.Square), reads=[y], writes=[scr])
    p.op("vector", lambda E: E.reduce_sum(s_rstd, sq, axis=AX.X), reads=[scr], writes=[scr])
    p.op("vector", lambda E: E.tensor_scalar(s_rstd, s_rstd, 1.0 / dcols, float(eps),
                                             ALU.mult, ALU.add), reads=[scr], writes=[scr])
    p.op("scalar", lambda E: E.activation(s_rstd, s_rstd, AF.Sqrt), reads=[scr], writes=[scr])
    p.op("vector", lambda E: E.reciprocal(s_rstd, s_rstd), reads=[scr], writes=[scr])
    p.op("vector", lambda E: E.tensor_scalar(y, y, s_rstd, None, ALU.mult),
         reads=[y, scr], writes=[y])


def emit_cast_transpose(p, dstT, src_f32, cast_buf, ident, psT, ncol=D):
    p.op("vector", lambda E: E.tensor_copy(cast_buf[:, 0:ncol], src_f32),
         reads=[src_f32], writes=[cast_buf])
    nch = ncol // 128
    for k in range(nch):
        p.op("tensor", lambda E, k=k: E.transpose(psT[:], cast_buf[:, k * 128:(k + 1) * 128],
                                                  ident[:]),
             reads=[cast_buf, ident], writes=[psT])
        p.op("scalar", lambda E, k=k: E.copy(dstT[:, k * 128:(k + 1) * 128], psT[:]),
             reads=[psT], writes=[dstT])


def emit_gemm_tokmajor(p, ps_bank, xT, wsb, K, N, cols, extra_reads=()):
    kc = K // 128
    c0, c1 = cols
    for k in range(kc):
        p.op("tensor",
             lambda E, k=k: E.matmul(ps_bank[:, 0:c1 - c0],
                                     xT[:, k * 128:(k + 1) * 128],
                                     wsb[:, k * N + c0:k * N + c1],
                                     start=(k == 0), stop=(k == kc - 1)),
             reads=[xT, wsb] + list(extra_reads), writes=[ps_bank])


def build_L1():
    p = Prog("L1")
    x0 = p.din("x0", [NTOK, D])
    wq_d = p.din("Wq", [D, 512]); wk_d = p.din("Wk", [D, 512])
    wv_d = p.din("Wv", [D, D]); wif_d = p.din("Wif", [D, 16])
    wog_d = p.din("Wog", [D, D])
    o_q = p.dout("q", [NTOK, 512]); o_k = p.dout("k", [NTOK, 512])
    o_v = p.dout("v", [NTOK, D]); o_if = p.dout("ifr", [NTOK, 16])
    o_o = p.dout("og", [NTOK, D])

    ident = make_identity(p)
    c_inv15 = p.constcol(1.0 / 15.0)
    c_15 = p.constcol(15.0)
    wq = load_weight_bf16(p, "wq", wq_d, D, 512)
    wk = load_weight_bf16(p, "wk", wk_d, D, 512)
    wv = load_weight_bf16(p, "wv", wv_d, D, D)
    wif = load_weight_bf16(p, "wif", wif_d, D, 16)
    wog = load_weight_bf16(p, "wog", wog_d, D, D)

    NB = 2
    xs = [p.sb(f"xt{i}", [128, D]) for i in range(NB)]
    ys = [p.sb(f"yy{i}", [128, D]) for i in range(NB)]
    cast = [p.sb(f"cast{i}", [128, D], BF16) for i in range(NB)]
    yT = [p.sb(f"yT{i}", [128, D], BF16) for i in range(NB)]
    scr = [p.sb(f"scr{i}", [128, D + 2]) for i in range(NB)]
    st_q = [p.sb(f"st_q{i}", [128, 512]) for i in range(NB)]
    st_k = [p.sb(f"st_k{i}", [128, 512]) for i in range(NB)]
    st_v = [p.sb(f"st_v{i}", [128, D]) for i in range(NB)]
    st_o = [p.sb(f"st_o{i}", [128, D]) for i in range(NB)]
    st_if = [p.sb(f"st_if{i}", [128, 16]) for i in range(NB)]
    tmp_t = [p.sb(f"tmp_t{i}", [128, D]) for i in range(NB)]

    psT = p.ps("psT", [128, 128], BF16)
    ps_a = p.ps("ps_a", [128, 512]); ps_b = p.ps("ps_b", [128, 512])
    ps_c = p.ps("ps_c", [128, 512]); ps_d = p.ps("ps_d", [128, 512])

    ntiles = NTOK // 128
    for t in range(ntiles):
        i = t % NB
        r0 = t * 128
        p.dma("sync", xs[i][:], x0[r0:r0 + 128, :])
        emit_ln(p, ys[i][:], xs[i][:], scr[i], 1e-5)
        emit_cast_transpose(p, yT[i], ys[i][:], cast[i], ident, psT)
        emit_gemm_tokmajor(p, ps_a, yT[i], wq, D, 512, (0, 512))
        p.op("vector", lambda E, i=i: E.tensor_copy(st_q[i][:], ps_a[:]),
             reads=[ps_a], writes=[st_q[i]])
        emit_gemm_tokmajor(p, ps_b, yT[i], wk, D, 512, (0, 512))
        p.op("vector", lambda E, i=i: E.tensor_copy(st_k[i][:], ps_b[:]),
             reads=[ps_b], writes=[st_k[i]])
        for h in range(2):
            bank = ps_c if h == 0 else ps_d
            emit_gemm_tokmajor(p, bank, yT[i], wv, D, D, (h * 512, (h + 1) * 512))
            p.op("vector", lambda E, i=i, h=h, bank=bank:
                 E.tensor_copy(st_v[i][:, h * 512:(h + 1) * 512], bank[:]),
                 reads=[bank], writes=[st_v[i]])
        emit_gemm_tokmajor(p, ps_a, yT[i], wif, D, 16, (0, 16))
        p.op("vector", lambda E, i=i: E.tensor_copy(st_if[i][:], ps_a[:, 0:16]),
             reads=[ps_a], writes=[st_if[i]])
        for h in range(2):
            bank = ps_b if h == 0 else ps_c
            emit_gemm_tokmajor(p, bank, yT[i], wog, D, D, (h * 512, (h + 1) * 512))
            p.op("scalar", lambda E, i=i, h=h, bank=bank:
                 E.activation(tmp_t[i][:, h * 512:(h + 1) * 512], bank[:], AF.Tanh,
                              scale=c_inv15),
                 reads=[bank, c_inv15], writes=[tmp_t[i]])
        p.op("scalar", lambda E, i=i: E.activation(st_o[i][:], tmp_t[i][:], AF.Sigmoid,
                                                   scale=c_15),
             reads=[tmp_t[i], c_15], writes=[st_o[i]])
        p.dma("gpsimd", o_q[r0:r0 + 128, :], st_q[i][:])
        p.dma("gpsimd", o_k[r0:r0 + 128, :], st_k[i][:])
        p.dma("gpsimd", o_v[r0:r0 + 128, :], st_v[i][:])
        p.dma("gpsimd", o_if[r0:r0 + 128, :], st_if[i][:])
        p.dma("gpsimd", o_o[r0:r0 + 128, :], st_o[i][:])
    p.barrier_end()
    return p.realize()


def run_L1(inputs):
    d = inputs
    x0 = d["L0_attn_x"]
    wif = np.ascontiguousarray(np.concatenate([d["L0_attn_Wi"], d["L0_attn_Wf"]], 1))
    nc = build_L1()
    in_maps = []
    for c in range(NCORES):
        xs = np.ascontiguousarray(
            x0[:, c * SEG:(c + 1) * SEG, :].reshape(NTOK, D))
        in_maps.append({
            "x0": xs, "Wq": d["L0_attn_Wq"], "Wk": d["L0_attn_Wk"],
            "Wv": d["L0_attn_Wv"], "Wif": wif, "Wog": d["L0_attn_Wog"],
        })
    res = run_bass_kernel_spmd(nc, in_maps, core_ids=list(range(NCORES)))
    outs = {}
    for key, w in [("q", 512), ("k", 512), ("v", D), ("ifr", 16), ("og", D)]:
        full = np.stack([res.results[c][key].reshape(B, SEG, w)
                         for c in range(NCORES)], axis=1)
        outs[key] = full.reshape(B, NCORES * SEG, w)
    return outs


def _kernel_impl(inputs):
    l1 = run_L1(inputs)
    l2 = run_L2(inputs, l1)
    l3 = run_L3(inputs, l1, l2)
    l4 = run_L4(inputs, l3)
    l4c = run_L4c(inputs, l4)
    l5 = run_L5(inputs, l3, l4c)
    return np.ascontiguousarray(l5["x4"].astype(np.float32))


SL = 16384
NSEG = 8
C = 128


def emit_mlstm_bh(p, t, bh, qT_d, kT_d, v_d, i_d, f_d, bicol, bfcol, lnw_rep, lnb_rep,
                  hn_out):
    ALU_ = ALU
    c_m1 = p.constcol(-1.0)
    c_1 = p.constcol(1.0)
    p.op("vector", lambda E: E.memset(t.c_f32[:], 0.0), writes=[t.c_f32])
    p.op("vector", lambda E: E.memset(t.mstate[:], 0.0), writes=[t.mstate])
    for seg in range(NSEG):
        s0 = seg * SEG
        p.dma("sync", t.qT[:], qT_d[bh, :, s0:s0 + SEG])
        p.dma("sync", t.kT[:], kT_d[bh, :, s0:s0 + SEG])
        p.dma("sync", t.v_sb[:].rearrange("p (cc d) -> p cc d", d=128),
              v_d[bh, s0:s0 + SEG, :].rearrange("(cc p) d -> p cc d", p=128))
        p.dma("sync", t.i_r[0:1, :], i_d[bh:bh + 1, s0:s0 + SEG])
        p.dma("sync", t.f_r[0:1, :], f_d[bh:bh + 1, s0:s0 + SEG])
        p.op("vector", lambda E: E.tensor_scalar(t.qTb[:], t.qT[:], 0.125, None, ALU_.mult),
             reads=[t.qT], writes=[t.qTb])
        p.op("vector", lambda E: E.tensor_copy(t.kTb[:], t.kT[:]), reads=[t.kT], writes=[t.kTb])
        p.op("vector", lambda E: E.tensor_copy(t.vb[:], t.v_sb[:]), reads=[t.v_sb], writes=[t.vb])
        R = lambda x: x[0:1, :]
        p.op("vector", lambda E: E.tensor_scalar(R(t.fb), R(t.f_r), bfcol, None, ALU_.add),
             reads=[t.f_r, bfcol], writes=[t.fb])
        p.op("vector", lambda E: E.tensor_scalar(R(t.g1), R(t.fb), -1.0, None, ALU_.mult),
             reads=[t.fb], writes=[t.g1])
        p.op("vector", lambda E: E.tensor_tensor(R(t.g2), R(t.fb), R(t.g1), ALU_.max),
             reads=[t.fb, t.g1], writes=[t.g2])
        p.op("scalar", lambda E: E.activation(R(t.g1), R(t.g2), AF.Exp, scale=c_m1[0:1, 0:1]),
             reads=[t.g2, c_m1], writes=[t.g1])
        p.op("scalar", lambda E: E.activation(R(t.g2), R(t.g1), AF.Ln, bias=c_1[0:1, 0:1]),
             reads=[t.g1, c_1], writes=[t.g2])
        p.op("vector", lambda E: E.tensor_scalar(R(t.g1), R(t.fb), 0.0, None, ALU_.min),
             reads=[t.fb], writes=[t.g1])
        p.op("vector", lambda E: E.tensor_tensor(R(t.flog), R(t.g1), R(t.g2), ALU_.subtract),
             reads=[t.g1, t.g2], writes=[t.flog])
        p.op("vector", lambda E: E.tensor_tensor_scan(R(t.Fc), R(t.ones), R(t.flog),
                                                      0.0, ALU_.mult, ALU_.add),
             reads=[t.ones, t.flog], writes=[t.Fc])
        p.op("vector", lambda E: E.tensor_scalar(R(t.g1), R(t.i_r), bicol, None, ALU_.add),
             reads=[t.i_r, bicol], writes=[t.g1])
        p.op("vector", lambda E: E.tensor_tensor(R(t.bg), R(t.g1), R(t.Fc), ALU_.subtract),
             reads=[t.g1, t.Fc], writes=[t.bg])
        p.op("vector", lambda E: E.tensor_tensor_scan(R(t.M), R(t.bg), R(t.bg),
                                                      t.mstate[0:1, 0:1], ALU_.max, ALU_.max),
             reads=[t.bg, t.mstate], writes=[t.M])
        for cc in range(SEG // C):
            c0 = cc * C
            if cc == 0:
                src = t.mstate[0:1, 0:1]
                rd = t.mstate
            else:
                src = t.M[0:1, c0 - 1:c0]
                rd = t.M
            p.op("vector", lambda E, c0=c0, src=src, rd=rd:
                 E.tensor_scalar(t.mp[0:1, c0:c0 + C], t.ones[0:1, c0:c0 + C], src,
                                 None, ALU_.mult),
                 reads=[t.ones, rd], writes=[t.mp])
            p.op("vector", lambda E, c0=c0:
                 E.tensor_scalar(t.nm[0:1, c0:c0 + C], t.ones[0:1, c0:c0 + C],
                                 t.M[0:1, c0 + C - 1:c0 + C], -1.0, ALU_.mult, ALU_.mult),
                 reads=[t.ones, t.M], writes=[t.nm])
        p.op("vector", lambda E: E.tensor_tensor(t.mstate[0:1, 0:1], t.Fc[0:1, SEG - 1:SEG],
                                                 t.M[0:1, SEG - 1:SEG], ALU_.add),
             reads=[t.Fc, t.M], writes=[t.mstate])
        for cc in range(SEG // C):
            c0 = cc * C
            sl = slice(c0, c0 + C)
            p.op("tensor", lambda E, sl=sl: E.transpose(t.psG[:, 0:1], t.bg[0:1, sl], t.identf[0:1, 0:1]),
                 reads=[t.bg, t.identf], writes=[t.psG])
            p.op("tensor", lambda E, sl=sl: E.transpose(t.psG[:, 1:2], t.mp[0:1, sl], t.identf[0:1, 0:1]),
                 reads=[t.mp, t.identf], writes=[t.psG])
            p.op("tensor", lambda E, sl=sl: E.transpose(t.psG[:, 2:3], t.nm[0:1, sl], t.identf[0:1, 0:1]),
                 reads=[t.nm, t.identf], writes=[t.psG])
            p.op("scalar", lambda E: E.copy(t.gcols[:, 0:3], t.psG[:, 0:3]),
                 reads=[t.psG], writes=[t.gcols])
            p.op("tensor", lambda E, sl=sl: E.matmul(t.psMrep[:], t.sel0[:], t.M[:, sl],
                                                     start=True, stop=True),
                 reads=[t.sel0, t.M], writes=[t.psMrep])
            p.op("scalar", lambda E: E.activation(t.W_sb[:], t.psMrep[:], AF.Exp,
                                                  scale=c_m1, bias=t.gcols[:, 0:1]),
                 reads=[t.psMrep, t.gcols, c_m1], writes=[t.W_sb])
            p.op("vector", lambda E: E.tensor_tensor(t.Wm[:], t.W_sb[:], t.trimask[:], ALU_.mult),
                 reads=[t.W_sb, t.trimask], writes=[t.Wm])
            p.op("tensor", lambda E, sl=sl: E.matmul(t.psSb[:], t.kTb[:, sl], t.qTb[:, sl],
                                                     start=True, stop=True),
                 reads=[t.kTb, t.qTb], writes=[t.psSb])
            p.op("vector", lambda E: E.tensor_tensor(t.DSb[:], t.Wm[:], t.psSb[:], ALU_.mult),
                 reads=[t.Wm, t.psSb], writes=[t.DSb])
            p.op("scalar", lambda E: E.activation(t.e_rep[:], t.psMrep[0:64, :], AF.Exp,
                                                  scale=c_m1[0:64, 0:1], bias=t.gcols[0:64, 1:2]),
                 reads=[t.psMrep, t.gcols, c_m1], writes=[t.e_rep])
            p.op("vector", lambda E, sl=sl: E.tensor_tensor(t.qs[:], t.qTb[0:64, sl],
                                                            t.e_rep[:], ALU_.mult),
                 reads=[t.qTb, t.e_rep], writes=[t.qs])
            p.op("vector", lambda E: E.tensor_copy(t.c_bf[:], t.c_f32[:]),
                 reads=[t.c_f32], writes=[t.c_bf])
            p.op("tensor", lambda E, cc=cc: E.matmul(t.psH[:], t.DSb[:],
                                                     t.vb[:, cc * 128:(cc + 1) * 128],
                                                     start=True, stop=False),
                 reads=[t.DSb, t.vb], writes=[t.psH])
            p.op("tensor", lambda E: E.matmul(t.psH[:], t.qs[:], t.c_bf[:],
                                              start=False, stop=True),
                 reads=[t.qs, t.c_bf], writes=[t.psH])
            p.op("scalar", lambda E: E.activation(t.wcol[:], t.gcols[:, 0:1], AF.Exp,
                                                  bias=t.gcols[:, 2:3]),
                 reads=[t.gcols], writes=[t.wcol])
            p.op("scalar", lambda E: E.activation(t.dec[:], t.gcols[0:64, 1:2], AF.Exp,
                                                  bias=t.gcols[0:64, 2:3]),
                 reads=[t.gcols], writes=[t.dec])
            p.op("tensor", lambda E, sl=sl: E.transpose(t.psKt[:], t.kT[:, sl], t.identf[0:64, 0:64]),
                 reads=[t.kT, t.identf], writes=[t.psKt])
            p.op("vector", lambda E: E.tensor_scalar(t.kw[:], t.psKt[:], t.wcol[:, 0:1],
                                                     None, ALU_.mult),
                 reads=[t.psKt, t.wcol], writes=[t.kw])
            p.op("tensor", lambda E, cc=cc: E.matmul(t.psC[:], t.kw[:],
                                                     t.vb[:, cc * 128:(cc + 1) * 128],
                                                     start=True, stop=True),
                 reads=[t.kw, t.vb], writes=[t.psC])
            p.op("vector", lambda E: E.tensor_scalar(t.c_f32[:], t.c_f32[:], t.dec[:, 0:1],
                                                     None, ALU_.mult),
                 reads=[t.c_f32, t.dec], writes=[t.c_f32])
            p.op("vector", lambda E: E.tensor_tensor(t.c_f32[:], t.c_f32[:], t.psC[:],
                                                     ALU_.add),
                 reads=[t.c_f32, t.psC], writes=[t.c_f32])
            scr = t.scr
            p.op("vector", lambda E: E.reduce_sum(scr[:, 0:1], t.psH[:], axis=AX.X),
                 reads=[t.psH], writes=[scr])
            p.op("vector", lambda E: E.tensor_scalar(scr[:, 0:1], scr[:, 0:1], 1.0 / 128,
                                                     None, ALU_.mult),
                 reads=[scr], writes=[scr])
            p.op("vector", lambda E: E.tensor_scalar(t.hbuf[:], t.psH[:], scr[:, 0:1],
                                                     None, ALU_.subtract),
                 reads=[t.psH, scr], writes=[t.hbuf])
            p.op("scalar", lambda E: E.activation(scr[:, 2:130], t.hbuf[:], AF.Square),
                 reads=[t.hbuf], writes=[scr])
            p.op("vector", lambda E: E.reduce_sum(scr[:, 1:2], scr[:, 2:130], axis=AX.X),
                 reads=[scr], writes=[scr])
            p.op("vector", lambda E: E.tensor_scalar(scr[:, 1:2], scr[:, 1:2], 1.0 / 128,
                                                     1e-6, ALU_.mult, ALU_.add),
                 reads=[scr], writes=[scr])
            p.op("scalar", lambda E: E.activation(scr[:, 1:2], scr[:, 1:2], AF.Sqrt),
                 reads=[scr], writes=[scr])
            p.op("vector", lambda E: E.reciprocal(scr[:, 1:2], scr[:, 1:2]),
                 reads=[scr], writes=[scr])
            p.op("vector", lambda E: E.scalar_tensor_tensor(t.hbuf[:], t.hbuf[:],
                                                            scr[:, 1:2], lnw_rep,
                                                            ALU_.mult, ALU_.mult),
                 reads=[t.hbuf, scr, lnw_rep], writes=[t.hbuf])
            p.op("vector", lambda E, c0=c0: E.tensor_tensor(
                t.hn_st[:, c0:c0 + C], t.hbuf[:], lnb_rep, ALU_.add),
                 reads=[t.hbuf, lnb_rep], writes=[t.hn_st])
        p.dma("gpsimd", hn_out[bh, s0:s0 + SEG, :].rearrange("(cc p) d -> p cc d", p=128),
              t.hn_st[:].rearrange("p (cc d) -> p cc d", d=128))


class _T:
    pass


def build_L2():
    p = Prog("L2")
    qT_d = p.din("qT", [2, 64, SL])
    kT_d = p.din("kT", [2, 64, SL])
    v_d = p.din("v2", [2, SL, 128])
    i_d = p.din("i2", [2, SL])
    f_d = p.din("f2", [2, SL])
    bi_d = p.din("bi2", [2, 1])
    bf_d = p.din("bf2", [2, 1])
    lnw_d = p.din("lnw2", [2, 128, 128])
    lnb_d = p.din("lnb2", [2, 128, 128])
    hn_out = p.dout("hn", [2, SL, 128])

    t = _T()
    identb = make_identity(p)
    t.identf = p.sb("identfp", [128, 128], F32)
    p.op("vector", lambda E: E.tensor_copy(t.identf[:], identb[:]),
         reads=[identb], writes=[t.identf])
    iop = p.sb("iop", [128, 128], F32)
    p.op("gpsimd", lambda E: E.iota(iop[:], pattern=[[0, 128]], channel_multiplier=1,
                                    allow_small_or_imprecise_dtypes=True), writes=[iop])
    t.sel0 = p.sb("sel0", [128, 128], F32)
    p.op("vector", lambda E: E.tensor_scalar(t.sel0[:], iop[:], 0.0, None, ALU.is_equal),
         reads=[iop], writes=[t.sel0])
    iop2 = p.sb("iop2", [128, 128], F32)
    p.op("gpsimd", lambda E: E.iota(iop2[:], pattern=[[1, 128]], channel_multiplier=-1,
                                    allow_small_or_imprecise_dtypes=True), writes=[iop2])
    t.trimask = p.sb("trimask", [128, 128], F32)
    p.op("vector", lambda E: E.tensor_scalar(t.trimask[:], iop2[:], 0.0, None, ALU.is_ge),
         reads=[iop2], writes=[t.trimask])

    t.qT = p.sb("qT_s", [64, SEG]); t.kT = p.sb("kT_s", [64, SEG])
    t.qTb = p.sb("qTb", [64, SEG], BF16); t.kTb = p.sb("kTb", [64, SEG], BF16)
    t.v_sb = p.sb("v_sb", [128, SEG]); t.vb = p.sb("vb", [128, SEG], BF16)
    t.i_r = p.sb("i_r", [128, SEG]); t.f_r = p.sb("f_r", [128, SEG])
    t.fb = p.sb("fb", [128, SEG]); t.g1 = p.sb("g1", [128, SEG])
    t.g2 = p.sb("g2", [128, SEG]); t.flog = p.sb("flog", [128, SEG])
    t.Fc = p.sb("Fcg", [128, SEG]); t.bg = p.sb("bgg", [128, SEG])
    t.M = p.sb("Mg", [128, SEG]); t.mp = p.sb("mpg", [128, SEG])
    t.nm = p.sb("nmg", [128, SEG]); t.ones = p.sb("ones1", [128, SEG])
    t.mstate = p.sb("mstate", [128, 1])
    t.gcols = p.sb("gcols", [128, 4])
    t.W_sb = p.sb("W_sb", [128, 128]); t.Wm = p.sb("Wm", [128, 128])
    t.DSb = p.sb("DSbb", [128, 128], BF16)
    t.e_rep = p.sb("e_rep", [64, 128], BF16)
    t.qs = p.sb("qs", [64, 128], BF16)
    t.c_f32 = p.sb("c_f32", [64, 128]); t.c_bf = p.sb("c_bfb", [64, 128], BF16)
    t.wcol = p.sb("wcol", [128, 1]); t.dec = p.sb("dec", [64, 1])
    t.kw = p.sb("kwb", [128, 64], BF16)
    t.scr = p.sb("scrL2", [128, 130])
    t.hbuf = p.sb("hbuf", [128, 128])
    t.hn_st = p.sb("hn_st", [128, SEG])
    bicol = p.sb("bicol", [1, 1]); bfcol = p.sb("bfcol", [1, 1])
    lnw_rep = p.sb("lnw_rep", [128, 128]); lnb_rep = p.sb("lnb_rep", [128, 128])

    t.psG = p.ps("psG", [128, 4], F32)
    t.psMrep = p.ps("psMrep", [128, 128], F32)
    t.psSb = p.ps("psSb", [128, 128], F32)
    t.psH = p.ps("psH", [128, 128], F32)
    t.psKt = p.ps("psKt", [128, 64], F32)
    t.psC = p.ps("psC", [64, 128], F32)

    p.op("vector", lambda E: E.memset(t.M[:], 0.0), writes=[t.M])
    p.op("vector", lambda E: E.memset(t.ones[:], 1.0), writes=[t.ones])

    for bh in range(2):
        p.dma("sync", bicol[0:1, :], bi_d[bh:bh + 1, :])
        p.dma("sync", bfcol[0:1, :], bf_d[bh:bh + 1, :])
        p.dma("sync", lnw_rep[:], lnw_d[bh, :, :])
        p.dma("sync", lnb_rep[:], lnb_d[bh, :, :])
        emit_mlstm_bh(p, t, bh, qT_d, kT_d, v_d, i_d, f_d,
                      bicol[0:1, 0:1], bfcol[0:1, 0:1], lnw_rep[:], lnb_rep[:], hn_out)
    p.barrier_end()
    return p.realize()


def run_L2(d, l1):
    q = l1["q"].reshape(B, SL, 8, 64)
    k = l1["k"].reshape(B, SL, 8, 64)
    v = l1["v"].reshape(B, SL, 8, 128)
    ifr = l1["ifr"]
    lnw, lnb = d["L0_attn_ln_w"], d["L0_attn_ln_b"]
    bi, bf = d["L0_attn_bi"], d["L0_attn_bf"]
    nc = build_L2()
    in_maps = []
    for c in range(NCORES):
        pairs = [(2 * c) // 8, (2 * c) % 8], [(2 * c + 1) // 8, (2 * c + 1) % 8]
        qT = np.stack([np.ascontiguousarray(q[b_, :, h_, :].T) for b_, h_ in pairs])
        kT = np.stack([np.ascontiguousarray(k[b_, :, h_, :].T) for b_, h_ in pairs])
        v2 = np.stack([np.ascontiguousarray(v[b_, :, h_, :]) for b_, h_ in pairs])
        i2 = np.stack([np.ascontiguousarray(ifr[b_, :, h_]) for b_, h_ in pairs])
        f2 = np.stack([np.ascontiguousarray(ifr[b_, :, 8 + h_]) for b_, h_ in pairs])
        bi2 = np.array([[bi[h_]] for b_, h_ in pairs], np.float32)
        bf2 = np.array([[bf[h_]] for b_, h_ in pairs], np.float32)
        lnw2 = np.stack([np.broadcast_to(lnw[h_][None, :], (128, 128)).copy()
                         for b_, h_ in pairs])
        lnb2 = np.stack([np.broadcast_to(lnb[h_][None, :], (128, 128)).copy()
                         for b_, h_ in pairs])
        in_maps.append({"qT": qT, "kT": kT, "v2": v2, "i2": i2, "f2": f2,
                        "bi2": bi2, "bf2": bf2, "lnw2": lnw2, "lnb2": lnb2})
    res = run_bass_kernel_spmd(nc, in_maps, core_ids=list(range(NCORES)))
    hn = np.zeros((B, SL, 8, 128), np.float32)
    for c in range(NCORES):
        for j in range(2):
            bh = 2 * c + j
            hn[bh // 8, :, bh % 8, :] = res.results[c]["hn"][j]
    return {"hn": hn.reshape(B, SL, 1024)}


GELU_C = 0.7978845608028654


def emit_gelu_tanh(p, out_bf, ps_in, b1col, t, c_gelu):
    p.op("vector", lambda E: E.tensor_scalar(t.zs[:], ps_in, b1col, None, ALU.add),
         reads=[ps_in, b1col], writes=[t.zs])
    p.op("scalar", lambda E: E.activation(t.z2[:], t.zs[:], AF.Square),
         reads=[t.zs], writes=[t.z2])
    p.op("vector", lambda E: E.tensor_tensor(t.z3[:], t.z2[:], t.zs[:], ALU.mult),
         reads=[t.z2, t.zs], writes=[t.z3])
    p.op("vector", lambda E: E.scalar_tensor_tensor(t.u[:], t.z3[:], 0.044715, t.zs[:],
                                                    ALU.mult, ALU.add),
         reads=[t.z3, t.zs], writes=[t.u])
    p.op("scalar", lambda E: E.activation(t.u[:], t.u[:], AF.Tanh, scale=c_gelu),
         reads=[t.u, c_gelu], writes=[t.u])
    p.op("vector", lambda E: E.tensor_tensor(t.z3[:], t.zs[:], t.u[:], ALU.mult),
         reads=[t.zs, t.u], writes=[t.z3])
    p.op("vector", lambda E: E.tensor_tensor(t.z2[:], t.z3[:], t.zs[:], ALU.add),
         reads=[t.z3, t.zs], writes=[t.z2])
    p.op("vector", lambda E: E.tensor_scalar(out_bf, t.z2[:], 0.5, None, ALU.mult),
         reads=[t.z2], writes=[out_bf])


def emit_L3(p, io):
    x0 = io["x0"]
    hn_d = io["hn"]
    og_d = io["og"]
    wout_d = io["Wout"]
    u1_d = io["U1"]; v1_d = io["V1"]
    u2_d = io["U2"]; v2_d = io["V2"]
    win_d = io["W_in"]
    b1s_d = io["b1s"]
    b2r_d = io["b2r"]
    lnwr_d = io["lnwr"]
    lnbr_d = io["lnbr"]
    o_x2 = io["x2"]
    o_y2 = io["y2"]
    o_xin = io["xinT"]
    o_zg = io["zgT"]

    ident = make_identity(p)
    c_gelu = p.constcol(GELU_C)
    wout = load_weight_bf16(p, "wout", wout_d, D, D)
    u1 = load_weight_bf16(p, "u1", u1_d, D, 256)
    v1 = load_weight_bf16(p, "v1", v1_d, 256, 4096)
    u2 = load_weight_bf16(p, "u2", u2_d, 4096, 256)
    v2 = load_weight_bf16(p, "v2", v2_d, 256, D)
    win = load_weight_bf16(p, "win", win_d, D, 4096)
    b1s = p.sb("b1s_s", [128, 32])
    b2r = p.sb("b2r_s", [128, D])
    lnwr = p.sb("lnwr_s", [128, D]); lnbr = p.sb("lnbr_s", [128, D])
    p.dma("sync", b1s[:], b1s_d)
    p.dma("sync", b2r[:], b2r_d)
    p.dma("sync", lnwr[:], lnwr_d)
    p.dma("sync", lnbr[:], lnbr_d)

    t = _T()
    NB = 1
    bufs = {}
    for nm, shape, dt in [("xb", [128, D + 2], F32), ("hnb", [128, D], F32),
                          ("ogb", [128, D], F32),
                          ("x1b", [128, D], F32), ("y1b", [128, D], F32),
                          ("castb", [128, D], BF16), ("xTb", [128, D], BF16),
                          ("x2b", [128, D], F32), ("y2b", [128, D], F32),
                          ("pbb", [128, 256], BF16), ("ptb", [128, 256], BF16),
                          ("htb", [128, 32 * 128], BF16),
                          ("qbb", [128, 256], BF16), ("qtb2", [128, 256], BF16),
                          ("xzst", [128, 16 * 128], F32), ("zgst", [128, 16 * 128], F32)]:
        bufs[nm] = [p.sb(f"{nm}{i}", shape, dt) for i in range(NB)]
    t.zs = bufs["ogb"][0][:, 0:512]
    t.z2 = bufs["ogb"][0][:, 512:1024]
    t.z3 = bufs["hnb"][0][:, 0:512]
    t.u = bufs["hnb"][0][:, 512:1024]
    psT = p.ps("psT", [128, 128], BF16)
    ps_a = p.ps("ps_a", [128, 512]); ps_b = p.ps("ps_b", [128, 512])
    psF = p.ps("psF", [128, 512])
    psQ = p.ps("psQ", [128, 256])
    psE = p.ps("psE", [128, 128])

    ntiles = NTOK // 128
    for ti in range(ntiles):
        i = ti % NB
        r0 = ti * 128
        xb = bufs["xb"][i]; hnb = bufs["hnb"][i]; ogb = bufs["ogb"][i]
        gb = hnb; x1b = bufs["x1b"][i]; y1b = bufs["y1b"][i]
        cast = bufs["castb"][i]; xT = bufs["xTb"][i]
        x2b = bufs["x2b"][i]; y2b = bufs["y2b"][i]; y2nb = y1b
        xnb = x1b; scr = xb
        pb = bufs["pbb"][i]; pt = bufs["ptb"][i]; ht = bufs["htb"][i]
        qb = bufs["qbb"][i]; qt = bufs["qtb2"][i]; wbuf = ogb
        xzst = bufs["xzst"][i]; zgst = bufs["zgst"][i]

        p.dma("sync", xb[:, 0:D], x0[r0:r0 + 128, :])
        p.dma("sync", hnb[:], hn_d[r0:r0 + 128, :])
        p.dma("sync", ogb[:], og_d[r0:r0 + 128, :])
        p.op("vector", lambda E, hnb=hnb, ogb=ogb, gb=gb:
             E.tensor_tensor(gb[:], hnb[:], ogb[:], ALU.mult),
             reads=[hnb, ogb], writes=[gb])
        emit_cast_transpose(p, xT, gb[:], cast, ident, psT)
        for h in range(2):
            bank = ps_a if h == 0 else ps_b
            emit_gemm_tokmajor(p, bank, xT, wout, D, D, (h * 512, (h + 1) * 512))
            p.op("vector", lambda E, bank=bank, h=h, x1b=x1b, xb=xb:
                 E.scalar_tensor_tensor(x1b[:, h * 512:(h + 1) * 512], bank[:], 1.0,
                                        xb[:, h * 512:(h + 1) * 512], ALU.mult, ALU.add),
                 reads=[bank, xb], writes=[x1b])
        emit_ln(p, y1b[:], x1b[:], scr, 1e-5)
        emit_cast_transpose(p, xT, y1b[:], cast, ident, psT)
        emit_gemm_tokmajor(p, psQ, xT, u1, D, 256, (0, 256))
        p.op("vector", lambda E, pb=pb: E.tensor_copy(pb[:], psQ[:]),
             reads=[psQ], writes=[pb])
        for kk in range(2):
            p.op("tensor", lambda E, kk=kk, pb=pb: E.transpose(
                psT[:], pb[:, kk * 128:(kk + 1) * 128], ident[:]),
                 reads=[pb, ident], writes=[psT])
            p.op("scalar", lambda E, kk=kk, pt=pt: E.copy(pt[:, kk * 128:(kk + 1) * 128], psT[:]),
                 reads=[psT], writes=[pt])
        for f4 in range(8):
            for j in range(4):
                fch = f4 * 4 + j
                for kk in range(2):
                    p.op("tensor", lambda E, fch=fch, kk=kk, j=j, pt=pt: E.matmul(
                        psF[:, j * 128:(j + 1) * 128],
                        v1[:, kk * 4096 + fch * 128:kk * 4096 + (fch + 1) * 128],
                        pt[:, kk * 128:(kk + 1) * 128],
                        start=(kk == 0), stop=(kk == 1)),
                         reads=[v1, pt], writes=[psF])
            for j in range(4):
                fch = f4 * 4 + j
                p.op("vector", lambda E, fch=fch, j=j: E.tensor_scalar(
                    t.zs[:, j * 128:(j + 1) * 128], psF[:, j * 128:(j + 1) * 128],
                    b1s[:, fch:fch + 1], None, ALU.add),
                     reads=[psF, b1s], writes=[t.zs])
            out_bf = ht[:, f4 * 512:(f4 + 1) * 512]
            p.op("scalar", lambda E: E.activation(t.z2[:], t.zs[:], AF.Square),
                 reads=[t.zs], writes=[t.z2])
            p.op("vector", lambda E: E.tensor_tensor(t.z3[:], t.z2[:], t.zs[:], ALU.mult),
                 reads=[t.z2, t.zs], writes=[t.z3])
            p.op("vector", lambda E: E.scalar_tensor_tensor(
                t.u[:], t.z3[:], 0.044715, t.zs[:], ALU.mult, ALU.add),
                 reads=[t.z3, t.zs], writes=[t.u])
            p.op("scalar", lambda E: E.activation(t.u[:], t.u[:], AF.Tanh, scale=c_gelu),
                 reads=[t.u, c_gelu], writes=[t.u])
            p.op("vector", lambda E: E.tensor_tensor(t.z3[:], t.zs[:], t.u[:], ALU.mult),
                 reads=[t.zs, t.u], writes=[t.z3])
            p.op("vector", lambda E: E.tensor_tensor(t.z2[:], t.z3[:], t.zs[:], ALU.add),
                 reads=[t.z3, t.zs], writes=[t.z2])
            p.op("vector", lambda E, out_bf=out_bf: E.tensor_scalar(
                out_bf, t.z2[:], 0.5, None, ALU.mult),
                 reads=[t.z2], writes=[ht])
        for kk in range(32):
            p.op("tensor", lambda E, kk=kk, ht=ht: E.matmul(
                psQ[:], ht[:, kk * 128:(kk + 1) * 128], u2[:, kk * 256:(kk + 1) * 256],
                start=(kk == 0), stop=(kk == 31)),
                 reads=[ht, u2], writes=[psQ])
        p.op("vector", lambda E, qb=qb: E.tensor_copy(qb[:], psQ[:]),
             reads=[psQ], writes=[qb])
        for kk in range(2):
            p.op("tensor", lambda E, kk=kk, qb=qb: E.transpose(
                psT[:], qb[:, kk * 128:(kk + 1) * 128], ident[:]),
                 reads=[qb, ident], writes=[psT])
            p.op("scalar", lambda E, kk=kk, qt=qt: E.copy(qt[:, kk * 128:(kk + 1) * 128], psT[:]),
                 reads=[psT], writes=[qt])
        p.op("vector", lambda E, wbuf=wbuf, x1b=x1b: E.tensor_tensor(
            wbuf[:], x1b[:], b2r[:], ALU.add),
             reads=[x1b, b2r], writes=[wbuf])
        for h in range(2):
            bank = ps_a if h == 0 else ps_b
            for kk in range(2):
                p.op("tensor", lambda E, kk=kk, h=h, bank=bank, qt=qt: E.matmul(
                    bank[:], qt[:, kk * 128:(kk + 1) * 128],
                    v2[:, kk * D + h * 512:kk * D + (h + 1) * 512],
                    start=(kk == 0), stop=(kk == 1)),
                     reads=[qt, v2], writes=[bank])
            p.op("vector", lambda E, bank=bank, h=h, x2b=x2b, wbuf=wbuf:
                 E.scalar_tensor_tensor(x2b[:, h * 512:(h + 1) * 512], bank[:], 1.0,
                                        wbuf[:, h * 512:(h + 1) * 512], ALU.mult, ALU.add),
                 reads=[bank, wbuf], writes=[x2b])
        emit_ln(p, y2b[:], x2b[:], scr, 1e-5)
        emit_ln(p, y2nb[:], y2b[:], scr, 1e-5)
        p.op("vector", lambda E, y2nb=y2nb, xnb=xnb: E.tensor_tensor(
            xnb[:], y2nb[:], lnwr[:], ALU.mult),
             reads=[y2nb, lnwr], writes=[xnb])
        p.op("vector", lambda E, xnb=xnb: E.tensor_tensor(xnb[:], xnb[:], lnbr[:], ALU.add),
             reads=[xnb, lnbr], writes=[xnb])
        emit_cast_transpose(p, xT, xnb[:], cast, ident, psT)
        for half, stage in ((0, xzst), (1, zgst)):
            for ech in range(16):
                e0 = half * 2048 + ech * 128
                for kk in range(8):
                    p.op("tensor", lambda E, e0=e0, kk=kk, xT=xT: E.matmul(
                        psE[:], win[:, kk * 4096 + e0:kk * 4096 + e0 + 128],
                        xT[:, kk * 128:(kk + 1) * 128],
                        start=(kk == 0), stop=(kk == 7)),
                         reads=[win, xT], writes=[psE])
                p.op("scalar", lambda E, ech=ech, stage=stage: E.copy(
                    stage[:, ech * 128:(ech + 1) * 128], psE[:]),
                     reads=[psE], writes=[stage])
        p.dma("gpsimd", o_x2[r0:r0 + 128, :], x2b[:])
        p.dma("gpsimd", o_y2[r0:r0 + 128, :], y2b[:])
        p.dma("gpsimd", o_xin[:, r0:r0 + 128].rearrange("(ech q) t -> q ech t", q=128),
              xzst[:].rearrange("q (ech t) -> q ech t", t=128))
        p.dma("gpsimd", o_zg[:, r0:r0 + 128].rearrange("(ech q) t -> q ech t", q=128),
              zgst[:].rearrange("q (ech t) -> q ech t", t=128))


def build_L3():
    p = Prog("build_L3")
    io = {
        "x0": p.din("x0", [NTOK, D]),
        "hn": p.din("hn", [NTOK, D]),
        "og": p.din("og", [NTOK, D]),
        "Wout": p.din("Wout", [D, D]),
        "U1": p.din("U1", [D, 256]),
        "V1": p.din("V1", [256, 4096]),
        "U2": p.din("U2", [4096, 256]),
        "V2": p.din("V2", [256, D]),
        "W_in": p.din("W_in", [D, 4096]),
        "b1s": p.din("b1s", [128, 32]),
        "b2r": p.din("b2r", [128, D]),
        "lnwr": p.din("lnwr", [128, D]),
        "lnbr": p.din("lnbr", [128, D]),
        "x2": p.dout("x2", [NTOK, D]),
        "y2": p.dout("y2", [NTOK, D]),
        "xinT": p.dout("xinT", [2048, NTOK]),
        "zgT": p.dout("zgT", [2048, NTOK]),
    }
    emit_L3(p, io)
    p.barrier_end()
    return p.realize()


def run_L3(d, l1, l2):
    x0 = d["L0_attn_x"]
    nc = build_L3()
    b1s = np.ascontiguousarray(d["L0_ffn_b1"].reshape(32, 128).T)
    b2r = np.ascontiguousarray(np.broadcast_to(d["L0_ffn_b2"][None, :], (128, D)))
    lnwr = np.ascontiguousarray(np.broadcast_to(d["L1_attn_ln_w"][None, :], (128, D)))
    lnbr = np.ascontiguousarray(np.broadcast_to(d["L1_attn_ln_b"][None, :], (128, D)))
    in_maps = []
    for c in range(NCORES):
        sl = slice(c * SEG, (c + 1) * SEG)
        in_maps.append({
            "x0": np.ascontiguousarray(x0[:, sl, :].reshape(NTOK, D)),
            "hn": np.ascontiguousarray(l2["hn"][:, sl, :].reshape(NTOK, D)),
            "og": np.ascontiguousarray(l1["og"][:, sl, :].reshape(NTOK, D)),
            "Wout": d["L0_attn_Wout"], "U1": d["L0_ffn_U1"], "V1": d["L0_ffn_V1"],
            "U2": d["L0_ffn_U2"], "V2": d["L0_ffn_V2"], "W_in": d["L1_attn_W_in"],
            "b1s": b1s, "b2r": b2r, "lnwr": lnwr, "lnbr": lnbr,
        })
    res = run_bass_kernel_spmd(nc, in_maps, core_ids=list(range(NCORES)))
    outs = {}
    for key, w in [("x2", D), ("y2", D)]:
        full = np.stack([res.results[c][key].reshape(B, SEG, w) for c in range(NCORES)],
                        axis=1)
        outs[key] = full.reshape(B, NCORES * SEG, w)
    for key in ("xinT", "zgT"):
        full = np.stack([res.results[c][key].reshape(2048, B, SEG)
                         for c in range(NCORES)], axis=2)
        outs[key] = full.reshape(2048, B, NCORES * SEG)
    return outs


def emit_softplus(p, out_ap, z_ap, t, c_m1, c_1, reads_extra=()):
    p.op("vector", lambda E: E.tensor_scalar(t.sp1[:], z_ap, -1.0, None, ALU.mult),
         reads=[z_ap] + list(reads_extra), writes=[t.sp1])
    p.op("vector", lambda E: E.tensor_tensor(t.sp2[:], z_ap, t.sp1[:], ALU.max),
         reads=[z_ap, t.sp1], writes=[t.sp2])
    p.op("scalar", lambda E: E.activation(t.sp1[:], t.sp2[:], AF.Exp, scale=c_m1),
         reads=[t.sp2, c_m1], writes=[t.sp1])
    p.op("scalar", lambda E: E.activation(t.sp2[:], t.sp1[:], AF.Ln, bias=c_1),
         reads=[t.sp1, c_1], writes=[t.sp2])
    p.op("vector", lambda E: E.tensor_scalar(t.sp1[:], z_ap, 0.0, None, ALU.max),
         reads=[z_ap], writes=[t.sp1])
    p.op("vector", lambda E: E.tensor_tensor(out_ap, t.sp1[:], t.sp2[:], ALU.add),
         reads=[t.sp1, t.sp2], writes=[out_ap])


def emit_L4(p, io):
    xin_d = io["xinh"]
    cw_d = io["cw4"]
    cb_d = io["cb4"]
    wxp_d = io["Wxp"]
    wdt_d = io["Wdt"]
    bdt_d = io["bdt"]
    o_dt = io["dtT"]
    o_B = io["Bf"]
    o_C = io["Cf"]
    o_xc = io["xcT"]

    t = _T()
    c_m1 = p.constcol(-1.0); c_1 = p.constcol(1.0)
    wxp = load_weight_bf16(p, "wxp", wxp_d, 2048, 96)
    wdt = p.sb("wdt", [64, 2048], BF16)
    p.op("gpsimd", lambda E: E.dma_start(out=wdt[:], in_=wdt_d), reads=[wdt_d],
         writes=[wdt], dma=True)
    cw = p.sb("cw_s", [128, 16 * 4]); cb = p.sb("cb_s", [128, 16])
    bdt = p.sb("bdt_s", [128, 16])
    p.dma("sync", cw[:].rearrange("p (e k) -> p e k", k=4), cw_d)
    p.dma("sync", cb[:], cb_d)
    p.dma("sync", bdt[:], bdt_d)

    xinb = p.sb("xinb", [128, HS])
    acc = p.sb("accc", [128, SEG])
    xcf = p.sb("xcf", [128, SEG])
    xcb = p.sb("xcbb", [128, 16 * SEG], BF16)
    dtrb = p.sb("dtrb", [64, SEG], BF16)
    bst = p.sb("bst", [16, SEG]); cst = p.sb("cst", [16, SEG])
    t.sp1 = p.sb("sp1", [128, 512]); t.sp2 = p.sb("sp2", [128, 512])
    dtt = p.sb("dtt", [128, 512])
    psDtr = p.ps("psDtr", [64, 512])
    psB = p.ps("psB", [16, 512]); psC2 = p.ps("psC2", [16, 512])
    psDt = p.ps("psDt", [128, 512])

    halo_sb = io.get("halo_sb")
    for b_ in range(B):
        tok0 = b_ * SEG
        for ech in range(16):
            if halo_sb is None:
                p.dma("sync", xinb[:], xin_d[ech * 128:(ech + 1) * 128, b_, :])
            else:
                p.dma("sync", xinb[:, 3:3 + SEG],
                      xin_d[ech * 128:(ech + 1) * 128, tok0:tok0 + SEG])
                p.op("vector", lambda E, ech=ech, b_=b_: E.tensor_copy(
                    xinb[:, 0:3], halo_sb[:, ech * 6 + b_ * 3:ech * 6 + b_ * 3 + 3]),
                     reads=[halo_sb], writes=[xinb])
            for k in range(4):
                wcol = lambda k=k, ech=ech: cw[:, ech * 4 + k:ech * 4 + k + 1]
                if k == 0:
                    p.op("vector", lambda E, k=k, wcol=wcol: E.tensor_scalar(
                        acc[:], xinb[:, k:k + SEG], wcol(), None, ALU.mult),
                         reads=[xinb, cw], writes=[acc])
                else:
                    p.op("vector", lambda E, k=k, wcol=wcol: E.scalar_tensor_tensor(
                        acc[:], xinb[:, k:k + SEG], wcol(), acc[:],
                        ALU.mult, ALU.add),
                         reads=[xinb, cw, acc], writes=[acc])
            p.op("scalar", lambda E, ech=ech: E.activation(xcf[:], acc[:], AF.Silu,
                                                           bias=cb[:, ech:ech + 1]),
                 reads=[acc, cb], writes=[xcf])
            p.dma("scalar", o_xc[ech * 128:(ech + 1) * 128, tok0:tok0 + SEG], xcf[:])
            p.op("vector", lambda E, ech=ech: E.tensor_copy(
                xcb[:, ech * SEG:(ech + 1) * SEG], xcf[:]),
                 reads=[xcf], writes=[xcb])
        for tt in range(SEG // 512):
            ts_ = slice(tt * 512, (tt + 1) * 512)
            for ech in range(16):
                st_ = (ech == 0); sp_ = (ech == 15)
                p.op("tensor", lambda E, ech=ech, ts_=ts_, st_=st_, sp_=sp_: E.matmul(
                    psDtr[:], wxp[:, ech * 96:ech * 96 + 64],
                    xcb[:, ech * SEG:(ech + 1) * SEG][:, ts_], start=st_, stop=sp_),
                     reads=[wxp, xcb], writes=[psDtr])
            for ech in range(16):
                st_ = (ech == 0); sp_ = (ech == 15)
                p.op("tensor", lambda E, ech=ech, ts_=ts_, st_=st_, sp_=sp_: E.matmul(
                    psB[:], wxp[:, ech * 96 + 64:ech * 96 + 80],
                    xcb[:, ech * SEG:(ech + 1) * SEG][:, ts_], start=st_, stop=sp_),
                     reads=[wxp, xcb], writes=[psB])
            for ech in range(16):
                st_ = (ech == 0); sp_ = (ech == 15)
                p.op("tensor", lambda E, ech=ech, ts_=ts_, st_=st_, sp_=sp_: E.matmul(
                    psC2[:], wxp[:, ech * 96 + 80:ech * 96 + 96],
                    xcb[:, ech * SEG:(ech + 1) * SEG][:, ts_], start=st_, stop=sp_),
                     reads=[wxp, xcb], writes=[psC2])
            p.op("vector", lambda E, ts_=ts_: E.tensor_copy(dtrb[:, ts_], psDtr[:]),
                 reads=[psDtr], writes=[dtrb])
            p.op("vector", lambda E, ts_=ts_: E.tensor_copy(bst[:, ts_], psB[:]),
                 reads=[psB], writes=[bst])
            p.op("vector", lambda E, ts_=ts_: E.tensor_copy(cst[:, ts_], psC2[:]),
                 reads=[psC2], writes=[cst])
        p.dma("scalar", o_B[:, tok0:tok0 + SEG], bst[:])
        p.dma("scalar", o_C[:, tok0:tok0 + SEG], cst[:])
        for m in range(16):
            for tt in range(SEG // 512):
                ts_ = slice(tt * 512, (tt + 1) * 512)
                p.op("tensor", lambda E, m=m, ts_=ts_: E.matmul(
                    psDt[:], wdt[:, m * 128:(m + 1) * 128], dtrb[:, ts_],
                    start=True, stop=True),
                     reads=[wdt, dtrb], writes=[psDt])
                p.op("vector", lambda E, m=m: E.tensor_scalar(
                    dtt[:], psDt[:], bdt[:, m:m + 1], None, ALU.add),
                     reads=[psDt, bdt], writes=[dtt])
                emit_softplus(p, dtt[:], dtt[:], t, c_m1, c_1)
                p.dma("scalar", o_dt[m * 128:(m + 1) * 128, tok0 + tt * 512:
                                     tok0 + (tt + 1) * 512], dtt[:])


def build_L4():
    p = Prog("build_L4")
    io = {
        "xinh": p.din("xinh", [2048, B, HS]),
        "cw4": p.din("cw4", [128, 16, 4]),
        "cb4": p.din("cb4", [128, 16]),
        "Wxp": p.din("Wxp", [2048, 96]),
        "Wdt": p.din("Wdt", [64, 2048]),
        "bdt": p.din("bdt", [128, 16]),
        "dtT": p.dout("dtT", [2048, NTOK]),
        "Bf": p.dout("Bf", [16, NTOK]),
        "Cf": p.dout("Cf", [16, NTOK]),
        "xcT": p.dout("xcT", [2048, NTOK]),
    }
    emit_L4(p, io)
    p.barrier_end()
    return p.realize()


def run_L4(d, l3):
    xinT = l3["xinT"]
    nc = build_L4()
    cw4 = np.ascontiguousarray(
        np.transpose(d["L1_attn_conv_w"].reshape(16, 128, 1, 4)[:, :, 0, :], (1, 0, 2)))
    cb4 = np.ascontiguousarray(d["L1_attn_conv_b"].reshape(16, 128).T)
    bdt = np.ascontiguousarray(d["L1_attn_b_dt"].reshape(16, 128).T)
    in_maps = []
    for c in range(NCORES):
        s0 = c * SEG
        halo = np.zeros((2048, B, 3), np.float32) if c == 0 else xinT[:, :, s0 - 3:s0]
        xinh = np.ascontiguousarray(np.concatenate([halo, xinT[:, :, s0:s0 + SEG]], 2))
        in_maps.append({"xinh": xinh, "cw4": cw4, "cb4": cb4,
                        "Wxp": d["L1_attn_W_xproj"], "Wdt": d["L1_attn_W_dt"],
                        "bdt": bdt})
    res = run_bass_kernel_spmd(nc, in_maps, core_ids=list(range(NCORES)))
    outs = {}
    for key, rows in [("dtT", 2048), ("Bf", 16), ("Cf", 16), ("xcT", 2048)]:
        full = np.stack([res.results[c][key].reshape(rows, B, SEG)
                         for c in range(NCORES)], axis=2)
        outs[key] = full.reshape(rows, B, NCORES * SEG)
    return outs


def emit_L4c(p, io):
    TC = 512
    dt_d = io["dts"]
    xc_d = io["xcs"]
    B_d = io["Bs"]
    C_d = io["Cs"]
    alog_d = io["Alog"]
    dpar_d = io["Dpar"]
    o_y = io["yssm"]

    t = _T()
    iop = p.sb("iop", [128, 128], F32)
    p.op("gpsimd", lambda E: E.iota(iop[:], pattern=[[0, 128]], channel_multiplier=1,
                                    allow_small_or_imprecise_dtypes=True), writes=[iop])
    sel = p.sb("seln", [128, 16 * 128], F32)
    for n in range(16):
        p.op("vector", lambda E, n=n: E.tensor_scalar(
            sel[:, n * 128:(n + 1) * 128], iop[:], float(n), None, ALU.is_equal),
             reads=[iop], writes=[sel])
    alog = p.sb("alog_s", [128, 2 * 16])
    p.dma("sync", alog[:].rearrange("p (d n) -> p d n", n=16),
          alog_d.rearrange("(d p) n -> p d n", p=128))
    A_sb = p.sb("A_sb", [128, 2 * 16])
    p.op("scalar", lambda E: E.activation(A_sb[:], alog[:], AF.Exp),
         reads=[alog], writes=[A_sb])
    p.op("vector", lambda E: E.tensor_scalar(A_sb[:], A_sb[:], -1.0, None, ALU.mult),
         reads=[A_sb], writes=[A_sb])
    dpar = p.sb("dpar_s", [128, 2])
    p.op("sync", lambda E: E.dma_start(out=dpar[:],
                                       in_=dpar_d.rearrange("(d p) o -> p (d o)", p=128),
                                       allow_slow_non_contiguous=True),
         reads=[dpar_d], writes=[dpar], dma=True)

    dtb = p.sb("dtb", [128, TC]); xcb2 = p.sb("xcb2", [128, TC])
    udt = p.sb("udt", [128, TC])
    dAb = p.sb("dAb", [128, TC]); bmb = p.sb("bmb", [128, TC])
    hb = p.sb("hb", [128, TC]); ytmp = p.sb("ytmp", [128, TC])
    yacc = p.sb("yacc", [128, TC])
    Bsb = p.sb("Bsb", [128, TC]); Csb = p.sb("Csb", [128, TC])
    hstate = p.sb("hstate", [128, 32])
    psBr = p.ps("psBr", [128, TC]); psCr = p.ps("psCr", [128, TC])
    p.op("vector", lambda E: E.memset(Bsb[:], 0.0), writes=[Bsb])
    p.op("vector", lambda E: E.memset(Csb[:], 0.0), writes=[Csb])

    for b_ in range(B):
        p.op("vector", lambda E: E.memset(hstate[:], 0.0), writes=[hstate])
        for tc in range(SL // TC):
            t0 = tc * TC
            p.dma("sync", Bsb[0:16, :], B_d[:, b_, t0:t0 + TC])
            p.dma("sync", Csb[0:16, :], C_d[:, b_, t0:t0 + TC])
            for dch in range(2):
                p.dma("sync", dtb[:], dt_d[dch * 128:(dch + 1) * 128, b_, t0:t0 + TC])
                p.dma("sync", xcb2[:], xc_d[dch * 128:(dch + 1) * 128, b_, t0:t0 + TC])
                p.op("vector", lambda E: E.tensor_tensor(udt[:], dtb[:], xcb2[:], ALU.mult),
                     reads=[dtb, xcb2], writes=[udt])
                for n in range(16):
                    scol = dch * 16 + n
                    p.op("scalar", lambda E, dch=dch, n=n: E.activation(
                        dAb[:], dtb[:], AF.Exp, scale=A_sb[:, dch * 16 + n:dch * 16 + n + 1]),
                         reads=[dtb, A_sb], writes=[dAb])
                    p.op("tensor", lambda E, n=n: E.matmul(
                        psBr[:], sel[:, n * 128:(n + 1) * 128], Bsb[:],
                        start=True, stop=True),
                         reads=[sel, Bsb], writes=[psBr])
                    p.op("vector", lambda E: E.tensor_tensor(bmb[:], udt[:], psBr[:],
                                                             ALU.mult),
                         reads=[udt, psBr], writes=[bmb])
                    p.op("vector", lambda E, scol=scol: E.tensor_tensor_scan(
                        hb[:], dAb[:], bmb[:], hstate[:, scol:scol + 1],
                        ALU.mult, ALU.add),
                         reads=[dAb, bmb, hstate], writes=[hb])
                    p.op("vector", lambda E, scol=scol: E.tensor_copy(
                        hstate[:, scol:scol + 1], hb[:, TC - 1:TC]),
                         reads=[hb], writes=[hstate])
                    p.op("tensor", lambda E, n=n: E.matmul(
                        psCr[:], sel[:, n * 128:(n + 1) * 128], Csb[:],
                        start=True, stop=True),
                         reads=[sel, Csb], writes=[psCr])
                    if n == 0:
                        p.op("vector", lambda E: E.tensor_tensor(yacc[:], hb[:], psCr[:],
                                                                 ALU.mult),
                             reads=[hb, psCr], writes=[yacc])
                    else:
                        p.op("vector", lambda E: E.tensor_tensor(ytmp[:], hb[:], psCr[:],
                                                                 ALU.mult),
                             reads=[hb, psCr], writes=[ytmp])
                        p.op("vector", lambda E: E.tensor_tensor(yacc[:], yacc[:], ytmp[:],
                                                                 ALU.add),
                             reads=[yacc, ytmp], writes=[yacc])
                p.op("vector", lambda E, dch=dch: E.scalar_tensor_tensor(
                    yacc[:], xcb2[:], dpar[:, dch:dch + 1], yacc[:], ALU.mult, ALU.add),
                     reads=[xcb2, dpar, yacc], writes=[yacc])
                p.dma("gpsimd", o_y[dch * 128:(dch + 1) * 128, b_, t0:t0 + TC], yacc[:])


def build_L4c():
    p = Prog("build_L4c")
    io = {
        "dts": p.din("dts", [256, B, SL]),
        "xcs": p.din("xcs", [256, B, SL]),
        "Bs": p.din("Bs", [16, B, SL]),
        "Cs": p.din("Cs", [16, B, SL]),
        "Alog": p.din("Alog", [256, 16]),
        "Dpar": p.din("Dpar", [256, 1]),
        "yssm": p.dout("yssm", [256, B, SL]),
    }
    emit_L4c(p, io)
    p.barrier_end()
    return p.realize()


def run_L4c(d, l4):
    nc = build_L4c()
    in_maps = []
    for c in range(NCORES):
        ch = slice(c * 256, (c + 1) * 256)
        in_maps.append({
            "dts": np.ascontiguousarray(l4["dtT"][ch]),
            "xcs": np.ascontiguousarray(l4["xcT"][ch]),
            "Bs": l4["Bf"], "Cs": l4["Cf"],
            "Alog": np.ascontiguousarray(d["L1_attn_A_log"][ch]),
            "Dpar": np.ascontiguousarray(d["L1_attn_D_param"][ch][:, None]),
        })
    res = run_bass_kernel_spmd(nc, in_maps, core_ids=list(range(NCORES)))
    full = np.concatenate([res.results[c]["yssm"] for c in range(NCORES)], axis=0)
    return {"yssm": full}


def emit_L5(p, io):
    x2_d = io["x2"]
    y2_d = io["y2"]
    ysm_d = io["ysm"]
    zg_d = io["zgT"]
    wo1_d = io["Wo1"]
    wr_d = io["Wr"]
    brr_d = io["brr"]
    w1_d = io["W1m"]
    b1s_d = io["b1s3"]
    w2_d = io["W2m"]
    b2r_d = io["b2r3"]
    lnw_d = io["lnw3r"]
    lnb_d = io["lnb3r"]
    o_x4 = io["x4"]

    w1sc = io.get("w1sc")
    if w1sc is None:
        w1sc = p.dtmp("w1sc", [8, 8, 128, 4096], BF16)
        w2sc = p.dtmp("w2sc", [8, 128, 32 * 1024], BF16)
        prestaged = False
    else:
        w2sc = io["w2sc"]
        prestaged = True
    x3sc = p.dtmp("x3sc", [NTOK, D])

    ident = make_identity(p)
    t = _T()
    arena = p.sb("arena", [128, 32768], BF16)
    wo1 = arena[:, 0:16 * D]
    w2s = arena[:, 0:32 * D]
    b1s = p.sb("b1s_s", [128, 8 * 32]); brr = p.sb("brr_s", [128, 8])
    lnwr = p.sb("lnwr_s", [128, D]); lnbr = p.sb("lnbr_s", [128, D])
    p.dma("sync", b1s[:], b1s_d)
    p.dma("sync", brr[:], brr_d)
    p.dma("sync", lnwr[:], lnw_d)
    p.dma("sync", lnbr[:], lnb_d)

    psT = p.ps("psT", [128, 128], BF16)
    ps1 = p.ps("ps1", [128, 512]); ps2 = p.ps("ps2", [128, 512])
    ps3 = p.ps("ps3", [128, 512]); ps4 = p.ps("ps4", [128, 512])
    psR = p.ps("psR", [128, 8])

    wrb = p.sb("wrb", [8, D], BF16)
    p.op("gpsimd", lambda E: E.dma_start(out=wrb[:], in_=wr_d), reads=[wr_d],
         writes=[wrb], dma=True)
    wrT = p.sb("wrT", [128, 8 * 8], BF16)
    for kk in range(8):
        p.op("tensor", lambda E, kk=kk: E.transpose(psT[:, 0:8],
                                                    wrb[:, kk * 128:(kk + 1) * 128],
                                                    ident[0:8, 0:8]),
             reads=[wrb, ident], writes=[psT])
        p.op("scalar", lambda E, kk=kk: E.copy(wrT[:, kk * 8:(kk + 1) * 8], psT[:, 0:8]),
             reads=[psT], writes=[wrT])

    hT = p.sb("hT5", [128, 32 * 512], BF16)
    if not prestaged:
        for e in range(8):
            p.op("gpsimd", lambda E, e=e: E.dma_start(
                out=arena[:].rearrange("q (g m) -> q g m", m=4096),
                in_=w1_d[e].rearrange("g q m -> q g m")),
                 reads=[w1_d], writes=[arena], dma=True)
            p.op("sync", lambda E, e=e: E.dma_start(
                out=w1sc[e].rearrange("g q m -> q g m"),
                in_=arena[:].rearrange("q (g m) -> q g m", m=4096)),
                 reads=[arena], writes=[w1sc], dma=True)
            for hh in range(4):
                p.op("gpsimd", lambda E, e=e, hh=hh: E.dma_start(
                    out=hT[:, 0:8192], in_=w2_d[e][:, hh * 8192:(hh + 1) * 8192]),
                     reads=[w2_d], writes=[hT], dma=True)
                p.op("sync", lambda E, e=e, hh=hh: E.dma_start(
                    out=w2sc[e][:, hh * 8192:(hh + 1) * 8192], in_=hT[:, 0:8192]),
                     reads=[hT], writes=[w2sc], dma=True)

    TB = 512
    x2t = p.sb("x2t", [128, D]); y2t = p.sb("y2t", [128, D])
    ygf = p.sb("ygf", [128, 16 * 128]); zgf = p.sb("zgf", [128, 16 * 128])
    ygb = p.sb("ygb", [128, 16 * 128], BF16)
    x3t = [p.sb(f"x3t{i}", [128, D]) for i in range(2)]
    acct = [p.sb(f"acct{i}", [128, D]) for i in range(4)]
    cwt = [p.sb(f"cwt{i}", [128, 8]) for i in range(4)]
    y3b = zgf[:, 1024:2048]
    y3c = ygb[:, 0:1024]
    y3T = p.sb("y3T", [128, 8 * TB], BF16)
    scr = ygf[:, 1022:2048]
    lgt = p.sb("lgt", [128, 8]); rsc = p.sb("rsc", [128, 8])
    w1big = [p.sb(f"w1b{i}", [128, 4096], BF16) for i in range(3)]
    b2s = [zgf[:, 0:1024]] * 2
    zb = y2t; znb = x2t

    ntb = NTOK // TB
    wo1_src = wo1_d.rearrange("(kc q) n -> q kc n", q=128)
    for bb in range(ntb):
        p.op("gpsimd", lambda E: E.dma_start(
            out=arena[:, 0:16 * D].rearrange("q (kc n) -> q kc n", kc=16),
            in_=wo1_src),
             reads=[wo1_d], writes=[arena], dma=True)
        for ti in range(4):
            r0 = bb * TB + ti * 128
            p.dma("sync", x2t[:], x2_d[r0:r0 + 128, :])
            p.dma("sync", y2t[:], y2_d[r0:r0 + 128, :])
            p.dma("sync", ygf[:].rearrange("q (e f) -> q e f", f=128),
                  ysm_d[:, r0:r0 + 128].rearrange("(e q) f -> q e f", q=128))
            p.dma("sync", zgf[:].rearrange("q (e f) -> q e f", f=128),
                  zg_d[:, r0:r0 + 128].rearrange("(e q) f -> q e f", q=128))
            p.op("scalar", lambda E: E.activation(zgf[:], zgf[:], AF.Silu),
                 reads=[zgf], writes=[zgf])
            p.op("vector", lambda E: E.tensor_tensor(ygf[:], ygf[:], zgf[:], ALU.mult),
                 reads=[ygf, zgf], writes=[ygf])
            p.op("vector", lambda E: E.tensor_copy(ygb[:], ygf[:]), reads=[ygf], writes=[ygb])
            p.op("vector", lambda E: E.tensor_tensor(x2t[:], x2t[:], y2t[:], ALU.add),
                 reads=[x2t, y2t], writes=[x2t])
            for h in range(2):
                bank = ps1 if h == 0 else ps2
                for kk in range(16):
                    p.op("tensor", lambda E, kk=kk, h=h, bank=bank: E.matmul(
                        bank[:], ygb[:, kk * 128:(kk + 1) * 128],
                        wo1[:, kk * D + h * 512:kk * D + (h + 1) * 512],
                        start=(kk == 0), stop=(kk == 15)),
                         reads=[ygb, arena], writes=[bank])
                p.op("vector", lambda E, bank=bank, h=h, ti=ti: E.scalar_tensor_tensor(
                    x3t[ti % 2][:, h * 512:(h + 1) * 512], bank[:], 1.0,
                    x2t[:, h * 512:(h + 1) * 512], ALU.mult, ALU.add),
                     reads=[bank, x2t], writes=[x3t[ti % 2]])
            emit_ln(p, y3b, x3t[ti % 2][:], scr, 1e-5)
            p.dma("gpsimd", x3sc[r0:r0 + 128, :], x3t[ti % 2][:])
            p.op("vector", lambda E: E.tensor_copy(y3c, y3b), reads=[y3b], writes=[y3c])
            for kk in range(8):
                p.op("tensor", lambda E, kk=kk: E.transpose(
                    psT[:], y3c[:, kk * 128:(kk + 1) * 128], ident[:]),
                     reads=[y3c, ident], writes=[psT])
                p.op("scalar", lambda E, kk=kk, ti=ti: E.copy(
                    y3T[:, kk * TB + ti * 128:kk * TB + (ti + 1) * 128], psT[:]),
                     reads=[psT], writes=[y3T])
            for kk in range(8):
                p.op("tensor", lambda E, kk=kk, ti=ti: E.matmul(
                    psR[:], y3T[:, kk * TB + ti * 128:kk * TB + (ti + 1) * 128],
                    wrT[:, kk * 8:(kk + 1) * 8], start=(kk == 0), stop=(kk == 7)),
                     reads=[y3T, wrT], writes=[psR])
            p.op("vector", lambda E: E.tensor_tensor(lgt[:], psR[:], brr[:], ALU.add),
                 reads=[psR, brr], writes=[lgt])
            p.op("vector", lambda E: E.reduce_max(scr[:, 0:1], lgt[:], axis=AX.X),
                 reads=[lgt], writes=[scr])
            p.op("vector", lambda E: E.tensor_scalar(rsc[:], lgt[:], scr[:, 0:1], None,
                                                     ALU.is_equal),
                 reads=[lgt, scr], writes=[rsc])
            p.op("vector", lambda E: E.scalar_tensor_tensor(lgt[:], rsc[:], -1e30, lgt[:],
                                                            ALU.mult, ALU.add),
                 reads=[rsc, lgt], writes=[lgt])
            p.op("vector", lambda E: E.reduce_max(scr[:, 1:2], lgt[:], axis=AX.X),
                 reads=[lgt], writes=[scr])
            p.op("vector", lambda E: E.tensor_tensor(scr[:, 2:3], scr[:, 1:2], scr[:, 0:1],
                                                     ALU.subtract),
                 reads=[scr], writes=[scr])
            p.op("scalar", lambda E: E.activation(scr[:, 2:3], scr[:, 2:3], AF.Exp),
                 reads=[scr], writes=[scr])
            p.op("vector", lambda E: E.tensor_scalar(scr[:, 3:4], scr[:, 2:3], 1.0, None,
                                                     ALU.add),
                 reads=[scr], writes=[scr])
            p.op("vector", lambda E: E.reciprocal(scr[:, 3:4], scr[:, 3:4]),
                 reads=[scr], writes=[scr])
            p.op("vector", lambda E: E.tensor_tensor(scr[:, 4:5], scr[:, 2:3], scr[:, 3:4],
                                                     ALU.mult),
                 reads=[scr], writes=[scr])
            p.op("vector", lambda E: E.tensor_scalar(lgt[:], lgt[:], scr[:, 1:2], None,
                                                     ALU.is_equal),
                 reads=[lgt, scr], writes=[lgt])
            p.op("vector", lambda E, ti=ti: E.tensor_scalar(cwt[ti][:], rsc[:],
                                                            scr[:, 3:4], None, ALU.mult),
                 reads=[rsc, scr], writes=[cwt[ti]])
            p.op("vector", lambda E, ti=ti: E.scalar_tensor_tensor(
                cwt[ti][:], lgt[:], scr[:, 4:5], cwt[ti][:], ALU.mult, ALU.add),
                 reads=[lgt, scr, cwt[ti]], writes=[cwt[ti]])
            p.op("vector", lambda E, ti=ti: E.memset(acct[ti][:], 0.0), writes=[acct[ti]])
        for e in range(8):
            p.dma("gpsimd", b2s[e % 2][:], b2r_d[e, :, :])
            for fch in range(32):
                g, j = fch // 4, fch % 4
                wbuf = w1big[g % 3]
                if j == 0:
                    p.op("sync", lambda E, e=e, g=g, wbuf=wbuf: E.dma_start(
                        out=wbuf[:], in_=w1sc[e, g]),
                         reads=[w1sc], writes=[wbuf], dma=True)
                bank = ps1 if fch % 2 == 0 else ps2
                for kk in range(8):
                    p.op("tensor", lambda E, kk=kk, j=j, wbuf=wbuf, bank=bank: E.matmul(
                        bank[:], wbuf[:, j * 1024 + kk * 128:j * 1024 + (kk + 1) * 128],
                        y3T[:, kk * TB:(kk + 1) * TB],
                        start=(kk == 0), stop=(kk == 7)),
                         reads=[wbuf, y3T], writes=[bank])
                p.op("scalar", lambda E, fch=fch, e=e, bank=bank: E.activation(
                    hT[:, fch * TB:(fch + 1) * TB], bank[:], AF.Gelu,
                    bias=b1s[:, e * 32 + fch:e * 32 + fch + 1]),
                     reads=[bank, b1s], writes=[hT])
            for hh in range(4):
                p.op("gpsimd", lambda E, e=e, hh=hh: E.dma_start(
                    out=w2s[:, hh * 8192:hh * 8192 + 8192],
                    in_=w2sc[e][:, hh * 8192:(hh + 1) * 8192]),
                     reads=[w2sc], writes=[arena], dma=True)
            for ti in range(4):
                for h in range(2):
                    bank = ps3 if h == 0 else ps4
                    for fch in range(32):
                        p.op("tensor", lambda E, fch=fch, ti=ti, h=h, bank=bank: E.matmul(
                            bank[:], hT[:, fch * TB + ti * 128:fch * TB + (ti + 1) * 128],
                            w2s[:, fch * D + h * 512:fch * D + (h + 1) * 512],
                            start=(fch == 0), stop=(fch == 31)),
                             reads=[hT, arena], writes=[bank])
                    p.op("vector", lambda E, bank=bank, ti=ti, h=h, e=e: E.scalar_tensor_tensor(
                        acct[ti][:, h * 512:(h + 1) * 512], bank[:],
                        cwt[ti][:, e:e + 1], acct[ti][:, h * 512:(h + 1) * 512],
                        ALU.mult, ALU.add),
                         reads=[bank, cwt[ti], acct[ti]], writes=[acct[ti]])
                p.op("vector", lambda E, ti=ti, e=e: E.scalar_tensor_tensor(
                    acct[ti][:], b2s[e % 2][:], cwt[ti][:, e:e + 1], acct[ti][:],
                    ALU.mult, ALU.add),
                     reads=[b2s[e % 2], cwt[ti], acct[ti]], writes=[acct[ti]])
        for ti in range(4):
            r0 = bb * TB + ti * 128
            p.dma("sync", x3t[ti % 2][:], x3sc[r0:r0 + 128, :])
            emit_ln(p, y3b, x3t[ti % 2][:], scr, 1e-5)
            p.op("vector", lambda E, ti=ti: E.tensor_tensor(zb[:], acct[ti][:], y3b,
                                                            ALU.add),
                 reads=[acct[ti], y3b], writes=[zb])
            emit_ln(p, znb[:], zb[:], scr, 1e-12)
            p.op("vector", lambda E: E.tensor_tensor(znb[:], znb[:], lnwr[:], ALU.mult),
                 reads=[znb, lnwr], writes=[znb])
            p.op("vector", lambda E: E.tensor_tensor(znb[:], znb[:], lnbr[:], ALU.add),
                 reads=[znb, lnbr], writes=[znb])
            p.op("vector", lambda E, ti=ti: E.tensor_tensor(znb[:], znb[:], x3t[ti % 2][:],
                                                            ALU.add),
                 reads=[znb, x3t[ti % 2]], writes=[znb])
            p.dma("gpsimd", o_x4[r0:r0 + 128, :], znb[:])


def emit_moe_stage(p, io, stg):
    w1_d = io["W1m"]; w2_d = io["W2m"]
    w1sc = io["w1sc"]; w2sc = io["w2sc"]
    for e in range(8):
        for hh in range(2):
            p.op("gpsimd", lambda E, e=e, hh=hh: E.dma_start(
                out=stg.rearrange("q (g m) -> q g m", m=4096),
                in_=w1_d[e][hh * 4:(hh + 1) * 4].rearrange("g q m -> q g m")),
                 reads=[w1_d], writes=[stg], dma=True)
            p.op("gpsimd", lambda E, e=e, hh=hh: E.dma_start(
                out=w1sc[e][hh * 4:(hh + 1) * 4].rearrange("g q m -> q g m"),
                in_=stg.rearrange("q (g m) -> q g m", m=4096)),
                 reads=[stg], writes=[w1sc], dma=True)
        for hh in range(4):
            p.op("gpsimd", lambda E, e=e, hh=hh: E.dma_start(
                out=stg[:, 0:8192], in_=w2_d[e][:, hh * 8192:(hh + 1) * 8192]),
                 reads=[w2_d], writes=[stg], dma=True)
            p.op("gpsimd", lambda E, e=e, hh=hh: E.dma_start(
                out=w2sc[e][:, hh * 8192:(hh + 1) * 8192], in_=stg[:, 0:8192]),
                 reads=[stg], writes=[w2sc], dma=True)


def build_L5():
    p = Prog("build_L5")
    io = {
        "x2": p.din("x2", [NTOK, D]),
        "y2": p.din("y2", [NTOK, D]),
        "ysm": p.din("ysm", [2048, NTOK]),
        "zgT": p.din("zgT", [2048, NTOK]),
        "Wo1": p.din("Wo1", [2048, D]),
        "Wr": p.din("Wr", [8, D]),
        "brr": p.din("brr", [128, 8]),
        "W1m": p.din("W1m", [8, 8, 128, 4096]),
        "b1s3": p.din("b1s3", [128, 8 * 32]),
        "W2m": p.din("W2m", [8, 128, 32 * 1024]),
        "b2r3": p.din("b2r3", [8, 128, D]),
        "lnw3r": p.din("lnw3r", [128, D]),
        "lnb3r": p.din("lnb3r", [128, D]),
        "x4": p.dout("x4", [NTOK, D]),
    }
    emit_L5(p, io)
    p.barrier_end()
    return p.realize()


def moe_w_host(d):
    w1h = np.ascontiguousarray(
        d["L1_ffn_W1"].reshape(8, 8, 4, 128, 8, 128).transpose(0, 1, 5, 2, 4, 3)
        .reshape(8, 8, 128, 4096))
    w2h = np.ascontiguousarray(
        d["L1_ffn_W2"].reshape(8, 1024, 32, 128).transpose(0, 3, 2, 1)
        .reshape(8, 128, 32 * 1024))
    return w1h, w2h


def run_L5(d, l3, l4c):
    nc = build_L5()
    w1h, w2h = moe_w_host(d)
    b1s3 = np.ascontiguousarray(
        np.transpose(d["L1_ffn_b1"].reshape(8, 32, 128), (2, 0, 1)).reshape(128, 256))
    brr = np.ascontiguousarray(np.broadcast_to(d["L1_ffn_br"][None, :], (128, 8)))
    b2r3 = np.ascontiguousarray(
        np.broadcast_to(d["L1_ffn_b2"][:, None, :], (8, 128, D)))
    lnw3r = np.ascontiguousarray(np.broadcast_to(d["L1_ffn_ln_w"][None, :], (128, D)))
    lnb3r = np.ascontiguousarray(np.broadcast_to(d["L1_ffn_ln_b"][None, :], (128, D)))
    ysm = l4c["yssm"]
    zgT = l3["zgT"]
    in_maps = []
    for c in range(NCORES):
        sl = slice(c * SEG, (c + 1) * SEG)
        in_maps.append({
            "x2": np.ascontiguousarray(l3["x2"][:, sl, :].reshape(NTOK, D)),
            "y2": np.ascontiguousarray(l3["y2"][:, sl, :].reshape(NTOK, D)),
            "ysm": np.ascontiguousarray(ysm[:, :, sl].reshape(2048, NTOK)),
            "zgT": np.ascontiguousarray(zgT[:, :, sl].reshape(2048, NTOK)),
            "Wo1": d["L1_attn_W_out"], "Wr": d["L1_ffn_Wr"], "brr": brr,
            "W1m": w1h, "b1s3": b1s3, "W2m": w2h,
            "b2r3": b2r3, "lnw3r": lnw3r, "lnb3r": lnb3r,
        })
    res = run_bass_kernel_spmd(nc, in_maps, core_ids=list(range(NCORES)))
    full = np.stack([res.results[c]["x4"].reshape(B, SEG, D) for c in range(NCORES)],
                    axis=1)
    return {"x4": full.reshape(B, NCORES * SEG, D)}


def kernel(**inputs):
    return run_fused(inputs)


LN8 = -2.0794415416798357

def emit_l1_fused(p, io):
    x0 = io["x0"]
    ident = make_identity(p)
    c_inv15 = p.constcol(1.0 / 15.0)
    c_15 = p.constcol(15.0)
    wq = load_weight_bf16(p, "wq", io["Wq"], D, 512)
    wk = load_weight_bf16(p, "wk", io["Wk"], D, 512)
    wv = load_weight_bf16(p, "wv", io["Wv"], D, D)
    wif = load_weight_bf16(p, "wif", io["Wif"], D, 16)
    wog = load_weight_bf16(p, "wog", io["Wog"], D, D)
    xs = p.sb("xs", [128, D]); ys = p.sb("ys", [128, D])
    cast = p.sb("cast", [128, D], BF16)
    scr = p.sb("scr", [128, D + 2])
    y0T = p.sb("y0T", [128, 8 * 512], BF16)
    stq = p.sb("stq", [128, 512], BF16); stk = p.sb("stk", [128, 512], BF16)
    stv = p.sb("stv", [128, D], BF16)
    sto = p.sb("sto", [128, D]); tmp = p.sb("tmpg", [128, D])
    stif = p.sb("stif", [16, 512])
    psT = p.ps("psT", [128, 128], BF16)
    ps_a = p.ps("ps_a", [128, 512]); ps_b = p.ps("ps_b", [128, 512])
    psI = p.ps("psI", [16, 512])

    for bb in range(NTOK // 512):
        t0 = bb * 512
        for ti in range(4):
            r0 = t0 + ti * 128
            p.dma("sync", xs[:], x0[r0:r0 + 128, :])
            emit_ln(p, ys[:], xs[:], scr, 1e-5)
            p.op("vector", lambda E: E.tensor_copy(cast[:], ys[:]), reads=[ys], writes=[cast])
            for kk in range(8):
                p.op("tensor", lambda E, kk=kk: E.transpose(psT[:], cast[:, kk * 128:(kk + 1) * 128], ident[:]),
                     reads=[cast, ident], writes=[psT])
                p.op("scalar", lambda E, kk=kk, ti=ti: E.copy(
                    y0T[:, kk * 512 + ti * 128:kk * 512 + (ti + 1) * 128], psT[:]),
                     reads=[psT], writes=[y0T])
        for pr in range(4):
            for which, wsb, scrout in (("q", wq, io["qT_scr"]), ("k", wk, io["kT_scr"])):
                st = stq if which == "q" else stk
                for kk in range(8):
                    p.op("tensor", lambda E, kk=kk, pr=pr, wsb=wsb: E.matmul(
                        ps_a[:], wsb[:, kk * 512 + pr * 128:kk * 512 + (pr + 1) * 128],
                        y0T[:, kk * 512:(kk + 1) * 512],
                        start=(kk == 0), stop=(kk == 7)),
                         reads=[wsb, y0T], writes=[ps_a])
                p.op("vector", lambda E, st=st: E.tensor_copy(st[:], ps_a[:]),
                     reads=[ps_a], writes=[st])
                p.dma("gpsimd", scrout[pr, :, t0:t0 + 512], st[:])
        for kk in range(8):
            p.op("tensor", lambda E, kk=kk: E.matmul(
                psI[:], wif[:, kk * 16:(kk + 1) * 16], y0T[:, kk * 512:(kk + 1) * 512],
                start=(kk == 0), stop=(kk == 7)),
                 reads=[wif, y0T], writes=[psI])
        p.op("vector", lambda E: E.tensor_copy(stif[:], psI[:]), reads=[psI], writes=[stif])
        p.dma("gpsimd", io["if_scr"][:, t0:t0 + 512], stif[:])
        for ti in range(4):
            r0 = t0 + ti * 128
            for h in range(2):
                bank = ps_a if h == 0 else ps_b
                for kk in range(8):
                    p.op("tensor", lambda E, kk=kk, ti=ti, h=h, bank=bank: E.matmul(
                        bank[:], y0T[:, kk * 512 + ti * 128:kk * 512 + (ti + 1) * 128],
                        wv[:, kk * D + h * 512:kk * D + (h + 1) * 512],
                        start=(kk == 0), stop=(kk == 7)),
                         reads=[y0T, wv], writes=[bank])
                p.op("vector", lambda E, bank=bank, h=h: E.tensor_copy(
                    stv[:, h * 512:(h + 1) * 512], bank[:]), reads=[bank], writes=[stv])
            p.dma("gpsimd", io["v_scr"][r0:r0 + 128, :], stv[:])
            for h in range(2):
                bank = ps_a if h == 0 else ps_b
                for kk in range(8):
                    p.op("tensor", lambda E, kk=kk, ti=ti, h=h, bank=bank: E.matmul(
                        bank[:], y0T[:, kk * 512 + ti * 128:kk * 512 + (ti + 1) * 128],
                        wog[:, kk * D + h * 512:kk * D + (h + 1) * 512],
                        start=(kk == 0), stop=(kk == 7)),
                         reads=[y0T, wog], writes=[bank])
                p.op("scalar", lambda E, bank=bank, h=h: E.activation(
                    tmp[:, h * 512:(h + 1) * 512], bank[:], AF.Tanh, scale=c_inv15),
                     reads=[bank, c_inv15], writes=[tmp])
            p.op("scalar", lambda E: E.activation(sto[:], tmp[:], AF.Sigmoid, scale=c_15),
                 reads=[tmp, c_15], writes=[sto])
            p.dma("gpsimd", io["og_scr"][r0:r0 + 128, :], sto[:])


def emit_mlstm_fused(p, io, phase, g):
    c_m1 = p.constcol(-1.0)
    c_1 = p.constcol(1.0)
    t = _T()
    identb = make_identity(p)
    t.identf = p.sb("identfp", [128, 128], F32)
    p.op("vector", lambda E: E.tensor_copy(t.identf[:], identb[:]),
         reads=[identb], writes=[t.identf])
    iop = p.sb("iop", [128, 128], F32)
    p.op("gpsimd", lambda E: E.iota(iop[:], pattern=[[0, 128]], channel_multiplier=1,
                                    allow_small_or_imprecise_dtypes=True), writes=[iop])
    t.sel0 = p.sb("sel0", [128, 128], F32)
    p.op("vector", lambda E: E.tensor_scalar(t.sel0[:], iop[:], 0.0, None, ALU.is_equal),
         reads=[iop], writes=[t.sel0])
    iop2 = p.sb("iop2", [128, 128], F32)
    p.op("gpsimd", lambda E: E.iota(iop2[:], pattern=[[1, 128]], channel_multiplier=-1,
                                    allow_small_or_imprecise_dtypes=True), writes=[iop2])
    t.trimask = p.sb("trimask", [128, 128], F32)
    p.op("vector", lambda E: E.tensor_scalar(t.trimask[:], iop2[:], 0.0, 0.125,
                                             ALU.is_ge, ALU.mult),
         reads=[iop2], writes=[t.trimask])

    t.qTb = p.sb("qTb", [64, SEG], BF16); t.kTb = p.sb("kTb", [64, SEG], BF16)
    t.vb = p.sb("vb", [128, SEG], BF16)
    t.i_r = p.sb("i_r", [1, SEG]); t.f_r = p.sb("f_r", [1, SEG])
    t.fb = p.sb("fb", [1, SEG]); t.g1 = p.sb("g1", [1, SEG])
    t.g2 = p.sb("g2", [1, SEG]); t.flog = p.sb("flog", [1, SEG])
    t.Fc = p.sb("Fcg", [1, SEG]); t.bg = p.sb("bgg", [1, SEG])
    t.M = p.sb("Mg", [128, SEG]); t.mp = p.sb("mpg", [1, SEG])
    t.nm = p.sb("nmg", [1, SEG])
    t.mstate = p.sb("mstate", [1, 1])
    t.gcols = p.sb("gcols", [128, 6])
    t.c_f32 = p.sb("c_f32", [64, 128]); t.c_bf = p.sb("c_bfb", [64, 128], BF16)
    t.wcol = p.sb("wcol", [128, 1]); t.dec = p.sb("dec", [64, 1])
    t.kw = p.sb("kwb", [128, 64], BF16)
    t.kf = p.sb("kf", [128, 64])
    t.psG = p.ps("psG", [128, 8], F32)
    t.psKt = p.ps("psKt", [128, 64], BF16)
    t.psC = p.ps("psC", [64, 128], F32)
    if phase == 2:
        t.W_sb = p.sb("W_sb", [128, 128]); t.Wm = p.sb("Wm", [128, 128])
        t.DSb = p.sb("DSbb", [128, 128], BF16)
        t.e_rep = p.sb("e_rep", [64, 128], BF16)
        t.qs = p.sb("qs", [64, 128], BF16)
        t.scr = p.sb("scrM", [128, 130])
        t.hbuf = p.sb("hbuf", [128, 128])
        t.hn_st = p.sb("hn_st", [128, SEG])
        t.lnw_rep = p.sb("lnw_rep", [128, 128]); t.lnb_rep = p.sb("lnb_rep", [128, 128])
        t.psMrep = p.ps("psMrep", [128, 128], F32)
        t.psSb = p.ps("psSb", [128, 128], F32)
        t.psH = p.ps("psH", [128, 128], F32)
    t.sc2 = p.sb("sc2", [1, 2])
    t.zro = p.sb("zro", [1, SEG])
    p.op("vector", lambda E: E.memset(t.M[:], 0.0), writes=[t.M])
    p.op("vector", lambda E: E.memset(t.zro[:], 0.0), writes=[t.zro])

    zrow = t.zro[0:1, :]
    for bh in range(16):
        b_, h_ = bh // 8, bh % 8
        pr, half = h_ // 2, h_ % 2
        sl_t = slice(b_ * SEG, (b_ + 1) * SEG)
        p.dma("sync", t.qTb[:], io["qT_scr"][pr, half * 64:half * 64 + 64, sl_t])
        p.dma("sync", t.kTb[:], io["kT_scr"][pr, half * 64:half * 64 + 64, sl_t])
        p.dma("sync", t.vb[:].rearrange("p (cc d) -> p cc d", d=128),
              io["v_scr"][sl_t, h_ * 128:(h_ + 1) * 128].rearrange(
                  "(cc p) d -> p cc d", p=128))
        p.dma("sync", t.i_r[0:1, :], io["if_scr"][h_:h_ + 1, sl_t])
        p.dma("sync", t.f_r[0:1, :], io["if_scr"][8 + h_:9 + h_, sl_t])
        if phase == 2:
            p.dma("sync", t.lnw_rep[:], io["lnw8"][h_, :, :])
            p.dma("sync", t.lnb_rep[:], io["lnb8"][h_, :, :])
        if phase == 1:
            p.op("vector", lambda E: E.memset(t.c_f32[:], 0.0), writes=[t.c_f32])
            p.op("vector", lambda E: E.memset(t.mstate[:], -1e30), writes=[t.mstate])
        else:
            p.op("vector", lambda E, bh=bh: E.tensor_copy(
                t.c_f32[:], g.c_in_sb[:, bh * 128:(bh + 1) * 128]),
                 reads=[g.c_in_sb], writes=[t.c_f32])
            p.op("vector", lambda E, bh=bh: E.tensor_copy(
                t.mstate[0:1, :], g.m_in_sb[0:1, bh:bh + 1]),
                 reads=[g.m_in_sb], writes=[t.mstate])
        R = lambda x: x[0:1, :]
        bicol = io["bif_sb"][h_:h_ + 1, 0:1] if False else None
        p.op("vector", lambda E, h_=h_: E.tensor_scalar(
            R(t.fb), R(t.f_r), g.bif[0:1, 8 + h_:9 + h_], None, ALU.add),
             reads=[t.f_r, g.bif], writes=[t.fb])
        p.op("vector", lambda E: E.tensor_scalar(R(t.g1), R(t.fb), -1.0, None, ALU.mult),
             reads=[t.fb], writes=[t.g1])
        p.op("vector", lambda E: E.tensor_tensor(R(t.g2), R(t.fb), R(t.g1), ALU.max),
             reads=[t.fb, t.g1], writes=[t.g2])
        p.op("scalar", lambda E: E.activation(R(t.g1), R(t.g2), AF.Exp, scale=c_m1[0:1, 0:1]),
             reads=[t.g2, c_m1], writes=[t.g1])
        p.op("scalar", lambda E: E.activation(R(t.g2), R(t.g1), AF.Ln, bias=c_1[0:1, 0:1]),
             reads=[t.g1, c_1], writes=[t.g2])
        p.op("vector", lambda E: E.tensor_scalar(R(t.g1), R(t.fb), 0.0, None, ALU.min),
             reads=[t.fb], writes=[t.g1])
        p.op("vector", lambda E: E.tensor_tensor(R(t.flog), R(t.g1), R(t.g2), ALU.subtract),
             reads=[t.g1, t.g2], writes=[t.flog])
        p.op("vector", lambda E: E.tensor_tensor_scan(R(t.Fc), R(t.flog), zrow,
                                                      0.0, ALU.add, ALU.add),
             reads=[t.flog, t.zro], writes=[t.Fc])
        p.op("vector", lambda E, h_=h_: E.tensor_scalar(
            R(t.g1), R(t.i_r), g.bif[0:1, h_:h_ + 1], None, ALU.add),
             reads=[t.i_r, g.bif], writes=[t.g1])
        p.op("vector", lambda E: E.tensor_tensor(R(t.bg), R(t.g1), R(t.Fc), ALU.subtract),
             reads=[t.g1, t.Fc], writes=[t.bg])
        p.op("vector", lambda E: E.tensor_tensor_scan(t.M[0:1, :], R(t.bg), R(t.bg),
                                                      t.mstate[0:1, 0:1], ALU.max, ALU.max),
             reads=[t.bg, t.mstate], writes=[t.M])
        for cc in range(SEG // C):
            c0 = cc * C
            if cc == 0:
                src_ap, src_rd = t.mstate[0:1, 0:1], t.mstate
            else:
                src_ap, src_rd = t.M[0:1, c0 - 1:c0], t.M
            p.op("vector", lambda E, c0=c0, src_ap=src_ap, src_rd=src_rd:
                 E.tensor_scalar(t.mp[0:1, c0:c0 + C], zrow[:, c0:c0 + C], src_ap,
                                 None, ALU.add),
                 reads=[t.zro, src_rd], writes=[t.mp])
            p.op("vector", lambda E, c0=c0: E.tensor_scalar(
                t.nm[0:1, c0:c0 + C], zrow[:, c0:c0 + C],
                t.M[0:1, c0 + C - 1:c0 + C], -1.0, ALU.add, ALU.mult),
                 reads=[t.zro, t.M], writes=[t.nm])
        p.op("vector", lambda E: E.tensor_tensor(t.sc2[0:1, 0:1], t.Fc[0:1, SEG - 1:SEG],
                                                 t.M[0:1, SEG - 1:SEG], ALU.add),
             reads=[t.Fc, t.M], writes=[t.sc2])
        for cc in range(SEG // C):
            c0 = cc * C
            sl = slice(c0, c0 + C)
            p.op("tensor", lambda E, sl=sl: E.transpose(t.psG[:, 0:1], t.bg[0:1, sl],
                                                        t.identf[0:1, 0:1]),
                 reads=[t.bg, t.identf], writes=[t.psG])
            p.op("tensor", lambda E, sl=sl: E.transpose(t.psG[:, 1:2], t.mp[0:1, sl],
                                                        t.identf[0:1, 0:1]),
                 reads=[t.mp, t.identf], writes=[t.psG])
            p.op("tensor", lambda E, sl=sl: E.transpose(t.psG[:, 2:3], t.nm[0:1, sl],
                                                        t.identf[0:1, 0:1]),
                 reads=[t.nm, t.identf], writes=[t.psG])
            p.op("scalar", lambda E: E.copy(t.gcols[:, 0:3], t.psG[:, 0:3]),
                 reads=[t.psG], writes=[t.gcols])
            if phase == 2:
                p.op("vector", lambda E: E.tensor_scalar(t.gcols[:, 3:4], t.gcols[:, 1:2],
                                                         LN8, None, ALU.add),
                     reads=[t.gcols], writes=[t.gcols])
            p.op("scalar", lambda E: E.activation(t.wcol[:], t.gcols[:, 0:1], AF.Exp,
                                                  bias=t.gcols[:, 2:3]),
                 reads=[t.gcols], writes=[t.wcol])
            p.op("scalar", lambda E: E.activation(t.dec[:], t.gcols[0:64, 1:2], AF.Exp,
                                                  bias=t.gcols[0:64, 2:3]),
                 reads=[t.gcols], writes=[t.dec])
            p.op("tensor", lambda E, sl=sl: E.transpose(t.psKt[:], t.kTb[:, sl],
                                                        identb[0:64, 0:64]),
                 reads=[t.kTb, identb], writes=[t.psKt])
            p.op("vector", lambda E: E.tensor_copy(t.kf[:], t.psKt[:]),
                 reads=[t.psKt], writes=[t.kf])
            p.op("vector", lambda E: E.tensor_scalar(t.kw[:], t.kf[:], t.wcol[:, 0:1],
                                                     None, ALU.mult),
                 reads=[t.kf, t.wcol], writes=[t.kw])
            if phase == 2:
                p.op("tensor", lambda E, sl=sl: E.matmul(t.psMrep[:], t.sel0[:], t.M[:, sl],
                                                         start=True, stop=True),
                     reads=[t.sel0, t.M], writes=[t.psMrep])
                p.op("scalar", lambda E: E.activation(t.W_sb[:], t.psMrep[:], AF.Exp,
                                                      scale=c_m1, bias=t.gcols[:, 0:1]),
                     reads=[t.psMrep, t.gcols, c_m1], writes=[t.W_sb])
                p.op("vector", lambda E: E.tensor_tensor(t.Wm[:], t.W_sb[:], t.trimask[:],
                                                         ALU.mult),
                     reads=[t.W_sb, t.trimask], writes=[t.Wm])
                p.op("tensor", lambda E, sl=sl: E.matmul(t.psSb[:], t.kTb[:, sl],
                                                         t.qTb[:, sl], start=True, stop=True),
                     reads=[t.kTb, t.qTb], writes=[t.psSb])
                p.op("vector", lambda E: E.tensor_tensor(t.DSb[:], t.Wm[:], t.psSb[:],
                                                         ALU.mult),
                     reads=[t.Wm, t.psSb], writes=[t.DSb])
                p.op("scalar", lambda E: E.activation(t.e_rep[:], t.psMrep[0:64, :], AF.Exp,
                                                      scale=c_m1[0:64, 0:1],
                                                      bias=t.gcols[0:64, 3:4]),
                     reads=[t.psMrep, t.gcols, c_m1], writes=[t.e_rep])
                p.op("vector", lambda E, sl=sl: E.tensor_tensor(t.qs[:], t.qTb[0:64, sl],
                                                                t.e_rep[:], ALU.mult),
                     reads=[t.qTb, t.e_rep], writes=[t.qs])
                p.op("vector", lambda E: E.tensor_copy(t.c_bf[:], t.c_f32[:]),
                     reads=[t.c_f32], writes=[t.c_bf])
                p.op("tensor", lambda E, cc=cc: E.matmul(t.psH[:], t.DSb[:],
                                                         t.vb[:, cc * 128:(cc + 1) * 128],
                                                         start=True, stop=False),
                     reads=[t.DSb, t.vb], writes=[t.psH])
                p.op("tensor", lambda E: E.matmul(t.psH[:], t.qs[:], t.c_bf[:],
                                                  start=False, stop=True),
                     reads=[t.qs, t.c_bf], writes=[t.psH])
            p.op("tensor", lambda E, cc=cc: E.matmul(t.psC[:], t.kw[:],
                                                     t.vb[:, cc * 128:(cc + 1) * 128],
                                                     start=True, stop=True),
                 reads=[t.kw, t.vb], writes=[t.psC])
            p.op("vector", lambda E: E.tensor_scalar(t.c_f32[:], t.c_f32[:], t.dec[:, 0:1],
                                                     None, ALU.mult),
                 reads=[t.c_f32, t.dec], writes=[t.c_f32])
            p.op("vector", lambda E: E.tensor_tensor(t.c_f32[:], t.c_f32[:], t.psC[:],
                                                     ALU.add),
                 reads=[t.c_f32, t.psC], writes=[t.c_f32])
            if phase == 2:
                scrm = t.scr
                p.op("vector", lambda E: E.reduce_sum(scrm[:, 0:1], t.psH[:], axis=AX.X),
                     reads=[t.psH], writes=[scrm])
                p.op("vector", lambda E: E.tensor_scalar(scrm[:, 0:1], scrm[:, 0:1],
                                                         1.0 / 128, None, ALU.mult),
                     reads=[scrm], writes=[scrm])
                p.op("vector", lambda E: E.tensor_scalar(t.hbuf[:], t.psH[:], scrm[:, 0:1],
                                                         None, ALU.subtract),
                     reads=[t.psH, scrm], writes=[t.hbuf])
                p.op("scalar", lambda E: E.activation(scrm[:, 2:130], t.hbuf[:], AF.Square),
                     reads=[t.hbuf], writes=[scrm])
                p.op("vector", lambda E: E.reduce_sum(scrm[:, 1:2], scrm[:, 2:130], axis=AX.X),
                     reads=[scrm], writes=[scrm])
                p.op("vector", lambda E: E.tensor_scalar(scrm[:, 1:2], scrm[:, 1:2],
                                                         1.0 / 128, 1e-6, ALU.mult, ALU.add),
                     reads=[scrm], writes=[scrm])
                p.op("scalar", lambda E: E.activation(scrm[:, 1:2], scrm[:, 1:2], AF.Sqrt),
                     reads=[scrm], writes=[scrm])
                p.op("vector", lambda E: E.reciprocal(scrm[:, 1:2], scrm[:, 1:2]),
                     reads=[scrm], writes=[scrm])
                p.op("vector", lambda E: E.scalar_tensor_tensor(t.hbuf[:], t.hbuf[:],
                                                                scrm[:, 1:2], t.lnw_rep[:],
                                                                ALU.mult, ALU.mult),
                     reads=[t.hbuf, scrm, t.lnw_rep], writes=[t.hbuf])
                p.op("vector", lambda E, c0=c0: E.tensor_tensor(
                    t.hn_st[:, c0:c0 + C], t.hbuf[:], t.lnb_rep[:], ALU.add),
                     reads=[t.hbuf, t.lnb_rep], writes=[t.hn_st])
        if phase == 1:
            p.dma("gpsimd", io["ccm_in"][bh:bh + 1, 0:8192].rearrange(
                "o (q f) -> (o q) f", q=64), t.c_f32[:])
            p.op("vector", lambda E: E.tensor_copy(t.sc2[0:1, 1:2], t.sc2[0:1, 0:1]),
                 reads=[t.sc2], writes=[t.sc2])
            p.op("vector", lambda E: E.tensor_copy(t.sc2[0:1, 0:1], t.Fc[0:1, SEG - 1:SEG]),
                 reads=[t.Fc, t.sc2], writes=[t.sc2])
            p.dma("gpsimd", io["ccm_in"][bh:bh + 1, 8192:8194], t.sc2[0:1, 0:2])
        else:
            p.dma("gpsimd",
                  io["hn_scr"][sl_t, h_ * 128:(h_ + 1) * 128].rearrange(
                      "(cc p) d -> p cc d", p=128),
                  t.hn_st[:].rearrange("p (cc d) -> p cc d", d=128))


def emit_combine_mlstm(p, io, g):
    scalF = p.sb("scalF", [64, 128]); scalM = p.sb("scalM", [64, 128])
    mzro = p.sb("mzro", [64, 8]); mneg = p.sb("mneg", [64, 8])
    c_j = p.sb("c_j", [64, 16 * 128])
    t1 = p.sb("t1c", [64, 16]); t2 = p.sb("t2c", [64, 16])
    w1 = p.sb("w1c", [64, 16]); w2 = p.sb("w2c", [64, 16])
    p.dma("sync", mzro[:], io["mzro64"])
    p.dma("sync", mneg[:], io["mneg64"])
    identb2 = make_identity(p, "identc")
    identf2 = p.sb("identf2", [128, 128], F32)
    p.op("vector", lambda E: E.tensor_copy(identf2[:], identb2[:]),
         reads=[identb2], writes=[identf2])
    ones64 = p.sb("ones64", [1, 64], F32)
    p.op("vector", lambda E: E.memset(ones64[:], 1.0), writes=[ones64])
    sc_raw = p.sb("sc_raw", [128, 2])
    sc_row = p.sb("sc_row", [1, 128])
    psScT = p.ps("psScT", [128, 128], F32)
    p.dma("sync", sc_raw[:], io["ccm_out"][:, 8192:8194])
    for col, dst in ((0, scalF), (1, scalM)):
        p.op("tensor", lambda E, col=col: E.transpose(psScT[0:1, 0:128],
                                                      sc_raw[:, col:col + 1], identf2[:]),
             reads=[sc_raw, identf2], writes=[psScT])
        p.op("vector", lambda E: E.tensor_copy(sc_row[:], psScT[0:1, 0:128]),
             reads=[psScT], writes=[sc_row])
        p.op("tensor", lambda E: E.matmul(psScT[0:64, 0:128], ones64[:], sc_row[:],
                                          start=True, stop=True),
             reads=[ones64, sc_row], writes=[psScT])
        p.op("vector", lambda E, dst=dst: E.tensor_copy(dst[:], psScT[0:64, 0:128]),
             reads=[psScT], writes=[dst])
    p.op("vector", lambda E: E.memset(g.m_in_sb[:], 0.0), writes=[g.m_in_sb])
    p.op("vector", lambda E: E.memset(g.c_in_sb[:], 0.0), writes=[g.c_in_sb])
    for j in range(8):
        p.dma("sync", c_j[:].rearrange("p (bh f) -> p bh f", f=128),
              io["ccm_out"][j * 16:(j + 1) * 16, 0:8192].rearrange(
                  "bh (p f) -> p bh f", p=64))
        p.op("vector", lambda E, j=j: E.tensor_scalar(t1[:], scalF[:, j * 16:(j + 1) * 16],
                                                      mzro[:, j:j + 1], None, ALU.mult),
             reads=[scalF, mzro], writes=[t1])
        p.op("vector", lambda E, j=j: E.tensor_scalar(t2[:], scalM[:, j * 16:(j + 1) * 16],
                                                      mneg[:, j:j + 1], None, ALU.add),
             reads=[scalM, mneg], writes=[t2])
        p.op("vector", lambda E: E.tensor_tensor(t1[:], g.m_in_sb[0:64, :], t1[:], ALU.add),
             reads=[g.m_in_sb, t1], writes=[t1])
        p.op("vector", lambda E: E.tensor_tensor(w1[:], t1[:], t2[:], ALU.max),
             reads=[t1, t2], writes=[w1])
        p.op("vector", lambda E: E.tensor_tensor(t1[:], t1[:], w1[:], ALU.subtract),
             reads=[t1, w1], writes=[t1])
        p.op("vector", lambda E: E.tensor_tensor(t2[:], t2[:], w1[:], ALU.subtract),
             reads=[t2, w1], writes=[t2])
        p.op("vector", lambda E: E.tensor_copy(g.m_in_sb[0:64, :], w1[:]),
             reads=[w1], writes=[g.m_in_sb])
        p.op("scalar", lambda E: E.activation(w1[:], t1[:], AF.Exp),
             reads=[t1], writes=[w1])
        p.op("scalar", lambda E: E.activation(w2[:], t2[:], AF.Exp),
             reads=[t2], writes=[w2])
        for bh in range(16):
            p.op("vector", lambda E, bh=bh: E.tensor_scalar(
                g.c_in_sb[:, bh * 128:(bh + 1) * 128],
                g.c_in_sb[:, bh * 128:(bh + 1) * 128], w1[:, bh:bh + 1], None, ALU.mult),
                 reads=[g.c_in_sb, w1], writes=[g.c_in_sb])
            p.op("vector", lambda E, bh=bh: E.scalar_tensor_tensor(
                g.c_in_sb[:, bh * 128:(bh + 1) * 128],
                c_j[:, bh * 128:(bh + 1) * 128], w2[:, bh:bh + 1],
                g.c_in_sb[:, bh * 128:(bh + 1) * 128], ALU.mult, ALU.add),
                 reads=[c_j, w2, g.c_in_sb], writes=[g.c_in_sb])


def emit_halo(p, io, g):
    mh = p.sb("mh", [128, 8])
    hall = p.sb("hall", [128, 8 * 16 * 6])
    p.dma("sync", mh[:], io["mhalo"])
    p.dma("sync", hall[:].rearrange("q (j e c) -> q j e c", e=16, c=6),
          io["cch_out"].rearrange("(j e q) c -> q j e c", q=128, e=16))
    for j in range(8):
        if j == 0:
            p.op("vector", lambda E, j=j: E.tensor_scalar(
                g.halo_sb[:], hall[:, j * 96:(j + 1) * 96], mh[:, j:j + 1], None, ALU.mult),
                 reads=[hall, mh], writes=[g.halo_sb])
        else:
            p.op("vector", lambda E, j=j: E.scalar_tensor_tensor(
                g.halo_sb[:], hall[:, j * 96:(j + 1) * 96], mh[:, j:j + 1],
                g.halo_sb[:], ALU.mult, ALU.add),
                 reads=[hall, mh, g.halo_sb], writes=[g.halo_sb])


def emit_mamba_p1(p, io, g):
    TC = 512
    iop = p.sb("iop", [128, 128], F32)
    p.op("gpsimd", lambda E: E.iota(iop[:], pattern=[[0, 128]], channel_multiplier=1,
                                    allow_small_or_imprecise_dtypes=True), writes=[iop])
    sel = p.sb("seln", [128, 16 * 128], F32)
    for n in range(16):
        p.op("vector", lambda E, n=n: E.tensor_scalar(
            sel[:, n * 128:(n + 1) * 128], iop[:], float(n), None, ALU.is_equal),
             reads=[iop], writes=[sel])
    dpar = p.sb("dpar_s", [128, 16])
    p.dma("sync", dpar[:], io["dpar16"])
    dtf = p.sb("dtf", [128, SEG]); xcf = p.sb("xcf", [128, SEG])
    Sf = p.sb("Sf", [128, SEG]); udt = p.sb("udt", [128, SEG])
    zer = p.sb("zer", [128, SEG])
    dAb = p.sb("dAb", [128, TC]); bmb = p.sb("bmb", [128, TC])
    hb = p.sb("hb", [128, TC]); ytmp = p.sb("ytmp", [128, TC])
    yacc = p.sb("yacc", [128, TC])
    Bsb = p.sb("Bsb", [128, SEG]); Csb = p.sb("Csb", [128, SEG])
    hstate = p.sb("hstate", [128, 256])
    sdtc = p.sb("sdtc", [128, 16])
    psBr = p.ps("psBr", [128, TC]); psCr = p.ps("psCr", [128, TC])
    p.op("vector", lambda E: E.memset(zer[:], 0.0), writes=[zer])
    p.op("vector", lambda E: E.memset(Bsb[:], 0.0), writes=[Bsb])
    p.op("vector", lambda E: E.memset(Csb[:], 0.0), writes=[Csb])
    for b_ in range(B):
        tok0 = b_ * SEG
        p.op("vector", lambda E: E.memset(hstate[:], 0.0), writes=[hstate])
        p.dma("sync", Bsb[0:16, :], io["Bf_scr"][:, tok0:tok0 + SEG])
        p.dma("sync", Csb[0:16, :], io["Cf_scr"][:, tok0:tok0 + SEG])
        for dch in range(16):
            p.dma("sync", dtf[:], io["dt_scr"][dch * 128:(dch + 1) * 128, tok0:tok0 + SEG])
            p.dma("sync", xcf[:], io["xc_scr"][dch * 128:(dch + 1) * 128, tok0:tok0 + SEG])
            p.op("vector", lambda E: E.tensor_tensor_scan(Sf[:], dtf[:], zer[:],
                                                          0.0, ALU.add, ALU.add),
                 reads=[dtf, zer], writes=[Sf])
            p.dma("gpsimd", io["S_scr"][dch * 128:(dch + 1) * 128, tok0:tok0 + SEG], Sf[:])
            p.op("vector", lambda E, dch=dch: E.tensor_copy(sdtc[:, dch:dch + 1],
                                                            Sf[:, SEG - 1:SEG]),
                 reads=[Sf], writes=[sdtc])
            p.op("vector", lambda E: E.tensor_tensor(udt[:], dtf[:], xcf[:], ALU.mult),
                 reads=[dtf, xcf], writes=[udt])
            for tc in range(SEG // TC):
                t0 = tc * TC
                for n in range(16):
                    scol = dch * 16 + n
                    p.op("scalar", lambda E, dch=dch, n=n, t0=t0: E.activation(
                        dAb[:], dtf[:, t0:t0 + TC], AF.Exp,
                        scale=g.A_sb[:, dch * 16 + n:dch * 16 + n + 1]),
                         reads=[dtf, g.A_sb], writes=[dAb])
                    p.op("tensor", lambda E, n=n, t0=t0: E.matmul(
                        psBr[:], sel[:, n * 128:(n + 1) * 128], Bsb[:, t0:t0 + TC],
                        start=True, stop=True),
                         reads=[sel, Bsb], writes=[psBr])
                    p.op("vector", lambda E, t0=t0: E.tensor_tensor(
                        bmb[:], udt[:, t0:t0 + TC], psBr[:], ALU.mult),
                         reads=[udt, psBr], writes=[bmb])
                    p.op("vector", lambda E, scol=scol: E.tensor_tensor_scan(
                        hb[:], dAb[:], bmb[:], hstate[:, scol:scol + 1],
                        ALU.mult, ALU.add),
                         reads=[dAb, bmb, hstate], writes=[hb])
                    p.op("vector", lambda E, scol=scol: E.tensor_copy(
                        hstate[:, scol:scol + 1], hb[:, TC - 1:TC]),
                         reads=[hb], writes=[hstate])
                    p.op("tensor", lambda E, n=n, t0=t0: E.matmul(
                        psCr[:], sel[:, n * 128:(n + 1) * 128], Csb[:, t0:t0 + TC],
                        start=True, stop=True),
                         reads=[sel, Csb], writes=[psCr])
                    if n == 0:
                        p.op("vector", lambda E: E.tensor_tensor(yacc[:], hb[:], psCr[:],
                                                                 ALU.mult),
                             reads=[hb, psCr], writes=[yacc])
                    else:
                        p.op("vector", lambda E: E.tensor_tensor(ytmp[:], hb[:], psCr[:],
                                                                 ALU.mult),
                             reads=[hb, psCr], writes=[ytmp])
                        p.op("vector", lambda E: E.tensor_tensor(yacc[:], yacc[:], ytmp[:],
                                                                 ALU.add),
                             reads=[yacc, ytmp], writes=[yacc])
                p.op("vector", lambda E, dch=dch, t0=t0: E.scalar_tensor_tensor(
                    yacc[:], xcf[:, t0:t0 + TC], dpar[:, dch:dch + 1], yacc[:],
                    ALU.mult, ALU.add),
                     reads=[xcf, dpar, yacc], writes=[yacc])
                p.dma("gpsimd", io["yssm_scr"][dch * 128:(dch + 1) * 128,
                                               tok0 + t0:tok0 + t0 + TC], yacc[:])
        ccs4 = io["ccs_in"].rearrange("(bb dch q) c -> q bb dch c", bb=2, q=128)
        comb = p.sb(f"comb{b_}", [128, 16 * 17])
        cv = comb[:].rearrange("q (dch c) -> q dch c", c=17)
        p.op("vector", lambda E, cv=cv: E.tensor_copy(
            cv[:, :, 0:16], hstate[:].rearrange("q (dch n) -> q dch n", n=16)),
             reads=[hstate], writes=[comb])
        p.op("vector", lambda E, cv=cv: E.tensor_copy(
            cv[:, :, 16:17], sdtc[:].rearrange("q (dch o) -> q dch o", o=1)),
             reads=[sdtc], writes=[comb])
        p.dma("gpsimd", ccs4[:, b_, :, 0:17], cv)


def emit_combine_mamba(p, io, g):
    m01 = p.sb("m01", [128, 8])
    hs_j = p.sb("hs_j", [128, 16 * 17])
    Ej = p.sb("Ej", [128, 256]); Dj = p.sb("Dj", [128, 256])
    p.dma("sync", m01[:], io["m01"])
    for b_ in range(B):
        hin = g.h_in_sb[:, b_ * 256:(b_ + 1) * 256]
        p.op("vector", lambda E, hin=hin: E.memset(hin, 0.0), writes=[g.h_in_sb])
        for j in range(8):
            rows = slice(j * 4096 + b_ * 2048, j * 4096 + (b_ + 1) * 2048)
            p.dma("sync", hs_j[:].rearrange("q (dch c) -> q dch c", c=17),
                  io["ccs_out"][rows, :].rearrange("(dch q) c -> q dch c", q=128))
            hsv = hs_j[:].rearrange("q (dch c) -> q dch c", c=17)
            sdt_j = hsv[:, :, 16]
            h_j = hsv[:, :, 0:16]
            p.op("vector", lambda E, j=j, sdt_j=sdt_j: E.tensor_tensor(
                Ej[:].rearrange("q (dch n) -> q dch n", n=16),
                g.A_sb[:].rearrange("q (dch n) -> q dch n", n=16),
                sdt_j.rearrange("q dch -> q dch ()").broadcast_to((128, 16, 16)),
                ALU.mult),
                 reads=[g.A_sb, hs_j], writes=[Ej])
            p.op("vector", lambda E, j=j: E.tensor_scalar(Ej[:], Ej[:], m01[:, j:j + 1],
                                                          None, ALU.mult),
                 reads=[Ej, m01], writes=[Ej])
            p.op("scalar", lambda E: E.activation(Dj[:], Ej[:], AF.Exp),
                 reads=[Ej], writes=[Dj])
            p.op("vector", lambda E, j=j, h_j=h_j: E.tensor_scalar(
                h_j, h_j, m01[:, j:j + 1], None, ALU.mult),
                 reads=[hs_j, m01], writes=[hs_j])
            p.op("vector", lambda E, hin=hin: E.tensor_tensor(hin, hin, Dj[:], ALU.mult),
                 reads=[g.h_in_sb, Dj], writes=[g.h_in_sb])
            p.op("vector", lambda E, hin=hin, h_j=h_j: E.tensor_tensor(
                hin.rearrange("q (dch n) -> q dch n", n=16), 
                hin.rearrange("q (dch n) -> q dch n", n=16), h_j, ALU.add),
                 reads=[g.h_in_sb, hs_j], writes=[g.h_in_sb])


def emit_mamba_p2(p, io, g):
    TC = 512
    iop = p.sb("iop", [128, 128], F32)
    p.op("gpsimd", lambda E: E.iota(iop[:], pattern=[[0, 128]], channel_multiplier=1,
                                    allow_small_or_imprecise_dtypes=True), writes=[iop])
    sel = p.sb("seln", [128, 16 * 128], F32)
    for n in range(16):
        p.op("vector", lambda E, n=n: E.tensor_scalar(
            sel[:, n * 128:(n + 1) * 128], iop[:], float(n), None, ALU.is_equal),
             reads=[iop], writes=[sel])
    Sf = p.sb("Sf", [128, 512])
    CnB = p.sb("CnB", [128, 16 * 512])
    Csb = p.sb("Csb", [128, SEG])
    P = p.sb("Pp", [128, TC]); hp = p.sb("hp", [128, TC])
    ycn = p.sb("ycn", [128, TC]); yacc = p.sb("yacc", [128, TC])
    ylo = p.sb("ylo", [128, TC])
    psCr = p.ps("psCr", [128, TC])
    p.op("vector", lambda E: E.memset(Csb[:], 0.0), writes=[Csb])
    for b_ in range(B):
        tok0 = b_ * SEG
        p.dma("sync", Csb[0:16, :], io["Cf_scr"][:, tok0:tok0 + SEG])
        for tc in range(SEG // TC):
            t0 = tc * TC
            for n in range(16):
                p.op("tensor", lambda E, n=n, t0=t0: E.matmul(
                    psCr[:], sel[:, n * 128:(n + 1) * 128], Csb[:, t0:t0 + TC],
                    start=True, stop=True),
                     reads=[sel, Csb], writes=[psCr])
                p.op("scalar", lambda E, n=n: E.copy(
                    CnB[:, n * TC:(n + 1) * TC], psCr[:]),
                     reads=[psCr], writes=[CnB])
            for dch in range(16):
                p.dma("sync", Sf[:], io["S_scr"][dch * 128:(dch + 1) * 128,
                                                 tok0 + t0:tok0 + t0 + TC])
                for n in range(16):
                    hcol = b_ * 256 + dch * 16 + n
                    p.op("scalar", lambda E, dch=dch, n=n: E.activation(
                        P[:], Sf[:], AF.Exp,
                        scale=g.A_sb[:, dch * 16 + n:dch * 16 + n + 1]),
                         reads=[Sf, g.A_sb], writes=[P])
                    p.op("vector", lambda E, hcol=hcol: E.tensor_scalar(
                        hp[:], P[:], g.h_in_sb[:, hcol:hcol + 1], None, ALU.mult),
                         reads=[P, g.h_in_sb], writes=[hp])
                    if n == 0:
                        p.op("vector", lambda E, n=n: E.tensor_tensor(
                            yacc[:], hp[:], CnB[:, n * TC:(n + 1) * TC], ALU.mult),
                             reads=[hp, CnB], writes=[yacc])
                    else:
                        p.op("vector", lambda E, n=n: E.tensor_tensor(
                            ycn[:], hp[:], CnB[:, n * TC:(n + 1) * TC], ALU.mult),
                             reads=[hp, CnB], writes=[ycn])
                        p.op("vector", lambda E: E.tensor_tensor(yacc[:], yacc[:], ycn[:],
                                                                 ALU.add),
                             reads=[yacc, ycn], writes=[yacc])
                p.dma("sync", ylo[:], io["yssm_scr"][dch * 128:(dch + 1) * 128,
                                                     tok0 + t0:tok0 + t0 + TC])
                p.op("vector", lambda E: E.tensor_tensor(ylo[:], ylo[:], yacc[:], ALU.add),
                     reads=[ylo, yacc], writes=[ylo])
                p.dma("gpsimd", io["yssm_scr"][dch * 128:(dch + 1) * 128,
                                               tok0 + t0:tok0 + t0 + TC], ylo[:])


def build_fused():
    p = Prog("fused")
    io = {}
    E_ = lambda n, shape, dt=F32: io.setdefault(n, p.din(n, shape, dt))
    S_ = lambda n, shape, dt=F32: io.setdefault(n, p.dtmp(n, shape, dt))
    E_("x0", [NTOK, D])
    for n, shape in [("Wq", [D, 512]), ("Wk", [D, 512]), ("Wv", [D, D]),
                     ("Wif", [D, 16]), ("Wog", [D, D]), ("Wout", [D, D]),
                     ("U1", [D, 256]), ("V1", [256, 4096]), ("U2", [4096, 256]),
                     ("V2", [256, D]), ("W_in", [D, 4096]),
                     ("b1s", [128, 32]), ("b2r", [128, D]),
                     ("lnwr", [128, D]), ("lnbr", [128, D]),
                     ("lnw8", [8, 128, 128]), ("lnb8", [8, 128, 128]),
                     ("bif_in", [1, 16]),
                     ("cw4", [128, 16, 4]), ("cb4", [128, 16]),
                     ("Wxp", [2048, 96]), ("Wdt", [64, 2048]), ("bdt", [128, 16]),
                     ("dpar16", [128, 16]), ("alog16", [128, 256]),
                     ("Wo1", [2048, D]), ("Wr", [8, D]), ("brr", [128, 8]),
                     ("W1m", [8, 8, 128, 4096]), ("b1s3", [128, 8 * 32]),
                     ("W2m", [8, 128, 32 * 1024]), ("b2r3", [8, 128, D]),
                     ("lnw3r", [128, D]), ("lnb3r", [128, D]),
                     ("mzro64", [64, 8]), ("mneg64", [64, 8]),
                     ("mhalo", [128, 8]), ("m01", [128, 8])]:
        E_(n, shape)
    io["x4"] = p.dout("x4", [NTOK, D])
    S_("qT_scr", [4, 128, NTOK], BF16); S_("kT_scr", [4, 128, NTOK], BF16)
    S_("v_scr", [NTOK, D], BF16)
    S_("if_scr", [16, NTOK]); S_("og_scr", [NTOK, D]); S_("hn_scr", [NTOK, D])
    S_("x2", [NTOK, D]); S_("y2", [NTOK, D])
    S_("xinT", [2048, NTOK]); S_("zgT", [2048, NTOK])
    S_("dtT", [2048, NTOK]); S_("xcT", [2048, NTOK])
    S_("Bf", [16, NTOK]); S_("Cf", [16, NTOK])
    S_("S_scr", [2048, NTOK]); S_("yssm_scr", [2048, NTOK])
    S_("w1sc", [8, 8, 128, 4096], BF16); S_("w2sc", [8, 128, 32 * 1024], BF16)
    S_("ccm_in", [16, 8194]); S_("ccm_out", [128, 8194])
    S_("cch_in", [2048, 6]); S_("cch_out", [16384, 6])
    S_("ccs_in", [4096, 17]); S_("ccs_out", [32768, 17])
    io["hn"] = io["hn_scr"]; io["og"] = io["og_scr"]
    io["xinh"] = io["xinT"]
    io["dt_scr"] = io["dtT"]; io["xc_scr"] = io["xcT"]
    io["Bf_scr"] = io["Bf"]; io["Cf_scr"] = io["Cf"]
    io["ysm"] = io["yssm_scr"]

    class G:
        pass
    g = G()
    g.m_in_sb = p.sb("g_m_in", [64, 16])
    g.c_in_sb = p.sb("g_c_in", [64, 16 * 128])
    g.h_in_sb = p.sb("g_h_in", [128, 2 * 256])
    g.A_sb = p.sb("g_A", [128, 256])
    g.halo_sb = p.sb("g_halo", [128, 16 * 6])
    g.bif = p.sb("g_bif", [1, 16])
    for v in (-1.0, 1.0, 1.0 / 15.0, 15.0, GELU_C):
        p.constcol(v)
    p.dma("sync", g.bif[0:1, :], io["bif_in"])
    p.dma("sync", g.A_sb[:], io["alog16"])
    p.op("scalar", lambda E: E.activation(g.A_sb[:], g.A_sb[:], AF.Exp),
         reads=[g.A_sb], writes=[g.A_sb])
    p.op("vector", lambda E: E.tensor_scalar(g.A_sb[:], g.A_sb[:], -1.0, None, ALU.mult),
         reads=[g.A_sb], writes=[g.A_sb])

    def sect(pfx, fn, *a):
        p.pfx = pfx
        p.section_open()
        fn(*a)
        p.section_close()
        p.pfx = ""

    def collective(cin, cout):
        p.op("gpsimd", lambda E: E.collective_compute(
            "AllGather", ALU.bypass, replica_groups=[list(range(NCORES))],
            ins=[cin.opt()], outs=[cout.opt()]),
             reads=[cin], writes=[cout])
        p.barrier_all()

    sect("a_", emit_l1_fused, p, io)
    sect("b_", emit_mlstm_fused, p, io, 1, g)
    collective(io["ccm_in"], io["ccm_out"])
    sect("c_", emit_combine_mlstm, p, io, g)
    sect("d_", emit_mlstm_fused, p, io, 2, g)
    sect("e_", emit_L3, p, io)
    for b_ in range(B):
        p.dma("sync", io["cch_in"][:, b_ * 3:(b_ + 1) * 3],
              io["xinT"][:, b_ * SEG + SEG - 3:b_ * SEG + SEG])
    collective(io["cch_in"], io["cch_out"])
    sect("f_", emit_halo, p, io, g)
    io["halo_sb"] = g.halo_sb
    def g_with_stage():
        stg = p.sb("moestg", [128, 16384], BF16)
        emit_L4(p, io)
        emit_moe_stage(p, io, stg[:])
    sect("g_", g_with_stage)
    sect("h_", emit_mamba_p1, p, io, g)
    collective(io["ccs_in"], io["ccs_out"])
    sect("i_", emit_combine_mamba, p, io, g)
    sect("j_", emit_mamba_p2, p, io, g)
    sect("k_", emit_L5, p, io)
    p.barrier_end()
    return p.realize()


def run_fused(d):
    x0 = d["L0_attn_x"]
    wif = np.ascontiguousarray(np.concatenate([d["L0_attn_Wi"], d["L0_attn_Wf"]], 1))
    lnw8 = np.ascontiguousarray(np.broadcast_to(
        d["L0_attn_ln_w"][:, None, :], (8, 128, 128)))
    lnb8 = np.ascontiguousarray(np.broadcast_to(
        d["L0_attn_ln_b"][:, None, :], (8, 128, 128)))
    bif = np.ascontiguousarray(np.concatenate(
        [d["L0_attn_bi"], d["L0_attn_bf"]])[None, :])
    b1s = np.ascontiguousarray(d["L0_ffn_b1"].reshape(32, 128).T)
    b2r = np.ascontiguousarray(np.broadcast_to(d["L0_ffn_b2"][None, :], (128, D)))
    lnwr = np.ascontiguousarray(np.broadcast_to(d["L1_attn_ln_w"][None, :], (128, D)))
    lnbr = np.ascontiguousarray(np.broadcast_to(d["L1_attn_ln_b"][None, :], (128, D)))
    cw4 = np.ascontiguousarray(
        np.transpose(d["L1_attn_conv_w"].reshape(16, 128, 1, 4)[:, :, 0, :], (1, 0, 2)))
    cb4 = np.ascontiguousarray(d["L1_attn_conv_b"].reshape(16, 128).T)
    bdt = np.ascontiguousarray(d["L1_attn_b_dt"].reshape(16, 128).T)
    dpar16 = np.ascontiguousarray(d["L1_attn_D_param"].reshape(16, 128).T)
    alog16 = np.ascontiguousarray(
        np.transpose(d["L1_attn_A_log"].reshape(16, 128, 16), (1, 0, 2)).reshape(128, 256))
    b1s3 = np.ascontiguousarray(
        np.transpose(d["L1_ffn_b1"].reshape(8, 32, 128), (2, 0, 1)).reshape(128, 256))
    brr = np.ascontiguousarray(np.broadcast_to(d["L1_ffn_br"][None, :], (128, 8)))
    b2r3 = np.ascontiguousarray(np.broadcast_to(d["L1_ffn_b2"][:, None, :], (8, 128, D)))
    lnw3r = np.ascontiguousarray(np.broadcast_to(d["L1_ffn_ln_w"][None, :], (128, D)))
    lnb3r = np.ascontiguousarray(np.broadcast_to(d["L1_ffn_ln_b"][None, :], (128, D)))
    w1h, w2h = moe_w_host(d)
    nc = build_fused()
    in_maps = []
    for c in range(NCORES):
        sl = slice(c * SEG, (c + 1) * SEG)
        mzro = np.zeros((64, 8), np.float32); mzro[:, :c] = 1.0
        mneg = np.full((64, 8), -1e30, np.float32); mneg[:, :c] = 0.0
        mhalo = np.zeros((128, 8), np.float32)
        if c > 0:
            mhalo[:, c - 1] = 1.0
        m01 = np.zeros((128, 8), np.float32); m01[:, :c] = 1.0
        in_maps.append({
            "x0": np.ascontiguousarray(x0[:, sl, :].reshape(NTOK, D)),
            "Wq": d["L0_attn_Wq"], "Wk": d["L0_attn_Wk"], "Wv": d["L0_attn_Wv"],
            "Wif": wif, "Wog": d["L0_attn_Wog"], "Wout": d["L0_attn_Wout"],
            "U1": d["L0_ffn_U1"], "V1": d["L0_ffn_V1"], "U2": d["L0_ffn_U2"],
            "V2": d["L0_ffn_V2"], "W_in": d["L1_attn_W_in"],
            "b1s": b1s, "b2r": b2r, "lnwr": lnwr, "lnbr": lnbr,
            "lnw8": lnw8, "lnb8": lnb8, "bif_in": bif,
            "cw4": cw4, "cb4": cb4, "Wxp": d["L1_attn_W_xproj"],
            "Wdt": d["L1_attn_W_dt"], "bdt": bdt,
            "dpar16": dpar16, "alog16": alog16,
            "Wo1": d["L1_attn_W_out"], "Wr": d["L1_ffn_Wr"], "brr": brr,
            "W1m": w1h, "b1s3": b1s3, "W2m": w2h,
            "b2r3": b2r3, "lnw3r": lnw3r, "lnb3r": lnb3r,
            "mzro64": mzro, "mneg64": mneg, "mhalo": mhalo, "m01": m01,
        })
    res = run_bass_kernel_spmd(nc, in_maps, core_ids=list(range(NCORES)))
    full = np.stack([res.results[c]["x4"].reshape(B, SEG, D) for c in range(NCORES)],
                    axis=1)
    return np.ascontiguousarray(full.reshape(B, NCORES * SEG, D).astype(np.float32))
